# version 02
# baseline (speedup 1.0000x reference)
import sys
import contextlib

import numpy as np
import ml_dtypes

sys.path.insert(0, "/opt/trn_rl_repo")
import concourse.bass as bass
import concourse.mybir as mybir
from concourse.bass_utils import run_bass_kernel_spmd

F32 = mybir.dt.float32
BF16 = mybir.dt.bfloat16
ALU = mybir.AluOpType
ACT = mybir.ActivationFunctionType
AXX = mybir.AxisListType.X

LN2 = float(np.log(2.0))
K1 = float(3 * 2**22)
K2 = float(3 * 2**51)
EPS = 1e-5

T_SEQ = 8192
D = 1024
HID = 4096
TOK = 2048
NT = 16
NSUP = 4
NSEQT = 64
NC = 128

ENGS = ("sync", "gpsimd", "tensor", "vector", "scalar")


class Sched:

    def __init__(self):
        self.q = {e: [] for e in ENGS}
        self.cnt = {}

    def wait(self, eng, sem_name):
        v = self.cnt.get(sem_name, 0)
        if v == 0:
            return
        self.q[eng].append(("w", sem_name, v))

    def wait_at(self, eng, sem_name, v):
        if v > 0:
            self.q[eng].append(("w", sem_name, v))

    def op(self, eng, fn, inc=None, amt=1):
        self.q[eng].append(("o", fn, inc, amt))
        if inc is not None:
            self.cnt[inc] = self.cnt.get(inc, 0) + amt

    def c(self, sem_name):
        return self.cnt.get(sem_name, 0)


def build_nc():
    nc = bass.Bass("TRN2", target_bir_lowering=False)

    xseq = nc.dram_tensor("xseq", [T_SEQ, D], F32, kind="ExternalInput").ap()
    xown = nc.dram_tensor("xown", [TOK, D], F32, kind="ExternalInput").ap()
    wq_in = nc.dram_tensor("wq_in", [D, D], F32, kind="ExternalInput").ap()
    wk_in = nc.dram_tensor("wk_in", [D, D], F32, kind="ExternalInput").ap()
    wv_in = nc.dram_tensor("wv_in", [D, D], F32, kind="ExternalInput").ap()
    wo_in = nc.dram_tensor("wo_in", [D, D], F32, kind="ExternalInput").ap()
    bqr_in = nc.dram_tensor("bqr_in", [128, 8], F32, kind="ExternalInput").ap()
    bkr_in = nc.dram_tensor("bkr_in", [128, 8], F32, kind="ExternalInput").ap()
    bv_in = nc.dram_tensor("bv_in", [1, D], F32, kind="ExternalInput").ap()
    bo_in = nc.dram_tensor("bo_in", [1, D], F32, kind="ExternalInput").ap()
    bfc_in = nc.dram_tensor("bfc_in", [1, HID], F32, kind="ExternalInput").ap()
    bpj_in = nc.dram_tensor("bpj_in", [1, D], F32, kind="ExternalInput").ap()
    wfc_in = nc.dram_tensor("wfc_in", [HID, D], F32, kind="ExternalInput").ap()
    wpj_in = nc.dram_tensor("wpj_in", [D, HID], F32, kind="ExternalInput").ap()
    mask_in = nc.dram_tensor("mask_in", [TOK, NC], BF16, kind="ExternalInput").ap()
    y_out = nc.dram_tensor("y", [TOK, D], F32, kind="ExternalOutput").ap()
    xmid_hbm = nc.dram_tensor("xmid_scr", [TOK, D], F32).ap()

    ident_h = nc.inline_tensor(np.eye(128, dtype=ml_dtypes.bfloat16), name="ident_h")
    base3_np = np.zeros((128, 3), dtype=ml_dtypes.bfloat16)
    base3_np[0, 0] = 1.0
    base3_np[1:65, 1] = 1.0
    base3_np[65:128, 2] = 1.0
    base3_h = nc.inline_tensor(base3_np, name="base3_h")
    ilen_np = np.full((128, 1), 1.0 / 64.0, dtype=np.float32)
    ilen_np[0, 0] = 1.0
    ilen_h = nc.inline_tensor(ilen_np, name="ilen_h")

    es = contextlib.ExitStack()
    S = Sched()

    with es:
        sb = lambda name, shape, dt: es.enter_context(nc.sbuf_tensor(name, shape, dt))
        psb = lambda name, shape: es.enter_context(nc.psum_tensor(name, shape, F32))

        ident = sb("ident", [128, 128], BF16)
        base3 = sb("base3", [128, 3], BF16)
        ilen = sb("ilen", [128, 1], F32)
        zeros_b = sb("zeros_b", [128, 128], BF16)
        ones_col = sb("ones_col", [128, 1], BF16)
        ones_row = sb("ones_row", [1, 128], BF16)
        mask_sb = sb("mask_sb", [128, NT * 128], BF16)
        bqr = sb("bqr", [128, 8], F32)
        bkr = sb("bkr", [128, 8], F32)

        Dt = sb("Dt", [128, 1], F32)
        DK2pp = sb("DK2pp", [128, 1], F32)
        EB1pp = sb("EB1pp", [128, 1], F32)

        bnscr = sb("bnscr", [128, 12], F32)
        mv_seq = sb("mv_seq", [128, 2 * NSEQT], F32)
        a_seq = sb("a_seq", [128, NSEQT], F32)
        am_seq = sb("am_seq", [128, NSEQT], F32)
        lnv_scr = sb("lnv_scr", [128, NSEQT], F32)
        mv_own = sb("mv_own", [128, 2 * NT], F32)
        a_own = sb("a_own", [128, NT], F32)
        nam_own = sb("nam_own", [128, NT], F32)

        summ_cm = sb("summ_cm", [128, D], BF16)
        summ_fm = sb("summ_fm", [128, 8 * 128], BF16)
        kfm = sb("kfm", [128, 8 * 128], BF16)
        vcm = sb("vcm", [128, D], BF16)
        sum_bias = sb("sum_bias", [128, 1], F32)

        wfcTq = sb("wfcTq", [128, 8 * HID], BF16)
        wpjTq = sb("wpjTq", [128, 32 * D], BF16)

        wbuf = sb("wbuf", [128, 4 * D], BF16)

        qa_amax = sb("qa_amax", [128, 128], F32)
        qa_e = sb("qa_e", [128, 128], F32)
        qa_s0 = sb("qa_s0", [128, 128], F32)
        qa_s = sb("qa_s", [128, 128], F32)
        qa_m = sb("qa_m", [128, 128], F32)
        qa_eb = sb("qa_eb", [128, 128], F32)

        a2c = sb("a2c", [128, 1], F32)
        nam2c = sb("nam2c", [128, 1], F32)
        lnv2 = sb("lnv2", [128, 1], F32)
        bo_r = sb("bo_r", [1, D], BF16)
        patt0 = sb("patt0", [128, 128], BF16)
        patt1 = sb("patt1", [128, 128], BF16)
        meanb = sb("meanb", [128, 1], BF16)
        ssum = sb("ssum", [128, 16], F32)
        srcp = sb("srcp", [128, 16], F32)

        es_pro = contextlib.ExitStack()
        sbp = lambda name, shape, dt: es_pro.enter_context(nc.sbuf_tensor(name, shape, dt))
        bv_r = sbp("bv_r", [1, D], BF16)
        wstr0 = sbp("wstr0", [128, 2048], F32)
        wstr1 = sbp("wstr1", [128, 2048], F32)
        wqnat = sbp("wqnat", [128, 4096], BF16)
        xsq0 = sbp("xsq0", [128, 2 * D], BF16)
        xsq1 = sbp("xsq1", [128, 2 * D], BF16)

        psA = psb("psA", [128, 1024])
        psB = psb("psB", [128, 1024])
        psC = psb("psC", [128, 1024])
        psD = psb("psD", [128, 512])
        psE = psb("psE", [128, 512])

        sem_names = ["LD", "LG", "ST0", "ST1", "PE", "VE", "AE", "GE", "XM",
                     "XS0", "XS1", "WF0", "WF1", "X20", "X21", "XOW", "WBF"]
        sems = {n: es.enter_context(nc.semaphore("s" + n)) for n in sem_names}
        block = es.enter_context(nc.Block())

        def w(eng, *names):
            for n in names:
                S.wait(eng, n)

        def dma(eng, sem_name, out, in_):
            S.op(eng, lambda E, o=out, i=in_: E.dma_start(out=o, in_=i), inc=sem_name, amt=16)

        def quant_consts(nb, amax_ap, e_ap, s0_ap, s_ap, m_ap, eb_ap):
            S.op("vector", lambda E, a=amax_ap: E.tensor_scalar(
                out=a, in0=a, scalar1=float(2.0**-100), scalar2=None, op0=ALU.max), inc="VE")
            w("scalar", "VE")
            S.op("scalar", lambda E, a=amax_ap, e=e_ap: E.activation(
                out=e, in_=a, func=ACT.Ln), inc="AE")
            w("vector", "AE")
            S.op("vector", lambda E, e=e_ap: E.tensor_scalar(
                out=e, in0=e, scalar1=float(1.0 / LN2), scalar2=float(1e-11 - 0.5),
                op0=ALU.mult, op1=ALU.add), inc="VE")
            S.wait_at("vector", "VE", S.c("VE"))
            S.op("vector", lambda E, e=e_ap: E.tensor_scalar(
                out=e, in0=e, scalar1=DK2pp[:, :], scalar2=EB1pp[:, :],
                op0=ALU.add, op1=ALU.subtract), inc="VE")
            S.wait_at("vector", "VE", S.c("VE"))
            S.op("vector", lambda E, e=e_ap: E.tensor_scalar(
                out=e, in0=e, scalar1=K1, scalar2=3.0,
                op0=ALU.subtract, op1=ALU.subtract), inc="VE")
            w("scalar", "VE")
            S.op("scalar", lambda E, e=e_ap, s0=s0_ap: E.activation(
                out=s0, in_=e, func=ACT.Exp, scale=LN2), inc="AE")
            w("vector", "AE")
            S.op("vector", lambda E, s0=s0_ap, m=m_ap: E.tensor_scalar(
                out=m, in0=s0, scalar1=K1, scalar2=None, op0=ALU.mult), inc="VE")
            S.wait_at("vector", "VE", S.c("VE"))
            S.op("vector", lambda E, s0=s0_ap, s=s_ap, m=m_ap: E.tensor_tensor(
                out=s, in0=s0, in1=m, op=ALU.add), inc="VE")
            S.wait_at("vector", "VE", S.c("VE"))
            S.op("vector", lambda E, s=s_ap, m=m_ap: E.tensor_tensor(
                out=s, in0=s, in1=m, op=ALU.subtract), inc="VE")
            S.wait_at("vector", "VE", S.c("VE"))
            S.op("vector", lambda E, s=s_ap, m=m_ap: E.tensor_scalar(
                out=m, in0=s, scalar1=K1, scalar2=None, op0=ALU.mult), inc="VE")
            S.op("vector", lambda E, s=s_ap, eb=eb_ap: E.tensor_scalar(
                out=eb, in0=s, scalar1=K2, scalar2=None, op0=ALU.mult), inc="VE")
            S.wait_at("vector", "VE", S.c("VE"))
            S.op("vector", lambda E, eb=eb_ap: E.tensor_scalar(
                out=eb, in0=eb, scalar1=Dt[:, :], scalar2=None, op0=ALU.mult), inc="VE")
            S.wait_at("vector", "VE", S.c("VE"))
            S.op("vector", lambda E, eb=eb_ap, m=m_ap: E.tensor_tensor(
                out=eb, in0=eb, in1=m, op=ALU.subtract), inc="VE")
            S.wait_at("vector", "VE", S.c("VE"))

        def quant_apply(v_ap, u_ap, q_ap, eb_bc):
            S.op("vector", lambda E: E.tensor_tensor(out=u_ap, in0=v_ap, in1=eb_bc,
                                                     op=ALU.subtract), inc="VE")
            S.wait_at("vector", "VE", S.c("VE"))
            S.op("vector", lambda E: E.tensor_tensor(out=q_ap, in0=u_ap, in1=eb_bc,
                                                     op=ALU.add), inc="VE")
            S.wait_at("vector", "VE", S.c("VE"))

        def transpose8(src_tile_fn, dst0_ap_fn, nchunk, evac_eng="vector", bank=None):
            pe_bank = bank if bank is not None else psE
            for j0 in range(0, nchunk, 4):
                jn = min(4, nchunk - j0)
                for j in range(j0, j0 + jn):
                    S.op("tensor", lambda E, s=src_tile_fn(j), d=pe_bank[:, 128 * (j - j0):128 * (j - j0) + 128]:
                         E.matmul(d, s, ident[:, :], start=True, stop=True),
                         inc="PE" if j == j0 + jn - 1 else None)
                w(evac_eng, "PE")
                ps3 = pe_bank[:, 0:128 * jn].rearrange("p (j c) -> p j c", c=128)
                dst3 = dst0_ap_fn(j0, jn)
                if evac_eng == "vector":
                    S.op("vector", lambda E, s=ps3, d=dst3: E.tensor_copy(out=d, in_=s), inc="VE")
                else:
                    S.op("scalar", lambda E, s=ps3, d=dst3: E.activation(
                        out=d, in_=s, func=ACT.Identity), inc="AE")
                S.wait(("tensor"), "VE" if evac_eng == "vector" else "AE")

        def strided_dst(base_ap_fn):
            def f(j0, jn):
                a0 = base_ap_fn(j0)
                if jn == 1:
                    return a0.unsqueeze(1)
                a1 = base_ap_fn(j0 + 1)
                raise RuntimeError("use explicit dst fns")
            return f

        dma("sync", "LD", ident[:], ident_h[:])
        dma("sync", "LD", base3[:], base3_h[:])
        dma("sync", "LD", ilen[:], ilen_h[:])
        dma("sync", "LD", bqr[:], bqr_in)
        dma("sync", "LD", bkr[:], bkr_in)
        dma("sync", "LD", mask_sb.ap().rearrange("p (t c) -> p t c", c=128),
            mask_in.rearrange("(t p) c -> p t c", p=128))
        dma("gpsimd", "LG", bv_r[:], bv_in)
        dma("gpsimd", "LG", bo_r[:], bo_in)

        S.op("vector", lambda E: E.memset(zeros_b[:], 0.0), inc="VE")
        S.op("vector", lambda E: E.memset(ones_col[:], 1.0), inc="VE")
        S.op("vector", lambda E: E.memset(ones_row[:], 1.0), inc="VE")
        S.op("vector", lambda E: E.memset(Dt[:], 1.0), inc="VE")
        S.wait_at("vector", "VE", S.c("VE"))
        S.op("vector", lambda E: E.tensor_scalar(out=Dt[:], in0=Dt[:], scalar1=K2,
                                                 scalar2=-K2, op0=ALU.add, op1=ALU.add), inc="VE")
        S.wait_at("vector", "VE", S.c("VE"))
        S.op("vector", lambda E: E.tensor_scalar(out=DK2pp[:], in0=Dt[:], scalar1=K2,
                                                 scalar2=None, op0=ALU.mult), inc="VE")
        S.wait_at("vector", "VE", S.c("VE"))
        S.op("vector", lambda E: E.tensor_scalar(out=EB1pp[:], in0=DK2pp[:], scalar1=K1,
                                                 scalar2=None, op0=ALU.subtract), inc="VE")
        S.wait_at("vector", "VE", S.c("VE"))

        w("tensor", "LD", "VE")
        S.op("tensor", lambda E: E.matmul(psA[:, 0:512], zeros_b[:], mask_sb[:, 0:512],
                                          start=True, stop=False))
        S.op("tensor", lambda E: E.matmul(psA[:, 512:1024], zeros_b[:], mask_sb[:, 512:1024],
                                          start=True, stop=False))
        S.op("tensor", lambda E: E.matmul(psD[:, 0:1], zeros_b[:], ones_col[:],
                                          start=True, stop=False), inc="PE")

        xs_view = xseq.rearrange("(c p) f -> p c f", p=128)
        NCH = 32
        for ch in range(NCH):
            buf = xsq0 if ch % 2 == 0 else xsq1
            if ch >= 2:
                S.wait_at("gpsimd", "PE", S.c("PE"))
            xs_sem = "XS0" if ch % 2 == 0 else "XS1"
            S.op("gpsimd",
                 lambda E, b=buf, c0=ch * 2: E.dma_start(
                     out=b.ap().rearrange("p (t f) -> p t f", f=D),
                     in_=xs_view[:, c0:c0 + 2, :]),
                 inc=xs_sem, amt=16)
            lg_after = S.c(xs_sem)
            S.wait_at("vector", xs_sem, lg_after)
            for tt in range(2):
                t = ch * 2 + tt
                S.op("vector", lambda E, b=buf, o=tt: E.bn_stats(
                    out=bnscr[:, 0:6], in_=b[:, o * D:o * D + 512]), inc="VE")
                S.op("vector", lambda E, b=buf, o=tt: E.bn_stats(
                    out=bnscr[:, 6:12], in_=b[:, o * D + 512:o * D + 1024]), inc="VE")
                S.wait_at("vector", "VE", S.c("VE"))
                S.op("vector", lambda E, t_=t: E.bn_aggr(
                    out=mv_seq[:, 2 * t_:2 * t_ + 2], in_=bnscr[:, :]), inc="VE")
                S.wait_at("vector", "VE", S.c("VE"))
            mvs = mv_seq.ap().rearrange("p (t two) -> p t two", two=2)
            var_sl = mvs[:, ch * 2:(ch + 1) * 2, 1:2]
            mean_sl = mvs[:, ch * 2:(ch + 1) * 2, 0:1]
            lv = lnv_scr[:, ch * 2:(ch + 1) * 2]
            lv3 = lnv_scr.ap().rearrange("p (t one) -> p t one", one=1)[:, ch * 2:(ch + 1) * 2, :]
            S.op("vector", lambda E, v=var_sl, o=lv3: E.tensor_scalar(
                out=o, in0=v, scalar1=EPS, scalar2=None, op0=ALU.add), inc="VE")
            w("scalar", "VE")
            a_sl = a_seq[:, ch * 2:(ch + 1) * 2]
            S.op("scalar", lambda E, i=lv, o=a_sl: E.activation(
                out=o, in_=i, func=ACT.Ln), inc="AE")
            S.wait_at("scalar", "AE", S.c("AE"))
            S.op("scalar", lambda E, o=a_sl: E.activation(
                out=o, in_=o, func=ACT.Exp, scale=-0.5), inc="AE")
            w("vector", "AE")
            am_sl3 = am_seq.ap().rearrange("p (t one) -> p t one", one=1)[:, ch * 2:(ch + 1) * 2, :]
            a_sl3 = a_seq.ap().rearrange("p (t one) -> p t one", one=1)[:, ch * 2:(ch + 1) * 2, :]
            S.op("vector", lambda E, a3=a_sl3, m3=mean_sl, o=am_sl3: E.tensor_tensor(
                out=o, in0=a3, in1=m3, op=ALU.mult), inc="VE")
            ve_now = S.c("VE")
            S.wait_at("tensor", xs_sem, lg_after)
            for tt in range(2):
                t = ch * 2 + tt
                ncols = 2 if t == 63 else 3
                pt = patt0 if t % 2 == 0 else patt1
                S.wait_at("vector", "PE", S.c("PE"))
                S.op("vector", lambda E, p=pt: E.memset(p[:, :], 0.0), inc="VE")
                S.wait_at("vector", "VE", S.c("VE"))
                S.op("vector", lambda E, t_=t, n=ncols, p=pt: E.tensor_scalar(
                    out=p[:, 2 * t_:2 * t_ + n],
                    in0=base3[:, 0:n], scalar1=a_seq[:, t_:t_ + 1],
                    scalar2=None, op0=ALU.mult), inc="VE")
                S.op("vector", lambda E, t_=t: E.tensor_copy(
                    out=meanb[:, :], in_=mv_seq[:, 2 * t_:2 * t_ + 1]), inc="VE")
                S.wait_at("tensor", "VE", S.c("VE"))
                S.op("tensor", lambda E, b=buf, o=tt, p=pt: E.matmul(
                    psA[:, 0:512], p[:, :], b[:, o * D:o * D + 512],
                    start=False, stop=False))
                S.op("tensor", lambda E, b=buf, o=tt, p=pt: E.matmul(
                    psA[:, 512:1024], p[:, :], b[:, o * D + 512:o * D + 1024],
                    start=False, stop=False))
                S.op("tensor", lambda E, p=pt: E.matmul(
                    psD[:, 0:1], p[:, :], meanb[:, :],
                    start=False, stop=False), inc="PE")
                S.wait_at("vector", "PE", S.c("PE"))

        S.op("tensor", lambda E: E.matmul(psA[:, 0:512], zeros_b[:], mask_sb[:, 0:512],
                                          start=False, stop=True))
        S.op("tensor", lambda E: E.matmul(psA[:, 512:1024], zeros_b[:], mask_sb[:, 512:1024],
                                          start=False, stop=True))
        S.op("tensor", lambda E: E.matmul(psD[:, 0:1], zeros_b[:], ones_col[:, :],
                                          start=False, stop=True), inc="PE")
        pe_summary_done = S.c("PE")

        S.wait_at("vector", "PE", pe_summary_done)
        S.op("vector", lambda E: E.tensor_copy(out=sum_bias[:], in_=psD[:, 0:1]), inc="VE")
        S.wait_at("vector", "VE", S.c("VE"))
        S.op("vector", lambda E: E.tensor_scalar(
            out=sum_bias[:], in0=sum_bias[:], scalar1=ilen[:, :], scalar2=-1.0,
            op0=ALU.mult, op1=ALU.mult), inc="VE")
        w("scalar", "VE")
        S.wait_at("scalar", "PE", pe_summary_done)
        S.op("scalar", lambda E: E.activation(
            out=summ_cm[:, :], in_=psA[:, :], func=ACT.Identity,
            bias=sum_bias[:, :], scale=ilen[:, :]), inc="AE")
        ae_summ = S.c("AE")
        S.wait_at("tensor", "AE", ae_summ)
        transpose8(lambda j: summ_cm[:, 128 * j:128 * j + 128],
                   lambda j0, jn: summ_fm.ap().rearrange(
                       "p (j c) -> p j c", c=128)[:, j0:j0 + jn, :], 8, evac_eng="vector")

        kdst = [psA[:, 0:128], psA[:, 512:640], psB[:, 0:128], psB[:, 512:640],
                psC[:, 0:128], psC[:, 512:640], psD[:, 0:128], psE[:, 0:128]]
        wk_v = wk_in.rearrange("(k p) f -> p k f", p=128)
        for hf in range(2):
            S.wait_at("gpsimd", "PE", S.c("PE"))
            S.op("gpsimd", lambda E, h=hf: E.dma_start(
                out=wbuf.ap().rearrange("p (k f) -> p k f", f=D),
                in_=wk_v[:, 4 * h:4 * h + 4, :]), inc="WBF", amt=16)
            S.wait_at("tensor", "WBF", S.c("WBF"))
            for m in range(8):
                for kl in range(4):
                    k = 4 * hf + kl
                    S.op("tensor", lambda E, m_=m, k_=k, kk=kl: E.matmul(
                        kdst[m_],
                        wbuf[:, kk * D + m_ * 128: kk * D + m_ * 128 + 128],
                        summ_fm[:, k_ * 128:k_ * 128 + 128],
                        start=(k_ == 0), stop=(k_ == 7)),
                        inc="PE" if (m == 7 and kl == 3) else None)
        w("scalar", "PE")
        for m in range(8):
            S.op("scalar", lambda E, m_=m: E.activation(
                out=kfm[:, m_ * 128:m_ * 128 + 128], in_=kdst[m_],
                func=ACT.Identity, bias=bkr[:, m_:m_ + 1]),
                inc="AE" if m == 7 else None)
        S.wait_at("tensor", "AE", S.c("AE"))

        wv_v = wv_in.rearrange("(k p) f -> p k f", p=128)
        for hf in range(2):
            S.wait_at("gpsimd", "PE", S.c("PE"))
            S.op("gpsimd", lambda E, h=hf: E.dma_start(
                out=wbuf.ap().rearrange("p (k f) -> p k f", f=D),
                in_=wv_v[:, 4 * h:4 * h + 4, :]), inc="WBF", amt=16)
            S.wait_at("tensor", "WBF", S.c("WBF"))
            for cl in (0, 512):
                vdst = psB[:, 0:512] if cl == 0 else psD[:, 0:512]
                for kl in range(4):
                    k = 4 * hf + kl
                    S.op("tensor", lambda E, k_=k, kk=kl, c=cl, dd=vdst: E.matmul(
                        dd, summ_fm[:, k_ * 128:k_ * 128 + 128],
                        wbuf[:, kk * D + c: kk * D + c + 512],
                        start=(k_ == 0), stop=False),
                        inc="PE" if (cl == 512 and kl == 3) else None)
            if hf == 1:
                S.wait_at("tensor", "LG", S.c("LG"))
                S.op("tensor", lambda E: E.matmul(
                    psB[:, 0:512], ones_row[0:1, :], bv_r[0:1, 0:512],
                    start=False, stop=True))
                S.op("tensor", lambda E: E.matmul(
                    psD[:, 0:512], ones_row[0:1, :], bv_r[0:1, 512:1024],
                    start=False, stop=True), inc="PE")
        w("vector", "PE")
        S.op("vector", lambda E: E.tensor_copy(out=vcm[:, 0:512], in_=psB[:, 0:512]), inc="VE")
        S.op("vector", lambda E: E.tensor_copy(out=vcm[:, 512:1024], in_=psD[:, 0:512]), inc="VE")
        ve_vcm = S.c("VE")

        wfc_v = wfc_in.rearrange("(n p) f -> p n f", p=128)
        wpj_v = wpj_in.rearrange("(n p) f -> p n f", p=128)
        jobs = [("fc", n) for n in range(32)] + [("pj", q) for q in range(16)]

        def job_dma(i):
            kind, n = jobs[i]
            par = i % 2
            buf = wstr0 if par == 0 else wstr1
            wf_sem = "WF0" if par == 0 else "WF1"
            if kind == "fc":
                dma("sync", wf_sem, buf[:, 0:D], wfc_v[:, n, :])
            else:
                dma("sync", wf_sem, buf[:, :], wpj_v[:, n // 2, 2048 * (n % 2):2048 * (n % 2) + 2048])
            return S.c(wf_sem)

        def job_nb(i):
            return 32 if jobs[i][0] == "fc" else 64

        wf_totals = {}
        def stage_R(i):
            par, nb = i % 2, job_nb(i)
            buf = wstr0 if par == 0 else wstr1
            o = 64 * par
            S.wait_at("vector", "WF0" if par == 0 else "WF1", wf_totals[i])
            v3 = buf.ap()[:, 0:32 * nb].rearrange("p (b e) -> p b e", e=32)
            S.op("vector", lambda E, v=v3, o_=o, nb_=nb: E.tensor_reduce(
                out=qa_amax[:, o_:o_ + nb_], in_=v, axis=AXX, op=ALU.max,
                apply_absolute_value=True), inc="VE")
            S.wait_at("vector", "VE", S.c("VE"))
            S.op("vector", lambda E, o_=o, nb_=nb: E.tensor_scalar(
                out=qa_amax[:, o_:o_ + nb_], in0=qa_amax[:, o_:o_ + nb_],
                scalar1=float(2.0**-100), scalar2=None, op0=ALU.max), inc="VE")

        def stage_L(i):
            par, nb = i % 2, job_nb(i)
            o = 64 * par
            w("scalar", "VE")
            S.op("scalar", lambda E, o_=o, nb_=nb: E.activation(
                out=qa_e[:, o_:o_ + nb_], in_=qa_amax[:, o_:o_ + nb_], func=ACT.Ln), inc="AE")

        def stage_E1(i):
            par, nb = i % 2, job_nb(i)
            o = 64 * par
            w("vector", "AE")
            S.op("vector", lambda E, o_=o, nb_=nb: E.tensor_scalar(
                out=qa_e[:, o_:o_ + nb_], in0=qa_e[:, o_:o_ + nb_],
                scalar1=float(1.0 / LN2), scalar2=float(1e-11 - 0.5),
                op0=ALU.mult, op1=ALU.add), inc="VE")
            S.wait_at("vector", "VE", S.c("VE"))
            S.op("vector", lambda E, o_=o, nb_=nb: E.tensor_scalar(
                out=qa_e[:, o_:o_ + nb_], in0=qa_e[:, o_:o_ + nb_],
                scalar1=DK2pp[:, :], scalar2=EB1pp[:, :], op0=ALU.add, op1=ALU.subtract), inc="VE")
            S.wait_at("vector", "VE", S.c("VE"))
            S.op("vector", lambda E, o_=o, nb_=nb: E.tensor_scalar(
                out=qa_e[:, o_:o_ + nb_], in0=qa_e[:, o_:o_ + nb_],
                scalar1=K1, scalar2=3.0, op0=ALU.subtract, op1=ALU.subtract), inc="VE")

        def stage_X(i):
            par, nb = i % 2, job_nb(i)
            o = 64 * par
            w("scalar", "VE")
            S.op("scalar", lambda E, o_=o, nb_=nb: E.activation(
                out=qa_s0[:, o_:o_ + nb_], in_=qa_e[:, o_:o_ + nb_], func=ACT.Exp,
                scale=LN2), inc="AE")

        def stage_S(i):
            par, nb = i % 2, job_nb(i)
            o = 64 * par
            w("vector", "AE")
            for fn in (
                lambda E, o_=o, nb_=nb: E.tensor_scalar(
                    out=qa_m[:, o_:o_ + nb_], in0=qa_s0[:, o_:o_ + nb_], scalar1=K1,
                    scalar2=None, op0=ALU.mult),
                lambda E, o_=o, nb_=nb: E.tensor_tensor(
                    out=qa_s[:, o_:o_ + nb_], in0=qa_s0[:, o_:o_ + nb_],
                    in1=qa_m[:, o_:o_ + nb_], op=ALU.add),
                lambda E, o_=o, nb_=nb: E.tensor_tensor(
                    out=qa_s[:, o_:o_ + nb_], in0=qa_s[:, o_:o_ + nb_],
                    in1=qa_m[:, o_:o_ + nb_], op=ALU.subtract),
                lambda E, o_=o, nb_=nb: E.tensor_scalar(
                    out=qa_m[:, o_:o_ + nb_], in0=qa_s[:, o_:o_ + nb_], scalar1=K1,
                    scalar2=None, op0=ALU.mult),
                lambda E, o_=o, nb_=nb: E.tensor_scalar(
                    out=qa_eb[:, o_:o_ + nb_], in0=qa_s[:, o_:o_ + nb_], scalar1=K2,
                    scalar2=None, op0=ALU.mult),
                lambda E, o_=o, nb_=nb: E.tensor_scalar(
                    out=qa_eb[:, o_:o_ + nb_], in0=qa_eb[:, o_:o_ + nb_],
                    scalar1=Dt[:, :], scalar2=None, op0=ALU.mult),
                lambda E, o_=o, nb_=nb: E.tensor_tensor(
                    out=qa_eb[:, o_:o_ + nb_], in0=qa_eb[:, o_:o_ + nb_],
                    in1=qa_m[:, o_:o_ + nb_], op=ALU.subtract),
            ):
                S.op("vector", fn, inc="VE")
                S.wait_at("vector", "VE", S.c("VE"))

        def stage_Q(i):
            par, nb = i % 2, job_nb(i)
            buf = wstr0 if par == 0 else wstr1
            o = 64 * par
            v3 = buf.ap()[:, 0:32 * nb].rearrange("p (b e) -> p b e", e=32)
            q3 = wqnat.ap()[:, 2048 * par:2048 * par + 32 * nb].rearrange("p (b e) -> p b e", e=32)
            ebb = qa_eb.ap()[:, o:o + nb].unsqueeze(2).broadcast_to([128, nb, 32])
            S.wait_at("vector", "PE", S.c("PE"))
            S.op("vector", lambda E, a=v3, b=ebb: E.tensor_tensor(
                out=a, in0=a, in1=b, op=ALU.subtract), inc="VE")
            S.wait_at("vector", "VE", S.c("VE"))
            S.op("vector", lambda E, a=v3, b=ebb, q=q3: E.tensor_tensor(
                out=q, in0=a, in1=b, op=ALU.add), inc="VE")
            S.wait_at("tensor", "VE", S.c("VE"))

        def stage_T(i):
            kind, n = jobs[i]
            par = i % 2
            base = 2048 * par
            if kind == "fc":
                transpose8(lambda j, b=base: wqnat[:, b + 128 * j:b + 128 * j + 128],
                           lambda j0, jn, h_=n: wfcTq.ap().rearrange(
                               "p (j r) -> p j r", r=HID)[:, j0:j0 + jn,
                                                          128 * h_:128 * h_ + 128],
                           8, evac_eng="scalar")
            else:
                h_, hf = n // 2, n % 2
                transpose8(lambda j, b=base: wqnat[:, b + 128 * j:b + 128 * j + 128],
                           lambda j0, jn, hh=h_, f_=hf: wpjTq.ap().rearrange(
                               "p (j r) -> p j r", r=D)[:, 16 * f_ + j0:16 * f_ + j0 + jn,
                                                        128 * hh:128 * hh + 128],
                           16, evac_eng="scalar")

        NJOB = len(jobs)
        wf_totals[0] = job_dma(0)
        wf_totals[1] = job_dma(1)
        for i in range(NJOB + 1):
            if i < NJOB:
                stage_R(i)
                stage_L(i)
            if i >= 1:
                stage_Q(i - 1)
                if i + 1 < NJOB:
                    S.wait_at("sync", "VE", S.c("VE"))
                    wf_totals[i + 1] = job_dma(i + 1)
            if i < NJOB:
                stage_E1(i)
                stage_X(i)
            if i >= 1:
                stage_T(i - 1)
            if i < NJOB:
                stage_S(i)

        es_pro.close()
        es_ph1 = contextlib.ExitStack()
        sb1 = lambda name, shape, dt: es_ph1.enter_context(nc.sbuf_tensor(name, shape, dt))
        xring = sb1("xring", [128, 4 * D], F32)
        lnbuf = sb1("lnbuf", [128, D], BF16)
        ln1fm = sb1("ln1fm", [128, 8 * 512], BF16)
        qfm = sb1("qfm", [128, 8 * 512], BF16)
        ebuf = sb1("ebuf", [128, 16 * 128], BF16)
        xmid_t = sb1("xmid_t", [128, D], F32)
        ofm = ln1fm

        for e in ("sync", "gpsimd", "tensor", "vector", "scalar"):
            S.wait_at(e, "VE", S.c("VE"))
            S.wait_at(e, "AE", S.c("AE"))
            S.wait_at(e, "PE", S.c("PE"))

        st_count = 0
        for s in range(NSUP):
            for tt in range(4):
                t = 4 * s + tt
                slot = t % 4
                if s >= 1 and tt == 0:
                    S.wait_at("sync", "XM", st_count)
                dma("sync", "XOW", xring[:, slot * D:(slot + 1) * D],
                    xown.rearrange("(t p) f -> p t f", p=128)[:, t, :])
            ld_xown = S.c("XOW")
            for tt in range(4):
                t = 4 * s + tt
                slot = t % 4
                xo = xring[:, slot * D:(slot + 1) * D]
                S.wait_at("vector", "XOW", ld_xown)
                S.op("vector", lambda E, x=xo: E.bn_stats(out=bnscr[:, 0:6], in_=x[:, 0:512]), inc="VE")
                S.op("vector", lambda E, x=xo: E.bn_stats(out=bnscr[:, 6:12], in_=x[:, 512:1024]), inc="VE")
                S.wait_at("vector", "VE", S.c("VE"))
                S.op("vector", lambda E, t_=t: E.bn_aggr(
                    out=mv_own[:, 2 * t_:2 * t_ + 2], in_=bnscr[:, :]), inc="VE")
                S.wait_at("vector", "VE", S.c("VE"))
                S.op("vector", lambda E, t_=t: E.tensor_scalar(
                    out=lnv2[:, 0:1], in0=mv_own[:, 2 * t_ + 1:2 * t_ + 2], scalar1=EPS,
                    scalar2=None, op0=ALU.add), inc="VE")
                w("scalar", "VE")
                S.op("scalar", lambda E, t_=t: E.activation(
                    out=a_own[:, t_:t_ + 1], in_=lnv2[:, 0:1], func=ACT.Ln), inc="AE")
                S.wait_at("scalar", "AE", S.c("AE"))
                S.op("scalar", lambda E, t_=t: E.activation(
                    out=a_own[:, t_:t_ + 1], in_=a_own[:, t_:t_ + 1], func=ACT.Exp,
                    scale=-0.5), inc="AE")
                w("vector", "AE")
                S.op("vector", lambda E, t_=t: E.tensor_tensor(
                    out=nam_own[:, t_:t_ + 1], in0=a_own[:, t_:t_ + 1],
                    in1=mv_own[:, 2 * t_:2 * t_ + 1], op=ALU.mult), inc="VE")
                S.wait_at("vector", "VE", S.c("VE"))
                S.op("vector", lambda E, t_=t: E.tensor_scalar(
                    out=nam_own[:, t_:t_ + 1], in0=nam_own[:, t_:t_ + 1], scalar1=-1.0,
                    scalar2=None, op0=ALU.mult), inc="VE")
                w("scalar", "VE")
                lslot = 0
                S.wait_at("scalar", "PE", S.c("PE"))
                S.op("scalar", lambda E, x=xo, t_=t, l=lslot: E.activation(
                    out=lnbuf[:, l * D:(l + 1) * D], in_=x, func=ACT.Identity,
                    scale=a_own[:, t_:t_ + 1], bias=nam_own[:, t_:t_ + 1]), inc="AE")
                S.wait_at("tensor", "AE", S.c("AE"))
                transpose8(lambda j, l=lslot: lnbuf[:, l * D + 128 * j: l * D + 128 * j + 128],
                           lambda j0, jn, u=tt: ln1fm.ap().rearrange(
                               "p (j q) -> p j q", q=512)[:, j0:j0 + jn, 128 * u:128 * u + 128],
                           8, evac_eng="vector")

            qdst = [psA[:, 0:512], psA[:, 512:1024], psB[:, 0:512], psB[:, 512:1024],
                    psC[:, 0:512], psC[:, 512:1024], psD[:, 0:512], psE[:, 0:512]]
            wq_v = wq_in.rearrange("(k p) f -> p k f", p=128)
            for hf in range(2):
                S.wait_at("gpsimd", "PE", S.c("PE"))
                S.op("gpsimd", lambda E, h=hf: E.dma_start(
                    out=wbuf.ap().rearrange("p (k f) -> p k f", f=D),
                    in_=wq_v[:, 4 * h:4 * h + 4, :]), inc="WBF", amt=16)
                S.wait_at("tensor", "WBF", S.c("WBF"))
                for m in range(8):
                    for kl in range(4):
                        k = 4 * hf + kl
                        S.op("tensor", lambda E, m_=m, k_=k, kk=kl: E.matmul(
                            qdst[m_], wbuf[:, kk * D + m_ * 128: kk * D + m_ * 128 + 128],
                            ln1fm[:, k_ * 512:(k_ + 1) * 512],
                            start=(k_ == 0), stop=(k_ == 7)),
                            inc="PE" if (m == 7 and kl == 3) else None)
            w("scalar", "PE")
            for m in range(8):
                S.op("scalar", lambda E, m_=m: E.activation(
                    out=qfm[:, m_ * 512:(m_ + 1) * 512], in_=qdst[m_],
                    func=ACT.Identity, bias=bqr[:, m_:m_ + 1]),
                    inc="AE" if m == 7 else None)
            S.wait_at("tensor", "AE", S.c("AE"))

            for u in range(4):
                t = 4 * s + u
                for m in range(8):
                    qsl_e = qfm[0:64, m * 512 + 128 * u: m * 512 + 128 * u + 128]
                    qsl_o = qfm[64:128, m * 512 + 128 * u: m * 512 + 128 * u + 128]
                    ksl_e = kfm[0:64, m * 128:(m + 1) * 128]
                    ksl_o = kfm[64:128, m * 128:(m + 1) * 128]
                    dstA = psA[:, 128 * (m % 4):128 * (m % 4) + 128] if m < 4 else \
                        psB[:, 128 * (m % 4):128 * (m % 4) + 128]
                    dstB = psA[:, 512 + 128 * (m % 4):512 + 128 * (m % 4) + 128] if m < 4 else \
                        psB[:, 512 + 128 * (m % 4):512 + 128 * (m % 4) + 128]
                    S.op("tensor", lambda E, q=qsl_e, kk=ksl_e, d=dstA: E.matmul(
                        d, q, kk, start=True, stop=True, tile_position=(0, 0)))
                    S.op("tensor", lambda E, q=qsl_o, kk=ksl_o, d=dstB: E.matmul(
                        d, q, kk, start=True, stop=True, tile_position=(64, 0)),
                        inc="PE" if m == 7 else None)
                w("scalar", "PE")
                S.op("scalar", lambda E: E.activation(
                    out=ebuf[:, 0:1024], in_=psA[:, :], func=ACT.Exp, scale=0.125), inc="AE")
                S.op("scalar", lambda E: E.activation(
                    out=ebuf[:, 1024:2048], in_=psB[:, :], func=ACT.Exp, scale=0.125), inc="AE")
                w("vector", "AE")
                e3 = ebuf.ap().rearrange("p (g c) -> p g c", c=128)
                msk = mask_sb.ap()[:, 128 * t:128 * t + 128].unsqueeze(1).broadcast_to([128, 16, 128])
                S.op("vector", lambda E, e=e3, mk=msk: E.tensor_tensor(
                    out=e, in0=e, in1=mk, op=ALU.mult), inc="VE")
                S.wait_at("vector", "VE", S.c("VE"))
                S.op("vector", lambda E, e=e3: E.tensor_reduce(
                    out=ssum[:, :], in_=e, axis=AXX, op=ALU.add), inc="VE")
                S.wait_at("vector", "VE", S.c("VE"))
                S.op("vector", lambda E: E.reciprocal(out=srcp[:, :], in_=ssum[:, :]), inc="VE")
                S.wait_at("vector", "VE", S.c("VE"))
                rb = srcp.ap().unsqueeze(2).broadcast_to([128, 16, 128])
                S.op("vector", lambda E, e=e3, r=rb: E.tensor_tensor(
                    out=e, in0=e, in1=r, op=ALU.mult), inc="VE")
                S.wait_at("tensor", "VE", S.c("VE"))
                transpose8(lambda j: ebuf[:, 128 * j:128 * j + 128],
                           lambda j0, jn: ebuf.ap().rearrange(
                               "p (j c) -> p j c", c=128)[:, j0:j0 + jn, :],
                           16, evac_eng="vector")
                for m0 in range(0, 8, 4):
                    for mm in range(4):
                        m = m0 + mm
                        g_e = (m // 4) * 8 + (m % 4)
                        g_o = (m // 4) * 8 + 4 + (m % 4)
                        h_e, h_o = 2 * m, 2 * m + 1
                        S.op("tensor", lambda E, g=g_e, h=h_e, c=128 * mm: E.matmul(
                            psD[0:64, c:c + 128], vcm[:, 64 * h:64 * h + 64],
                            ebuf[:, 128 * g:128 * g + 128], start=True, stop=True,
                            tile_position=(0, 0)))
                        S.op("tensor", lambda E, g=g_o, h=h_o, c=128 * mm: E.matmul(
                            psD[64:128, c:c + 128], vcm[:, 64 * h:64 * h + 64],
                            ebuf[:, 128 * g:128 * g + 128], start=True, stop=True,
                            tile_position=(0, 64)), inc="PE" if mm == 3 else None)
                    S.wait_at("vector", "PE", S.c("PE"))
                    dst3 = ofm.ap().rearrange("p (m q) -> p m q", q=512)[
                        :, m0:m0 + 4, 128 * u:128 * u + 128]
                    ps3 = psD.ap().rearrange("p (m c) -> p m c", c=128)
                    S.op("vector", lambda E, d=dst3, s=ps3: E.tensor_copy(
                        out=d, in_=s), inc="VE")
                    S.wait_at("tensor", "VE", S.c("VE"))

            odst = [(psA[:, 0:512], psA[:, 512:1024]), (psB[:, 0:512], psB[:, 512:1024]),
                    (psC[:, 0:512], psC[:, 512:1024]), (psD[:, 0:512], psE[:, 0:512])]
            wo_v = wo_in.rearrange("(k p) f -> p k f", p=128)
            for hf in range(2):
                S.wait_at("gpsimd", "PE", S.c("PE"))
                S.op("gpsimd", lambda E, h=hf: E.dma_start(
                    out=wbuf.ap().rearrange("p (k f) -> p k f", f=D),
                    in_=wo_v[:, 4 * h:4 * h + 4, :]), inc="WBF", amt=16)
                S.wait_at("tensor", "WBF", S.c("WBF"))
                for u in range(4):
                    for half in range(2):
                        cl = half * 512
                        dd = odst[u][half]
                        for ml in range(4):
                            m = 4 * hf + ml
                            S.op("tensor", lambda E, m_=m, mm=ml, c=cl, u_=u, d=dd: E.matmul(
                                d, ofm[:, m_ * 512 + 128 * u_: m_ * 512 + 128 * u_ + 128],
                                wbuf[:, mm * D + c: mm * D + c + 512],
                                start=(m_ == 0), stop=False),
                                inc="PE" if (u == 3 and half == 1 and ml == 3) else None)
                        if hf == 1:
                            S.wait_at("tensor", "LG", S.c("LG"))
                            S.op("tensor", lambda E, c=cl, d=dd: E.matmul(
                                d, ones_row[0:1, :], bo_r[0:1, c:c + 512],
                                start=False, stop=True),
                                inc="PE" if (u == 3 and half == 1) else None)
            w("vector", "PE")
            for u in range(4):
                t = 4 * s + u
                slot = t % 4
                xo = xring[:, slot * D:(slot + 1) * D]
                S.op("vector", lambda E, d0=odst[u][0], x=xo: E.tensor_tensor(
                    out=xmid_t[:, 0:512], in0=d0, in1=x[:, 0:512], op=ALU.add), inc="VE")
                S.op("vector", lambda E, d1=odst[u][1], x=xo: E.tensor_tensor(
                    out=xmid_t[:, 512:1024], in0=d1, in1=x[:, 512:1024], op=ALU.add), inc="VE")
                S.wait_at("sync", "VE", S.c("VE"))
                S.op("sync", lambda E, t_=t: E.dma_start(
                    out=xmid_hbm.rearrange("(t p) f -> p t f", p=128)[:, t_, :],
                    in_=xmid_t[:, :]), inc="XM", amt=16)
                st_count = S.c("XM")
                S.wait_at("vector", "XM", st_count)

        es_ph1.close()
        es_ph2 = contextlib.ExitStack()
        sb2 = lambda name, shape, dt: es_ph2.enter_context(nc.sbuf_tensor(name, shape, dt))
        x2r0 = sb2("x2r0", [128, D], F32)
        x2r1 = sb2("x2r1", [128, D], F32)
        xln2 = sb2("xln2", [128, D], F32)
        xqb = sb2("xqb", [128, D], BF16)
        xqfm = sb2("xqfm", [128, 2 * 8 * 128], BF16)
        hbuf = sb2("hbuf", [128, 2 * D], F32)
        hqb = sb2("hqb", [128, D], BF16)
        hqT = sb2("hqT", [128, 2 * 8 * 128], BF16)
        outb = sb2("outb", [128, 2 * D], F32)
        bfc_r = sb2("bfc_r", [1, HID], BF16)
        bpj_r = sb2("bpj_r", [1, D], BF16)
        for e in ("sync", "gpsimd", "tensor", "vector", "scalar"):
            S.wait_at(e, "XM", 16 * NT)
        dma("gpsimd", "LG", bfc_r[:], bfc_in)
        dma("gpsimd", "LG", bpj_r[:], bpj_in)
        lg_bias2 = S.c("LG")
        S.wait_at("tensor", "LG", lg_bias2)

        def p2_front(t):
            buf = x2r0 if t % 2 == 0 else x2r1
            S.wait_at("sync", "VE", S.c("VE"))
            x2_sem = "X20" if t % 2 == 0 else "X21"
            dma("sync", x2_sem, buf[:, :], xmid_hbm.rearrange("(t p) f -> p t f", p=128)[:, t, :])
            S.wait_at("vector", x2_sem, S.c(x2_sem))
            S.op("vector", lambda E, b=buf: E.bn_stats(out=bnscr[:, 0:6], in_=b[:, 0:512]), inc="VE")
            S.op("vector", lambda E, b=buf: E.bn_stats(out=bnscr[:, 6:12], in_=b[:, 512:1024]), inc="VE")
            S.wait_at("vector", "VE", S.c("VE"))
            S.op("vector", lambda E: E.bn_aggr(out=mv_own[:, 0:2], in_=bnscr[:, :]), inc="VE")
            S.wait_at("vector", "VE", S.c("VE"))
            S.op("vector", lambda E: E.tensor_scalar(
                out=lnv2[:, 0:1], in0=mv_own[:, 1:2], scalar1=EPS, scalar2=None,
                op0=ALU.add), inc="VE")
            w("scalar", "VE")
            S.op("scalar", lambda E: E.activation(out=a2c[:], in_=lnv2[:, 0:1], func=ACT.Ln), inc="AE")
            S.wait_at("scalar", "AE", S.c("AE"))
            S.op("scalar", lambda E: E.activation(out=a2c[:], in_=a2c[:], func=ACT.Exp,
                                                  scale=-0.5), inc="AE")
            w("vector", "AE")
            S.op("vector", lambda E: E.tensor_tensor(out=nam2c[:], in0=a2c[:],
                                                     in1=mv_own[:, 0:1], op=ALU.mult), inc="VE")
            S.wait_at("vector", "VE", S.c("VE"))
            S.op("vector", lambda E: E.tensor_scalar(out=nam2c[:], in0=nam2c[:], scalar1=-1.0,
                                                     scalar2=None, op0=ALU.mult), inc="VE")
            w("scalar", "VE")
            S.op("scalar", lambda E, b=buf: E.activation(
                out=xln2[:, :], in_=b[:, :], func=ACT.Identity, scale=a2c[:, :],
                bias=nam2c[:, :]), inc="AE")
            w("vector", "AE")
            o = 96
            v3 = xln2.ap().rearrange("p (b e) -> p b e", e=32)
            S.op("vector", lambda E, v=v3, o_=o: E.tensor_reduce(
                out=qa_amax[:, o_:o_ + 32], in_=v, axis=AXX, op=ALU.max,
                apply_absolute_value=True), inc="VE")
            S.wait_at("vector", "VE", S.c("VE"))
            quant_consts(32, qa_amax[:, o:o + 32], qa_e[:, o:o + 32], qa_s0[:, o:o + 32],
                         qa_s[:, o:o + 32], qa_m[:, o:o + 32], qa_eb[:, o:o + 32])
            ebb = qa_eb.ap()[:, o:o + 32].unsqueeze(2).broadcast_to([128, 32, 32])
            q3 = xqb.ap().rearrange("p (b e) -> p b e", e=32)
            S.wait_at("vector", "PE", S.c("PE"))
            quant_apply(v3, v3, q3, ebb)
            S.wait_at("tensor", "VE", S.c("VE"))
            xpar = t % 2
            transpose8(lambda j: xqb[:, 128 * j:128 * j + 128],
                       lambda j0, jn, xp=xpar: xqfm.ap().rearrange(
                           "p (j c) -> p j c", c=128)[:, 8 * xp + j0:8 * xp + j0 + jn, :],
                       8, evac_eng="vector", bank=psD)

        def p2_fc1(t, g):
            xpar = t % 2
            dst = psA if g % 2 == 0 else psB
            for k in range(8):
                S.op("tensor", lambda E, k_=k, g_=g, d=dst, xp=xpar: E.matmul(
                    d[:, 0:512], xqfm[:, 1024 * xp + 128 * k_:1024 * xp + 128 * k_ + 128],
                    wfcTq[:, k_ * HID + g_ * D: k_ * HID + g_ * D + 512],
                    start=(k_ == 0), stop=False))
                S.op("tensor", lambda E, k_=k, g_=g, d=dst, xp=xpar: E.matmul(
                    d[:, 512:1024], xqfm[:, 1024 * xp + 128 * k_:1024 * xp + 128 * k_ + 128],
                    wfcTq[:, k_ * HID + g_ * D + 512: k_ * HID + g_ * D + 1024],
                    start=(k_ == 0), stop=False))
            S.op("tensor", lambda E, g_=g, d=dst: E.matmul(
                d[:, 0:512], ones_row[0:1, :], bfc_r[0:1, g_ * D:g_ * D + 512],
                start=False, stop=True))
            S.op("tensor", lambda E, g_=g, d=dst: E.matmul(
                d[:, 512:1024], ones_row[0:1, :], bfc_r[0:1, g_ * D + 512:g_ * D + 1024],
                start=False, stop=True), inc="PE")

        def p2_back(t):
            for g in range(4):
                dst = psA if g % 2 == 0 else psB
                gs = g % 2
                w("scalar", "PE")
                S.wait_at("scalar", "VE", S.c("VE"))
                S.op("scalar", lambda E, d=dst, s_=gs: E.activation(
                    out=hbuf[:, s_ * D:(s_ + 1) * D], in_=d[:, :], func=ACT.Gelu), inc="AE")
                if g < 3:
                    p2_fc1(t, g + 1)
                w("vector", "AE")
                hg = hbuf.ap()[:, gs * D:(gs + 1) * D].rearrange("p (b e) -> p b e", e=32)
                S.op("vector", lambda E, v=hg: E.tensor_reduce(
                    out=qa_amax[:, 0:32], in_=v, axis=AXX, op=ALU.max,
                    apply_absolute_value=True), inc="VE")
                S.wait_at("vector", "VE", S.c("VE"))
                quant_consts(32, qa_amax[:, 0:32], qa_e[:, 0:32], qa_s0[:, 0:32],
                             qa_s[:, 0:32], qa_m[:, 0:32], qa_eb[:, 0:32])
                ebg = qa_eb.ap()[:, 0:32].unsqueeze(2).broadcast_to([128, 32, 32])
                qg = hqb.ap().rearrange("p (b e) -> p b e", e=32)
                S.wait_at("vector", "PE", S.c("PE"))
                quant_apply(hg, hg, qg, ebg)
                S.wait_at("tensor", "VE", S.c("VE"))
                transpose8(lambda j: hqb[:, 128 * j:128 * j + 128],
                           lambda j0, jn, s_=gs: hqT.ap().rearrange(
                               "p (j c) -> p j c", c=128)[:, 8 * s_ + j0:8 * s_ + j0 + jn, :],
                           8, evac_eng="vector")
                for half in range(2):
                    cl = half * 512
                    for cj in range(8):
                        c = 8 * g + cj
                        S.op("tensor", lambda E, c_=c, cc=cl, cjj=cj, s_=gs: E.matmul(
                            psC[:, cc:cc + 512],
                            hqT[:, 1024 * s_ + 128 * cjj:1024 * s_ + 128 * cjj + 128],
                            wpjTq[:, c_ * D + cc: c_ * D + cc + 512],
                            start=(c_ == 0), stop=False))
                    if g == 3:
                        S.op("tensor", lambda E, cc=cl: E.matmul(
                            psC[:, cc:cc + 512], ones_row[0:1, :], bpj_r[0:1, cc:cc + 512],
                            start=False, stop=True),
                            inc="PE" if half == 1 else None)
            w("vector", "PE")
            oslot = t % 2
            buf = x2r0 if t % 2 == 0 else x2r1
            S.wait_at("vector", "ST0" if t % 2 == 0 else "ST1",
                      S.c("ST0" if t % 2 == 0 else "ST1"))
            S.op("vector", lambda E, b=buf, o=oslot: E.tensor_tensor(
                out=outb[:, o * D:(o + 1) * D], in0=psC[:, :], in1=b[:, :], op=ALU.add), inc="VE")
            S.wait_at("sync", "VE", S.c("VE"))
            st_sem = "ST0" if t % 2 == 0 else "ST1"
            S.op("sync", lambda E, t_=t, o=oslot: E.dma_start(
                out=y_out.rearrange("(t p) f -> p t f", p=128)[:, t_, :],
                in_=outb[:, o * D:(o + 1) * D]), inc=st_sem, amt=16)

        p2_front(0)
        for t in range(NT):
            p2_fc1(t, 0)
            if t + 1 < NT:
                p2_front(t + 1)
            p2_back(t)

        S.wait_at("sync", "ST0", 16 * (NT // 2))
        S.wait_at("sync", "ST1", 16 * (NT // 2))

        es_ph2.close()
        def runner(eng_name):
            def run(E):
                for item in S.q[eng_name]:
                    if item[0] == "w":
                        E.wait_ge(sems[item[1]], item[2])
                    else:
                        ins = item[1](E)
                        if item[2] is not None:
                            ins.then_inc(sems[item[2]], item[3])
            return run

        block.sync(runner("sync"))
        block.gpsimd(runner("gpsimd"))
        block.tensor(runner("tensor"))
        block.vector(runner("vector"))
        block.scalar(runner("scalar"))

    return nc


_NC_CACHE = None


def _get_nc():
    global _NC_CACHE
    if _NC_CACHE is None:
        _NC_CACHE = build_nc()
    return _NC_CACHE


def kernel(L0_attn_x, L0_attn_Wq, L0_attn_bq, L0_attn_Wk, L0_attn_bk, L0_attn_Wv,
           L0_attn_bv, L0_attn_Wo, L0_attn_bo, L0_ffn_w_fc, L0_ffn_b_fc,
           L0_ffn_w_proj, L0_ffn_b_proj):
    x = np.ascontiguousarray(np.asarray(L0_attn_x, dtype=np.float32))
    B = x.shape[0]
    wq = np.ascontiguousarray(np.asarray(L0_attn_Wq, dtype=np.float32))
    wk = np.ascontiguousarray(np.asarray(L0_attn_Wk, dtype=np.float32))
    wv = np.ascontiguousarray(np.asarray(L0_attn_Wv, dtype=np.float32))
    wo = np.ascontiguousarray(np.asarray(L0_attn_Wo, dtype=np.float32))
    bq = np.asarray(L0_attn_bq, dtype=np.float32)
    bk = np.asarray(L0_attn_bk, dtype=np.float32)
    bv = np.asarray(L0_attn_bv, dtype=np.float32).reshape(1, D)
    bo = np.asarray(L0_attn_bo, dtype=np.float32).reshape(1, D)
    wfc = np.ascontiguousarray(np.asarray(L0_ffn_w_fc, dtype=np.float32))
    bfc = np.asarray(L0_ffn_b_fc, dtype=np.float32).reshape(1, HID)
    wpj = np.ascontiguousarray(np.asarray(L0_ffn_w_proj, dtype=np.float32))
    bpj = np.asarray(L0_ffn_b_proj, dtype=np.float32).reshape(1, D)

    bqr = np.ascontiguousarray(bq.reshape(8, 128).T)
    bkr = np.ascontiguousarray(bk.reshape(8, 128).T)

    in_maps = []
    for c in range(8):
        b = c // 4
        J = c % 4
        pos = 2048 * J + np.arange(2048)[:, None]
        ends = 64 * np.arange(128)[None, :]
        mask01 = (ends <= pos).astype(ml_dtypes.bfloat16)
        in_maps.append({
            "xseq": x[b],
            "xown": np.ascontiguousarray(x[b, 2048 * J:2048 * (J + 1)]),
            "wq_in": wq, "wk_in": wk, "wv_in": wv, "wo_in": wo,
            "bqr_in": bqr, "bkr_in": bkr,
            "bv_in": bv, "bo_in": bo, "bfc_in": bfc, "bpj_in": bpj,
            "wfc_in": wfc, "wpj_in": wpj,
            "mask_in": mask01,
        })
    nc = _get_nc()
    res = run_bass_kernel_spmd(nc, in_maps, core_ids=list(range(8)))
    out = np.empty((B, T_SEQ, D), dtype=np.float32)
    for c in range(8):
        b, J = c // 4, c % 4
        out[b, 2048 * J:2048 * (J + 1)] = res.results[c]["y"]
    return out


# version 03
# speedup vs baseline: 1.0187x; 1.0187x over previous
import sys
import contextlib

import numpy as np
import ml_dtypes

sys.path.insert(0, "/opt/trn_rl_repo")
import concourse.bass as bass
import concourse.mybir as mybir
from concourse.bass_utils import run_bass_kernel_spmd

F32 = mybir.dt.float32
BF16 = mybir.dt.bfloat16
ALU = mybir.AluOpType
ACT = mybir.ActivationFunctionType
AXX = mybir.AxisListType.X

LN2 = float(np.log(2.0))
K1 = float(3 * 2**22)
K2 = float(3 * 2**51)
EPS = 1e-5

T_SEQ = 8192
D = 1024
HID = 4096
TOK = 2048
NT = 16
NSUP = 4
NSEQT = 64
NC = 128

ENGS = ("sync", "gpsimd", "tensor", "vector", "scalar")


class Sched:

    def __init__(self):
        self.q = {e: [] for e in ENGS}
        self.cnt = {}

    def wait(self, eng, sem_name):
        v = self.cnt.get(sem_name, 0)
        if v == 0:
            return
        self.q[eng].append(("w", sem_name, v))

    def wait_at(self, eng, sem_name, v):
        if v > 0:
            self.q[eng].append(("w", sem_name, v))

    def op(self, eng, fn, inc=None, amt=1):
        self.q[eng].append(("o", fn, inc, amt))
        if inc is not None:
            self.cnt[inc] = self.cnt.get(inc, 0) + amt

    def c(self, sem_name):
        return self.cnt.get(sem_name, 0)


def build_nc():
    nc = bass.Bass("TRN2", target_bir_lowering=False)

    xseq = nc.dram_tensor("xseq", [T_SEQ, D], F32, kind="ExternalInput").ap()
    xown = nc.dram_tensor("xown", [TOK, D], F32, kind="ExternalInput").ap()
    wq_in = nc.dram_tensor("wq_in", [D, D], F32, kind="ExternalInput").ap()
    wk_in = nc.dram_tensor("wk_in", [D, D], F32, kind="ExternalInput").ap()
    wv_in = nc.dram_tensor("wv_in", [D, D], F32, kind="ExternalInput").ap()
    wo_in = nc.dram_tensor("wo_in", [D, D], F32, kind="ExternalInput").ap()
    bqr_in = nc.dram_tensor("bqr_in", [128, 8], F32, kind="ExternalInput").ap()
    bkr_in = nc.dram_tensor("bkr_in", [128, 8], F32, kind="ExternalInput").ap()
    bv_in = nc.dram_tensor("bv_in", [1, D], F32, kind="ExternalInput").ap()
    bo_in = nc.dram_tensor("bo_in", [1, D], F32, kind="ExternalInput").ap()
    bfc_in = nc.dram_tensor("bfc_in", [1, HID], F32, kind="ExternalInput").ap()
    bpj_in = nc.dram_tensor("bpj_in", [1, D], F32, kind="ExternalInput").ap()
    wfc_in = nc.dram_tensor("wfc_in", [HID, D], F32, kind="ExternalInput").ap()
    wpj_in = nc.dram_tensor("wpj_in", [D, HID], F32, kind="ExternalInput").ap()
    mask_in = nc.dram_tensor("mask_in", [TOK, NC], BF16, kind="ExternalInput").ap()
    y_out = nc.dram_tensor("y", [TOK, D], F32, kind="ExternalOutput").ap()
    xmid_hbm = nc.dram_tensor("xmid_scr", [TOK, D], F32).ap()

    ident_h = nc.inline_tensor(np.eye(128, dtype=ml_dtypes.bfloat16), name="ident_h")
    base3_np = np.zeros((128, 3), dtype=ml_dtypes.bfloat16)
    base3_np[0, 0] = 1.0
    base3_np[1:65, 1] = 1.0
    base3_np[65:128, 2] = 1.0
    base3_h = nc.inline_tensor(base3_np, name="base3_h")
    ilen_np = np.full((128, 1), 1.0 / 64.0, dtype=np.float32)
    ilen_np[0, 0] = 1.0
    ilen_h = nc.inline_tensor(ilen_np, name="ilen_h")

    es = contextlib.ExitStack()
    S = Sched()

    with es:
        sb = lambda name, shape, dt: es.enter_context(nc.sbuf_tensor(name, shape, dt))
        psb = lambda name, shape: es.enter_context(nc.psum_tensor(name, shape, F32))

        ident = sb("ident", [128, 128], BF16)
        base3 = sb("base3", [128, 3], BF16)
        ilen = sb("ilen", [128, 1], F32)
        zeros_b = sb("zeros_b", [128, 128], BF16)
        ones_col = sb("ones_col", [128, 1], BF16)
        ones_row = sb("ones_row", [1, 128], BF16)
        mask_sb = sb("mask_sb", [128, NT * 128], BF16)
        bqr = sb("bqr", [128, 8], F32)
        bkr = sb("bkr", [128, 8], F32)

        Dt = sb("Dt", [128, 1], F32)
        DK2pp = sb("DK2pp", [128, 1], F32)
        EB1pp = sb("EB1pp", [128, 1], F32)

        bnscr = sb("bnscr", [128, 12], F32)
        mv_seq = sb("mv_seq", [128, 2 * NSEQT], F32)
        a_seq = sb("a_seq", [128, NSEQT], F32)
        am_seq = sb("am_seq", [128, NSEQT], F32)
        lnv_scr = sb("lnv_scr", [128, NSEQT], F32)
        mv_own = sb("mv_own", [128, 2 * NT], F32)
        a_own = sb("a_own", [128, NT], F32)
        nam_own = sb("nam_own", [128, NT], F32)

        summ_cm = sb("summ_cm", [128, D], BF16)
        summ_fm = sb("summ_fm", [128, 8 * 128], BF16)
        kfm = sb("kfm", [128, 8 * 128], BF16)
        vcm = sb("vcm", [128, D], BF16)
        sum_bias = sb("sum_bias", [128, 1], F32)

        wfcTq = sb("wfcTq", [128, 8 * HID], BF16)
        wpjTq = sb("wpjTq", [128, 32 * D], BF16)

        wbuf = sb("wbuf", [128, 4 * D], BF16)

        qa_amax = sb("qa_amax", [128, 128], F32)
        qa_e = sb("qa_e", [128, 128], F32)
        qa_s0 = sb("qa_s0", [128, 128], F32)
        qa_s = sb("qa_s", [128, 128], F32)
        qa_m = sb("qa_m", [128, 128], F32)
        qa_eb = sb("qa_eb", [128, 128], F32)

        a2c = sb("a2c", [128, 1], F32)
        nam2c = sb("nam2c", [128, 1], F32)
        lnv2 = sb("lnv2", [128, 1], F32)
        bo_r = sb("bo_r", [1, D], BF16)
        patt0 = sb("patt0", [128, 128], BF16)
        patt1 = sb("patt1", [128, 128], BF16)
        meanb = sb("meanb", [128, 1], BF16)
        ssum = sb("ssum", [128, 16], F32)
        srcp = sb("srcp", [128, 16], F32)

        es_pro = contextlib.ExitStack()
        sbp = lambda name, shape, dt: es_pro.enter_context(nc.sbuf_tensor(name, shape, dt))
        bv_r = sbp("bv_r", [1, D], BF16)
        wstr0 = sbp("wstr0", [128, 2048], F32)
        wstr1 = sbp("wstr1", [128, 2048], F32)
        wqnat = sbp("wqnat", [128, 4096], BF16)
        xsq0 = sbp("xsq0", [128, 2 * D], BF16)
        xsq1 = sbp("xsq1", [128, 2 * D], BF16)

        psA = psb("psA", [128, 1024])
        psB = psb("psB", [128, 1024])
        psC = psb("psC", [128, 1024])
        psD = psb("psD", [128, 512])
        psE = psb("psE", [128, 512])

        sem_names = ["LD", "LG", "ST0", "ST1", "PE", "VE", "AE", "GE", "XM",
                     "XS0", "XS1", "WF0", "WF1", "X20", "X21", "XOW", "WBF"]
        sems = {n: es.enter_context(nc.semaphore("s" + n)) for n in sem_names}
        block = es.enter_context(nc.Block())

        def w(eng, *names):
            for n in names:
                S.wait(eng, n)

        def dma(eng, sem_name, out, in_):
            S.op(eng, lambda E, o=out, i=in_: E.dma_start(out=o, in_=i), inc=sem_name, amt=16)

        def quant_consts(nb, amax_ap, e_ap, s0_ap, s_ap, m_ap, eb_ap):
            S.op("vector", lambda E, a=amax_ap: E.tensor_scalar(
                out=a, in0=a, scalar1=float(2.0**-100), scalar2=None, op0=ALU.max), inc="VE")
            w("scalar", "VE")
            S.op("scalar", lambda E, a=amax_ap, e=e_ap: E.activation(
                out=e, in_=a, func=ACT.Ln), inc="AE")
            w("vector", "AE")
            S.op("vector", lambda E, e=e_ap: E.tensor_scalar(
                out=e, in0=e, scalar1=float(1.0 / LN2), scalar2=float(1e-11 - 0.5),
                op0=ALU.mult, op1=ALU.add), inc="VE")
            S.wait_at("vector", "VE", S.c("VE"))
            S.op("vector", lambda E, e=e_ap: E.tensor_scalar(
                out=e, in0=e, scalar1=DK2pp[:, :], scalar2=EB1pp[:, :],
                op0=ALU.add, op1=ALU.subtract), inc="VE")
            S.wait_at("vector", "VE", S.c("VE"))
            S.op("vector", lambda E, e=e_ap: E.tensor_scalar(
                out=e, in0=e, scalar1=K1, scalar2=3.0,
                op0=ALU.subtract, op1=ALU.subtract), inc="VE")
            w("scalar", "VE")
            S.op("scalar", lambda E, e=e_ap, s0=s0_ap: E.activation(
                out=s0, in_=e, func=ACT.Exp, scale=LN2), inc="AE")
            w("vector", "AE")
            S.op("vector", lambda E, s0=s0_ap, m=m_ap: E.tensor_scalar(
                out=m, in0=s0, scalar1=K1, scalar2=None, op0=ALU.mult), inc="VE")
            S.wait_at("vector", "VE", S.c("VE"))
            S.op("vector", lambda E, s0=s0_ap, s=s_ap, m=m_ap: E.tensor_tensor(
                out=s, in0=s0, in1=m, op=ALU.add), inc="VE")
            S.wait_at("vector", "VE", S.c("VE"))
            S.op("vector", lambda E, s=s_ap, m=m_ap: E.tensor_tensor(
                out=s, in0=s, in1=m, op=ALU.subtract), inc="VE")
            S.wait_at("vector", "VE", S.c("VE"))
            S.op("vector", lambda E, s=s_ap, m=m_ap: E.tensor_scalar(
                out=m, in0=s, scalar1=K1, scalar2=None, op0=ALU.mult), inc="VE")
            S.op("vector", lambda E, s=s_ap, eb=eb_ap: E.tensor_scalar(
                out=eb, in0=s, scalar1=K2, scalar2=None, op0=ALU.mult), inc="VE")
            S.wait_at("vector", "VE", S.c("VE"))
            S.op("vector", lambda E, eb=eb_ap: E.tensor_scalar(
                out=eb, in0=eb, scalar1=Dt[:, :], scalar2=None, op0=ALU.mult), inc="VE")
            S.wait_at("vector", "VE", S.c("VE"))
            S.op("vector", lambda E, eb=eb_ap, m=m_ap: E.tensor_tensor(
                out=eb, in0=eb, in1=m, op=ALU.subtract), inc="VE")
            S.wait_at("vector", "VE", S.c("VE"))

        def quant_apply(v_ap, u_ap, q_ap, eb_bc):
            S.op("vector", lambda E: E.tensor_tensor(out=u_ap, in0=v_ap, in1=eb_bc,
                                                     op=ALU.subtract), inc="VE")
            S.wait_at("vector", "VE", S.c("VE"))
            S.op("vector", lambda E: E.tensor_tensor(out=q_ap, in0=u_ap, in1=eb_bc,
                                                     op=ALU.add), inc="VE")
            S.wait_at("vector", "VE", S.c("VE"))

        def transpose8(src_tile_fn, dst0_ap_fn, nchunk, evac_eng="vector", bank=None):
            pe_bank = bank if bank is not None else psE
            for j0 in range(0, nchunk, 4):
                jn = min(4, nchunk - j0)
                for j in range(j0, j0 + jn):
                    S.op("tensor", lambda E, s=src_tile_fn(j), d=pe_bank[:, 128 * (j - j0):128 * (j - j0) + 128]:
                         E.matmul(d, s, ident[:, :], start=True, stop=True),
                         inc="PE" if j == j0 + jn - 1 else None)
                w(evac_eng, "PE")
                ps3 = pe_bank[:, 0:128 * jn].rearrange("p (j c) -> p j c", c=128)
                dst3 = dst0_ap_fn(j0, jn)
                if evac_eng == "vector":
                    S.op("vector", lambda E, s=ps3, d=dst3: E.tensor_copy(out=d, in_=s), inc="VE")
                else:
                    S.op("scalar", lambda E, s=ps3, d=dst3: E.activation(
                        out=d, in_=s, func=ACT.Identity), inc="AE")
                S.wait(("tensor"), "VE" if evac_eng == "vector" else "AE")

        def strided_dst(base_ap_fn):
            def f(j0, jn):
                a0 = base_ap_fn(j0)
                if jn == 1:
                    return a0.unsqueeze(1)
                a1 = base_ap_fn(j0 + 1)
                raise RuntimeError("use explicit dst fns")
            return f

        dma("sync", "LD", ident[:], ident_h[:])
        dma("sync", "LD", base3[:], base3_h[:])
        dma("sync", "LD", ilen[:], ilen_h[:])
        dma("sync", "LD", bqr[:], bqr_in)
        dma("sync", "LD", bkr[:], bkr_in)
        dma("sync", "LD", mask_sb.ap().rearrange("p (t c) -> p t c", c=128),
            mask_in.rearrange("(t p) c -> p t c", p=128))
        dma("gpsimd", "LG", bv_r[:], bv_in)
        dma("gpsimd", "LG", bo_r[:], bo_in)

        S.op("vector", lambda E: E.memset(zeros_b[:], 0.0), inc="VE")
        S.op("vector", lambda E: E.memset(ones_col[:], 1.0), inc="VE")
        S.op("vector", lambda E: E.memset(ones_row[:], 1.0), inc="VE")
        S.op("vector", lambda E: E.memset(Dt[:], 1.0), inc="VE")
        S.wait_at("vector", "VE", S.c("VE"))
        S.op("vector", lambda E: E.tensor_scalar(out=Dt[:], in0=Dt[:], scalar1=K2,
                                                 scalar2=-K2, op0=ALU.add, op1=ALU.add), inc="VE")
        S.wait_at("vector", "VE", S.c("VE"))
        S.op("vector", lambda E: E.tensor_scalar(out=DK2pp[:], in0=Dt[:], scalar1=K2,
                                                 scalar2=None, op0=ALU.mult), inc="VE")
        S.wait_at("vector", "VE", S.c("VE"))
        S.op("vector", lambda E: E.tensor_scalar(out=EB1pp[:], in0=DK2pp[:], scalar1=K1,
                                                 scalar2=None, op0=ALU.subtract), inc="VE")
        S.wait_at("vector", "VE", S.c("VE"))

        w("tensor", "LD", "VE")
        S.op("tensor", lambda E: E.matmul(psA[:, 0:512], zeros_b[:], mask_sb[:, 0:512],
                                          start=True, stop=False))
        S.op("tensor", lambda E: E.matmul(psA[:, 512:1024], zeros_b[:], mask_sb[:, 512:1024],
                                          start=True, stop=False))
        S.op("tensor", lambda E: E.matmul(psD[:, 0:1], zeros_b[:], ones_col[:],
                                          start=True, stop=False), inc="PE")

        xs_view = xseq.rearrange("(c p) f -> p c f", p=128)
        NCH = 32
        for ch in range(NCH):
            buf = xsq0 if ch % 2 == 0 else xsq1
            if ch >= 2:
                S.wait_at("gpsimd", "PE", S.c("PE"))
            xs_sem = "XS0" if ch % 2 == 0 else "XS1"
            S.op("gpsimd",
                 lambda E, b=buf, c0=ch * 2: E.dma_start(
                     out=b.ap().rearrange("p (t f) -> p t f", f=D),
                     in_=xs_view[:, c0:c0 + 2, :]),
                 inc=xs_sem, amt=16)
            lg_after = S.c(xs_sem)
            S.wait_at("vector", xs_sem, lg_after)
            for tt in range(2):
                t = ch * 2 + tt
                S.op("vector", lambda E, b=buf, o=tt: E.bn_stats(
                    out=bnscr[:, 0:6], in_=b[:, o * D:o * D + 512]), inc="VE")
                S.op("vector", lambda E, b=buf, o=tt: E.bn_stats(
                    out=bnscr[:, 6:12], in_=b[:, o * D + 512:o * D + 1024]), inc="VE")
                S.wait_at("vector", "VE", S.c("VE"))
                S.op("vector", lambda E, t_=t: E.bn_aggr(
                    out=mv_seq[:, 2 * t_:2 * t_ + 2], in_=bnscr[:, :]), inc="VE")
                S.wait_at("vector", "VE", S.c("VE"))
            mvs = mv_seq.ap().rearrange("p (t two) -> p t two", two=2)
            var_sl = mvs[:, ch * 2:(ch + 1) * 2, 1:2]
            mean_sl = mvs[:, ch * 2:(ch + 1) * 2, 0:1]
            lv = lnv_scr[:, ch * 2:(ch + 1) * 2]
            lv3 = lnv_scr.ap().rearrange("p (t one) -> p t one", one=1)[:, ch * 2:(ch + 1) * 2, :]
            S.op("vector", lambda E, v=var_sl, o=lv3: E.tensor_scalar(
                out=o, in0=v, scalar1=EPS, scalar2=None, op0=ALU.add), inc="VE")
            w("scalar", "VE")
            a_sl = a_seq[:, ch * 2:(ch + 1) * 2]
            S.op("scalar", lambda E, i=lv, o=a_sl: E.activation(
                out=o, in_=i, func=ACT.Ln), inc="AE")
            S.wait_at("scalar", "AE", S.c("AE"))
            S.op("scalar", lambda E, o=a_sl: E.activation(
                out=o, in_=o, func=ACT.Exp, scale=-0.5), inc="AE")
            w("vector", "AE")
            am_sl3 = am_seq.ap().rearrange("p (t one) -> p t one", one=1)[:, ch * 2:(ch + 1) * 2, :]
            a_sl3 = a_seq.ap().rearrange("p (t one) -> p t one", one=1)[:, ch * 2:(ch + 1) * 2, :]
            S.op("vector", lambda E, a3=a_sl3, m3=mean_sl, o=am_sl3: E.tensor_tensor(
                out=o, in0=a3, in1=m3, op=ALU.mult), inc="VE")
            ve_now = S.c("VE")
            S.wait_at("tensor", xs_sem, lg_after)
            for tt in range(2):
                t = ch * 2 + tt
                ncols = 2 if t == 63 else 3
                pt = patt0 if t % 2 == 0 else patt1
                S.wait_at("vector", "PE", S.c("PE"))
                S.op("vector", lambda E, p=pt: E.memset(p[:, :], 0.0), inc="VE")
                S.wait_at("vector", "VE", S.c("VE"))
                S.op("vector", lambda E, t_=t, n=ncols, p=pt: E.tensor_scalar(
                    out=p[:, 2 * t_:2 * t_ + n],
                    in0=base3[:, 0:n], scalar1=a_seq[:, t_:t_ + 1],
                    scalar2=None, op0=ALU.mult), inc="VE")
                S.op("vector", lambda E, t_=t: E.tensor_copy(
                    out=meanb[:, :], in_=mv_seq[:, 2 * t_:2 * t_ + 1]), inc="VE")
                S.wait_at("tensor", "VE", S.c("VE"))
                S.op("tensor", lambda E, b=buf, o=tt, p=pt: E.matmul(
                    psA[:, 0:512], p[:, :], b[:, o * D:o * D + 512],
                    start=False, stop=False))
                S.op("tensor", lambda E, b=buf, o=tt, p=pt: E.matmul(
                    psA[:, 512:1024], p[:, :], b[:, o * D + 512:o * D + 1024],
                    start=False, stop=False))
                S.op("tensor", lambda E, p=pt: E.matmul(
                    psD[:, 0:1], p[:, :], meanb[:, :],
                    start=False, stop=False), inc="PE")
                S.wait_at("vector", "PE", S.c("PE"))

        S.op("tensor", lambda E: E.matmul(psA[:, 0:512], zeros_b[:], mask_sb[:, 0:512],
                                          start=False, stop=True))
        S.op("tensor", lambda E: E.matmul(psA[:, 512:1024], zeros_b[:], mask_sb[:, 512:1024],
                                          start=False, stop=True))
        S.op("tensor", lambda E: E.matmul(psD[:, 0:1], zeros_b[:], ones_col[:, :],
                                          start=False, stop=True), inc="PE")
        pe_summary_done = S.c("PE")

        S.wait_at("vector", "PE", pe_summary_done)
        S.op("vector", lambda E: E.tensor_copy(out=sum_bias[:], in_=psD[:, 0:1]), inc="VE")
        S.wait_at("vector", "VE", S.c("VE"))
        S.op("vector", lambda E: E.tensor_scalar(
            out=sum_bias[:], in0=sum_bias[:], scalar1=ilen[:, :], scalar2=-1.0,
            op0=ALU.mult, op1=ALU.mult), inc="VE")
        w("scalar", "VE")
        S.wait_at("scalar", "PE", pe_summary_done)
        S.op("scalar", lambda E: E.activation(
            out=summ_cm[:, :], in_=psA[:, :], func=ACT.Identity,
            bias=sum_bias[:, :], scale=ilen[:, :]), inc="AE")
        ae_summ = S.c("AE")
        S.wait_at("tensor", "AE", ae_summ)
        transpose8(lambda j: summ_cm[:, 128 * j:128 * j + 128],
                   lambda j0, jn: summ_fm.ap().rearrange(
                       "p (j c) -> p j c", c=128)[:, j0:j0 + jn, :], 8, evac_eng="vector")

        kdst = [psA[:, 0:128], psA[:, 512:640], psB[:, 0:128], psB[:, 512:640],
                psC[:, 0:128], psC[:, 512:640], psD[:, 0:128], psE[:, 0:128]]
        wk_v = wk_in.rearrange("(k p) f -> p k f", p=128)
        for hf in range(2):
            S.wait_at("gpsimd", "PE", S.c("PE"))
            S.op("gpsimd", lambda E, h=hf: E.dma_start(
                out=wbuf.ap().rearrange("p (k f) -> p k f", f=D),
                in_=wk_v[:, 4 * h:4 * h + 4, :]), inc="WBF", amt=16)
            S.wait_at("tensor", "WBF", S.c("WBF"))
            for m in range(8):
                for kl in range(4):
                    k = 4 * hf + kl
                    S.op("tensor", lambda E, m_=m, k_=k, kk=kl: E.matmul(
                        kdst[m_],
                        wbuf[:, kk * D + m_ * 128: kk * D + m_ * 128 + 128],
                        summ_fm[:, k_ * 128:k_ * 128 + 128],
                        start=(k_ == 0), stop=(k_ == 7)),
                        inc="PE" if (m == 7 and kl == 3) else None)
        w("scalar", "PE")
        for m in range(8):
            S.op("scalar", lambda E, m_=m: E.activation(
                out=kfm[:, m_ * 128:m_ * 128 + 128], in_=kdst[m_],
                func=ACT.Identity, bias=bkr[:, m_:m_ + 1]),
                inc="AE" if m == 7 else None)
        S.wait_at("tensor", "AE", S.c("AE"))

        wv_v = wv_in.rearrange("(k p) f -> p k f", p=128)
        for hf in range(2):
            S.wait_at("gpsimd", "PE", S.c("PE"))
            S.op("gpsimd", lambda E, h=hf: E.dma_start(
                out=wbuf.ap().rearrange("p (k f) -> p k f", f=D),
                in_=wv_v[:, 4 * h:4 * h + 4, :]), inc="WBF", amt=16)
            S.wait_at("tensor", "WBF", S.c("WBF"))
            for cl in (0, 512):
                vdst = psB[:, 0:512] if cl == 0 else psD[:, 0:512]
                for kl in range(4):
                    k = 4 * hf + kl
                    S.op("tensor", lambda E, k_=k, kk=kl, c=cl, dd=vdst: E.matmul(
                        dd, summ_fm[:, k_ * 128:k_ * 128 + 128],
                        wbuf[:, kk * D + c: kk * D + c + 512],
                        start=(k_ == 0), stop=False),
                        inc="PE" if (cl == 512 and kl == 3) else None)
            if hf == 1:
                S.wait_at("tensor", "LG", S.c("LG"))
                S.op("tensor", lambda E: E.matmul(
                    psB[:, 0:512], ones_row[0:1, :], bv_r[0:1, 0:512],
                    start=False, stop=True))
                S.op("tensor", lambda E: E.matmul(
                    psD[:, 0:512], ones_row[0:1, :], bv_r[0:1, 512:1024],
                    start=False, stop=True), inc="PE")
        w("vector", "PE")
        S.op("vector", lambda E: E.tensor_copy(out=vcm[:, 0:512], in_=psB[:, 0:512]), inc="VE")
        S.op("vector", lambda E: E.tensor_copy(out=vcm[:, 512:1024], in_=psD[:, 0:512]), inc="VE")
        ve_vcm = S.c("VE")

        wfc_v = wfc_in.rearrange("(n p) f -> p n f", p=128)
        wpj_v = wpj_in.rearrange("(n p) f -> p n f", p=128)
        jobs = [("fc", n) for n in range(32)] + [("pj", q) for q in range(16)]

        def job_dma(i):
            kind, n = jobs[i]
            par = i % 2
            buf = wstr0 if par == 0 else wstr1
            wf_sem = "WF0" if par == 0 else "WF1"
            if kind == "fc":
                dma("sync", wf_sem, buf[:, 0:D], wfc_v[:, n, :])
            else:
                dma("sync", wf_sem, buf[:, :], wpj_v[:, n // 2, 2048 * (n % 2):2048 * (n % 2) + 2048])
            return S.c(wf_sem)

        def job_nb(i):
            return 32 if jobs[i][0] == "fc" else 64

        wf_totals = {}
        def stage_R(i):
            par, nb = i % 2, job_nb(i)
            buf = wstr0 if par == 0 else wstr1
            o = 64 * par
            S.wait_at("vector", "WF0" if par == 0 else "WF1", wf_totals[i])
            v3 = buf.ap()[:, 0:32 * nb].rearrange("p (b e) -> p b e", e=32)
            S.op("vector", lambda E, v=v3, o_=o, nb_=nb: E.tensor_reduce(
                out=qa_amax[:, o_:o_ + nb_], in_=v, axis=AXX, op=ALU.max,
                apply_absolute_value=True), inc="VE")
            S.wait_at("vector", "VE", S.c("VE"))
            S.op("vector", lambda E, o_=o, nb_=nb: E.tensor_scalar(
                out=qa_amax[:, o_:o_ + nb_], in0=qa_amax[:, o_:o_ + nb_],
                scalar1=float(2.0**-100), scalar2=None, op0=ALU.max), inc="VE")

        def stage_L(i):
            par, nb = i % 2, job_nb(i)
            o = 64 * par
            w("scalar", "VE")
            S.op("scalar", lambda E, o_=o, nb_=nb: E.activation(
                out=qa_e[:, o_:o_ + nb_], in_=qa_amax[:, o_:o_ + nb_], func=ACT.Ln), inc="AE")

        def stage_E1(i):
            par, nb = i % 2, job_nb(i)
            o = 64 * par
            w("vector", "AE")
            S.op("vector", lambda E, o_=o, nb_=nb: E.tensor_scalar(
                out=qa_e[:, o_:o_ + nb_], in0=qa_e[:, o_:o_ + nb_],
                scalar1=float(1.0 / LN2), scalar2=float(1e-11 - 0.5),
                op0=ALU.mult, op1=ALU.add), inc="VE")
            S.wait_at("vector", "VE", S.c("VE"))
            S.op("vector", lambda E, o_=o, nb_=nb: E.tensor_scalar(
                out=qa_e[:, o_:o_ + nb_], in0=qa_e[:, o_:o_ + nb_],
                scalar1=DK2pp[:, :], scalar2=EB1pp[:, :], op0=ALU.add, op1=ALU.subtract), inc="VE")
            S.wait_at("vector", "VE", S.c("VE"))
            S.op("vector", lambda E, o_=o, nb_=nb: E.tensor_scalar(
                out=qa_e[:, o_:o_ + nb_], in0=qa_e[:, o_:o_ + nb_],
                scalar1=K1, scalar2=3.0, op0=ALU.subtract, op1=ALU.subtract), inc="VE")

        def stage_X(i):
            par, nb = i % 2, job_nb(i)
            o = 64 * par
            w("scalar", "VE")
            S.op("scalar", lambda E, o_=o, nb_=nb: E.activation(
                out=qa_s0[:, o_:o_ + nb_], in_=qa_e[:, o_:o_ + nb_], func=ACT.Exp,
                scale=LN2), inc="AE")

        def stage_S(i):
            par, nb = i % 2, job_nb(i)
            o = 64 * par
            w("vector", "AE")
            for fn in (
                lambda E, o_=o, nb_=nb: E.tensor_scalar(
                    out=qa_m[:, o_:o_ + nb_], in0=qa_s0[:, o_:o_ + nb_], scalar1=K1,
                    scalar2=None, op0=ALU.mult),
                lambda E, o_=o, nb_=nb: E.tensor_tensor(
                    out=qa_s[:, o_:o_ + nb_], in0=qa_s0[:, o_:o_ + nb_],
                    in1=qa_m[:, o_:o_ + nb_], op=ALU.add),
                lambda E, o_=o, nb_=nb: E.tensor_tensor(
                    out=qa_s[:, o_:o_ + nb_], in0=qa_s[:, o_:o_ + nb_],
                    in1=qa_m[:, o_:o_ + nb_], op=ALU.subtract),
                lambda E, o_=o, nb_=nb: E.tensor_scalar(
                    out=qa_m[:, o_:o_ + nb_], in0=qa_s[:, o_:o_ + nb_], scalar1=K1,
                    scalar2=None, op0=ALU.mult),
                lambda E, o_=o, nb_=nb: E.tensor_scalar(
                    out=qa_eb[:, o_:o_ + nb_], in0=qa_s[:, o_:o_ + nb_], scalar1=K2,
                    scalar2=None, op0=ALU.mult),
                lambda E, o_=o, nb_=nb: E.tensor_scalar(
                    out=qa_eb[:, o_:o_ + nb_], in0=qa_eb[:, o_:o_ + nb_],
                    scalar1=Dt[:, :], scalar2=None, op0=ALU.mult),
                lambda E, o_=o, nb_=nb: E.tensor_tensor(
                    out=qa_eb[:, o_:o_ + nb_], in0=qa_eb[:, o_:o_ + nb_],
                    in1=qa_m[:, o_:o_ + nb_], op=ALU.subtract),
            ):
                S.op("vector", fn, inc="VE")
                S.wait_at("vector", "VE", S.c("VE"))

        def stage_Q(i):
            par, nb = i % 2, job_nb(i)
            buf = wstr0 if par == 0 else wstr1
            o = 64 * par
            v3 = buf.ap()[:, 0:32 * nb].rearrange("p (b e) -> p b e", e=32)
            q3 = wqnat.ap()[:, 2048 * par:2048 * par + 32 * nb].rearrange("p (b e) -> p b e", e=32)
            ebb = qa_eb.ap()[:, o:o + nb].unsqueeze(2).broadcast_to([128, nb, 32])
            S.wait_at("vector", "PE", S.c("PE"))
            S.op("vector", lambda E, a=v3, b=ebb: E.tensor_tensor(
                out=a, in0=a, in1=b, op=ALU.subtract), inc="VE")
            S.wait_at("vector", "VE", S.c("VE"))
            S.op("vector", lambda E, a=v3, b=ebb, q=q3: E.tensor_tensor(
                out=q, in0=a, in1=b, op=ALU.add), inc="VE")
            S.wait_at("tensor", "VE", S.c("VE"))

        def stage_T(i):
            kind, n = jobs[i]
            par = i % 2
            base = 2048 * par
            if kind == "fc":
                transpose8(lambda j, b=base: wqnat[:, b + 128 * j:b + 128 * j + 128],
                           lambda j0, jn, h_=n: wfcTq.ap().rearrange(
                               "p (j r) -> p j r", r=HID)[:, j0:j0 + jn,
                                                          128 * h_:128 * h_ + 128],
                           8, evac_eng="scalar")
            else:
                h_, hf = n // 2, n % 2
                transpose8(lambda j, b=base: wqnat[:, b + 128 * j:b + 128 * j + 128],
                           lambda j0, jn, hh=h_, f_=hf: wpjTq.ap().rearrange(
                               "p (j r) -> p j r", r=D)[:, 16 * f_ + j0:16 * f_ + j0 + jn,
                                                        128 * hh:128 * hh + 128],
                           16, evac_eng="scalar")

        NJOB = len(jobs)
        wf_totals[0] = job_dma(0)
        wf_totals[1] = job_dma(1)
        for i in range(NJOB + 1):
            if i < NJOB:
                stage_R(i)
                stage_L(i)
            if i >= 1:
                stage_Q(i - 1)
                if i + 1 < NJOB:
                    S.wait_at("sync", "VE", S.c("VE"))
                    wf_totals[i + 1] = job_dma(i + 1)
            if i < NJOB:
                stage_E1(i)
                stage_X(i)
            if i >= 1:
                stage_T(i - 1)
            if i < NJOB:
                stage_S(i)

        es_pro.close()
        es_ph1 = contextlib.ExitStack()
        sb1 = lambda name, shape, dt: es_ph1.enter_context(nc.sbuf_tensor(name, shape, dt))
        xring = sb1("xring", [128, 4 * D], F32)
        lnbuf = sb1("lnbuf", [128, D], BF16)
        ln1fm = sb1("ln1fm", [128, 8 * 512], BF16)
        qfm = sb1("qfm", [128, 8 * 512], BF16)
        ebuf = sb1("ebuf", [128, 2 * 16 * 128], BF16)
        xmid_t = sb1("xmid_t", [128, D], F32)
        ofm = ln1fm

        for e in ("sync", "gpsimd", "tensor", "vector", "scalar"):
            S.wait_at(e, "VE", S.c("VE"))
            S.wait_at(e, "AE", S.c("AE"))
            S.wait_at(e, "PE", S.c("PE"))

        st_count = 0

        def p1_loads_stats(s):
            S.wait_at("sync", "VE", S.c("VE"))
            for tt in range(4):
                t = 4 * s + tt
                slot = t % 4
                dma("sync", "XOW", xring[:, slot * D:(slot + 1) * D],
                    xown.rearrange("(t p) f -> p t f", p=128)[:, t, :])
            ld_xown = S.c("XOW")
            S.wait_at("vector", "XOW", ld_xown)
            for tt in range(4):
                t = 4 * s + tt
                slot = t % 4
                xo = xring[:, slot * D:(slot + 1) * D]
                S.op("vector", lambda E, x=xo: E.bn_stats(out=bnscr[:, 0:6], in_=x[:, 0:512]), inc="VE")
                S.op("vector", lambda E, x=xo: E.bn_stats(out=bnscr[:, 6:12], in_=x[:, 512:1024]), inc="VE")
                S.wait_at("vector", "VE", S.c("VE"))
                S.op("vector", lambda E, t_=t: E.bn_aggr(
                    out=mv_own[:, 2 * t_:2 * t_ + 2], in_=bnscr[:, :]), inc="VE")
                S.wait_at("vector", "VE", S.c("VE"))
            mvo = mv_own.ap().rearrange("p (t two) -> p t two", two=2)
            var_sl = mvo[:, 4 * s:4 * s + 4, 1:2]
            mean_sl = mvo[:, 4 * s:4 * s + 4, 0:1]
            lv3 = lnv_scr.ap().rearrange("p (t one) -> p t one", one=1)[:, 0:4, :]
            S.op("vector", lambda E, v=var_sl, o=lv3: E.tensor_scalar(
                out=o, in0=v, scalar1=EPS, scalar2=None, op0=ALU.add), inc="VE")
            w("scalar", "VE")
            a_sl = a_own[:, 4 * s:4 * s + 4]
            S.op("scalar", lambda E, o=a_sl: E.activation(
                out=o, in_=lnv_scr[:, 0:4], func=ACT.Ln), inc="AE")
            S.wait_at("scalar", "AE", S.c("AE"))
            S.op("scalar", lambda E, o=a_sl: E.activation(
                out=o, in_=o, func=ACT.Exp, scale=-0.5), inc="AE")
            w("vector", "AE")
            a3 = a_own.ap().rearrange("p (t one) -> p t one", one=1)[:, 4 * s:4 * s + 4, :]
            n3 = nam_own.ap().rearrange("p (t one) -> p t one", one=1)[:, 4 * s:4 * s + 4, :]
            S.op("vector", lambda E, a=a3, m=mean_sl, o=n3: E.tensor_tensor(
                out=o, in0=a, in1=m, op=ALU.mult), inc="VE")
            S.wait_at("vector", "VE", S.c("VE"))
            S.op("vector", lambda E, o=n3: E.tensor_scalar(
                out=nam_own[:, 4 * s:4 * s + 4], in0=nam_own[:, 4 * s:4 * s + 4],
                scalar1=-1.0, scalar2=None, op0=ALU.mult), inc="VE")

        def p1_lnq(s):
            S.wait_at("tensor", "VE", S.c("VE"))
            for tt in range(4):
                t = 4 * s + tt
                slot = t % 4
                xo = xring[:, slot * D:(slot + 1) * D]
                w("scalar", "VE")
                S.wait_at("scalar", "PE", S.c("PE"))
                S.op("scalar", lambda E, x=xo, t_=t: E.activation(
                    out=lnbuf[:, 0:D], in_=x, func=ACT.Identity,
                    scale=a_own[:, t_:t_ + 1], bias=nam_own[:, t_:t_ + 1]), inc="AE")
                S.wait_at("tensor", "AE", S.c("AE"))
                transpose8(lambda j: lnbuf[:, 128 * j:128 * j + 128],
                           lambda j0, jn, u=tt: ln1fm.ap().rearrange(
                               "p (j q) -> p j q", q=512)[:, j0:j0 + jn, 128 * u:128 * u + 128],
                           8, evac_eng="vector")
            qdst = [psA[:, 0:512], psA[:, 512:1024], psB[:, 0:512], psB[:, 512:1024],
                    psC[:, 0:512], psC[:, 512:1024], psD[:, 0:512], psE[:, 0:512]]
            wq_v = wq_in.rearrange("(k p) f -> p k f", p=128)
            for hf in range(2):
                S.wait_at("gpsimd", "PE", S.c("PE"))
                S.op("gpsimd", lambda E, h=hf: E.dma_start(
                    out=wbuf.ap().rearrange("p (k f) -> p k f", f=D),
                    in_=wq_v[:, 4 * h:4 * h + 4, :]), inc="WBF", amt=16)
                S.wait_at("tensor", "WBF", S.c("WBF"))
                for m in range(8):
                    for kl in range(4):
                        k = 4 * hf + kl
                        S.op("tensor", lambda E, m_=m, k_=k, kk=kl: E.matmul(
                            qdst[m_], wbuf[:, kk * D + m_ * 128: kk * D + m_ * 128 + 128],
                            ln1fm[:, k_ * 512:(k_ + 1) * 512],
                            start=(k_ == 0), stop=(k_ == 7)),
                            inc="PE" if (m == 7 and kl == 3) else None)
            w("scalar", "PE")
            for m in range(8):
                S.op("scalar", lambda E, m_=m: E.activation(
                    out=qfm[:, m_ * 512:(m_ + 1) * 512], in_=qdst[m_],
                    func=ACT.Identity, bias=bqr[:, m_:m_ + 1]),
                    inc="AE" if m == 7 else None)
            S.wait_at("tensor", "AE", S.c("AE"))
            S.wait_at("gpsimd", "PE", S.c("PE"))
            wo_v = wo_in.rearrange("(k p) f -> p k f", p=128)
            S.op("gpsimd", lambda E: E.dma_start(
                out=wbuf.ap().rearrange("p (k f) -> p k f", f=D),
                in_=wo_v[:, 0:4, :]), inc="WBF", amt=16)

        def p1_scores(s, u, epar):
            t = 4 * s + u
            S.wait_at("tensor", "AE", S.c("AE"))
            for m in range(8):
                qsl_e = qfm[0:64, m * 512 + 128 * u: m * 512 + 128 * u + 128]
                qsl_o = qfm[64:128, m * 512 + 128 * u: m * 512 + 128 * u + 128]
                ksl_e = kfm[0:64, m * 128:(m + 1) * 128]
                ksl_o = kfm[64:128, m * 128:(m + 1) * 128]
                dstA = psA[:, 128 * (m % 4):128 * (m % 4) + 128] if m < 4 else \
                    psB[:, 128 * (m % 4):128 * (m % 4) + 128]
                dstB = psA[:, 512 + 128 * (m % 4):512 + 128 * (m % 4) + 128] if m < 4 else \
                    psB[:, 512 + 128 * (m % 4):512 + 128 * (m % 4) + 128]
                S.op("tensor", lambda E, q=qsl_e, kk=ksl_e, d=dstA: E.matmul(
                    d, q, kk, start=True, stop=True, tile_position=(0, 0)))
                S.op("tensor", lambda E, q=qsl_o, kk=ksl_o, d=dstB: E.matmul(
                    d, q, kk, start=True, stop=True, tile_position=(64, 0)),
                    inc="PE" if m == 7 else None)
            w("scalar", "PE")
            S.wait_at("scalar", "PE", S.c("PE"))
            eb = 2048 * epar
            S.op("scalar", lambda E, o=eb: E.activation(
                out=ebuf[:, o:o + 1024], in_=psA[:, :], func=ACT.Exp, scale=0.125), inc="AE")
            S.op("scalar", lambda E, o=eb: E.activation(
                out=ebuf[:, o + 1024:o + 2048], in_=psB[:, :], func=ACT.Exp, scale=0.125), inc="AE")

        def p1_softmax_av(s, u, epar):
            t = 4 * s + u
            eb = 2048 * epar
            w("vector", "AE")
            e3 = ebuf.ap()[:, eb:eb + 2048].rearrange("p (g c) -> p g c", c=128)
            msk = mask_sb.ap()[:, 128 * t:128 * t + 128].unsqueeze(1).broadcast_to([128, 16, 128])
            S.op("vector", lambda E, e=e3, mk=msk: E.tensor_tensor(
                out=e, in0=e, in1=mk, op=ALU.mult), inc="VE")
            S.wait_at("vector", "VE", S.c("VE"))
            S.op("vector", lambda E, e=e3: E.tensor_reduce(
                out=ssum[:, :], in_=e, axis=AXX, op=ALU.add), inc="VE")
            S.wait_at("vector", "VE", S.c("VE"))
            S.op("vector", lambda E: E.reciprocal(out=srcp[:, :], in_=ssum[:, :]), inc="VE")
            S.wait_at("vector", "VE", S.c("VE"))
            rb = srcp.ap().unsqueeze(2).broadcast_to([128, 16, 128])
            S.op("vector", lambda E, e=e3, r=rb: E.tensor_tensor(
                out=e, in0=e, in1=r, op=ALU.mult), inc="VE")
            S.wait_at("tensor", "VE", S.c("VE"))
            transpose8(lambda j, o=eb: ebuf[:, o + 128 * j:o + 128 * j + 128],
                       lambda j0, jn, o=eb: ebuf.ap()[:, o:o + 2048].rearrange(
                           "p (j c) -> p j c", c=128)[:, j0:j0 + jn, :],
                       16, evac_eng="vector")
            for m0 in range(0, 8, 4):
                for mm in range(4):
                    m = m0 + mm
                    g_e = (m // 4) * 8 + (m % 4)
                    g_o = (m // 4) * 8 + 4 + (m % 4)
                    h_e, h_o = 2 * m, 2 * m + 1
                    S.op("tensor", lambda E, g=g_e, h=h_e, c=128 * mm, o=eb: E.matmul(
                        psD[0:64, c:c + 128], vcm[:, 64 * h:64 * h + 64],
                        ebuf[:, o + 128 * g:o + 128 * g + 128], start=True, stop=True,
                        tile_position=(0, 0)))
                    S.op("tensor", lambda E, g=g_o, h=h_o, c=128 * mm, o=eb: E.matmul(
                        psD[64:128, c:c + 128], vcm[:, 64 * h:64 * h + 64],
                        ebuf[:, o + 128 * g:o + 128 * g + 128], start=True, stop=True,
                        tile_position=(0, 64)), inc="PE" if mm == 3 else None)
                S.wait_at("vector", "PE", S.c("PE"))
                dst3 = ofm.ap().rearrange("p (m q) -> p m q", q=512)[
                    :, m0:m0 + 4, 128 * u:128 * u + 128]
                ps3 = psD.ap().rearrange("p (m c) -> p m c", c=128)
                S.op("vector", lambda E, d=dst3, s_=ps3: E.tensor_copy(
                    out=d, in_=s_), inc="VE")
                S.wait_at("tensor", "VE", S.c("VE"))

        def p1_wo_res(s):
            global_st = None
            odst = [(psA[:, 0:512], psA[:, 512:1024]), (psB[:, 0:512], psB[:, 512:1024]),
                    (psC[:, 0:512], psC[:, 512:1024]), (psD[:, 0:512], psE[:, 0:512])]
            wo_v = wo_in.rearrange("(k p) f -> p k f", p=128)
            for hf in range(2):
                if hf == 1:
                    S.wait_at("gpsimd", "PE", S.c("PE"))
                    S.op("gpsimd", lambda E: E.dma_start(
                        out=wbuf.ap().rearrange("p (k f) -> p k f", f=D),
                        in_=wo_v[:, 4:8, :]), inc="WBF", amt=16)
                S.wait_at("tensor", "WBF", S.c("WBF"))
                for u in range(4):
                    for half in range(2):
                        cl = half * 512
                        dd = odst[u][half]
                        for ml in range(4):
                            m = 4 * hf + ml
                            S.op("tensor", lambda E, m_=m, mm=ml, c=cl, u_=u, d=dd: E.matmul(
                                d, ofm[:, m_ * 512 + 128 * u_: m_ * 512 + 128 * u_ + 128],
                                wbuf[:, mm * D + c: mm * D + c + 512],
                                start=(m_ == 0), stop=False),
                                inc="PE" if (u == 3 and half == 1 and ml == 3) else None)
                        if hf == 1:
                            S.wait_at("tensor", "LG", S.c("LG"))
                            S.op("tensor", lambda E, c=cl, d=dd: E.matmul(
                                d, ones_row[0:1, :], bo_r[0:1, c:c + 512],
                                start=False, stop=True),
                                inc="PE" if (u == 3 and half == 1) else None)
            w("vector", "PE")
            for u in range(4):
                t = 4 * s + u
                slot = t % 4
                xo = xring[:, slot * D:(slot + 1) * D]
                S.op("vector", lambda E, d0=odst[u][0], x=xo: E.tensor_tensor(
                    out=xmid_t[:, 0:512], in0=d0, in1=x[:, 0:512], op=ALU.add), inc="VE")
                S.op("vector", lambda E, d1=odst[u][1], x=xo: E.tensor_tensor(
                    out=xmid_t[:, 512:1024], in0=d1, in1=x[:, 512:1024], op=ALU.add), inc="VE")
                S.wait_at("sync", "VE", S.c("VE"))
                S.op("sync", lambda E, t_=t: E.dma_start(
                    out=xmid_hbm.rearrange("(t p) f -> p t f", p=128)[:, t_, :],
                    in_=xmid_t[:, :]), inc="XM", amt=16)
                S.wait_at("vector", "XM", S.c("XM"))

        p1_next_loads_stats = p1_loads_stats

        p1_loads_stats(0)
        for s in range(NSUP):
            p1_lnq(s)
            p1_scores(s, 0, 0)
            for u in range(4):
                if u + 1 < 4:
                    p1_scores(s, u + 1, (u + 1) % 2)
                p1_softmax_av(s, u, u % 2)
            p1_wo_res(s)
            if s + 1 < NSUP:
                p1_next_loads_stats(s + 1)
        es_ph1.close()
        es_ph2 = contextlib.ExitStack()
        sb2 = lambda name, shape, dt: es_ph2.enter_context(nc.sbuf_tensor(name, shape, dt))
        x2r0 = sb2("x2r0", [128, D], F32)
        x2r1 = sb2("x2r1", [128, D], F32)
        xln2 = sb2("xln2", [128, D], F32)
        xqb = sb2("xqb", [128, D], BF16)
        xqfm = sb2("xqfm", [128, 2 * 8 * 128], BF16)
        hbuf = sb2("hbuf", [128, 2 * D], F32)
        hqb = sb2("hqb", [128, D], BF16)
        hqT = sb2("hqT", [128, 2 * 8 * 128], BF16)
        outb = sb2("outb", [128, 2 * D], F32)
        bfc_r = sb2("bfc_r", [1, HID], BF16)
        bpj_r = sb2("bpj_r", [1, D], BF16)
        for e in ("sync", "gpsimd", "tensor", "vector", "scalar"):
            S.wait_at(e, "XM", 16 * NT)
        dma("gpsimd", "LG", bfc_r[:], bfc_in)
        dma("gpsimd", "LG", bpj_r[:], bpj_in)
        lg_bias2 = S.c("LG")
        S.wait_at("tensor", "LG", lg_bias2)

        def p2_front(t):
            buf = x2r0 if t % 2 == 0 else x2r1
            S.wait_at("sync", "VE", S.c("VE"))
            x2_sem = "X20" if t % 2 == 0 else "X21"
            dma("sync", x2_sem, buf[:, :], xmid_hbm.rearrange("(t p) f -> p t f", p=128)[:, t, :])
            S.wait_at("vector", x2_sem, S.c(x2_sem))
            S.op("vector", lambda E, b=buf: E.bn_stats(out=bnscr[:, 0:6], in_=b[:, 0:512]), inc="VE")
            S.op("vector", lambda E, b=buf: E.bn_stats(out=bnscr[:, 6:12], in_=b[:, 512:1024]), inc="VE")
            S.wait_at("vector", "VE", S.c("VE"))
            S.op("vector", lambda E: E.bn_aggr(out=mv_own[:, 0:2], in_=bnscr[:, :]), inc="VE")
            S.wait_at("vector", "VE", S.c("VE"))
            S.op("vector", lambda E: E.tensor_scalar(
                out=lnv2[:, 0:1], in0=mv_own[:, 1:2], scalar1=EPS, scalar2=None,
                op0=ALU.add), inc="VE")
            w("scalar", "VE")
            S.op("scalar", lambda E: E.activation(out=a2c[:], in_=lnv2[:, 0:1], func=ACT.Ln), inc="AE")
            S.wait_at("scalar", "AE", S.c("AE"))
            S.op("scalar", lambda E: E.activation(out=a2c[:], in_=a2c[:], func=ACT.Exp,
                                                  scale=-0.5), inc="AE")
            w("vector", "AE")
            S.op("vector", lambda E: E.tensor_tensor(out=nam2c[:], in0=a2c[:],
                                                     in1=mv_own[:, 0:1], op=ALU.mult), inc="VE")
            S.wait_at("vector", "VE", S.c("VE"))
            S.op("vector", lambda E: E.tensor_scalar(out=nam2c[:], in0=nam2c[:], scalar1=-1.0,
                                                     scalar2=None, op0=ALU.mult), inc="VE")
            w("scalar", "VE")
            S.op("scalar", lambda E, b=buf: E.activation(
                out=xln2[:, :], in_=b[:, :], func=ACT.Identity, scale=a2c[:, :],
                bias=nam2c[:, :]), inc="AE")
            w("vector", "AE")
            o = 96
            v3 = xln2.ap().rearrange("p (b e) -> p b e", e=32)
            S.op("vector", lambda E, v=v3, o_=o: E.tensor_reduce(
                out=qa_amax[:, o_:o_ + 32], in_=v, axis=AXX, op=ALU.max,
                apply_absolute_value=True), inc="VE")
            S.wait_at("vector", "VE", S.c("VE"))
            quant_consts(32, qa_amax[:, o:o + 32], qa_e[:, o:o + 32], qa_s0[:, o:o + 32],
                         qa_s[:, o:o + 32], qa_m[:, o:o + 32], qa_eb[:, o:o + 32])
            ebb = qa_eb.ap()[:, o:o + 32].unsqueeze(2).broadcast_to([128, 32, 32])
            q3 = xqb.ap().rearrange("p (b e) -> p b e", e=32)
            S.wait_at("vector", "PE", S.c("PE"))
            quant_apply(v3, v3, q3, ebb)
            S.wait_at("tensor", "VE", S.c("VE"))
            xpar = t % 2
            transpose8(lambda j: xqb[:, 128 * j:128 * j + 128],
                       lambda j0, jn, xp=xpar: xqfm.ap().rearrange(
                           "p (j c) -> p j c", c=128)[:, 8 * xp + j0:8 * xp + j0 + jn, :],
                       8, evac_eng="vector", bank=psD)

        def p2_fc1(t, g):
            xpar = t % 2
            dst = psA if g % 2 == 0 else psB
            for k in range(8):
                S.op("tensor", lambda E, k_=k, g_=g, d=dst, xp=xpar: E.matmul(
                    d[:, 0:512], xqfm[:, 1024 * xp + 128 * k_:1024 * xp + 128 * k_ + 128],
                    wfcTq[:, k_ * HID + g_ * D: k_ * HID + g_ * D + 512],
                    start=(k_ == 0), stop=False))
                S.op("tensor", lambda E, k_=k, g_=g, d=dst, xp=xpar: E.matmul(
                    d[:, 512:1024], xqfm[:, 1024 * xp + 128 * k_:1024 * xp + 128 * k_ + 128],
                    wfcTq[:, k_ * HID + g_ * D + 512: k_ * HID + g_ * D + 1024],
                    start=(k_ == 0), stop=False))
            S.op("tensor", lambda E, g_=g, d=dst: E.matmul(
                d[:, 0:512], ones_row[0:1, :], bfc_r[0:1, g_ * D:g_ * D + 512],
                start=False, stop=True))
            S.op("tensor", lambda E, g_=g, d=dst: E.matmul(
                d[:, 512:1024], ones_row[0:1, :], bfc_r[0:1, g_ * D + 512:g_ * D + 1024],
                start=False, stop=True), inc="PE")

        def p2_back(t):
            for g in range(4):
                dst = psA if g % 2 == 0 else psB
                gs = g % 2
                w("scalar", "PE")
                S.wait_at("scalar", "VE", S.c("VE"))
                S.op("scalar", lambda E, d=dst, s_=gs: E.activation(
                    out=hbuf[:, s_ * D:(s_ + 1) * D], in_=d[:, :], func=ACT.Gelu), inc="AE")
                if g < 3:
                    p2_fc1(t, g + 1)
                w("vector", "AE")
                hg = hbuf.ap()[:, gs * D:(gs + 1) * D].rearrange("p (b e) -> p b e", e=32)
                S.op("vector", lambda E, v=hg: E.tensor_reduce(
                    out=qa_amax[:, 0:32], in_=v, axis=AXX, op=ALU.max,
                    apply_absolute_value=True), inc="VE")
                S.wait_at("vector", "VE", S.c("VE"))
                quant_consts(32, qa_amax[:, 0:32], qa_e[:, 0:32], qa_s0[:, 0:32],
                             qa_s[:, 0:32], qa_m[:, 0:32], qa_eb[:, 0:32])
                ebg = qa_eb.ap()[:, 0:32].unsqueeze(2).broadcast_to([128, 32, 32])
                qg = hqb.ap().rearrange("p (b e) -> p b e", e=32)
                S.wait_at("vector", "PE", S.c("PE"))
                quant_apply(hg, hg, qg, ebg)
                S.wait_at("tensor", "VE", S.c("VE"))
                transpose8(lambda j: hqb[:, 128 * j:128 * j + 128],
                           lambda j0, jn, s_=gs: hqT.ap().rearrange(
                               "p (j c) -> p j c", c=128)[:, 8 * s_ + j0:8 * s_ + j0 + jn, :],
                           8, evac_eng="vector")
                for half in range(2):
                    cl = half * 512
                    for cj in range(8):
                        c = 8 * g + cj
                        S.op("tensor", lambda E, c_=c, cc=cl, cjj=cj, s_=gs: E.matmul(
                            psC[:, cc:cc + 512],
                            hqT[:, 1024 * s_ + 128 * cjj:1024 * s_ + 128 * cjj + 128],
                            wpjTq[:, c_ * D + cc: c_ * D + cc + 512],
                            start=(c_ == 0), stop=False))
                    if g == 3:
                        S.op("tensor", lambda E, cc=cl: E.matmul(
                            psC[:, cc:cc + 512], ones_row[0:1, :], bpj_r[0:1, cc:cc + 512],
                            start=False, stop=True),
                            inc="PE" if half == 1 else None)
            w("vector", "PE")
            oslot = t % 2
            buf = x2r0 if t % 2 == 0 else x2r1
            S.wait_at("vector", "ST0" if t % 2 == 0 else "ST1",
                      S.c("ST0" if t % 2 == 0 else "ST1"))
            S.op("vector", lambda E, b=buf, o=oslot: E.tensor_tensor(
                out=outb[:, o * D:(o + 1) * D], in0=psC[:, :], in1=b[:, :], op=ALU.add), inc="VE")
            S.wait_at("sync", "VE", S.c("VE"))
            st_sem = "ST0" if t % 2 == 0 else "ST1"
            S.op("sync", lambda E, t_=t, o=oslot: E.dma_start(
                out=y_out.rearrange("(t p) f -> p t f", p=128)[:, t_, :],
                in_=outb[:, o * D:(o + 1) * D]), inc=st_sem, amt=16)

        p2_front(0)
        for t in range(NT):
            p2_fc1(t, 0)
            if t + 1 < NT:
                p2_front(t + 1)
            p2_back(t)

        S.wait_at("sync", "ST0", 16 * (NT // 2))
        S.wait_at("sync", "ST1", 16 * (NT // 2))

        es_ph2.close()
        def runner(eng_name):
            def run(E):
                for item in S.q[eng_name]:
                    if item[0] == "w":
                        E.wait_ge(sems[item[1]], item[2])
                    else:
                        ins = item[1](E)
                        if item[2] is not None:
                            ins.then_inc(sems[item[2]], item[3])
            return run

        block.sync(runner("sync"))
        block.gpsimd(runner("gpsimd"))
        block.tensor(runner("tensor"))
        block.vector(runner("vector"))
        block.scalar(runner("scalar"))

    return nc


_NC_CACHE = None


def _get_nc():
    global _NC_CACHE
    if _NC_CACHE is None:
        _NC_CACHE = build_nc()
    return _NC_CACHE


def kernel(L0_attn_x, L0_attn_Wq, L0_attn_bq, L0_attn_Wk, L0_attn_bk, L0_attn_Wv,
           L0_attn_bv, L0_attn_Wo, L0_attn_bo, L0_ffn_w_fc, L0_ffn_b_fc,
           L0_ffn_w_proj, L0_ffn_b_proj):
    x = np.ascontiguousarray(np.asarray(L0_attn_x, dtype=np.float32))
    B = x.shape[0]
    wq = np.ascontiguousarray(np.asarray(L0_attn_Wq, dtype=np.float32))
    wk = np.ascontiguousarray(np.asarray(L0_attn_Wk, dtype=np.float32))
    wv = np.ascontiguousarray(np.asarray(L0_attn_Wv, dtype=np.float32))
    wo = np.ascontiguousarray(np.asarray(L0_attn_Wo, dtype=np.float32))
    bq = np.asarray(L0_attn_bq, dtype=np.float32)
    bk = np.asarray(L0_attn_bk, dtype=np.float32)
    bv = np.asarray(L0_attn_bv, dtype=np.float32).reshape(1, D)
    bo = np.asarray(L0_attn_bo, dtype=np.float32).reshape(1, D)
    wfc = np.ascontiguousarray(np.asarray(L0_ffn_w_fc, dtype=np.float32))
    bfc = np.asarray(L0_ffn_b_fc, dtype=np.float32).reshape(1, HID)
    wpj = np.ascontiguousarray(np.asarray(L0_ffn_w_proj, dtype=np.float32))
    bpj = np.asarray(L0_ffn_b_proj, dtype=np.float32).reshape(1, D)

    bqr = np.ascontiguousarray(bq.reshape(8, 128).T)
    bkr = np.ascontiguousarray(bk.reshape(8, 128).T)

    in_maps = []
    for c in range(8):
        b = c // 4
        J = c % 4
        pos = 2048 * J + np.arange(2048)[:, None]
        ends = 64 * np.arange(128)[None, :]
        mask01 = (ends <= pos).astype(ml_dtypes.bfloat16)
        in_maps.append({
            "xseq": x[b],
            "xown": np.ascontiguousarray(x[b, 2048 * J:2048 * (J + 1)]),
            "wq_in": wq, "wk_in": wk, "wv_in": wv, "wo_in": wo,
            "bqr_in": bqr, "bkr_in": bkr,
            "bv_in": bv, "bo_in": bo, "bfc_in": bfc, "bpj_in": bpj,
            "wfc_in": wfc, "wpj_in": wpj,
            "mask_in": mask01,
        })
    nc = _get_nc()
    res = run_bass_kernel_spmd(nc, in_maps, core_ids=list(range(8)))
    out = np.empty((B, T_SEQ, D), dtype=np.float32)
    for c in range(8):
        b, J = c // 4, c % 4
        out[b, 2048 * J:2048 * (J + 1)] = res.results[c]["y"]
    return out


# version 04
# speedup vs baseline: 1.0509x; 1.0316x over previous
import sys
import contextlib

import numpy as np
import ml_dtypes

sys.path.insert(0, "/opt/trn_rl_repo")
import concourse.bass as bass
import concourse.mybir as mybir
from concourse.bass_utils import run_bass_kernel_spmd

F32 = mybir.dt.float32
BF16 = mybir.dt.bfloat16
ALU = mybir.AluOpType
ACT = mybir.ActivationFunctionType
AXX = mybir.AxisListType.X

LN2 = float(np.log(2.0))
K1 = float(3 * 2**22)
K2 = float(3 * 2**51)
EPS = 1e-5

T_SEQ = 8192
D = 1024
HID = 4096
TOK = 2048
NT = 16
NSUP = 4
NSEQT = 64
NC = 128

ENGS = ("sync", "gpsimd", "tensor", "vector", "scalar")


class Sched:

    def __init__(self):
        self.q = {e: [] for e in ENGS}
        self.cnt = {}

    def wait(self, eng, sem_name):
        v = self.cnt.get(sem_name, 0)
        if v == 0:
            return
        self.q[eng].append(("w", sem_name, v))

    def wait_at(self, eng, sem_name, v):
        if v > 0:
            self.q[eng].append(("w", sem_name, v))

    def op(self, eng, fn, inc=None, amt=1):
        self.q[eng].append(("o", fn, inc, amt))
        if inc is not None:
            self.cnt[inc] = self.cnt.get(inc, 0) + amt

    def c(self, sem_name):
        return self.cnt.get(sem_name, 0)


def build_nc():
    nc = bass.Bass("TRN2", target_bir_lowering=False)

    xseq = nc.dram_tensor("xseq", [T_SEQ, D], F32, kind="ExternalInput").ap()
    xown = nc.dram_tensor("xown", [TOK, D], F32, kind="ExternalInput").ap()
    wq_in = nc.dram_tensor("wq_in", [D, D], F32, kind="ExternalInput").ap()
    wk_in = nc.dram_tensor("wk_in", [D, D], F32, kind="ExternalInput").ap()
    wv_in = nc.dram_tensor("wv_in", [D, D], F32, kind="ExternalInput").ap()
    wo_in = nc.dram_tensor("wo_in", [D, D], F32, kind="ExternalInput").ap()
    bqr_in = nc.dram_tensor("bqr_in", [128, 8], F32, kind="ExternalInput").ap()
    bkr_in = nc.dram_tensor("bkr_in", [128, 8], F32, kind="ExternalInput").ap()
    bv_in = nc.dram_tensor("bv_in", [1, D], F32, kind="ExternalInput").ap()
    bo_in = nc.dram_tensor("bo_in", [1, D], F32, kind="ExternalInput").ap()
    bfc_in = nc.dram_tensor("bfc_in", [1, HID], F32, kind="ExternalInput").ap()
    bpj_in = nc.dram_tensor("bpj_in", [1, D], F32, kind="ExternalInput").ap()
    wfc_in = nc.dram_tensor("wfc_in", [HID, D], F32, kind="ExternalInput").ap()
    wpj_in = nc.dram_tensor("wpj_in", [D, HID], F32, kind="ExternalInput").ap()
    mask_in = nc.dram_tensor("mask_in", [TOK, NC], BF16, kind="ExternalInput").ap()
    y_out = nc.dram_tensor("y", [TOK, D], F32, kind="ExternalOutput").ap()
    xmid_hbm = nc.dram_tensor("xmid_scr", [TOK, D], F32).ap()

    ident_h = nc.inline_tensor(np.eye(128, dtype=ml_dtypes.bfloat16), name="ident_h")
    base3_np = np.zeros((128, 3), dtype=ml_dtypes.bfloat16)
    base3_np[0, 0] = 1.0
    base3_np[1:65, 1] = 1.0
    base3_np[65:128, 2] = 1.0
    base3_h = nc.inline_tensor(base3_np, name="base3_h")
    ilen_np = np.full((128, 1), 1.0 / 64.0, dtype=np.float32)
    ilen_np[0, 0] = 1.0
    ilen_h = nc.inline_tensor(ilen_np, name="ilen_h")

    es = contextlib.ExitStack()
    S = Sched()

    with es:
        sb = lambda name, shape, dt: es.enter_context(nc.sbuf_tensor(name, shape, dt))
        psb = lambda name, shape: es.enter_context(nc.psum_tensor(name, shape, F32))

        ident = sb("ident", [128, 128], BF16)
        base3 = sb("base3", [128, 3], BF16)
        ilen = sb("ilen", [128, 1], F32)
        zeros_b = sb("zeros_b", [128, 128], BF16)
        ones_col = sb("ones_col", [128, 1], BF16)
        ones_row = sb("ones_row", [1, 128], BF16)
        mask_sb = sb("mask_sb", [128, NT * 128], BF16)
        bqr = sb("bqr", [128, 8], F32)
        bkr = sb("bkr", [128, 8], F32)

        Dt = sb("Dt", [128, 1], F32)
        DK2pp = sb("DK2pp", [128, 1], F32)
        EB1pp = sb("EB1pp", [128, 1], F32)

        bnscr = sb("bnscr", [128, 12], F32)
        mv_seq = sb("mv_seq", [128, 2 * NSEQT], F32)
        a_seq = sb("a_seq", [128, NSEQT], F32)
        am_seq = sb("am_seq", [128, NSEQT], F32)
        lnv_scr = sb("lnv_scr", [128, NSEQT], F32)
        mv_own = sb("mv_own", [128, 2 * NT], F32)
        a_own = sb("a_own", [128, NT], F32)
        nam_own = sb("nam_own", [128, NT], F32)

        summ_cm = sb("summ_cm", [128, D], BF16)
        summ_fm = sb("summ_fm", [128, 8 * 128], BF16)
        kfm = sb("kfm", [128, 8 * 128], BF16)
        vcm = sb("vcm", [128, D], BF16)
        sum_bias = sb("sum_bias", [128, 1], F32)

        wfcTq = sb("wfcTq", [128, 8 * HID], BF16)
        wpjTq = sb("wpjTq", [128, 32 * D], BF16)

        wbuf = sb("wbuf", [128, 4 * D], BF16)

        qa_amax = sb("qa_amax", [128, 128], F32)
        qa_e = sb("qa_e", [128, 128], F32)
        qa_s0 = sb("qa_s0", [128, 128], F32)
        qa_s = sb("qa_s", [128, 128], F32)
        qa_m = sb("qa_m", [128, 128], F32)
        qa_eb = sb("qa_eb", [128, 128], F32)

        a2c = sb("a2c", [128, 1], F32)
        nam2c = sb("nam2c", [128, 1], F32)
        lnv2 = sb("lnv2", [128, 1], F32)
        bo_r = sb("bo_r", [1, D], BF16)
        patt0 = sb("patt0", [128, 128], BF16)
        patt1 = sb("patt1", [128, 128], BF16)
        meanb = sb("meanb", [128, 1], BF16)
        ssum = sb("ssum", [128, 16], F32)
        srcp = sb("srcp", [128, 16], F32)

        es_pro = contextlib.ExitStack()
        sbp = lambda name, shape, dt: es_pro.enter_context(nc.sbuf_tensor(name, shape, dt))
        bv_r = sbp("bv_r", [1, D], BF16)
        wstr0 = sbp("wstr0", [128, 2048], F32)
        wstr1 = sbp("wstr1", [128, 2048], F32)
        wqnat = sbp("wqnat", [128, 4096], BF16)
        xsq0 = sbp("xsq0", [128, 2 * D], BF16)
        xsq1 = sbp("xsq1", [128, 2 * D], BF16)
        xsq2 = sbp("xsq2", [128, 2 * D], BF16)

        psA = psb("psA", [128, 1024])
        psB = psb("psB", [128, 1024])
        psC = psb("psC", [128, 1024])
        psD = psb("psD", [128, 512])
        psE = psb("psE", [128, 512])

        sem_names = ["LD", "LG", "ST0", "ST1", "PE", "VE", "AE", "GE", "XM",
                     "XS0", "XS1", "WF0", "WF1", "X20", "X21", "XOW", "WBF"]
        sems = {n: es.enter_context(nc.semaphore("s" + n)) for n in sem_names}
        block = es.enter_context(nc.Block())

        def w(eng, *names):
            for n in names:
                S.wait(eng, n)

        def dma(eng, sem_name, out, in_):
            S.op(eng, lambda E, o=out, i=in_: E.dma_start(out=o, in_=i), inc=sem_name, amt=16)

        def quant_consts(nb, amax_ap, e_ap, s0_ap, s_ap, m_ap, eb_ap):
            S.op("vector", lambda E, a=amax_ap: E.tensor_scalar(
                out=a, in0=a, scalar1=float(2.0**-100), scalar2=None, op0=ALU.max), inc="VE")
            w("scalar", "VE")
            S.op("scalar", lambda E, a=amax_ap, e=e_ap: E.activation(
                out=e, in_=a, func=ACT.Ln), inc="AE")
            w("vector", "AE")
            S.op("vector", lambda E, e=e_ap: E.tensor_scalar(
                out=e, in0=e, scalar1=float(1.0 / LN2), scalar2=float(1e-11 - 0.5),
                op0=ALU.mult, op1=ALU.add), inc="VE")
            S.wait_at("vector", "VE", S.c("VE"))
            S.op("vector", lambda E, e=e_ap: E.tensor_scalar(
                out=e, in0=e, scalar1=DK2pp[:, :], scalar2=EB1pp[:, :],
                op0=ALU.add, op1=ALU.subtract), inc="VE")
            S.wait_at("vector", "VE", S.c("VE"))
            S.op("vector", lambda E, e=e_ap: E.tensor_scalar(
                out=e, in0=e, scalar1=K1, scalar2=3.0,
                op0=ALU.subtract, op1=ALU.subtract), inc="VE")
            w("scalar", "VE")
            S.op("scalar", lambda E, e=e_ap, s0=s0_ap: E.activation(
                out=s0, in_=e, func=ACT.Exp, scale=LN2), inc="AE")
            w("vector", "AE")
            S.op("vector", lambda E, s0=s0_ap, m=m_ap: E.tensor_scalar(
                out=m, in0=s0, scalar1=K1, scalar2=None, op0=ALU.mult), inc="VE")
            S.wait_at("vector", "VE", S.c("VE"))
            S.op("vector", lambda E, s0=s0_ap, s=s_ap, m=m_ap: E.tensor_tensor(
                out=s, in0=s0, in1=m, op=ALU.add), inc="VE")
            S.wait_at("vector", "VE", S.c("VE"))
            S.op("vector", lambda E, s=s_ap, m=m_ap: E.tensor_tensor(
                out=s, in0=s, in1=m, op=ALU.subtract), inc="VE")
            S.wait_at("vector", "VE", S.c("VE"))
            S.op("vector", lambda E, s=s_ap, m=m_ap: E.tensor_scalar(
                out=m, in0=s, scalar1=K1, scalar2=None, op0=ALU.mult), inc="VE")
            S.op("vector", lambda E, s=s_ap, eb=eb_ap: E.tensor_scalar(
                out=eb, in0=s, scalar1=K2, scalar2=None, op0=ALU.mult), inc="VE")
            S.wait_at("vector", "VE", S.c("VE"))
            S.op("vector", lambda E, eb=eb_ap: E.tensor_scalar(
                out=eb, in0=eb, scalar1=Dt[:, :], scalar2=None, op0=ALU.mult), inc="VE")
            S.wait_at("vector", "VE", S.c("VE"))
            S.op("vector", lambda E, eb=eb_ap, m=m_ap: E.tensor_tensor(
                out=eb, in0=eb, in1=m, op=ALU.subtract), inc="VE")
            S.wait_at("vector", "VE", S.c("VE"))

        def quant_apply(v_ap, u_ap, q_ap, eb_bc):
            S.op("vector", lambda E: E.tensor_tensor(out=u_ap, in0=v_ap, in1=eb_bc,
                                                     op=ALU.subtract), inc="VE")
            S.wait_at("vector", "VE", S.c("VE"))
            S.op("vector", lambda E: E.tensor_tensor(out=q_ap, in0=u_ap, in1=eb_bc,
                                                     op=ALU.add), inc="VE")
            S.wait_at("vector", "VE", S.c("VE"))

        def transpose8(src_tile_fn, dst0_ap_fn, nchunk, evac_eng="vector", bank=None):
            pe_bank = bank if bank is not None else psE
            for j0 in range(0, nchunk, 4):
                jn = min(4, nchunk - j0)
                for j in range(j0, j0 + jn):
                    S.op("tensor", lambda E, s=src_tile_fn(j), d=pe_bank[:, 128 * (j - j0):128 * (j - j0) + 128]:
                         E.matmul(d, s, ident[:, :], start=True, stop=True),
                         inc="PE" if j == j0 + jn - 1 else None)
                w(evac_eng, "PE")
                ps3 = pe_bank[:, 0:128 * jn].rearrange("p (j c) -> p j c", c=128)
                dst3 = dst0_ap_fn(j0, jn)
                if evac_eng == "vector":
                    S.op("vector", lambda E, s=ps3, d=dst3: E.tensor_copy(out=d, in_=s), inc="VE")
                else:
                    S.op("scalar", lambda E, s=ps3, d=dst3: E.activation(
                        out=d, in_=s, func=ACT.Identity), inc="AE")
                S.wait(("tensor"), "VE" if evac_eng == "vector" else "AE")

        def strided_dst(base_ap_fn):
            def f(j0, jn):
                a0 = base_ap_fn(j0)
                if jn == 1:
                    return a0.unsqueeze(1)
                a1 = base_ap_fn(j0 + 1)
                raise RuntimeError("use explicit dst fns")
            return f

        dma("sync", "LD", ident[:], ident_h[:])
        dma("sync", "LD", base3[:], base3_h[:])
        dma("sync", "LD", ilen[:], ilen_h[:])
        dma("sync", "LD", bqr[:], bqr_in)
        dma("sync", "LD", bkr[:], bkr_in)
        dma("sync", "LD", mask_sb.ap().rearrange("p (t c) -> p t c", c=128),
            mask_in.rearrange("(t p) c -> p t c", p=128))
        dma("gpsimd", "LG", bv_r[:], bv_in)
        dma("gpsimd", "LG", bo_r[:], bo_in)

        S.op("vector", lambda E: E.memset(zeros_b[:], 0.0), inc="VE")
        S.op("vector", lambda E: E.memset(ones_col[:], 1.0), inc="VE")
        S.op("vector", lambda E: E.memset(ones_row[:], 1.0), inc="VE")
        S.op("vector", lambda E: E.memset(Dt[:], 1.0), inc="VE")
        S.wait_at("vector", "VE", S.c("VE"))
        S.op("vector", lambda E: E.tensor_scalar(out=Dt[:], in0=Dt[:], scalar1=K2,
                                                 scalar2=-K2, op0=ALU.add, op1=ALU.add), inc="VE")
        S.wait_at("vector", "VE", S.c("VE"))
        S.op("vector", lambda E: E.tensor_scalar(out=DK2pp[:], in0=Dt[:], scalar1=K2,
                                                 scalar2=None, op0=ALU.mult), inc="VE")
        S.wait_at("vector", "VE", S.c("VE"))
        S.op("vector", lambda E: E.tensor_scalar(out=EB1pp[:], in0=DK2pp[:], scalar1=K1,
                                                 scalar2=None, op0=ALU.subtract), inc="VE")
        S.wait_at("vector", "VE", S.c("VE"))

        w("tensor", "LD", "VE")
        S.op("tensor", lambda E: E.matmul(psA[:, 0:512], zeros_b[:], mask_sb[:, 0:512],
                                          start=True, stop=False))
        S.op("tensor", lambda E: E.matmul(psA[:, 512:1024], zeros_b[:], mask_sb[:, 512:1024],
                                          start=True, stop=False))
        S.op("tensor", lambda E: E.matmul(psD[:, 0:1], zeros_b[:], ones_col[:],
                                          start=True, stop=False), inc="PE")

        xs_view = xseq.rearrange("(c p) f -> p c f", p=128)
        NCH = 32
        xsqs = [xsq0, xsq1, xsq2]

        def xs_dma(ch):
            buf = xsqs[ch % 3]
            xs_sem = ["XS0", "XS1", "GE"][ch % 3]
            S.op("gpsimd",
                 lambda E, b=buf, c0=ch * 2: E.dma_start(
                     out=b.ap().rearrange("p (t f) -> p t f", f=D),
                     in_=xs_view[:, c0:c0 + 2, :]),
                 inc=xs_sem, amt=16)
            return xs_sem, S.c(xs_sem)

        def xs_stats(ch, sem_tot):
            buf = xsqs[ch % 3]
            xs_sem, tot = sem_tot
            S.wait_at("vector", xs_sem, tot)
            for tt in range(2):
                t = ch * 2 + tt
                S.op("vector", lambda E, b=buf, o=tt: E.bn_stats(
                    out=bnscr[:, 0:6], in_=b[:, o * D:o * D + 512]), inc="VE")
                S.op("vector", lambda E, b=buf, o=tt: E.bn_stats(
                    out=bnscr[:, 6:12], in_=b[:, o * D + 512:o * D + 1024]), inc="VE")
                S.wait_at("vector", "VE", S.c("VE"))
                S.op("vector", lambda E, t_=t: E.bn_aggr(
                    out=mv_seq[:, 2 * t_:2 * t_ + 2], in_=bnscr[:, :]), inc="VE")
                S.wait_at("vector", "VE", S.c("VE"))
            mvs = mv_seq.ap().rearrange("p (t two) -> p t two", two=2)
            var_sl = mvs[:, ch * 2:(ch + 1) * 2, 1:2]
            mean_sl = mvs[:, ch * 2:(ch + 1) * 2, 0:1]
            lv = lnv_scr[:, ch * 2:(ch + 1) * 2]
            lv3 = lnv_scr.ap().rearrange("p (t one) -> p t one", one=1)[:, ch * 2:(ch + 1) * 2, :]
            S.op("vector", lambda E, v=var_sl, o=lv3: E.tensor_scalar(
                out=o, in0=v, scalar1=EPS, scalar2=None, op0=ALU.add), inc="VE")
            w("scalar", "VE")
            a_sl = a_seq[:, ch * 2:(ch + 1) * 2]
            S.op("scalar", lambda E, i=lv, o=a_sl: E.activation(
                out=o, in_=i, func=ACT.Ln), inc="AE")
            S.wait_at("scalar", "AE", S.c("AE"))
            S.op("scalar", lambda E, o=a_sl: E.activation(
                out=o, in_=o, func=ACT.Exp, scale=-0.5), inc="AE")
            w("vector", "AE")
            am_sl3 = am_seq.ap().rearrange("p (t one) -> p t one", one=1)[:, ch * 2:(ch + 1) * 2, :]
            a_sl3 = a_seq.ap().rearrange("p (t one) -> p t one", one=1)[:, ch * 2:(ch + 1) * 2, :]
            S.op("vector", lambda E, a3=a_sl3, m3=mean_sl, o=am_sl3: E.tensor_tensor(
                out=o, in0=a3, in1=m3, op=ALU.mult), inc="VE")

        def xs_windows(ch):
            buf = xsqs[ch % 3]
            for tt in range(2):
                t = ch * 2 + tt
                ncols = 2 if t == 63 else 3
                pt = patt0 if t % 2 == 0 else patt1
                S.wait_at("vector", "PE", S.c("PE"))
                S.op("vector", lambda E, p=pt: E.memset(p[:, :], 0.0), inc="VE")
                S.wait_at("vector", "VE", S.c("VE"))
                S.op("vector", lambda E, t_=t, n=ncols, p=pt: E.tensor_scalar(
                    out=p[:, 2 * t_:2 * t_ + n],
                    in0=base3[:, 0:n], scalar1=a_seq[:, t_:t_ + 1],
                    scalar2=None, op0=ALU.mult), inc="VE")
                S.op("vector", lambda E, t_=t: E.tensor_copy(
                    out=meanb[:, :], in_=mv_seq[:, 2 * t_:2 * t_ + 1]), inc="VE")
                S.wait_at("tensor", "VE", S.c("VE"))
                S.op("tensor", lambda E, b=buf, o=tt, p=pt: E.matmul(
                    psA[:, 0:512], p[:, :], b[:, o * D:o * D + 512],
                    start=False, stop=False))
                S.op("tensor", lambda E, b=buf, o=tt, p=pt: E.matmul(
                    psA[:, 512:1024], p[:, :], b[:, o * D + 512:o * D + 1024],
                    start=False, stop=False))
                S.op("tensor", lambda E, p=pt: E.matmul(
                    psD[:, 0:1], p[:, :], meanb[:, :],
                    start=False, stop=False), inc="PE")
                S.wait_at("vector", "PE", S.c("PE"))

        sems_ch = {}
        sems_ch[0] = xs_dma(0)
        sems_ch[1] = xs_dma(1)
        for ch in range(NCH + 1):
            if ch < NCH:
                xs_stats(ch, sems_ch[ch])
            if ch >= 1:
                xs_windows(ch - 1)
                if ch + 1 < NCH:
                    S.wait_at("gpsimd", "PE", S.c("PE"))
                    sems_ch[ch + 1] = xs_dma(ch + 1)

        S.op("tensor", lambda E: E.matmul(psA[:, 0:512], zeros_b[:], mask_sb[:, 0:512],
                                          start=False, stop=True))
        S.op("tensor", lambda E: E.matmul(psA[:, 512:1024], zeros_b[:], mask_sb[:, 512:1024],
                                          start=False, stop=True))
        S.op("tensor", lambda E: E.matmul(psD[:, 0:1], zeros_b[:], ones_col[:, :],
                                          start=False, stop=True), inc="PE")
        pe_summary_done = S.c("PE")

        S.wait_at("vector", "PE", pe_summary_done)
        S.op("vector", lambda E: E.tensor_copy(out=sum_bias[:], in_=psD[:, 0:1]), inc="VE")
        S.wait_at("vector", "VE", S.c("VE"))
        S.op("vector", lambda E: E.tensor_scalar(
            out=sum_bias[:], in0=sum_bias[:], scalar1=ilen[:, :], scalar2=-1.0,
            op0=ALU.mult, op1=ALU.mult), inc="VE")
        w("scalar", "VE")
        S.wait_at("scalar", "PE", pe_summary_done)
        S.op("scalar", lambda E: E.activation(
            out=summ_cm[:, :], in_=psA[:, :], func=ACT.Identity,
            bias=sum_bias[:, :], scale=ilen[:, :]), inc="AE")
        ae_summ = S.c("AE")
        S.wait_at("tensor", "AE", ae_summ)
        transpose8(lambda j: summ_cm[:, 128 * j:128 * j + 128],
                   lambda j0, jn: summ_fm.ap().rearrange(
                       "p (j c) -> p j c", c=128)[:, j0:j0 + jn, :], 8, evac_eng="vector")

        kdst = [psA[:, 0:128], psA[:, 512:640], psB[:, 0:128], psB[:, 512:640],
                psC[:, 0:128], psC[:, 512:640], psD[:, 0:128], psE[:, 0:128]]
        wk_v = wk_in.rearrange("(k p) f -> p k f", p=128)
        for hf in range(2):
            S.wait_at("gpsimd", "PE", S.c("PE"))
            S.op("gpsimd", lambda E, h=hf: E.dma_start(
                out=wbuf.ap().rearrange("p (k f) -> p k f", f=D),
                in_=wk_v[:, 4 * h:4 * h + 4, :]), inc="WBF", amt=16)
            S.wait_at("tensor", "WBF", S.c("WBF"))
            for m in range(8):
                for kl in range(4):
                    k = 4 * hf + kl
                    S.op("tensor", lambda E, m_=m, k_=k, kk=kl: E.matmul(
                        kdst[m_],
                        wbuf[:, kk * D + m_ * 128: kk * D + m_ * 128 + 128],
                        summ_fm[:, k_ * 128:k_ * 128 + 128],
                        start=(k_ == 0), stop=(k_ == 7)),
                        inc="PE" if (m == 7 and kl == 3) else None)
        w("scalar", "PE")
        for m in range(8):
            S.op("scalar", lambda E, m_=m: E.activation(
                out=kfm[:, m_ * 128:m_ * 128 + 128], in_=kdst[m_],
                func=ACT.Identity, bias=bkr[:, m_:m_ + 1]),
                inc="AE" if m == 7 else None)
        S.wait_at("tensor", "AE", S.c("AE"))

        wv_v = wv_in.rearrange("(k p) f -> p k f", p=128)
        for hf in range(2):
            S.wait_at("gpsimd", "PE", S.c("PE"))
            S.op("gpsimd", lambda E, h=hf: E.dma_start(
                out=wbuf.ap().rearrange("p (k f) -> p k f", f=D),
                in_=wv_v[:, 4 * h:4 * h + 4, :]), inc="WBF", amt=16)
            S.wait_at("tensor", "WBF", S.c("WBF"))
            for cl in (0, 512):
                vdst = psB[:, 0:512] if cl == 0 else psD[:, 0:512]
                for kl in range(4):
                    k = 4 * hf + kl
                    S.op("tensor", lambda E, k_=k, kk=kl, c=cl, dd=vdst: E.matmul(
                        dd, summ_fm[:, k_ * 128:k_ * 128 + 128],
                        wbuf[:, kk * D + c: kk * D + c + 512],
                        start=(k_ == 0), stop=False),
                        inc="PE" if (cl == 512 and kl == 3) else None)
            if hf == 1:
                S.wait_at("tensor", "LG", S.c("LG"))
                S.op("tensor", lambda E: E.matmul(
                    psB[:, 0:512], ones_row[0:1, :], bv_r[0:1, 0:512],
                    start=False, stop=True))
                S.op("tensor", lambda E: E.matmul(
                    psD[:, 0:512], ones_row[0:1, :], bv_r[0:1, 512:1024],
                    start=False, stop=True), inc="PE")
        w("vector", "PE")
        S.op("vector", lambda E: E.tensor_copy(out=vcm[:, 0:512], in_=psB[:, 0:512]), inc="VE")
        S.op("vector", lambda E: E.tensor_copy(out=vcm[:, 512:1024], in_=psD[:, 0:512]), inc="VE")
        ve_vcm = S.c("VE")

        wfc_v = wfc_in.rearrange("(n p) f -> p n f", p=128)
        wpj_v = wpj_in.rearrange("(n p) f -> p n f", p=128)
        jobs = [("fc", n) for n in range(32)] + [("pj", q) for q in range(16)]

        def job_dma(i):
            kind, n = jobs[i]
            par = i % 2
            buf = wstr0 if par == 0 else wstr1
            wf_sem = "WF0" if par == 0 else "WF1"
            if kind == "fc":
                dma("sync", wf_sem, buf[:, 0:D], wfc_v[:, n, :])
            else:
                dma("sync", wf_sem, buf[:, :], wpj_v[:, n // 2, 2048 * (n % 2):2048 * (n % 2) + 2048])
            return S.c(wf_sem)

        def job_nb(i):
            return 32 if jobs[i][0] == "fc" else 64

        wf_totals = {}
        def stage_R(i):
            par, nb = i % 2, job_nb(i)
            buf = wstr0 if par == 0 else wstr1
            o = 64 * par
            S.wait_at("vector", "WF0" if par == 0 else "WF1", wf_totals[i])
            v3 = buf.ap()[:, 0:32 * nb].rearrange("p (b e) -> p b e", e=32)
            S.op("vector", lambda E, v=v3, o_=o, nb_=nb: E.tensor_reduce(
                out=qa_amax[:, o_:o_ + nb_], in_=v, axis=AXX, op=ALU.max,
                apply_absolute_value=True), inc="VE")
            S.wait_at("vector", "VE", S.c("VE"))
            S.op("vector", lambda E, o_=o, nb_=nb: E.tensor_scalar(
                out=qa_amax[:, o_:o_ + nb_], in0=qa_amax[:, o_:o_ + nb_],
                scalar1=float(2.0**-100), scalar2=None, op0=ALU.max), inc="VE")

        def stage_L(i):
            par, nb = i % 2, job_nb(i)
            o = 64 * par
            w("scalar", "VE")
            S.op("scalar", lambda E, o_=o, nb_=nb: E.activation(
                out=qa_e[:, o_:o_ + nb_], in_=qa_amax[:, o_:o_ + nb_], func=ACT.Ln), inc="AE")

        def stage_E1(i):
            par, nb = i % 2, job_nb(i)
            o = 64 * par
            w("vector", "AE")
            S.op("vector", lambda E, o_=o, nb_=nb: E.tensor_scalar(
                out=qa_e[:, o_:o_ + nb_], in0=qa_e[:, o_:o_ + nb_],
                scalar1=float(1.0 / LN2), scalar2=float(1e-11 - 0.5),
                op0=ALU.mult, op1=ALU.add), inc="VE")
            S.wait_at("vector", "VE", S.c("VE"))
            S.op("vector", lambda E, o_=o, nb_=nb: E.tensor_scalar(
                out=qa_e[:, o_:o_ + nb_], in0=qa_e[:, o_:o_ + nb_],
                scalar1=DK2pp[:, :], scalar2=EB1pp[:, :], op0=ALU.add, op1=ALU.subtract), inc="VE")
            S.wait_at("vector", "VE", S.c("VE"))
            S.op("vector", lambda E, o_=o, nb_=nb: E.tensor_scalar(
                out=qa_e[:, o_:o_ + nb_], in0=qa_e[:, o_:o_ + nb_],
                scalar1=K1, scalar2=3.0, op0=ALU.subtract, op1=ALU.subtract), inc="VE")

        def stage_X(i):
            par, nb = i % 2, job_nb(i)
            o = 64 * par
            w("scalar", "VE")
            S.op("scalar", lambda E, o_=o, nb_=nb: E.activation(
                out=qa_s0[:, o_:o_ + nb_], in_=qa_e[:, o_:o_ + nb_], func=ACT.Exp,
                scale=LN2), inc="AE")

        def stage_S(i):
            par, nb = i % 2, job_nb(i)
            o = 64 * par
            w("vector", "AE")
            for fn in (
                lambda E, o_=o, nb_=nb: E.tensor_scalar(
                    out=qa_m[:, o_:o_ + nb_], in0=qa_s0[:, o_:o_ + nb_], scalar1=K1,
                    scalar2=None, op0=ALU.mult),
                lambda E, o_=o, nb_=nb: E.tensor_tensor(
                    out=qa_s[:, o_:o_ + nb_], in0=qa_s0[:, o_:o_ + nb_],
                    in1=qa_m[:, o_:o_ + nb_], op=ALU.add),
                lambda E, o_=o, nb_=nb: E.tensor_tensor(
                    out=qa_s[:, o_:o_ + nb_], in0=qa_s[:, o_:o_ + nb_],
                    in1=qa_m[:, o_:o_ + nb_], op=ALU.subtract),
                lambda E, o_=o, nb_=nb: E.tensor_scalar(
                    out=qa_m[:, o_:o_ + nb_], in0=qa_s[:, o_:o_ + nb_], scalar1=K1,
                    scalar2=None, op0=ALU.mult),
                lambda E, o_=o, nb_=nb: E.tensor_scalar(
                    out=qa_eb[:, o_:o_ + nb_], in0=qa_s[:, o_:o_ + nb_], scalar1=K2,
                    scalar2=None, op0=ALU.mult),
                lambda E, o_=o, nb_=nb: E.tensor_scalar(
                    out=qa_eb[:, o_:o_ + nb_], in0=qa_eb[:, o_:o_ + nb_],
                    scalar1=Dt[:, :], scalar2=None, op0=ALU.mult),
                lambda E, o_=o, nb_=nb: E.tensor_tensor(
                    out=qa_eb[:, o_:o_ + nb_], in0=qa_eb[:, o_:o_ + nb_],
                    in1=qa_m[:, o_:o_ + nb_], op=ALU.subtract),
            ):
                S.op("vector", fn, inc="VE")
                S.wait_at("vector", "VE", S.c("VE"))

        def stage_Q(i):
            par, nb = i % 2, job_nb(i)
            buf = wstr0 if par == 0 else wstr1
            o = 64 * par
            v3 = buf.ap()[:, 0:32 * nb].rearrange("p (b e) -> p b e", e=32)
            q3 = wqnat.ap()[:, 2048 * par:2048 * par + 32 * nb].rearrange("p (b e) -> p b e", e=32)
            ebb = qa_eb.ap()[:, o:o + nb].unsqueeze(2).broadcast_to([128, nb, 32])
            S.wait_at("vector", "PE", S.c("PE"))
            S.op("vector", lambda E, a=v3, b=ebb: E.tensor_tensor(
                out=a, in0=a, in1=b, op=ALU.subtract), inc="VE")
            S.wait_at("vector", "VE", S.c("VE"))
            S.op("vector", lambda E, a=v3, b=ebb, q=q3: E.tensor_tensor(
                out=q, in0=a, in1=b, op=ALU.add), inc="VE")
            S.wait_at("tensor", "VE", S.c("VE"))

        def stage_T(i):
            kind, n = jobs[i]
            par = i % 2
            base = 2048 * par
            if kind == "fc":
                transpose8(lambda j, b=base: wqnat[:, b + 128 * j:b + 128 * j + 128],
                           lambda j0, jn, h_=n: wfcTq.ap().rearrange(
                               "p (j r) -> p j r", r=HID)[:, j0:j0 + jn,
                                                          128 * h_:128 * h_ + 128],
                           8, evac_eng="vector")
            else:
                h_, hf = n // 2, n % 2
                transpose8(lambda j, b=base: wqnat[:, b + 128 * j:b + 128 * j + 128],
                           lambda j0, jn, hh=h_, f_=hf: wpjTq.ap().rearrange(
                               "p (j r) -> p j r", r=D)[:, 16 * f_ + j0:16 * f_ + j0 + jn,
                                                        128 * hh:128 * hh + 128],
                           16, evac_eng="vector")

        NJOB = len(jobs)
        wf_totals[0] = job_dma(0)
        wf_totals[1] = job_dma(1)
        for i in range(NJOB + 1):
            if i < NJOB:
                stage_R(i)
                stage_L(i)
            if i >= 1:
                stage_Q(i - 1)
                if i + 1 < NJOB:
                    S.wait_at("sync", "VE", S.c("VE"))
                    wf_totals[i + 1] = job_dma(i + 1)
            if i < NJOB:
                stage_E1(i)
                stage_X(i)
            if i >= 1:
                stage_T(i - 1)
            if i < NJOB:
                stage_S(i)

        es_pro.close()
        es_ph1 = contextlib.ExitStack()
        sb1 = lambda name, shape, dt: es_ph1.enter_context(nc.sbuf_tensor(name, shape, dt))
        xring = sb1("xring", [128, 4 * D], F32)
        lnbuf = sb1("lnbuf", [128, D], BF16)
        ln1fm = sb1("ln1fm", [128, 8 * 512], BF16)
        qfm = sb1("qfm", [128, 8 * 512], BF16)
        ebuf = sb1("ebuf", [128, 2 * 16 * 128], BF16)
        xmid_t = sb1("xmid_t", [128, D], F32)
        ofm = ln1fm

        for e in ("sync", "gpsimd", "tensor", "vector", "scalar"):
            S.wait_at(e, "VE", S.c("VE"))
            S.wait_at(e, "AE", S.c("AE"))
            S.wait_at(e, "PE", S.c("PE"))

        st_count = 0

        def p1_loads_stats(s):
            S.wait_at("sync", "VE", S.c("VE"))
            for tt in range(4):
                t = 4 * s + tt
                slot = t % 4
                dma("sync", "XOW", xring[:, slot * D:(slot + 1) * D],
                    xown.rearrange("(t p) f -> p t f", p=128)[:, t, :])
            ld_xown = S.c("XOW")
            S.wait_at("vector", "XOW", ld_xown)
            for tt in range(4):
                t = 4 * s + tt
                slot = t % 4
                xo = xring[:, slot * D:(slot + 1) * D]
                S.op("vector", lambda E, x=xo: E.bn_stats(out=bnscr[:, 0:6], in_=x[:, 0:512]), inc="VE")
                S.op("vector", lambda E, x=xo: E.bn_stats(out=bnscr[:, 6:12], in_=x[:, 512:1024]), inc="VE")
                S.wait_at("vector", "VE", S.c("VE"))
                S.op("vector", lambda E, t_=t: E.bn_aggr(
                    out=mv_own[:, 2 * t_:2 * t_ + 2], in_=bnscr[:, :]), inc="VE")
                S.wait_at("vector", "VE", S.c("VE"))
            mvo = mv_own.ap().rearrange("p (t two) -> p t two", two=2)
            var_sl = mvo[:, 4 * s:4 * s + 4, 1:2]
            mean_sl = mvo[:, 4 * s:4 * s + 4, 0:1]
            lv3 = lnv_scr.ap().rearrange("p (t one) -> p t one", one=1)[:, 0:4, :]
            S.op("vector", lambda E, v=var_sl, o=lv3: E.tensor_scalar(
                out=o, in0=v, scalar1=EPS, scalar2=None, op0=ALU.add), inc="VE")
            w("scalar", "VE")
            a_sl = a_own[:, 4 * s:4 * s + 4]
            S.op("scalar", lambda E, o=a_sl: E.activation(
                out=o, in_=lnv_scr[:, 0:4], func=ACT.Ln), inc="AE")
            S.wait_at("scalar", "AE", S.c("AE"))
            S.op("scalar", lambda E, o=a_sl: E.activation(
                out=o, in_=o, func=ACT.Exp, scale=-0.5), inc="AE")
            w("vector", "AE")
            a3 = a_own.ap().rearrange("p (t one) -> p t one", one=1)[:, 4 * s:4 * s + 4, :]
            n3 = nam_own.ap().rearrange("p (t one) -> p t one", one=1)[:, 4 * s:4 * s + 4, :]
            S.op("vector", lambda E, a=a3, m=mean_sl, o=n3: E.tensor_tensor(
                out=o, in0=a, in1=m, op=ALU.mult), inc="VE")
            S.wait_at("vector", "VE", S.c("VE"))
            S.op("vector", lambda E, o=n3: E.tensor_scalar(
                out=nam_own[:, 4 * s:4 * s + 4], in0=nam_own[:, 4 * s:4 * s + 4],
                scalar1=-1.0, scalar2=None, op0=ALU.mult), inc="VE")

        def p1_lnq(s):
            S.wait_at("tensor", "VE", S.c("VE"))
            for tt in range(4):
                t = 4 * s + tt
                slot = t % 4
                xo = xring[:, slot * D:(slot + 1) * D]
                w("scalar", "VE")
                S.wait_at("scalar", "PE", S.c("PE"))
                S.op("scalar", lambda E, x=xo, t_=t: E.activation(
                    out=lnbuf[:, 0:D], in_=x, func=ACT.Identity,
                    scale=a_own[:, t_:t_ + 1], bias=nam_own[:, t_:t_ + 1]), inc="AE")
                S.wait_at("tensor", "AE", S.c("AE"))
                transpose8(lambda j: lnbuf[:, 128 * j:128 * j + 128],
                           lambda j0, jn, u=tt: ln1fm.ap().rearrange(
                               "p (j q) -> p j q", q=512)[:, j0:j0 + jn, 128 * u:128 * u + 128],
                           8, evac_eng="vector")
            qdst = [psA[:, 0:512], psA[:, 512:1024], psB[:, 0:512], psB[:, 512:1024],
                    psC[:, 0:512], psC[:, 512:1024], psD[:, 0:512], psE[:, 0:512]]
            wq_v = wq_in.rearrange("(k p) f -> p k f", p=128)
            for hf in range(2):
                S.wait_at("gpsimd", "PE", S.c("PE"))
                S.op("gpsimd", lambda E, h=hf: E.dma_start(
                    out=wbuf.ap().rearrange("p (k f) -> p k f", f=D),
                    in_=wq_v[:, 4 * h:4 * h + 4, :]), inc="WBF", amt=16)
                S.wait_at("tensor", "WBF", S.c("WBF"))
                for m in range(8):
                    for kl in range(4):
                        k = 4 * hf + kl
                        S.op("tensor", lambda E, m_=m, k_=k, kk=kl: E.matmul(
                            qdst[m_], wbuf[:, kk * D + m_ * 128: kk * D + m_ * 128 + 128],
                            ln1fm[:, k_ * 512:(k_ + 1) * 512],
                            start=(k_ == 0), stop=(k_ == 7)),
                            inc="PE" if (m == 7 and kl == 3) else None)
            w("scalar", "PE")
            for m in range(8):
                S.op("scalar", lambda E, m_=m: E.activation(
                    out=qfm[:, m_ * 512:(m_ + 1) * 512], in_=qdst[m_],
                    func=ACT.Identity, bias=bqr[:, m_:m_ + 1]),
                    inc="AE" if m == 7 else None)
            S.wait_at("tensor", "AE", S.c("AE"))
            S.wait_at("gpsimd", "PE", S.c("PE"))
            wo_v = wo_in.rearrange("(k p) f -> p k f", p=128)
            S.op("gpsimd", lambda E: E.dma_start(
                out=wbuf.ap().rearrange("p (k f) -> p k f", f=D),
                in_=wo_v[:, 0:4, :]), inc="WBF", amt=16)

        def p1_scores(s, u, epar):
            t = 4 * s + u
            S.wait_at("tensor", "AE", S.c("AE"))
            for m in range(8):
                qsl_e = qfm[0:64, m * 512 + 128 * u: m * 512 + 128 * u + 128]
                qsl_o = qfm[64:128, m * 512 + 128 * u: m * 512 + 128 * u + 128]
                ksl_e = kfm[0:64, m * 128:(m + 1) * 128]
                ksl_o = kfm[64:128, m * 128:(m + 1) * 128]
                dstA = psA[:, 128 * (m % 4):128 * (m % 4) + 128] if m < 4 else \
                    psB[:, 128 * (m % 4):128 * (m % 4) + 128]
                dstB = psA[:, 512 + 128 * (m % 4):512 + 128 * (m % 4) + 128] if m < 4 else \
                    psB[:, 512 + 128 * (m % 4):512 + 128 * (m % 4) + 128]
                S.op("tensor", lambda E, q=qsl_e, kk=ksl_e, d=dstA: E.matmul(
                    d, q, kk, start=True, stop=True, tile_position=(0, 0)))
                S.op("tensor", lambda E, q=qsl_o, kk=ksl_o, d=dstB: E.matmul(
                    d, q, kk, start=True, stop=True, tile_position=(64, 0)),
                    inc="PE" if m == 7 else None)
            w("scalar", "PE")
            S.wait_at("scalar", "PE", S.c("PE"))
            eb = 2048 * epar
            S.op("scalar", lambda E, o=eb: E.activation(
                out=ebuf[:, o:o + 1024], in_=psA[:, :], func=ACT.Exp, scale=0.125), inc="AE")
            S.op("scalar", lambda E, o=eb: E.activation(
                out=ebuf[:, o + 1024:o + 2048], in_=psB[:, :], func=ACT.Exp, scale=0.125), inc="AE")

        def p1_softmax_av(s, u, epar):
            t = 4 * s + u
            eb = 2048 * epar
            w("vector", "AE")
            e3 = ebuf.ap()[:, eb:eb + 2048].rearrange("p (g c) -> p g c", c=128)
            msk = mask_sb.ap()[:, 128 * t:128 * t + 128].unsqueeze(1).broadcast_to([128, 16, 128])
            S.op("vector", lambda E, e=e3, mk=msk: E.tensor_tensor(
                out=e, in0=e, in1=mk, op=ALU.mult), inc="VE")
            S.wait_at("vector", "VE", S.c("VE"))
            S.op("vector", lambda E, e=e3: E.tensor_reduce(
                out=ssum[:, :], in_=e, axis=AXX, op=ALU.add), inc="VE")
            S.wait_at("vector", "VE", S.c("VE"))
            S.op("vector", lambda E: E.reciprocal(out=srcp[:, :], in_=ssum[:, :]), inc="VE")
            S.wait_at("vector", "VE", S.c("VE"))
            rb = srcp.ap().unsqueeze(2).broadcast_to([128, 16, 128])
            S.op("vector", lambda E, e=e3, r=rb: E.tensor_tensor(
                out=e, in0=e, in1=r, op=ALU.mult), inc="VE")
            S.wait_at("tensor", "VE", S.c("VE"))
            transpose8(lambda j, o=eb: ebuf[:, o + 128 * j:o + 128 * j + 128],
                       lambda j0, jn, o=eb: ebuf.ap()[:, o:o + 2048].rearrange(
                           "p (j c) -> p j c", c=128)[:, j0:j0 + jn, :],
                       16, evac_eng="vector")
            for m0 in range(0, 8, 4):
                for mm in range(4):
                    m = m0 + mm
                    g_e = (m // 4) * 8 + (m % 4)
                    g_o = (m // 4) * 8 + 4 + (m % 4)
                    h_e, h_o = 2 * m, 2 * m + 1
                    S.op("tensor", lambda E, g=g_e, h=h_e, c=128 * mm, o=eb: E.matmul(
                        psD[0:64, c:c + 128], vcm[:, 64 * h:64 * h + 64],
                        ebuf[:, o + 128 * g:o + 128 * g + 128], start=True, stop=True,
                        tile_position=(0, 0)))
                    S.op("tensor", lambda E, g=g_o, h=h_o, c=128 * mm, o=eb: E.matmul(
                        psD[64:128, c:c + 128], vcm[:, 64 * h:64 * h + 64],
                        ebuf[:, o + 128 * g:o + 128 * g + 128], start=True, stop=True,
                        tile_position=(0, 64)), inc="PE" if mm == 3 else None)
                S.wait_at("vector", "PE", S.c("PE"))
                dst3 = ofm.ap().rearrange("p (m q) -> p m q", q=512)[
                    :, m0:m0 + 4, 128 * u:128 * u + 128]
                ps3 = psD.ap().rearrange("p (m c) -> p m c", c=128)
                S.op("vector", lambda E, d=dst3, s_=ps3: E.tensor_copy(
                    out=d, in_=s_), inc="VE")
                S.wait_at("tensor", "VE", S.c("VE"))

        def p1_wo_res(s):
            global_st = None
            odst = [(psA[:, 0:512], psA[:, 512:1024]), (psB[:, 0:512], psB[:, 512:1024]),
                    (psC[:, 0:512], psC[:, 512:1024]), (psD[:, 0:512], psE[:, 0:512])]
            wo_v = wo_in.rearrange("(k p) f -> p k f", p=128)
            for hf in range(2):
                if hf == 1:
                    S.wait_at("gpsimd", "PE", S.c("PE"))
                    S.op("gpsimd", lambda E: E.dma_start(
                        out=wbuf.ap().rearrange("p (k f) -> p k f", f=D),
                        in_=wo_v[:, 4:8, :]), inc="WBF", amt=16)
                S.wait_at("tensor", "WBF", S.c("WBF"))
                for u in range(4):
                    for half in range(2):
                        cl = half * 512
                        dd = odst[u][half]
                        for ml in range(4):
                            m = 4 * hf + ml
                            S.op("tensor", lambda E, m_=m, mm=ml, c=cl, u_=u, d=dd: E.matmul(
                                d, ofm[:, m_ * 512 + 128 * u_: m_ * 512 + 128 * u_ + 128],
                                wbuf[:, mm * D + c: mm * D + c + 512],
                                start=(m_ == 0), stop=False),
                                inc="PE" if (u == 3 and half == 1 and ml == 3) else None)
                        if hf == 1:
                            S.wait_at("tensor", "LG", S.c("LG"))
                            S.op("tensor", lambda E, c=cl, d=dd: E.matmul(
                                d, ones_row[0:1, :], bo_r[0:1, c:c + 512],
                                start=False, stop=True),
                                inc="PE" if (u == 3 and half == 1) else None)
            w("vector", "PE")
            for u in range(4):
                t = 4 * s + u
                slot = t % 4
                xo = xring[:, slot * D:(slot + 1) * D]
                S.op("vector", lambda E, d0=odst[u][0], x=xo: E.tensor_tensor(
                    out=xmid_t[:, 0:512], in0=d0, in1=x[:, 0:512], op=ALU.add), inc="VE")
                S.op("vector", lambda E, d1=odst[u][1], x=xo: E.tensor_tensor(
                    out=xmid_t[:, 512:1024], in0=d1, in1=x[:, 512:1024], op=ALU.add), inc="VE")
                S.wait_at("sync", "VE", S.c("VE"))
                S.op("sync", lambda E, t_=t: E.dma_start(
                    out=xmid_hbm.rearrange("(t p) f -> p t f", p=128)[:, t_, :],
                    in_=xmid_t[:, :]), inc="XM", amt=16)
                S.wait_at("vector", "XM", S.c("XM"))

        p1_next_loads_stats = p1_loads_stats

        p1_loads_stats(0)
        for s in range(NSUP):
            p1_lnq(s)
            p1_scores(s, 0, 0)
            for u in range(4):
                if u + 1 < 4:
                    p1_scores(s, u + 1, (u + 1) % 2)
                p1_softmax_av(s, u, u % 2)
            p1_wo_res(s)
            if s + 1 < NSUP:
                p1_next_loads_stats(s + 1)
        es_ph1.close()
        es_ph2 = contextlib.ExitStack()
        sb2 = lambda name, shape, dt: es_ph2.enter_context(nc.sbuf_tensor(name, shape, dt))
        x2r0 = sb2("x2r0", [128, D], F32)
        x2r1 = sb2("x2r1", [128, D], F32)
        xln2 = sb2("xln2", [128, D], F32)
        xqb = sb2("xqb", [128, D], BF16)
        xqfm = sb2("xqfm", [128, 2 * 8 * 128], BF16)
        hbuf = sb2("hbuf", [128, 2 * D], F32)
        hqb = sb2("hqb", [128, D], BF16)
        hqT = sb2("hqT", [128, 2 * 8 * 128], BF16)
        outb = sb2("outb", [128, 2 * D], F32)
        bfc_r = sb2("bfc_r", [1, HID], BF16)
        bpj_r = sb2("bpj_r", [1, D], BF16)
        for e in ("sync", "gpsimd", "tensor", "vector", "scalar"):
            S.wait_at(e, "XM", 16 * NT)
        dma("gpsimd", "LG", bfc_r[:], bfc_in)
        dma("gpsimd", "LG", bpj_r[:], bpj_in)
        lg_bias2 = S.c("LG")
        S.wait_at("tensor", "LG", lg_bias2)

        def p2_front(t):
            buf = x2r0 if t % 2 == 0 else x2r1
            S.wait_at("sync", "VE", S.c("VE"))
            x2_sem = "X20" if t % 2 == 0 else "X21"
            dma("sync", x2_sem, buf[:, :], xmid_hbm.rearrange("(t p) f -> p t f", p=128)[:, t, :])
            S.wait_at("vector", x2_sem, S.c(x2_sem))
            S.op("vector", lambda E, b=buf: E.bn_stats(out=bnscr[:, 0:6], in_=b[:, 0:512]), inc="VE")
            S.op("vector", lambda E, b=buf: E.bn_stats(out=bnscr[:, 6:12], in_=b[:, 512:1024]), inc="VE")
            S.wait_at("vector", "VE", S.c("VE"))
            S.op("vector", lambda E: E.bn_aggr(out=mv_own[:, 0:2], in_=bnscr[:, :]), inc="VE")
            S.wait_at("vector", "VE", S.c("VE"))
            S.op("vector", lambda E: E.tensor_scalar(
                out=lnv2[:, 0:1], in0=mv_own[:, 1:2], scalar1=EPS, scalar2=None,
                op0=ALU.add), inc="VE")
            w("scalar", "VE")
            S.op("scalar", lambda E: E.activation(out=a2c[:], in_=lnv2[:, 0:1], func=ACT.Ln), inc="AE")
            S.wait_at("scalar", "AE", S.c("AE"))
            S.op("scalar", lambda E: E.activation(out=a2c[:], in_=a2c[:], func=ACT.Exp,
                                                  scale=-0.5), inc="AE")
            w("vector", "AE")
            S.op("vector", lambda E: E.tensor_tensor(out=nam2c[:], in0=a2c[:],
                                                     in1=mv_own[:, 0:1], op=ALU.mult), inc="VE")
            S.wait_at("vector", "VE", S.c("VE"))
            S.op("vector", lambda E: E.tensor_scalar(out=nam2c[:], in0=nam2c[:], scalar1=-1.0,
                                                     scalar2=None, op0=ALU.mult), inc="VE")
            S.wait_at("vector", "VE", S.c("VE"))
            S.op("vector", lambda E, b=buf: E.tensor_scalar(
                out=xln2[:, :], in0=b[:, :], scalar1=a2c[:, :], scalar2=nam2c[:, :],
                op0=ALU.mult, op1=ALU.add), inc="VE")
            S.wait_at("vector", "VE", S.c("VE"))
            o = 96
            v3 = xln2.ap().rearrange("p (b e) -> p b e", e=32)
            S.op("vector", lambda E, v=v3, o_=o: E.tensor_reduce(
                out=qa_amax[:, o_:o_ + 32], in_=v, axis=AXX, op=ALU.max,
                apply_absolute_value=True), inc="VE")
            S.wait_at("vector", "VE", S.c("VE"))
            quant_consts(32, qa_amax[:, o:o + 32], qa_e[:, o:o + 32], qa_s0[:, o:o + 32],
                         qa_s[:, o:o + 32], qa_m[:, o:o + 32], qa_eb[:, o:o + 32])
            ebb = qa_eb.ap()[:, o:o + 32].unsqueeze(2).broadcast_to([128, 32, 32])
            q3 = xqb.ap().rearrange("p (b e) -> p b e", e=32)
            S.wait_at("vector", "PE", S.c("PE"))
            quant_apply(v3, v3, q3, ebb)
            S.wait_at("tensor", "VE", S.c("VE"))
            xpar = t % 2
            transpose8(lambda j: xqb[:, 128 * j:128 * j + 128],
                       lambda j0, jn, xp=xpar: xqfm.ap().rearrange(
                           "p (j c) -> p j c", c=128)[:, 8 * xp + j0:8 * xp + j0 + jn, :],
                       8, evac_eng="vector", bank=psD)

        def p2_fc1(t, g):
            xpar = t % 2
            dst = psA if g % 2 == 0 else psB
            for k in range(8):
                S.op("tensor", lambda E, k_=k, g_=g, d=dst, xp=xpar: E.matmul(
                    d[:, 0:512], xqfm[:, 1024 * xp + 128 * k_:1024 * xp + 128 * k_ + 128],
                    wfcTq[:, k_ * HID + g_ * D: k_ * HID + g_ * D + 512],
                    start=(k_ == 0), stop=False))
                S.op("tensor", lambda E, k_=k, g_=g, d=dst, xp=xpar: E.matmul(
                    d[:, 512:1024], xqfm[:, 1024 * xp + 128 * k_:1024 * xp + 128 * k_ + 128],
                    wfcTq[:, k_ * HID + g_ * D + 512: k_ * HID + g_ * D + 1024],
                    start=(k_ == 0), stop=False))
            S.op("tensor", lambda E, g_=g, d=dst: E.matmul(
                d[:, 0:512], ones_row[0:1, :], bfc_r[0:1, g_ * D:g_ * D + 512],
                start=False, stop=True))
            S.op("tensor", lambda E, g_=g, d=dst: E.matmul(
                d[:, 512:1024], ones_row[0:1, :], bfc_r[0:1, g_ * D + 512:g_ * D + 1024],
                start=False, stop=True), inc="PE")

        def p2_gelu(t, g):
            dst = psA if g % 2 == 0 else psB
            gs = g % 2
            w("scalar", "PE")
            S.wait_at("scalar", "VE", S.c("VE"))
            S.op("scalar", lambda E, d=dst, s_=gs: E.activation(
                out=hbuf[:, s_ * D:(s_ + 1) * D], in_=d[:, :], func=ACT.Gelu), inc="AE")

        def p2_pair(t, pair, do_front):
            g0, g1 = 2 * pair, 2 * pair + 1
            p2_gelu(t, g0)
            p2_fc1(t, g1)
            p2_gelu(t, g1)
            if do_front:
                p2_front(t + 1)
            w("vector", "AE")
            for gi, g in enumerate((g0, g1)):
                gs = g % 2
                hg = hbuf.ap()[:, gs * D:(gs + 1) * D].rearrange("p (b e) -> p b e", e=32)
                S.op("vector", lambda E, v=hg, o=32 * gi: E.tensor_reduce(
                    out=qa_amax[:, o:o + 32], in_=v, axis=AXX, op=ALU.max,
                    apply_absolute_value=True), inc="VE")
            S.wait_at("vector", "VE", S.c("VE"))
            quant_consts(64, qa_amax[:, 0:64], qa_e[:, 0:64], qa_s0[:, 0:64],
                         qa_s[:, 0:64], qa_m[:, 0:64], qa_eb[:, 0:64])
            for gi, g in enumerate((g0, g1)):
                gs = g % 2
                hg = hbuf.ap()[:, gs * D:(gs + 1) * D].rearrange("p (b e) -> p b e", e=32)
                ebg = qa_eb.ap()[:, 32 * gi:32 * gi + 32].unsqueeze(2).broadcast_to([128, 32, 32])
                qg = hqb.ap().rearrange("p (b e) -> p b e", e=32)
                S.wait_at("vector", "PE", S.c("PE"))
                quant_apply(hg, hg, qg, ebg)
                S.wait_at("tensor", "VE", S.c("VE"))
                transpose8(lambda j: hqb[:, 128 * j:128 * j + 128],
                           lambda j0, jn, s_=gs: hqT.ap().rearrange(
                               "p (j c) -> p j c", c=128)[:, 8 * s_ + j0:8 * s_ + j0 + jn, :],
                           8, evac_eng="vector")
                for half in range(2):
                    cl = half * 512
                    for cj in range(8):
                        c = 8 * g + cj
                        S.op("tensor", lambda E, c_=c, cc=cl, cjj=cj, s_=gs: E.matmul(
                            psC[:, cc:cc + 512],
                            hqT[:, 1024 * s_ + 128 * cjj:1024 * s_ + 128 * cjj + 128],
                            wpjTq[:, c_ * D + cc: c_ * D + cc + 512],
                            start=(c_ == 0), stop=False))
                    if g == 3:
                        S.op("tensor", lambda E, cc=cl: E.matmul(
                            psC[:, cc:cc + 512], ones_row[0:1, :], bpj_r[0:1, cc:cc + 512],
                            start=False, stop=True),
                            inc="PE" if half == 1 else None)
            if pair == 0:
                S.wait_at("tensor", "AE", S.c("AE"))
                p2_fc1(t, 2)

        def p2_back(t, do_front):
            p2_pair(t, 0, do_front)
            p2_pair(t, 1, False)
            w("vector", "PE")
            oslot = t % 2
            buf = x2r0 if t % 2 == 0 else x2r1
            S.wait_at("vector", "ST0" if t % 2 == 0 else "ST1",
                      S.c("ST0" if t % 2 == 0 else "ST1"))
            S.op("vector", lambda E, b=buf, o=oslot: E.tensor_tensor(
                out=outb[:, o * D:(o + 1) * D], in0=psC[:, :], in1=b[:, :], op=ALU.add), inc="VE")
            S.wait_at("sync", "VE", S.c("VE"))
            st_sem = "ST0" if t % 2 == 0 else "ST1"
            S.op("sync", lambda E, t_=t, o=oslot: E.dma_start(
                out=y_out.rearrange("(t p) f -> p t f", p=128)[:, t_, :],
                in_=outb[:, o * D:(o + 1) * D]), inc=st_sem, amt=16)

        p2_front(0)
        for t in range(NT):
            p2_fc1(t, 0)
            p2_back(t, t + 1 < NT)

        S.wait_at("sync", "ST0", 16 * (NT // 2))
        S.wait_at("sync", "ST1", 16 * (NT // 2))

        es_ph2.close()
        def runner(eng_name):
            def run(E):
                for item in S.q[eng_name]:
                    if item[0] == "w":
                        E.wait_ge(sems[item[1]], item[2])
                    else:
                        ins = item[1](E)
                        if item[2] is not None:
                            ins.then_inc(sems[item[2]], item[3])
            return run

        block.sync(runner("sync"))
        block.gpsimd(runner("gpsimd"))
        block.tensor(runner("tensor"))
        block.vector(runner("vector"))
        block.scalar(runner("scalar"))

    return nc


_NC_CACHE = None


def _get_nc():
    global _NC_CACHE
    if _NC_CACHE is None:
        _NC_CACHE = build_nc()
    return _NC_CACHE


def kernel(L0_attn_x, L0_attn_Wq, L0_attn_bq, L0_attn_Wk, L0_attn_bk, L0_attn_Wv,
           L0_attn_bv, L0_attn_Wo, L0_attn_bo, L0_ffn_w_fc, L0_ffn_b_fc,
           L0_ffn_w_proj, L0_ffn_b_proj):
    x = np.ascontiguousarray(np.asarray(L0_attn_x, dtype=np.float32))
    B = x.shape[0]
    wq = np.ascontiguousarray(np.asarray(L0_attn_Wq, dtype=np.float32))
    wk = np.ascontiguousarray(np.asarray(L0_attn_Wk, dtype=np.float32))
    wv = np.ascontiguousarray(np.asarray(L0_attn_Wv, dtype=np.float32))
    wo = np.ascontiguousarray(np.asarray(L0_attn_Wo, dtype=np.float32))
    bq = np.asarray(L0_attn_bq, dtype=np.float32)
    bk = np.asarray(L0_attn_bk, dtype=np.float32)
    bv = np.asarray(L0_attn_bv, dtype=np.float32).reshape(1, D)
    bo = np.asarray(L0_attn_bo, dtype=np.float32).reshape(1, D)
    wfc = np.ascontiguousarray(np.asarray(L0_ffn_w_fc, dtype=np.float32))
    bfc = np.asarray(L0_ffn_b_fc, dtype=np.float32).reshape(1, HID)
    wpj = np.ascontiguousarray(np.asarray(L0_ffn_w_proj, dtype=np.float32))
    bpj = np.asarray(L0_ffn_b_proj, dtype=np.float32).reshape(1, D)

    bqr = np.ascontiguousarray(bq.reshape(8, 128).T)
    bkr = np.ascontiguousarray(bk.reshape(8, 128).T)

    in_maps = []
    for c in range(8):
        b = c // 4
        J = c % 4
        pos = 2048 * J + np.arange(2048)[:, None]
        ends = 64 * np.arange(128)[None, :]
        mask01 = (ends <= pos).astype(ml_dtypes.bfloat16)
        in_maps.append({
            "xseq": x[b],
            "xown": np.ascontiguousarray(x[b, 2048 * J:2048 * (J + 1)]),
            "wq_in": wq, "wk_in": wk, "wv_in": wv, "wo_in": wo,
            "bqr_in": bqr, "bkr_in": bkr,
            "bv_in": bv, "bo_in": bo, "bfc_in": bfc, "bpj_in": bpj,
            "wfc_in": wfc, "wpj_in": wpj,
            "mask_in": mask01,
        })
    nc = _get_nc()
    res = run_bass_kernel_spmd(nc, in_maps, core_ids=list(range(8)))
    out = np.empty((B, T_SEQ, D), dtype=np.float32)
    for c in range(8):
        b, J = c // 4, c % 4
        out[b, 2048 * J:2048 * (J + 1)] = res.results[c]["y"]
    return out


# version 05
# speedup vs baseline: 1.1593x; 1.1032x over previous
import sys
import contextlib

import numpy as np
import ml_dtypes

sys.path.insert(0, "/opt/trn_rl_repo")
import concourse.bass as bass
import concourse.mybir as mybir
from concourse.bass_utils import run_bass_kernel_spmd

F32 = mybir.dt.float32
BF16 = mybir.dt.bfloat16
ALU = mybir.AluOpType
ACT = mybir.ActivationFunctionType
AXX = mybir.AxisListType.X

LN2 = float(np.log(2.0))
K1 = float(3 * 2**22)
K2 = float(3 * 2**51)
EPS = 1e-5

T_SEQ = 8192
D = 1024
HID = 4096
TOK = 2048
NT = 16
NSUP = 4
NSEQT = 64
NC = 128

ENGS = ("sync", "gpsimd", "tensor", "vector", "scalar")


class Sched:

    def __init__(self):
        self.q = {e: [] for e in ENGS}
        self.cnt = {}

    def wait(self, eng, sem_name):
        v = self.cnt.get(sem_name, 0)
        if v == 0:
            return
        self.q[eng].append(("w", sem_name, v))

    def wait_at(self, eng, sem_name, v):
        if v > 0:
            self.q[eng].append(("w", sem_name, v))

    def op(self, eng, fn, inc=None, amt=1):
        self.q[eng].append(("o", fn, inc, amt))
        if inc is not None:
            self.cnt[inc] = self.cnt.get(inc, 0) + amt

    def c(self, sem_name):
        return self.cnt.get(sem_name, 0)


def build_nc():
    nc = bass.Bass("TRN2", target_bir_lowering=False)

    xseq = nc.dram_tensor("xseq", [T_SEQ, D], F32, kind="ExternalInput").ap()
    xown = nc.dram_tensor("xown", [TOK, D], F32, kind="ExternalInput").ap()
    wq_in = nc.dram_tensor("wq_in", [D, D], F32, kind="ExternalInput").ap()
    wk_in = nc.dram_tensor("wk_in", [D, D], F32, kind="ExternalInput").ap()
    wv_in = nc.dram_tensor("wv_in", [D, D], F32, kind="ExternalInput").ap()
    wo_in = nc.dram_tensor("wo_in", [D, D], F32, kind="ExternalInput").ap()
    bqr_in = nc.dram_tensor("bqr_in", [128, 8], F32, kind="ExternalInput").ap()
    bkr_in = nc.dram_tensor("bkr_in", [128, 8], F32, kind="ExternalInput").ap()
    bv_in = nc.dram_tensor("bv_in", [1, D], F32, kind="ExternalInput").ap()
    bo_in = nc.dram_tensor("bo_in", [1, D], F32, kind="ExternalInput").ap()
    bfc_in = nc.dram_tensor("bfc_in", [1, HID], F32, kind="ExternalInput").ap()
    bpj_in = nc.dram_tensor("bpj_in", [1, D], F32, kind="ExternalInput").ap()
    wfc_in = nc.dram_tensor("wfc_in", [HID, D], F32, kind="ExternalInput").ap()
    wpj_in = nc.dram_tensor("wpj_in", [D, HID], F32, kind="ExternalInput").ap()
    mask_in = nc.dram_tensor("mask_in", [TOK, NC], BF16, kind="ExternalInput").ap()
    y_out = nc.dram_tensor("y", [TOK, D], F32, kind="ExternalOutput").ap()
    xmid_hbm = nc.dram_tensor("xmid_scr", [TOK, D], F32).ap()

    ident_h = nc.inline_tensor(np.eye(128, dtype=ml_dtypes.bfloat16), name="ident_h")
    base3_np = np.zeros((128, 3), dtype=ml_dtypes.bfloat16)
    base3_np[0, 0] = 1.0
    base3_np[1:65, 1] = 1.0
    base3_np[65:128, 2] = 1.0
    base3_h = nc.inline_tensor(base3_np, name="base3_h")
    ilen_np = np.full((128, 1), 1.0 / 64.0, dtype=np.float32)
    ilen_np[0, 0] = 1.0
    ilen_h = nc.inline_tensor(ilen_np, name="ilen_h")

    es = contextlib.ExitStack()
    S = Sched()

    with es:
        sb = lambda name, shape, dt: es.enter_context(nc.sbuf_tensor(name, shape, dt))
        psb = lambda name, shape: es.enter_context(nc.psum_tensor(name, shape, F32))

        ident = sb("ident", [128, 128], BF16)
        base3 = sb("base3", [128, 3], BF16)
        ilen = sb("ilen", [128, 1], F32)
        zeros_b = sb("zeros_b", [128, 128], BF16)
        ones_col = sb("ones_col", [128, 1], BF16)
        ones_row = sb("ones_row", [1, 128], BF16)
        mask_sb = sb("mask_sb", [128, NT * 128], BF16)
        bqr = sb("bqr", [128, 8], F32)
        bkr = sb("bkr", [128, 8], F32)

        Dt = sb("Dt", [128, 1], F32)
        DK2pp = sb("DK2pp", [128, 1], F32)
        EB1pp = sb("EB1pp", [128, 1], F32)

        bnscr = sb("bnscr", [128, 12], F32)
        mv_seq = sb("mv_seq", [128, 2 * NSEQT], F32)
        a_seq = sb("a_seq", [128, NSEQT], F32)
        am_seq = sb("am_seq", [128, NSEQT], F32)
        lnv_scr = sb("lnv_scr", [128, NSEQT], F32)
        mv_own = sb("mv_own", [128, 2 * NT], F32)
        a_own = sb("a_own", [128, NT], F32)
        nam_own = sb("nam_own", [128, NT], F32)

        summ_cm = sb("summ_cm", [128, D], BF16)
        summ_fm = sb("summ_fm", [128, 8 * 128], BF16)
        kfm = sb("kfm", [128, 8 * 128], BF16)
        vcm = sb("vcm", [128, D], BF16)
        sum_bias = sb("sum_bias", [128, 1], F32)

        wfcTq = sb("wfcTq", [128, 8 * HID], BF16)
        wpjTq = sb("wpjTq", [128, 32 * D], BF16)

        wbuf = sb("wbuf", [128, 4 * D], BF16)

        qa_amax = sb("qa_amax", [128, 128], F32)
        qa_e = sb("qa_e", [128, 128], F32)
        qa_s0 = sb("qa_s0", [128, 128], F32)
        qa_s = sb("qa_s", [128, 128], F32)
        qa_m = sb("qa_m", [128, 128], F32)
        qa_eb = sb("qa_eb", [128, 128], F32)

        a2c = sb("a2c", [128, 1], F32)
        nam2c = sb("nam2c", [128, 1], F32)
        lnv2 = sb("lnv2", [128, 1], F32)
        bo_r = sb("bo_r", [1, D], BF16)
        patt0 = sb("patt0", [128, 128], BF16)
        patt1 = sb("patt1", [128, 128], BF16)
        meanb = sb("meanb", [128, 1], BF16)
        ssum = sb("ssum", [128, 16], F32)
        srcp = sb("srcp", [128, 16], F32)

        es_pro = contextlib.ExitStack()
        sbp = lambda name, shape, dt: es_pro.enter_context(nc.sbuf_tensor(name, shape, dt))
        bv_r = sbp("bv_r", [1, D], BF16)
        wstr0 = sbp("wstr0", [128, 2048], F32)
        wstr1 = sbp("wstr1", [128, 2048], F32)
        wqnat = sbp("wqnat", [128, 4096], BF16)
        xsq0 = sbp("xsq0", [128, 2 * D], BF16)
        xsq1 = sbp("xsq1", [128, 2 * D], BF16)
        xsq2 = sbp("xsq2", [128, 2 * D], BF16)

        psA = psb("psA", [128, 1024])
        psB = psb("psB", [128, 1024])
        psC = psb("psC", [128, 1024])
        psD = psb("psD", [128, 512])
        psE = psb("psE", [128, 512])

        sem_names = ["LD", "LG", "ST0", "ST1", "PE", "VE", "AE", "GE", "XM",
                     "XS0", "XS1", "WF0", "WF1", "X20", "X21", "XOW", "WBF"]
        sems = {n: es.enter_context(nc.semaphore("s" + n)) for n in sem_names}
        block = es.enter_context(nc.Block())

        def w(eng, *names):
            for n in names:
                S.wait(eng, n)

        def dma(eng, sem_name, out, in_):
            S.op(eng, lambda E, o=out, i=in_: E.dma_start(out=o, in_=i), inc=sem_name, amt=16)

        def quant_consts(nb, amax_ap, e_ap, s0_ap, s_ap, m_ap, eb_ap):
            S.op("vector", lambda E, a=amax_ap: E.tensor_scalar(
                out=a, in0=a, scalar1=float(2.0**-100), scalar2=None, op0=ALU.max), inc="VE")
            w("scalar", "VE")
            S.op("scalar", lambda E, a=amax_ap, e=e_ap: E.activation(
                out=e, in_=a, func=ACT.Ln), inc="AE")
            w("vector", "AE")
            S.op("vector", lambda E, e=e_ap: E.tensor_scalar(
                out=e, in0=e, scalar1=float(1.0 / LN2), scalar2=float(1e-11 - 0.5),
                op0=ALU.mult, op1=ALU.add), inc="VE")
            S.wait_at("vector", "VE", S.c("VE"))
            S.op("vector", lambda E, e=e_ap: E.tensor_scalar(
                out=e, in0=e, scalar1=DK2pp[:, :], scalar2=EB1pp[:, :],
                op0=ALU.add, op1=ALU.subtract), inc="VE")
            S.wait_at("vector", "VE", S.c("VE"))
            S.op("vector", lambda E, e=e_ap: E.tensor_scalar(
                out=e, in0=e, scalar1=K1, scalar2=3.0,
                op0=ALU.subtract, op1=ALU.subtract), inc="VE")
            w("scalar", "VE")
            S.op("scalar", lambda E, e=e_ap, s0=s0_ap: E.activation(
                out=s0, in_=e, func=ACT.Exp, scale=LN2), inc="AE")
            w("vector", "AE")
            S.op("vector", lambda E, s0=s0_ap, m=m_ap: E.tensor_scalar(
                out=m, in0=s0, scalar1=K1, scalar2=None, op0=ALU.mult), inc="VE")
            S.wait_at("vector", "VE", S.c("VE"))
            S.op("vector", lambda E, s0=s0_ap, s=s_ap, m=m_ap: E.tensor_tensor(
                out=s, in0=s0, in1=m, op=ALU.add), inc="VE")
            S.wait_at("vector", "VE", S.c("VE"))
            S.op("vector", lambda E, s=s_ap, m=m_ap: E.tensor_tensor(
                out=s, in0=s, in1=m, op=ALU.subtract), inc="VE")
            S.wait_at("vector", "VE", S.c("VE"))
            S.op("vector", lambda E, s=s_ap, m=m_ap: E.tensor_scalar(
                out=m, in0=s, scalar1=K1, scalar2=None, op0=ALU.mult), inc="VE")
            S.op("vector", lambda E, s=s_ap, eb=eb_ap: E.tensor_scalar(
                out=eb, in0=s, scalar1=K2, scalar2=None, op0=ALU.mult), inc="VE")
            S.wait_at("vector", "VE", S.c("VE"))
            S.op("vector", lambda E, eb=eb_ap: E.tensor_scalar(
                out=eb, in0=eb, scalar1=Dt[:, :], scalar2=None, op0=ALU.mult), inc="VE")
            S.wait_at("vector", "VE", S.c("VE"))
            S.op("vector", lambda E, eb=eb_ap, m=m_ap: E.tensor_tensor(
                out=eb, in0=eb, in1=m, op=ALU.subtract), inc="VE")
            S.wait_at("vector", "VE", S.c("VE"))

        def quant_apply(v_ap, u_ap, q_ap, eb_bc):
            S.op("vector", lambda E: E.tensor_tensor(out=u_ap, in0=v_ap, in1=eb_bc,
                                                     op=ALU.subtract), inc="VE")
            S.wait_at("vector", "VE", S.c("VE"))
            S.op("vector", lambda E: E.tensor_tensor(out=q_ap, in0=u_ap, in1=eb_bc,
                                                     op=ALU.add), inc="VE")
            S.wait_at("vector", "VE", S.c("VE"))

        _evmark = {}

        def transpose8(src_tile_fn, dst0_ap_fn, nchunk, evac_eng="scalar", bank=None,
                       banks=None):
            bks = banks if banks is not None else [bank if bank is not None else psE]
            sem = "VE" if evac_eng == "vector" else "AE"
            for gi, j0 in enumerate(range(0, nchunk, 4)):
                bk = bks[gi % len(bks)]
                mk = _evmark.get(id(bk))
                if mk is not None:
                    S.wait_at("tensor", mk[0], mk[1])
                jn = min(4, nchunk - j0)
                for j in range(j0, j0 + jn):
                    S.op("tensor", lambda E, s=src_tile_fn(j), d=bk[:, 128 * (j - j0):128 * (j - j0) + 128]:
                         E.matmul(d, s, ident[:, :], start=True, stop=True),
                         inc="PE" if j == j0 + jn - 1 else None)
                w(evac_eng, "PE")
                ps3 = bk[:, 0:128 * jn].rearrange("p (j c) -> p j c", c=128)
                dst3 = dst0_ap_fn(j0, jn)
                if evac_eng == "vector":
                    S.op("vector", lambda E, s=ps3, d=dst3: E.tensor_copy(out=d, in_=s), inc="VE")
                else:
                    S.op("scalar", lambda E, s=ps3, d=dst3: E.activation(
                        out=d, in_=s, func=ACT.Identity), inc="AE")
                _evmark[id(bk)] = (sem, S.c(sem))
            S.wait(("tensor"), sem)

        def strided_dst(base_ap_fn):
            def f(j0, jn):
                a0 = base_ap_fn(j0)
                if jn == 1:
                    return a0.unsqueeze(1)
                a1 = base_ap_fn(j0 + 1)
                raise RuntimeError("use explicit dst fns")
            return f

        dma("sync", "LD", ident[:], ident_h[:])
        dma("sync", "LD", base3[:], base3_h[:])
        dma("sync", "LD", ilen[:], ilen_h[:])
        dma("sync", "LD", bqr[:], bqr_in)
        dma("sync", "LD", bkr[:], bkr_in)
        dma("sync", "LD", mask_sb.ap().rearrange("p (t c) -> p t c", c=128),
            mask_in.rearrange("(t p) c -> p t c", p=128))
        dma("gpsimd", "LG", bv_r[:], bv_in)
        dma("gpsimd", "LG", bo_r[:], bo_in)

        S.op("vector", lambda E: E.memset(zeros_b[:], 0.0), inc="VE")
        S.op("vector", lambda E: E.memset(ones_col[:], 1.0), inc="VE")
        S.op("vector", lambda E: E.memset(ones_row[:], 1.0), inc="VE")
        S.op("vector", lambda E: E.memset(Dt[:], 1.0), inc="VE")
        S.wait_at("vector", "VE", S.c("VE"))
        S.op("vector", lambda E: E.tensor_scalar(out=Dt[:], in0=Dt[:], scalar1=K2,
                                                 scalar2=-K2, op0=ALU.add, op1=ALU.add), inc="VE")
        S.wait_at("vector", "VE", S.c("VE"))
        S.op("vector", lambda E: E.tensor_scalar(out=DK2pp[:], in0=Dt[:], scalar1=K2,
                                                 scalar2=None, op0=ALU.mult), inc="VE")
        S.wait_at("vector", "VE", S.c("VE"))
        S.op("vector", lambda E: E.tensor_scalar(out=EB1pp[:], in0=DK2pp[:], scalar1=K1,
                                                 scalar2=None, op0=ALU.subtract), inc="VE")
        S.wait_at("vector", "VE", S.c("VE"))

        w("tensor", "LD", "VE")
        S.op("tensor", lambda E: E.matmul(psA[:, 0:512], zeros_b[:], mask_sb[:, 0:512],
                                          start=True, stop=False))
        S.op("tensor", lambda E: E.matmul(psA[:, 512:1024], zeros_b[:], mask_sb[:, 512:1024],
                                          start=True, stop=False))
        S.op("tensor", lambda E: E.matmul(psD[:, 0:1], zeros_b[:], ones_col[:],
                                          start=True, stop=False), inc="PE")

        wfc_v = wfc_in.rearrange("(n p) f -> p n f", p=128)
        wpj_v = wpj_in.rearrange("(n p) f -> p n f", p=128)
        jobs = [("fc", n) for n in range(32)] + [("pj", q) for q in range(16)]

        def job_dma(i):
            kind, n = jobs[i]
            par = i % 2
            buf = wstr0 if par == 0 else wstr1
            wf_sem = "WF0" if par == 0 else "WF1"
            if kind == "fc":
                dma("sync", wf_sem, buf[:, 0:D], wfc_v[:, n, :])
            else:
                dma("sync", wf_sem, buf[:, :], wpj_v[:, n // 2, 2048 * (n % 2):2048 * (n % 2) + 2048])
            return S.c(wf_sem)

        def job_nb(i):
            return 32 if jobs[i][0] == "fc" else 64

        wf_totals = {}
        def stage_R(i):
            par, nb = i % 2, job_nb(i)
            buf = wstr0 if par == 0 else wstr1
            o = 64 * par
            S.wait_at("vector", "WF0" if par == 0 else "WF1", wf_totals[i])
            v3 = buf.ap()[:, 0:32 * nb].rearrange("p (b e) -> p b e", e=32)
            S.op("vector", lambda E, v=v3, o_=o, nb_=nb: E.tensor_reduce(
                out=qa_amax[:, o_:o_ + nb_], in_=v, axis=AXX, op=ALU.max,
                apply_absolute_value=True), inc="VE")
            S.wait_at("vector", "VE", S.c("VE"))
            S.op("vector", lambda E, o_=o, nb_=nb: E.tensor_scalar(
                out=qa_amax[:, o_:o_ + nb_], in0=qa_amax[:, o_:o_ + nb_],
                scalar1=float(2.0**-100), scalar2=None, op0=ALU.max), inc="VE")

        def stage_L(i):
            par, nb = i % 2, job_nb(i)
            o = 64 * par
            w("scalar", "VE")
            S.op("scalar", lambda E, o_=o, nb_=nb: E.activation(
                out=qa_e[:, o_:o_ + nb_], in_=qa_amax[:, o_:o_ + nb_], func=ACT.Ln), inc="AE")

        def stage_E1(i):
            par, nb = i % 2, job_nb(i)
            o = 64 * par
            w("vector", "AE")
            S.op("vector", lambda E, o_=o, nb_=nb: E.tensor_scalar(
                out=qa_e[:, o_:o_ + nb_], in0=qa_e[:, o_:o_ + nb_],
                scalar1=float(1.0 / LN2), scalar2=float(1e-11 - 0.5),
                op0=ALU.mult, op1=ALU.add), inc="VE")
            S.wait_at("vector", "VE", S.c("VE"))
            S.op("vector", lambda E, o_=o, nb_=nb: E.tensor_scalar(
                out=qa_e[:, o_:o_ + nb_], in0=qa_e[:, o_:o_ + nb_],
                scalar1=DK2pp[:, :], scalar2=EB1pp[:, :], op0=ALU.add, op1=ALU.subtract), inc="VE")
            S.wait_at("vector", "VE", S.c("VE"))
            S.op("vector", lambda E, o_=o, nb_=nb: E.tensor_scalar(
                out=qa_e[:, o_:o_ + nb_], in0=qa_e[:, o_:o_ + nb_],
                scalar1=K1, scalar2=3.0, op0=ALU.subtract, op1=ALU.subtract), inc="VE")

        def stage_X(i):
            par, nb = i % 2, job_nb(i)
            o = 64 * par
            w("scalar", "VE")
            S.op("scalar", lambda E, o_=o, nb_=nb: E.activation(
                out=qa_s0[:, o_:o_ + nb_], in_=qa_e[:, o_:o_ + nb_], func=ACT.Exp,
                scale=LN2), inc="AE")

        def stage_S(i):
            par, nb = i % 2, job_nb(i)
            o = 64 * par
            w("vector", "AE")
            for fn in (
                lambda E, o_=o, nb_=nb: E.tensor_scalar(
                    out=qa_m[:, o_:o_ + nb_], in0=qa_s0[:, o_:o_ + nb_], scalar1=K1,
                    scalar2=None, op0=ALU.mult),
                lambda E, o_=o, nb_=nb: E.tensor_tensor(
                    out=qa_s[:, o_:o_ + nb_], in0=qa_s0[:, o_:o_ + nb_],
                    in1=qa_m[:, o_:o_ + nb_], op=ALU.add),
                lambda E, o_=o, nb_=nb: E.tensor_tensor(
                    out=qa_s[:, o_:o_ + nb_], in0=qa_s[:, o_:o_ + nb_],
                    in1=qa_m[:, o_:o_ + nb_], op=ALU.subtract),
                lambda E, o_=o, nb_=nb: E.tensor_scalar(
                    out=qa_m[:, o_:o_ + nb_], in0=qa_s[:, o_:o_ + nb_], scalar1=K1,
                    scalar2=None, op0=ALU.mult),
                lambda E, o_=o, nb_=nb: E.tensor_scalar(
                    out=qa_eb[:, o_:o_ + nb_], in0=qa_s[:, o_:o_ + nb_], scalar1=K2,
                    scalar2=None, op0=ALU.mult),
                lambda E, o_=o, nb_=nb: E.tensor_scalar(
                    out=qa_eb[:, o_:o_ + nb_], in0=qa_eb[:, o_:o_ + nb_],
                    scalar1=Dt[:, :], scalar2=None, op0=ALU.mult),
                lambda E, o_=o, nb_=nb: E.tensor_tensor(
                    out=qa_eb[:, o_:o_ + nb_], in0=qa_eb[:, o_:o_ + nb_],
                    in1=qa_m[:, o_:o_ + nb_], op=ALU.subtract),
            ):
                S.op("vector", fn, inc="VE")
                S.wait_at("vector", "VE", S.c("VE"))

        def stage_Q(i):
            par, nb = i % 2, job_nb(i)
            buf = wstr0 if par == 0 else wstr1
            o = 64 * par
            v3 = buf.ap()[:, 0:32 * nb].rearrange("p (b e) -> p b e", e=32)
            q3 = wqnat.ap()[:, 2048 * par:2048 * par + 32 * nb].rearrange("p (b e) -> p b e", e=32)
            ebb = qa_eb.ap()[:, o:o + nb].unsqueeze(2).broadcast_to([128, nb, 32])
            S.wait_at("vector", "PE", S.c("PE"))
            S.op("vector", lambda E, a=v3, b=ebb: E.tensor_tensor(
                out=a, in0=a, in1=b, op=ALU.subtract), inc="VE")
            S.wait_at("vector", "VE", S.c("VE"))
            S.op("vector", lambda E, a=v3, b=ebb, q=q3: E.tensor_tensor(
                out=q, in0=a, in1=b, op=ALU.add), inc="VE")
            S.wait_at("tensor", "VE", S.c("VE"))

        def stage_T(i):
            kind, n = jobs[i]
            par = i % 2
            base = 2048 * par
            if kind == "fc":
                transpose8(lambda j, b=base: wqnat[:, b + 128 * j:b + 128 * j + 128],
                           lambda j0, jn, h_=n: wfcTq.ap().rearrange(
                               "p (j r) -> p j r", r=HID)[:, j0:j0 + jn,
                                                          128 * h_:128 * h_ + 128],
                           8, evac_eng="scalar", banks=[psE, psC])
            else:
                h_, hf = n // 2, n % 2
                transpose8(lambda j, b=base: wqnat[:, b + 128 * j:b + 128 * j + 128],
                           lambda j0, jn, hh=h_, f_=hf: wpjTq.ap().rearrange(
                               "p (j r) -> p j r", r=D)[:, 16 * f_ + j0:16 * f_ + j0 + jn,
                                                        128 * hh:128 * hh + 128],
                           16, evac_eng="scalar", banks=[psE, psC])

        NJOB = len(jobs)
        wf_totals[0] = job_dma(0)
        wf_totals[1] = job_dma(1)
        _wstate = {"i": 0}

        def w_step():
            i = _wstate["i"]
            if i > NJOB:
                return False
            _wstate["i"] = i + 1
            if i < NJOB:
                stage_R(i)
                stage_L(i)
            if i >= 1:
                stage_Q(i - 1)
                if i + 1 < NJOB:
                    S.wait_at("sync", "VE", S.c("VE"))
                    wf_totals[i + 1] = job_dma(i + 1)
            if i < NJOB:
                stage_E1(i)
                stage_X(i)
            if i >= 1:
                S.wait_at("tensor", "VE", S.c("VE"))
                stage_T(i - 1)
            if i < NJOB:
                stage_S(i)
            return True

        xs_view = xseq.rearrange("(c p) f -> p c f", p=128)
        NCH = 32
        xsqs = [xsq0, xsq1, xsq2]

        def xs_dma(ch):
            buf = xsqs[ch % 3]
            xs_sem = ["XS0", "XS1", "GE"][ch % 3]
            S.op("gpsimd",
                 lambda E, b=buf, c0=ch * 2: E.dma_start(
                     out=b.ap().rearrange("p (t f) -> p t f", f=D),
                     in_=xs_view[:, c0:c0 + 2, :]),
                 inc=xs_sem, amt=16)
            return xs_sem, S.c(xs_sem)

        def xs_stats(ch, sem_tot):
            buf = xsqs[ch % 3]
            xs_sem, tot = sem_tot
            S.wait_at("vector", xs_sem, tot)
            for tt in range(2):
                t = ch * 2 + tt
                S.op("vector", lambda E, b=buf, o=tt: E.bn_stats(
                    out=bnscr[:, 0:6], in_=b[:, o * D:o * D + 512]), inc="VE")
                S.op("vector", lambda E, b=buf, o=tt: E.bn_stats(
                    out=bnscr[:, 6:12], in_=b[:, o * D + 512:o * D + 1024]), inc="VE")
                S.wait_at("vector", "VE", S.c("VE"))
                S.op("vector", lambda E, t_=t: E.bn_aggr(
                    out=mv_seq[:, 2 * t_:2 * t_ + 2], in_=bnscr[:, :]), inc="VE")
                S.wait_at("vector", "VE", S.c("VE"))
            mvs = mv_seq.ap().rearrange("p (t two) -> p t two", two=2)
            var_sl = mvs[:, ch * 2:(ch + 1) * 2, 1:2]
            mean_sl = mvs[:, ch * 2:(ch + 1) * 2, 0:1]
            lv = lnv_scr[:, ch * 2:(ch + 1) * 2]
            lv3 = lnv_scr.ap().rearrange("p (t one) -> p t one", one=1)[:, ch * 2:(ch + 1) * 2, :]
            S.op("vector", lambda E, v=var_sl, o=lv3: E.tensor_scalar(
                out=o, in0=v, scalar1=EPS, scalar2=None, op0=ALU.add), inc="VE")
            w("scalar", "VE")
            a_sl = a_seq[:, ch * 2:(ch + 1) * 2]
            S.op("scalar", lambda E, i=lv, o=a_sl: E.activation(
                out=o, in_=i, func=ACT.Ln), inc="AE")
            S.wait_at("scalar", "AE", S.c("AE"))
            S.op("scalar", lambda E, o=a_sl: E.activation(
                out=o, in_=o, func=ACT.Exp, scale=-0.5), inc="AE")
            w("vector", "AE")
            am_sl3 = am_seq.ap().rearrange("p (t one) -> p t one", one=1)[:, ch * 2:(ch + 1) * 2, :]
            a_sl3 = a_seq.ap().rearrange("p (t one) -> p t one", one=1)[:, ch * 2:(ch + 1) * 2, :]
            S.op("vector", lambda E, a3=a_sl3, m3=mean_sl, o=am_sl3: E.tensor_tensor(
                out=o, in0=a3, in1=m3, op=ALU.mult), inc="VE")

        def xs_windows(ch):
            buf = xsqs[ch % 3]
            for tt in range(2):
                t = ch * 2 + tt
                ncols = 2 if t == 63 else 3
                pt = patt0 if t % 2 == 0 else patt1
                S.wait_at("vector", "PE", S.c("PE"))
                S.op("vector", lambda E, p=pt: E.memset(p[:, :], 0.0), inc="VE")
                S.wait_at("vector", "VE", S.c("VE"))
                S.op("vector", lambda E, t_=t, n=ncols, p=pt: E.tensor_scalar(
                    out=p[:, 2 * t_:2 * t_ + n],
                    in0=base3[:, 0:n], scalar1=a_seq[:, t_:t_ + 1],
                    scalar2=None, op0=ALU.mult), inc="VE")
                S.op("vector", lambda E, t_=t: E.tensor_copy(
                    out=meanb[:, :], in_=mv_seq[:, 2 * t_:2 * t_ + 1]), inc="VE")
                S.wait_at("tensor", "VE", S.c("VE"))
                S.op("tensor", lambda E, b=buf, o=tt, p=pt: E.matmul(
                    psA[:, 0:512], p[:, :], b[:, o * D:o * D + 512],
                    start=False, stop=False))
                S.op("tensor", lambda E, b=buf, o=tt, p=pt: E.matmul(
                    psA[:, 512:1024], p[:, :], b[:, o * D + 512:o * D + 1024],
                    start=False, stop=False))
                S.op("tensor", lambda E, p=pt: E.matmul(
                    psD[:, 0:1], p[:, :], meanb[:, :],
                    start=False, stop=False), inc="PE")
                S.wait_at("vector", "PE", S.c("PE"))

        sems_ch = {}
        sems_ch[0] = xs_dma(0)
        sems_ch[1] = xs_dma(1)
        for ch in range(NCH + 1):
            if ch < NCH:
                xs_stats(ch, sems_ch[ch])
            if ch >= 1:
                xs_windows(ch - 1)
                if ch + 1 < NCH:
                    S.wait_at("gpsimd", "PE", S.c("PE"))
                    sems_ch[ch + 1] = xs_dma(ch + 1)
            w_step()
            if ch % 2 == 0:
                w_step()
        while w_step():
            pass

        S.op("tensor", lambda E: E.matmul(psA[:, 0:512], zeros_b[:], mask_sb[:, 0:512],
                                          start=False, stop=True))
        S.op("tensor", lambda E: E.matmul(psA[:, 512:1024], zeros_b[:], mask_sb[:, 512:1024],
                                          start=False, stop=True))
        S.op("tensor", lambda E: E.matmul(psD[:, 0:1], zeros_b[:], ones_col[:, :],
                                          start=False, stop=True), inc="PE")
        pe_summary_done = S.c("PE")

        S.wait_at("vector", "PE", pe_summary_done)
        S.op("vector", lambda E: E.tensor_copy(out=sum_bias[:], in_=psD[:, 0:1]), inc="VE")
        S.wait_at("vector", "VE", S.c("VE"))
        S.op("vector", lambda E: E.tensor_scalar(
            out=sum_bias[:], in0=sum_bias[:], scalar1=ilen[:, :], scalar2=-1.0,
            op0=ALU.mult, op1=ALU.mult), inc="VE")
        w("scalar", "VE")
        S.wait_at("scalar", "PE", pe_summary_done)
        S.op("scalar", lambda E: E.activation(
            out=summ_cm[:, :], in_=psA[:, :], func=ACT.Identity,
            bias=sum_bias[:, :], scale=ilen[:, :]), inc="AE")
        ae_summ = S.c("AE")
        S.wait_at("tensor", "AE", ae_summ)
        transpose8(lambda j: summ_cm[:, 128 * j:128 * j + 128],
                   lambda j0, jn: summ_fm.ap().rearrange(
                       "p (j c) -> p j c", c=128)[:, j0:j0 + jn, :], 8,
                   evac_eng="scalar", banks=[psE, psC])

        kdst = [psA[:, 0:128], psA[:, 512:640], psB[:, 0:128], psB[:, 512:640],
                psC[:, 0:128], psC[:, 512:640], psD[:, 0:128], psE[:, 0:128]]
        wk_v = wk_in.rearrange("(k p) f -> p k f", p=128)
        for hf in range(2):
            S.wait_at("gpsimd", "PE", S.c("PE"))
            S.op("gpsimd", lambda E, h=hf: E.dma_start(
                out=wbuf.ap().rearrange("p (k f) -> p k f", f=D),
                in_=wk_v[:, 4 * h:4 * h + 4, :]), inc="WBF", amt=16)
            S.wait_at("tensor", "WBF", S.c("WBF"))
            for m in range(8):
                for kl in range(4):
                    k = 4 * hf + kl
                    S.op("tensor", lambda E, m_=m, k_=k, kk=kl: E.matmul(
                        kdst[m_],
                        wbuf[:, kk * D + m_ * 128: kk * D + m_ * 128 + 128],
                        summ_fm[:, k_ * 128:k_ * 128 + 128],
                        start=(k_ == 0), stop=(k_ == 7)),
                        inc="PE" if (m == 7 and kl == 3) else None)
        w("scalar", "PE")
        for m in range(8):
            S.op("scalar", lambda E, m_=m: E.activation(
                out=kfm[:, m_ * 128:m_ * 128 + 128], in_=kdst[m_],
                func=ACT.Identity, bias=bkr[:, m_:m_ + 1]),
                inc="AE" if m == 7 else None)
        S.wait_at("tensor", "AE", S.c("AE"))

        wv_v = wv_in.rearrange("(k p) f -> p k f", p=128)
        for hf in range(2):
            S.wait_at("gpsimd", "PE", S.c("PE"))
            S.op("gpsimd", lambda E, h=hf: E.dma_start(
                out=wbuf.ap().rearrange("p (k f) -> p k f", f=D),
                in_=wv_v[:, 4 * h:4 * h + 4, :]), inc="WBF", amt=16)
            S.wait_at("tensor", "WBF", S.c("WBF"))
            for cl in (0, 512):
                vdst = psB[:, 0:512] if cl == 0 else psD[:, 0:512]
                for kl in range(4):
                    k = 4 * hf + kl
                    S.op("tensor", lambda E, k_=k, kk=kl, c=cl, dd=vdst: E.matmul(
                        dd, summ_fm[:, k_ * 128:k_ * 128 + 128],
                        wbuf[:, kk * D + c: kk * D + c + 512],
                        start=(k_ == 0), stop=False),
                        inc="PE" if (cl == 512 and kl == 3) else None)
            if hf == 1:
                S.wait_at("tensor", "LG", S.c("LG"))
                S.op("tensor", lambda E: E.matmul(
                    psB[:, 0:512], ones_row[0:1, :], bv_r[0:1, 0:512],
                    start=False, stop=True))
                S.op("tensor", lambda E: E.matmul(
                    psD[:, 0:512], ones_row[0:1, :], bv_r[0:1, 512:1024],
                    start=False, stop=True), inc="PE")
        w("scalar", "PE")
        S.op("scalar", lambda E: E.activation(out=vcm[:, 0:512], in_=psB[:, 0:512],
                                              func=ACT.Identity), inc="AE")
        S.op("scalar", lambda E: E.activation(out=vcm[:, 512:1024], in_=psD[:, 0:512],
                                              func=ACT.Identity), inc="AE")

        es_pro.close()
        es_ph1 = contextlib.ExitStack()
        sb1 = lambda name, shape, dt: es_ph1.enter_context(nc.sbuf_tensor(name, shape, dt))
        xring = sb1("xring", [128, 4 * D], F32)
        lnbuf = sb1("lnbuf", [128, D], BF16)
        ln1fm = sb1("ln1fm", [128, 8 * 512], BF16)
        qfm = sb1("qfm", [128, 8 * 512], BF16)
        ebuf = sb1("ebuf", [128, 2 * 16 * 128], BF16)
        xmid_t = sb1("xmid_t", [128, D], F32)
        ofm = ln1fm

        for e in ("sync", "gpsimd", "tensor", "vector", "scalar"):
            S.wait_at(e, "VE", S.c("VE"))
            S.wait_at(e, "AE", S.c("AE"))
            S.wait_at(e, "PE", S.c("PE"))

        st_count = 0

        def p1_loads_stats(s):
            S.wait_at("sync", "VE", S.c("VE"))
            for tt in range(4):
                t = 4 * s + tt
                slot = t % 4
                dma("sync", "XOW", xring[:, slot * D:(slot + 1) * D],
                    xown.rearrange("(t p) f -> p t f", p=128)[:, t, :])
            ld_xown = S.c("XOW")
            S.wait_at("vector", "XOW", ld_xown)
            for tt in range(4):
                t = 4 * s + tt
                slot = t % 4
                xo = xring[:, slot * D:(slot + 1) * D]
                S.op("vector", lambda E, x=xo: E.bn_stats(out=bnscr[:, 0:6], in_=x[:, 0:512]), inc="VE")
                S.op("vector", lambda E, x=xo: E.bn_stats(out=bnscr[:, 6:12], in_=x[:, 512:1024]), inc="VE")
                S.wait_at("vector", "VE", S.c("VE"))
                S.op("vector", lambda E, t_=t: E.bn_aggr(
                    out=mv_own[:, 2 * t_:2 * t_ + 2], in_=bnscr[:, :]), inc="VE")
                S.wait_at("vector", "VE", S.c("VE"))
            mvo = mv_own.ap().rearrange("p (t two) -> p t two", two=2)
            var_sl = mvo[:, 4 * s:4 * s + 4, 1:2]
            mean_sl = mvo[:, 4 * s:4 * s + 4, 0:1]
            lv3 = lnv_scr.ap().rearrange("p (t one) -> p t one", one=1)[:, 0:4, :]
            S.op("vector", lambda E, v=var_sl, o=lv3: E.tensor_scalar(
                out=o, in0=v, scalar1=EPS, scalar2=None, op0=ALU.add), inc="VE")
            w("scalar", "VE")
            a_sl = a_own[:, 4 * s:4 * s + 4]
            S.op("scalar", lambda E, o=a_sl: E.activation(
                out=o, in_=lnv_scr[:, 0:4], func=ACT.Ln), inc="AE")
            S.wait_at("scalar", "AE", S.c("AE"))
            S.op("scalar", lambda E, o=a_sl: E.activation(
                out=o, in_=o, func=ACT.Exp, scale=-0.5), inc="AE")
            w("vector", "AE")
            a3 = a_own.ap().rearrange("p (t one) -> p t one", one=1)[:, 4 * s:4 * s + 4, :]
            n3 = nam_own.ap().rearrange("p (t one) -> p t one", one=1)[:, 4 * s:4 * s + 4, :]
            S.op("vector", lambda E, a=a3, m=mean_sl, o=n3: E.tensor_tensor(
                out=o, in0=a, in1=m, op=ALU.mult), inc="VE")
            S.wait_at("vector", "VE", S.c("VE"))
            S.op("vector", lambda E, o=n3: E.tensor_scalar(
                out=nam_own[:, 4 * s:4 * s + 4], in0=nam_own[:, 4 * s:4 * s + 4],
                scalar1=-1.0, scalar2=None, op0=ALU.mult), inc="VE")

        def p1_lnq(s):
            S.wait_at("tensor", "VE", S.c("VE"))
            for tt in range(4):
                t = 4 * s + tt
                slot = t % 4
                xo = xring[:, slot * D:(slot + 1) * D]
                w("scalar", "VE")
                S.wait_at("scalar", "PE", S.c("PE"))
                S.op("scalar", lambda E, x=xo, t_=t: E.activation(
                    out=lnbuf[:, 0:D], in_=x, func=ACT.Identity,
                    scale=a_own[:, t_:t_ + 1], bias=nam_own[:, t_:t_ + 1]), inc="AE")
                S.wait_at("tensor", "AE", S.c("AE"))
                transpose8(lambda j: lnbuf[:, 128 * j:128 * j + 128],
                           lambda j0, jn, u=tt: ln1fm.ap().rearrange(
                               "p (j q) -> p j q", q=512)[:, j0:j0 + jn, 128 * u:128 * u + 128],
                           8, evac_eng="scalar", banks=[psE, psD])
            qdst = [psA[:, 0:512], psA[:, 512:1024], psB[:, 0:512], psB[:, 512:1024],
                    psC[:, 0:512], psC[:, 512:1024], psD[:, 0:512], psE[:, 0:512]]
            wq_v = wq_in.rearrange("(k p) f -> p k f", p=128)
            for hf in range(2):
                S.wait_at("gpsimd", "PE", S.c("PE"))
                S.op("gpsimd", lambda E, h=hf: E.dma_start(
                    out=wbuf.ap().rearrange("p (k f) -> p k f", f=D),
                    in_=wq_v[:, 4 * h:4 * h + 4, :]), inc="WBF", amt=16)
                S.wait_at("tensor", "WBF", S.c("WBF"))
                for m in range(8):
                    for kl in range(4):
                        k = 4 * hf + kl
                        S.op("tensor", lambda E, m_=m, k_=k, kk=kl: E.matmul(
                            qdst[m_], wbuf[:, kk * D + m_ * 128: kk * D + m_ * 128 + 128],
                            ln1fm[:, k_ * 512:(k_ + 1) * 512],
                            start=(k_ == 0), stop=(k_ == 7)),
                            inc="PE" if (m == 7 and kl == 3) else None)
            w("scalar", "PE")
            for m in range(8):
                S.op("scalar", lambda E, m_=m: E.activation(
                    out=qfm[:, m_ * 512:(m_ + 1) * 512], in_=qdst[m_],
                    func=ACT.Identity, bias=bqr[:, m_:m_ + 1]),
                    inc="AE" if m == 7 else None)
            S.wait_at("tensor", "AE", S.c("AE"))
            _evmark[id(psD)] = ("AE", S.c("AE"))
            _evmark[id(psE)] = ("AE", S.c("AE"))
            S.wait_at("gpsimd", "PE", S.c("PE"))
            wo_v = wo_in.rearrange("(k p) f -> p k f", p=128)
            S.op("gpsimd", lambda E: E.dma_start(
                out=wbuf.ap().rearrange("p (k f) -> p k f", f=D),
                in_=wo_v[:, 0:4, :]), inc="WBF", amt=16)

        def p1_scores(s, u, epar):
            t = 4 * s + u
            S.wait_at("tensor", "AE", S.c("AE"))
            for m in range(8):
                qsl_e = qfm[0:64, m * 512 + 128 * u: m * 512 + 128 * u + 128]
                qsl_o = qfm[64:128, m * 512 + 128 * u: m * 512 + 128 * u + 128]
                ksl_e = kfm[0:64, m * 128:(m + 1) * 128]
                ksl_o = kfm[64:128, m * 128:(m + 1) * 128]
                dstA = psA[:, 128 * (m % 4):128 * (m % 4) + 128] if m < 4 else \
                    psB[:, 128 * (m % 4):128 * (m % 4) + 128]
                dstB = psA[:, 512 + 128 * (m % 4):512 + 128 * (m % 4) + 128] if m < 4 else \
                    psB[:, 512 + 128 * (m % 4):512 + 128 * (m % 4) + 128]
                S.op("tensor", lambda E, q=qsl_e, kk=ksl_e, d=dstA: E.matmul(
                    d, q, kk, start=True, stop=True, tile_position=(0, 0)))
                S.op("tensor", lambda E, q=qsl_o, kk=ksl_o, d=dstB: E.matmul(
                    d, q, kk, start=True, stop=True, tile_position=(64, 0)),
                    inc="PE" if m == 7 else None)
            w("scalar", "PE")
            S.wait_at("scalar", "PE", S.c("PE"))
            eb = 2048 * epar
            S.op("scalar", lambda E, o=eb: E.activation(
                out=ebuf[:, o:o + 1024], in_=psA[:, :], func=ACT.Exp, scale=0.125), inc="AE")
            S.op("scalar", lambda E, o=eb: E.activation(
                out=ebuf[:, o + 1024:o + 2048], in_=psB[:, :], func=ACT.Exp, scale=0.125), inc="AE")

        def p1_softmax_av(s, u, epar):
            t = 4 * s + u
            eb = 2048 * epar
            w("vector", "AE")
            e3 = ebuf.ap()[:, eb:eb + 2048].rearrange("p (g c) -> p g c", c=128)
            msk = mask_sb.ap()[:, 128 * t:128 * t + 128].unsqueeze(1).broadcast_to([128, 16, 128])
            S.op("vector", lambda E, e=e3, mk=msk: E.tensor_tensor(
                out=e, in0=e, in1=mk, op=ALU.mult), inc="VE")
            S.wait_at("vector", "VE", S.c("VE"))
            S.op("vector", lambda E, e=e3: E.tensor_reduce(
                out=ssum[:, :], in_=e, axis=AXX, op=ALU.add), inc="VE")
            S.wait_at("vector", "VE", S.c("VE"))
            S.op("vector", lambda E: E.reciprocal(out=srcp[:, :], in_=ssum[:, :]), inc="VE")
            S.wait_at("vector", "VE", S.c("VE"))
            rb = srcp.ap().unsqueeze(2).broadcast_to([128, 16, 128])
            S.op("vector", lambda E, e=e3, r=rb: E.tensor_tensor(
                out=e, in0=e, in1=r, op=ALU.mult), inc="VE")
            S.wait_at("tensor", "VE", S.c("VE"))
            transpose8(lambda j, o=eb: ebuf[:, o + 128 * j:o + 128 * j + 128],
                       lambda j0, jn, o=eb: ebuf.ap()[:, o:o + 2048].rearrange(
                           "p (j c) -> p j c", c=128)[:, j0:j0 + jn, :],
                       16, evac_eng="scalar", banks=[psE, psD])
            for m0 in range(0, 8, 4):
                for mm in range(4):
                    m = m0 + mm
                    g_e = (m // 4) * 8 + (m % 4)
                    g_o = (m // 4) * 8 + 4 + (m % 4)
                    h_e, h_o = 2 * m, 2 * m + 1
                    S.op("tensor", lambda E, g=g_e, h=h_e, c=128 * mm, o=eb: E.matmul(
                        psD[0:64, c:c + 128], vcm[:, 64 * h:64 * h + 64],
                        ebuf[:, o + 128 * g:o + 128 * g + 128], start=True, stop=True,
                        tile_position=(0, 0)))
                    S.op("tensor", lambda E, g=g_o, h=h_o, c=128 * mm, o=eb: E.matmul(
                        psD[64:128, c:c + 128], vcm[:, 64 * h:64 * h + 64],
                        ebuf[:, o + 128 * g:o + 128 * g + 128], start=True, stop=True,
                        tile_position=(0, 64)), inc="PE" if mm == 3 else None)
                S.wait_at("scalar", "PE", S.c("PE"))
                dst3 = ofm.ap().rearrange("p (m q) -> p m q", q=512)[
                    :, m0:m0 + 4, 128 * u:128 * u + 128]
                ps3 = psD.ap().rearrange("p (m c) -> p m c", c=128)
                S.op("scalar", lambda E, d=dst3, s_=ps3: E.activation(
                    out=d, in_=s_, func=ACT.Identity), inc="AE")
                S.wait_at("tensor", "AE", S.c("AE"))
                _evmark[id(psD)] = ("AE", S.c("AE"))

        def p1_wo_res(s):
            global_st = None
            odst = [(psA[:, 0:512], psA[:, 512:1024]), (psB[:, 0:512], psB[:, 512:1024]),
                    (psC[:, 0:512], psC[:, 512:1024]), (psD[:, 0:512], psE[:, 0:512])]
            wo_v = wo_in.rearrange("(k p) f -> p k f", p=128)
            for hf in range(2):
                if hf == 1:
                    S.wait_at("gpsimd", "PE", S.c("PE"))
                    S.op("gpsimd", lambda E: E.dma_start(
                        out=wbuf.ap().rearrange("p (k f) -> p k f", f=D),
                        in_=wo_v[:, 4:8, :]), inc="WBF", amt=16)
                S.wait_at("tensor", "WBF", S.c("WBF"))
                for u in range(4):
                    for half in range(2):
                        cl = half * 512
                        dd = odst[u][half]
                        for ml in range(4):
                            m = 4 * hf + ml
                            S.op("tensor", lambda E, m_=m, mm=ml, c=cl, u_=u, d=dd: E.matmul(
                                d, ofm[:, m_ * 512 + 128 * u_: m_ * 512 + 128 * u_ + 128],
                                wbuf[:, mm * D + c: mm * D + c + 512],
                                start=(m_ == 0), stop=False),
                                inc="PE" if (u == 3 and half == 1 and ml == 3) else None)
                        if hf == 1:
                            S.wait_at("tensor", "LG", S.c("LG"))
                            S.op("tensor", lambda E, c=cl, d=dd: E.matmul(
                                d, ones_row[0:1, :], bo_r[0:1, c:c + 512],
                                start=False, stop=True),
                                inc="PE" if (u == 3 and half == 1) else None)
            w("vector", "PE")
            for u in range(4):
                t = 4 * s + u
                slot = t % 4
                xo = xring[:, slot * D:(slot + 1) * D]
                S.op("vector", lambda E, d0=odst[u][0], x=xo: E.tensor_tensor(
                    out=xmid_t[:, 0:512], in0=d0, in1=x[:, 0:512], op=ALU.add), inc="VE")
                S.op("vector", lambda E, d1=odst[u][1], x=xo: E.tensor_tensor(
                    out=xmid_t[:, 512:1024], in0=d1, in1=x[:, 512:1024], op=ALU.add), inc="VE")
                S.wait_at("sync", "VE", S.c("VE"))
                S.op("sync", lambda E, t_=t: E.dma_start(
                    out=xmid_hbm.rearrange("(t p) f -> p t f", p=128)[:, t_, :],
                    in_=xmid_t[:, :]), inc="XM", amt=16)
                S.wait_at("vector", "XM", S.c("XM"))
            _evmark[id(psD)] = ("VE", S.c("VE"))
            _evmark[id(psE)] = ("VE", S.c("VE"))

        p1_next_loads_stats = p1_loads_stats

        p1_loads_stats(0)
        for s in range(NSUP):
            p1_lnq(s)
            p1_scores(s, 0, 0)
            for u in range(4):
                if u + 1 < 4:
                    p1_scores(s, u + 1, (u + 1) % 2)
                p1_softmax_av(s, u, u % 2)
            p1_wo_res(s)
            if s + 1 < NSUP:
                p1_next_loads_stats(s + 1)
        es_ph1.close()
        es_ph2 = contextlib.ExitStack()
        sb2 = lambda name, shape, dt: es_ph2.enter_context(nc.sbuf_tensor(name, shape, dt))
        x2r0 = sb2("x2r0", [128, D], F32)
        x2r1 = sb2("x2r1", [128, D], F32)
        xln2 = sb2("xln2", [128, D], F32)
        xqb = sb2("xqb", [128, D], BF16)
        xqfm = sb2("xqfm", [128, 2 * 8 * 128], BF16)
        hbuf = sb2("hbuf", [128, 2 * D], F32)
        hqb = sb2("hqb", [128, D], BF16)
        hqT = sb2("hqT", [128, 2 * 8 * 128], BF16)
        outb = sb2("outb", [128, 2 * D], F32)
        bfc_r = sb2("bfc_r", [1, HID], BF16)
        bpj_r = sb2("bpj_r", [1, D], BF16)
        for e in ("sync", "gpsimd", "tensor", "vector", "scalar"):
            S.wait_at(e, "XM", 16 * NT)
        dma("gpsimd", "LG", bfc_r[:], bfc_in)
        dma("gpsimd", "LG", bpj_r[:], bpj_in)
        lg_bias2 = S.c("LG")
        S.wait_at("tensor", "LG", lg_bias2)

        def p2_front(t):
            buf = x2r0 if t % 2 == 0 else x2r1
            S.wait_at("sync", "VE", S.c("VE"))
            x2_sem = "X20" if t % 2 == 0 else "X21"
            dma("sync", x2_sem, buf[:, :], xmid_hbm.rearrange("(t p) f -> p t f", p=128)[:, t, :])
            S.wait_at("vector", x2_sem, S.c(x2_sem))
            S.op("vector", lambda E, b=buf: E.bn_stats(out=bnscr[:, 0:6], in_=b[:, 0:512]), inc="VE")
            S.op("vector", lambda E, b=buf: E.bn_stats(out=bnscr[:, 6:12], in_=b[:, 512:1024]), inc="VE")
            S.wait_at("vector", "VE", S.c("VE"))
            S.op("vector", lambda E: E.bn_aggr(out=mv_own[:, 0:2], in_=bnscr[:, :]), inc="VE")
            S.wait_at("vector", "VE", S.c("VE"))
            S.op("vector", lambda E: E.tensor_scalar(
                out=lnv2[:, 0:1], in0=mv_own[:, 1:2], scalar1=EPS, scalar2=None,
                op0=ALU.add), inc="VE")
            w("scalar", "VE")
            S.op("scalar", lambda E: E.activation(out=a2c[:], in_=lnv2[:, 0:1], func=ACT.Ln), inc="AE")
            S.wait_at("scalar", "AE", S.c("AE"))
            S.op("scalar", lambda E: E.activation(out=a2c[:], in_=a2c[:], func=ACT.Exp,
                                                  scale=-0.5), inc="AE")
            w("vector", "AE")
            S.op("vector", lambda E: E.tensor_tensor(out=nam2c[:], in0=a2c[:],
                                                     in1=mv_own[:, 0:1], op=ALU.mult), inc="VE")
            S.wait_at("vector", "VE", S.c("VE"))
            S.op("vector", lambda E: E.tensor_scalar(out=nam2c[:], in0=nam2c[:], scalar1=-1.0,
                                                     scalar2=None, op0=ALU.mult), inc="VE")
            S.wait_at("vector", "VE", S.c("VE"))
            S.op("vector", lambda E, b=buf: E.tensor_scalar(
                out=xln2[:, :], in0=b[:, :], scalar1=a2c[:, :], scalar2=nam2c[:, :],
                op0=ALU.mult, op1=ALU.add), inc="VE")
            S.wait_at("vector", "VE", S.c("VE"))
            o = 96
            v3 = xln2.ap().rearrange("p (b e) -> p b e", e=32)
            S.op("vector", lambda E, v=v3, o_=o: E.tensor_reduce(
                out=qa_amax[:, o_:o_ + 32], in_=v, axis=AXX, op=ALU.max,
                apply_absolute_value=True), inc="VE")
            S.wait_at("vector", "VE", S.c("VE"))
            quant_consts(32, qa_amax[:, o:o + 32], qa_e[:, o:o + 32], qa_s0[:, o:o + 32],
                         qa_s[:, o:o + 32], qa_m[:, o:o + 32], qa_eb[:, o:o + 32])
            ebb = qa_eb.ap()[:, o:o + 32].unsqueeze(2).broadcast_to([128, 32, 32])
            q3 = xqb.ap().rearrange("p (b e) -> p b e", e=32)
            S.wait_at("vector", "PE", S.c("PE"))
            quant_apply(v3, v3, q3, ebb)
            S.wait_at("tensor", "VE", S.c("VE"))
            xpar = t % 2
            transpose8(lambda j: xqb[:, 128 * j:128 * j + 128],
                       lambda j0, jn, xp=xpar: xqfm.ap().rearrange(
                           "p (j c) -> p j c", c=128)[:, 8 * xp + j0:8 * xp + j0 + jn, :],
                       8, evac_eng="scalar", banks=[psE, psD])

        def p2_fc1(t, g):
            xpar = t % 2
            dst = psA if g % 2 == 0 else psB
            for k in range(8):
                S.op("tensor", lambda E, k_=k, g_=g, d=dst, xp=xpar: E.matmul(
                    d[:, 0:512], xqfm[:, 1024 * xp + 128 * k_:1024 * xp + 128 * k_ + 128],
                    wfcTq[:, k_ * HID + g_ * D: k_ * HID + g_ * D + 512],
                    start=(k_ == 0), stop=False))
                S.op("tensor", lambda E, k_=k, g_=g, d=dst, xp=xpar: E.matmul(
                    d[:, 512:1024], xqfm[:, 1024 * xp + 128 * k_:1024 * xp + 128 * k_ + 128],
                    wfcTq[:, k_ * HID + g_ * D + 512: k_ * HID + g_ * D + 1024],
                    start=(k_ == 0), stop=False))
            S.op("tensor", lambda E, g_=g, d=dst: E.matmul(
                d[:, 0:512], ones_row[0:1, :], bfc_r[0:1, g_ * D:g_ * D + 512],
                start=False, stop=True))
            S.op("tensor", lambda E, g_=g, d=dst: E.matmul(
                d[:, 512:1024], ones_row[0:1, :], bfc_r[0:1, g_ * D + 512:g_ * D + 1024],
                start=False, stop=True), inc="PE")

        def p2_gelu(t, g):
            dst = psA if g % 2 == 0 else psB
            gs = g % 2
            w("scalar", "PE")
            S.wait_at("scalar", "VE", S.c("VE"))
            S.op("scalar", lambda E, d=dst, s_=gs: E.activation(
                out=hbuf[:, s_ * D:(s_ + 1) * D], in_=d[:, :], func=ACT.Gelu), inc="AE")

        def p2_pair(t, pair, do_front):
            g0, g1 = 2 * pair, 2 * pair + 1
            p2_gelu(t, g0)
            p2_fc1(t, g1)
            p2_gelu(t, g1)
            if do_front:
                p2_front(t + 1)
            w("vector", "AE")
            for gi, g in enumerate((g0, g1)):
                gs = g % 2
                hg = hbuf.ap()[:, gs * D:(gs + 1) * D].rearrange("p (b e) -> p b e", e=32)
                S.op("vector", lambda E, v=hg, o=32 * gi: E.tensor_reduce(
                    out=qa_amax[:, o:o + 32], in_=v, axis=AXX, op=ALU.max,
                    apply_absolute_value=True), inc="VE")
            S.wait_at("vector", "VE", S.c("VE"))
            quant_consts(64, qa_amax[:, 0:64], qa_e[:, 0:64], qa_s0[:, 0:64],
                         qa_s[:, 0:64], qa_m[:, 0:64], qa_eb[:, 0:64])
            for gi, g in enumerate((g0, g1)):
                gs = g % 2
                hg = hbuf.ap()[:, gs * D:(gs + 1) * D].rearrange("p (b e) -> p b e", e=32)
                ebg = qa_eb.ap()[:, 32 * gi:32 * gi + 32].unsqueeze(2).broadcast_to([128, 32, 32])
                qg = hqb.ap().rearrange("p (b e) -> p b e", e=32)
                S.wait_at("vector", "PE", S.c("PE"))
                quant_apply(hg, hg, qg, ebg)
                S.wait_at("tensor", "VE", S.c("VE"))
                transpose8(lambda j: hqb[:, 128 * j:128 * j + 128],
                           lambda j0, jn, s_=gs: hqT.ap().rearrange(
                               "p (j c) -> p j c", c=128)[:, 8 * s_ + j0:8 * s_ + j0 + jn, :],
                           8, evac_eng="scalar", banks=[psE, psD])
                for half in range(2):
                    cl = half * 512
                    for cj in range(8):
                        c = 8 * g + cj
                        S.op("tensor", lambda E, c_=c, cc=cl, cjj=cj, s_=gs: E.matmul(
                            psC[:, cc:cc + 512],
                            hqT[:, 1024 * s_ + 128 * cjj:1024 * s_ + 128 * cjj + 128],
                            wpjTq[:, c_ * D + cc: c_ * D + cc + 512],
                            start=(c_ == 0), stop=False))
                    if g == 3:
                        S.op("tensor", lambda E, cc=cl: E.matmul(
                            psC[:, cc:cc + 512], ones_row[0:1, :], bpj_r[0:1, cc:cc + 512],
                            start=False, stop=True),
                            inc="PE" if half == 1 else None)
            if pair == 0:
                S.wait_at("tensor", "AE", S.c("AE"))
                p2_fc1(t, 2)

        def p2_back(t, do_front):
            p2_pair(t, 0, do_front)
            p2_pair(t, 1, False)
            w("vector", "PE")
            oslot = t % 2
            buf = x2r0 if t % 2 == 0 else x2r1
            S.wait_at("vector", "ST0" if t % 2 == 0 else "ST1",
                      S.c("ST0" if t % 2 == 0 else "ST1"))
            S.op("vector", lambda E, b=buf, o=oslot: E.tensor_tensor(
                out=outb[:, o * D:(o + 1) * D], in0=psC[:, :], in1=b[:, :], op=ALU.add), inc="VE")
            S.wait_at("sync", "VE", S.c("VE"))
            st_sem = "ST0" if t % 2 == 0 else "ST1"
            S.op("sync", lambda E, t_=t, o=oslot: E.dma_start(
                out=y_out.rearrange("(t p) f -> p t f", p=128)[:, t_, :],
                in_=outb[:, o * D:(o + 1) * D]), inc=st_sem, amt=16)

        p2_front(0)
        for t in range(NT):
            p2_fc1(t, 0)
            p2_back(t, t + 1 < NT)

        S.wait_at("sync", "ST0", 16 * (NT // 2))
        S.wait_at("sync", "ST1", 16 * (NT // 2))

        es_ph2.close()
        def runner(eng_name):
            def run(E):
                for item in S.q[eng_name]:
                    if item[0] == "w":
                        E.wait_ge(sems[item[1]], item[2])
                    else:
                        ins = item[1](E)
                        if item[2] is not None:
                            ins.then_inc(sems[item[2]], item[3])
            return run

        block.sync(runner("sync"))
        block.gpsimd(runner("gpsimd"))
        block.tensor(runner("tensor"))
        block.vector(runner("vector"))
        block.scalar(runner("scalar"))

    return nc


_NC_CACHE = None


def _get_nc():
    global _NC_CACHE
    if _NC_CACHE is None:
        _NC_CACHE = build_nc()
    return _NC_CACHE


def kernel(L0_attn_x, L0_attn_Wq, L0_attn_bq, L0_attn_Wk, L0_attn_bk, L0_attn_Wv,
           L0_attn_bv, L0_attn_Wo, L0_attn_bo, L0_ffn_w_fc, L0_ffn_b_fc,
           L0_ffn_w_proj, L0_ffn_b_proj):
    x = np.ascontiguousarray(np.asarray(L0_attn_x, dtype=np.float32))
    B = x.shape[0]
    wq = np.ascontiguousarray(np.asarray(L0_attn_Wq, dtype=np.float32))
    wk = np.ascontiguousarray(np.asarray(L0_attn_Wk, dtype=np.float32))
    wv = np.ascontiguousarray(np.asarray(L0_attn_Wv, dtype=np.float32))
    wo = np.ascontiguousarray(np.asarray(L0_attn_Wo, dtype=np.float32))
    bq = np.asarray(L0_attn_bq, dtype=np.float32)
    bk = np.asarray(L0_attn_bk, dtype=np.float32)
    bv = np.asarray(L0_attn_bv, dtype=np.float32).reshape(1, D)
    bo = np.asarray(L0_attn_bo, dtype=np.float32).reshape(1, D)
    wfc = np.ascontiguousarray(np.asarray(L0_ffn_w_fc, dtype=np.float32))
    bfc = np.asarray(L0_ffn_b_fc, dtype=np.float32).reshape(1, HID)
    wpj = np.ascontiguousarray(np.asarray(L0_ffn_w_proj, dtype=np.float32))
    bpj = np.asarray(L0_ffn_b_proj, dtype=np.float32).reshape(1, D)

    bqr = np.ascontiguousarray(bq.reshape(8, 128).T)
    bkr = np.ascontiguousarray(bk.reshape(8, 128).T)

    in_maps = []
    for c in range(8):
        b = c // 4
        J = c % 4
        pos = 2048 * J + np.arange(2048)[:, None]
        ends = 64 * np.arange(128)[None, :]
        mask01 = (ends <= pos).astype(ml_dtypes.bfloat16)
        in_maps.append({
            "xseq": x[b],
            "xown": np.ascontiguousarray(x[b, 2048 * J:2048 * (J + 1)]),
            "wq_in": wq, "wk_in": wk, "wv_in": wv, "wo_in": wo,
            "bqr_in": bqr, "bkr_in": bkr,
            "bv_in": bv, "bo_in": bo, "bfc_in": bfc, "bpj_in": bpj,
            "wfc_in": wfc, "wpj_in": wpj,
            "mask_in": mask01,
        })
    nc = _get_nc()
    res = run_bass_kernel_spmd(nc, in_maps, core_ids=list(range(8)))
    out = np.empty((B, T_SEQ, D), dtype=np.float32)
    for c in range(8):
        b, J = c // 4, c % 4
        out[b, 2048 * J:2048 * (J + 1)] = res.results[c]["y"]
    return out


# version 06
# speedup vs baseline: 1.2744x; 1.0993x over previous
import sys
import contextlib

import numpy as np
import ml_dtypes

sys.path.insert(0, "/opt/trn_rl_repo")
import concourse.bass as bass
import concourse.mybir as mybir
from concourse.bass_utils import run_bass_kernel_spmd

F32 = mybir.dt.float32
BF16 = mybir.dt.bfloat16
ALU = mybir.AluOpType
ACT = mybir.ActivationFunctionType
AXX = mybir.AxisListType.X

LN2 = float(np.log(2.0))
K1 = float(3 * 2**22)
K2 = float(3 * 2**51)
EPS = 1e-5

T_SEQ = 8192
D = 1024
HID = 4096
TOK = 2048
NT = 16
NSUP = 4
NSEQT = 64
NC = 128

ENGS = ("sync", "gpsimd", "tensor", "vector", "scalar")


class Sched:

    def __init__(self):
        self.q = {e: [] for e in ENGS}
        self.cnt = {}

    def wait(self, eng, sem_name):
        v = self.cnt.get(sem_name, 0)
        if v == 0:
            return
        self.q[eng].append(("w", sem_name, v))

    def wait_at(self, eng, sem_name, v):
        if v > 0:
            self.q[eng].append(("w", sem_name, v))

    def op(self, eng, fn, inc=None, amt=1):
        self.q[eng].append(("o", fn, inc, amt))
        if inc is not None:
            self.cnt[inc] = self.cnt.get(inc, 0) + amt

    def c(self, sem_name):
        return self.cnt.get(sem_name, 0)


def build_nc():
    nc = bass.Bass("TRN2", target_bir_lowering=False)

    xseq = nc.dram_tensor("xseq", [T_SEQ, D], F32, kind="ExternalInput").ap()
    xown = nc.dram_tensor("xown", [TOK, D], F32, kind="ExternalInput").ap()
    wq_in = nc.dram_tensor("wq_in", [D, D], F32, kind="ExternalInput").ap()
    wk_in = nc.dram_tensor("wk_in", [D, D], F32, kind="ExternalInput").ap()
    wv_in = nc.dram_tensor("wv_in", [D, D], F32, kind="ExternalInput").ap()
    wo_in = nc.dram_tensor("wo_in", [D, D], F32, kind="ExternalInput").ap()
    bqr_in = nc.dram_tensor("bqr_in", [128, 8], F32, kind="ExternalInput").ap()
    bkr_in = nc.dram_tensor("bkr_in", [128, 8], F32, kind="ExternalInput").ap()
    bv_in = nc.dram_tensor("bv_in", [1, D], F32, kind="ExternalInput").ap()
    bo_in = nc.dram_tensor("bo_in", [1, D], F32, kind="ExternalInput").ap()
    bfc_in = nc.dram_tensor("bfc_in", [1, HID], F32, kind="ExternalInput").ap()
    bpj_in = nc.dram_tensor("bpj_in", [1, D], F32, kind="ExternalInput").ap()
    wfc_in = nc.dram_tensor("wfc_in", [HID, D], F32, kind="ExternalInput").ap()
    wpj_in = nc.dram_tensor("wpj_in", [D, HID], F32, kind="ExternalInput").ap()
    mask_in = nc.dram_tensor("mask_in", [TOK, NC], BF16, kind="ExternalInput").ap()
    y_out = nc.dram_tensor("y", [TOK, D], F32, kind="ExternalOutput").ap()
    xmid_hbm = nc.dram_tensor("xmid_scr", [TOK, D], F32).ap()

    ident_h = nc.inline_tensor(np.eye(128, dtype=ml_dtypes.bfloat16), name="ident_h")
    base3_np = np.zeros((128, 3), dtype=ml_dtypes.bfloat16)
    base3_np[0, 0] = 1.0
    base3_np[1:65, 1] = 1.0
    base3_np[65:128, 2] = 1.0
    base3_h = nc.inline_tensor(base3_np, name="base3_h")
    ilen_np = np.full((128, 1), 1.0 / 64.0, dtype=np.float32)
    ilen_np[0, 0] = 1.0
    ilen_h = nc.inline_tensor(ilen_np, name="ilen_h")

    es = contextlib.ExitStack()
    S = Sched()

    with es:
        sb = lambda name, shape, dt: es.enter_context(nc.sbuf_tensor(name, shape, dt))
        psb = lambda name, shape: es.enter_context(nc.psum_tensor(name, shape, F32))

        ident = sb("ident", [128, 128], BF16)
        base3 = sb("base3", [128, 3], BF16)
        ilen = sb("ilen", [128, 1], F32)
        zeros_b = sb("zeros_b", [128, 128], BF16)
        ones_col = sb("ones_col", [128, 1], BF16)
        ones_row = sb("ones_row", [1, 128], BF16)
        mask_sb = sb("mask_sb", [128, NT * 128], BF16)
        bqr = sb("bqr", [128, 8], F32)
        bkr = sb("bkr", [128, 8], F32)

        Dt = sb("Dt", [128, 1], F32)
        DK2pp = sb("DK2pp", [128, 1], F32)
        EB1pp = sb("EB1pp", [128, 1], F32)
        TINYpp = sb("TINYpp", [128, 1], F32)

        bnscr = sb("bnscr", [128, 12], F32)
        mv_seq = sb("mv_seq", [128, 2 * NSEQT], F32)
        a_seq = sb("a_seq", [128, NSEQT], F32)
        am_seq = sb("am_seq", [128, NSEQT], F32)
        lnv_scr = sb("lnv_scr", [128, NSEQT], F32)
        mv_own = sb("mv_own", [128, 2 * NT], F32)
        a_own = sb("a_own", [128, NT], F32)
        nam_own = sb("nam_own", [128, NT], F32)

        summ_cm = sb("summ_cm", [128, D], BF16)
        summ_fm = sb("summ_fm", [128, 8 * 128], BF16)
        kfm = sb("kfm", [128, 8 * 128], BF16)
        vcm = sb("vcm", [128, D], BF16)
        sum_bias = sb("sum_bias", [128, 1], F32)

        wfcTq = sb("wfcTq", [128, 8 * HID], BF16)
        wpjTq = sb("wpjTq", [128, 32 * D], BF16)

        wbuf = sb("wbuf", [128, 4 * D], BF16)

        qa_amax = sb("qa_amax", [128, 160], F32)
        qa_e = sb("qa_e", [128, 160], F32)
        qa_s0 = sb("qa_s0", [128, 160], F32)
        qa_s = sb("qa_s", [128, 160], F32)
        qa_m = sb("qa_m", [128, 160], F32)
        qa_eb = sb("qa_eb", [128, 160], F32)

        a2c = sb("a2c", [128, 1], F32)
        nam2c = sb("nam2c", [128, 1], F32)
        lnv2 = sb("lnv2", [128, 1], F32)
        bo_r = sb("bo_r", [1, D], BF16)
        patt0 = sb("patt0", [128, 128], BF16)
        patt1 = sb("patt1", [128, 128], BF16)
        meanb = sb("meanb", [128, 1], BF16)
        ssum = sb("ssum", [128, 16], F32)
        srcp = sb("srcp", [128, 16], F32)

        es_pro = contextlib.ExitStack()
        sbp = lambda name, shape, dt: es_pro.enter_context(nc.sbuf_tensor(name, shape, dt))
        bv_r = sbp("bv_r", [1, D], BF16)
        wstr0 = sbp("wstr0", [128, 2048], F32)
        wstr1 = sbp("wstr1", [128, 2048], F32)
        wqnat = sbp("wqnat", [128, 4096], BF16)
        xsq0 = sbp("xsq0", [128, 2 * D], BF16)
        xsq1 = sbp("xsq1", [128, 2 * D], BF16)
        xsq2 = sbp("xsq2", [128, 2 * D], BF16)

        psA = psb("psA", [128, 1024])
        psB = psb("psB", [128, 1024])
        psC = psb("psC", [128, 1024])
        psD = psb("psD", [128, 512])
        psE = psb("psE", [128, 512])

        sem_names = ["LD", "LG", "ST0", "ST1", "PE", "VE", "AE", "GE", "XM",
                     "XS0", "XS1", "WF0", "WF1", "X20", "X21", "XOW", "WBF"]
        sems = {n: es.enter_context(nc.semaphore("s" + n)) for n in sem_names}
        block = es.enter_context(nc.Block())

        def w(eng, *names):
            for n in names:
                S.wait(eng, n)

        def dma(eng, sem_name, out, in_):
            S.op(eng, lambda E, o=out, i=in_: E.dma_start(out=o, in_=i), inc=sem_name, amt=16)

        def quant_consts(nb, amax_ap, e_ap, s0_ap, s_ap, m_ap, eb_ap):
            w("scalar", "VE")
            S.op("scalar", lambda E, a=amax_ap, e=e_ap: E.activation(
                out=e, in_=a, func=ACT.Ln, bias=TINYpp[:, :], scale=1.0), inc="AE")
            w("vector", "AE")
            S.op("vector", lambda E, e=e_ap: E.tensor_scalar(
                out=e, in0=e, scalar1=float(1.0 / LN2), scalar2=float(1e-11 - 0.5),
                op0=ALU.mult, op1=ALU.add), inc="VE")
            S.wait_at("vector", "VE", S.c("VE"))
            S.op("vector", lambda E, e=e_ap: E.tensor_scalar(
                out=e, in0=e, scalar1=DK2pp[:, :], scalar2=EB1pp[:, :],
                op0=ALU.add, op1=ALU.subtract), inc="VE")
            S.wait_at("vector", "VE", S.c("VE"))
            S.op("vector", lambda E, e=e_ap: E.tensor_scalar(
                out=e, in0=e, scalar1=K1, scalar2=3.0,
                op0=ALU.subtract, op1=ALU.subtract), inc="VE")
            w("scalar", "VE")
            S.op("scalar", lambda E, e=e_ap, s0=s0_ap: E.activation(
                out=s0, in_=e, func=ACT.Exp, scale=LN2), inc="AE")
            w("vector", "AE")
            S.op("vector", lambda E, s0=s0_ap, m=m_ap: E.tensor_scalar(
                out=m, in0=s0, scalar1=K1, scalar2=None, op0=ALU.mult), inc="VE")
            S.wait_at("vector", "VE", S.c("VE"))
            S.op("vector", lambda E, s0=s0_ap, s=s_ap, m=m_ap: E.tensor_tensor(
                out=s, in0=s0, in1=m, op=ALU.add), inc="VE")
            S.wait_at("vector", "VE", S.c("VE"))
            S.op("vector", lambda E, s=s_ap, m=m_ap: E.tensor_tensor(
                out=s, in0=s, in1=m, op=ALU.subtract), inc="VE")
            S.wait_at("vector", "VE", S.c("VE"))
            S.op("vector", lambda E, s=s_ap, m=m_ap: E.tensor_scalar(
                out=m, in0=s, scalar1=K1, scalar2=None, op0=ALU.mult), inc="VE")
            S.op("vector", lambda E, s=s_ap, eb=eb_ap: E.tensor_scalar(
                out=eb, in0=s, scalar1=K2, scalar2=Dt[:, :], op0=ALU.mult,
                op1=ALU.mult), inc="VE")
            S.wait_at("vector", "VE", S.c("VE"))
            S.op("vector", lambda E, eb=eb_ap, m=m_ap: E.tensor_tensor(
                out=eb, in0=eb, in1=m, op=ALU.subtract), inc="VE")
            S.wait_at("vector", "VE", S.c("VE"))

        def quant_apply(v_ap, u_ap, q_ap, eb_bc):
            S.op("vector", lambda E: E.tensor_tensor(out=u_ap, in0=v_ap, in1=eb_bc,
                                                     op=ALU.subtract), inc="VE")
            S.wait_at("vector", "VE", S.c("VE"))
            S.op("vector", lambda E: E.tensor_tensor(out=q_ap, in0=u_ap, in1=eb_bc,
                                                     op=ALU.add), inc="VE")
            S.wait_at("vector", "VE", S.c("VE"))

        _evmark = {}

        def transpose8(src_tile_fn, dst0_ap_fn, nchunk, evac_eng="scalar", bank=None,
                       banks=None):
            bks = banks if banks is not None else [bank if bank is not None else psE]
            sem = "VE" if evac_eng == "vector" else "AE"
            for gi, j0 in enumerate(range(0, nchunk, 4)):
                bk = bks[gi % len(bks)]
                mk = _evmark.get(id(bk))
                if mk is not None:
                    S.wait_at("tensor", mk[0], mk[1])
                jn = min(4, nchunk - j0)
                for j in range(j0, j0 + jn):
                    S.op("tensor", lambda E, s=src_tile_fn(j), d=bk[:, 128 * (j - j0):128 * (j - j0) + 128]:
                         E.matmul(d, s, ident[:, :], start=True, stop=True),
                         inc="PE" if j == j0 + jn - 1 else None)
                w(evac_eng, "PE")
                ps3 = bk[:, 0:128 * jn].rearrange("p (j c) -> p j c", c=128)
                dst3 = dst0_ap_fn(j0, jn)
                if evac_eng == "vector":
                    S.op("vector", lambda E, s=ps3, d=dst3: E.tensor_copy(out=d, in_=s), inc="VE")
                else:
                    S.op("scalar", lambda E, s=ps3, d=dst3: E.activation(
                        out=d, in_=s, func=ACT.Identity), inc="AE")
                _evmark[id(bk)] = (sem, S.c(sem))
            S.wait(("tensor"), sem)

        def strided_dst(base_ap_fn):
            def f(j0, jn):
                a0 = base_ap_fn(j0)
                if jn == 1:
                    return a0.unsqueeze(1)
                a1 = base_ap_fn(j0 + 1)
                raise RuntimeError("use explicit dst fns")
            return f

        dma("sync", "LD", ident[:], ident_h[:])
        dma("sync", "LD", base3[:], base3_h[:])
        dma("sync", "LD", ilen[:], ilen_h[:])
        dma("sync", "LD", bqr[:], bqr_in)
        dma("sync", "LD", bkr[:], bkr_in)
        dma("sync", "LD", mask_sb.ap().rearrange("p (t c) -> p t c", c=128),
            mask_in.rearrange("(t p) c -> p t c", p=128))
        dma("gpsimd", "LG", bv_r[:], bv_in)
        dma("gpsimd", "LG", bo_r[:], bo_in)

        S.op("vector", lambda E: E.memset(zeros_b[:], 0.0), inc="VE")
        S.op("vector", lambda E: E.memset(ones_col[:], 1.0), inc="VE")
        S.op("vector", lambda E: E.memset(ones_row[:], 1.0), inc="VE")
        S.op("vector", lambda E: E.memset(Dt[:], 1.0), inc="VE")
        S.op("vector", lambda E: E.memset(TINYpp[:], float(2.0**-100)), inc="VE")
        S.wait_at("vector", "VE", S.c("VE"))
        S.op("vector", lambda E: E.tensor_scalar(out=Dt[:], in0=Dt[:], scalar1=K2,
                                                 scalar2=-K2, op0=ALU.add, op1=ALU.add), inc="VE")
        S.wait_at("vector", "VE", S.c("VE"))
        S.op("vector", lambda E: E.tensor_scalar(out=DK2pp[:], in0=Dt[:], scalar1=K2,
                                                 scalar2=None, op0=ALU.mult), inc="VE")
        S.wait_at("vector", "VE", S.c("VE"))
        S.op("vector", lambda E: E.tensor_scalar(out=EB1pp[:], in0=DK2pp[:], scalar1=K1,
                                                 scalar2=None, op0=ALU.subtract), inc="VE")
        S.wait_at("vector", "VE", S.c("VE"))

        w("tensor", "LD", "VE")
        S.op("tensor", lambda E: E.matmul(psA[:, 0:512], zeros_b[:], mask_sb[:, 0:512],
                                          start=True, stop=False))
        S.op("tensor", lambda E: E.matmul(psA[:, 512:1024], zeros_b[:], mask_sb[:, 512:1024],
                                          start=True, stop=False))
        S.op("tensor", lambda E: E.matmul(psD[:, 0:1], zeros_b[:], ones_col[:],
                                          start=True, stop=False), inc="PE")

        wfc_v = wfc_in.rearrange("(n p) f -> p n f", p=128)
        wpj_v = wpj_in.rearrange("(n p) f -> p n f", p=128)
        jobs = [("fc", n) for n in range(32)] + [("pj", q) for q in range(16)]

        def job_dma(i):
            kind, n = jobs[i]
            par = i % 2
            buf = wstr0 if par == 0 else wstr1
            wf_sem = "WF0" if par == 0 else "WF1"
            if kind == "fc":
                dma("sync", wf_sem, buf[:, 0:D], wfc_v[:, n, :])
            else:
                dma("sync", wf_sem, buf[:, :], wpj_v[:, n // 2, 2048 * (n % 2):2048 * (n % 2) + 2048])
            return S.c(wf_sem)

        def job_nb(i):
            return 32 if jobs[i][0] == "fc" else 64

        wf_totals = {}
        def stage_R(i):
            par, nb = i % 2, job_nb(i)
            buf = wstr0 if par == 0 else wstr1
            o = 64 * par
            S.wait_at("vector", "WF0" if par == 0 else "WF1", wf_totals[i])
            v3 = buf.ap()[:, 0:32 * nb].rearrange("p (b e) -> p b e", e=32)
            S.op("vector", lambda E, v=v3, o_=o, nb_=nb: E.tensor_reduce(
                out=qa_amax[:, o_:o_ + nb_], in_=v, axis=AXX, op=ALU.max,
                apply_absolute_value=True), inc="VE")

        def stage_L(i):
            par, nb = i % 2, job_nb(i)
            o = 64 * par
            w("scalar", "VE")
            S.op("scalar", lambda E, o_=o, nb_=nb: E.activation(
                out=qa_e[:, o_:o_ + nb_], in_=qa_amax[:, o_:o_ + nb_], func=ACT.Ln,
                bias=TINYpp[:, :], scale=1.0), inc="AE")

        def stage_E1(i):
            par, nb = i % 2, job_nb(i)
            o = 64 * par
            w("vector", "AE")
            S.op("vector", lambda E, o_=o, nb_=nb: E.tensor_scalar(
                out=qa_e[:, o_:o_ + nb_], in0=qa_e[:, o_:o_ + nb_],
                scalar1=float(1.0 / LN2), scalar2=float(1e-11 - 0.5),
                op0=ALU.mult, op1=ALU.add), inc="VE")
            S.wait_at("vector", "VE", S.c("VE"))
            S.op("vector", lambda E, o_=o, nb_=nb: E.tensor_scalar(
                out=qa_e[:, o_:o_ + nb_], in0=qa_e[:, o_:o_ + nb_],
                scalar1=DK2pp[:, :], scalar2=EB1pp[:, :], op0=ALU.add, op1=ALU.subtract), inc="VE")
            S.wait_at("vector", "VE", S.c("VE"))
            S.op("vector", lambda E, o_=o, nb_=nb: E.tensor_scalar(
                out=qa_e[:, o_:o_ + nb_], in0=qa_e[:, o_:o_ + nb_],
                scalar1=K1, scalar2=3.0, op0=ALU.subtract, op1=ALU.subtract), inc="VE")

        def stage_X(i):
            par, nb = i % 2, job_nb(i)
            o = 64 * par
            w("scalar", "VE")
            S.op("scalar", lambda E, o_=o, nb_=nb: E.activation(
                out=qa_s0[:, o_:o_ + nb_], in_=qa_e[:, o_:o_ + nb_], func=ACT.Exp,
                scale=LN2), inc="AE")

        def stage_S(i):
            par, nb = i % 2, job_nb(i)
            o = 64 * par
            w("vector", "AE")
            for fn in (
                lambda E, o_=o, nb_=nb: E.tensor_scalar(
                    out=qa_m[:, o_:o_ + nb_], in0=qa_s0[:, o_:o_ + nb_], scalar1=K1,
                    scalar2=None, op0=ALU.mult),
                lambda E, o_=o, nb_=nb: E.tensor_tensor(
                    out=qa_s[:, o_:o_ + nb_], in0=qa_s0[:, o_:o_ + nb_],
                    in1=qa_m[:, o_:o_ + nb_], op=ALU.add),
                lambda E, o_=o, nb_=nb: E.tensor_tensor(
                    out=qa_s[:, o_:o_ + nb_], in0=qa_s[:, o_:o_ + nb_],
                    in1=qa_m[:, o_:o_ + nb_], op=ALU.subtract),
                lambda E, o_=o, nb_=nb: E.tensor_scalar(
                    out=qa_m[:, o_:o_ + nb_], in0=qa_s[:, o_:o_ + nb_], scalar1=K1,
                    scalar2=None, op0=ALU.mult),
                lambda E, o_=o, nb_=nb: E.tensor_scalar(
                    out=qa_eb[:, o_:o_ + nb_], in0=qa_s[:, o_:o_ + nb_], scalar1=K2,
                    scalar2=Dt[:, :], op0=ALU.mult, op1=ALU.mult),
                lambda E, o_=o, nb_=nb: E.tensor_tensor(
                    out=qa_eb[:, o_:o_ + nb_], in0=qa_eb[:, o_:o_ + nb_],
                    in1=qa_m[:, o_:o_ + nb_], op=ALU.subtract),
            ):
                S.op("vector", fn, inc="VE")
                S.wait_at("vector", "VE", S.c("VE"))

        def stage_Q(i):
            par, nb = i % 2, job_nb(i)
            buf = wstr0 if par == 0 else wstr1
            o = 64 * par
            v3 = buf.ap()[:, 0:32 * nb].rearrange("p (b e) -> p b e", e=32)
            q3 = wqnat.ap()[:, 2048 * par:2048 * par + 32 * nb].rearrange("p (b e) -> p b e", e=32)
            ebb = qa_eb.ap()[:, o:o + nb].unsqueeze(2).broadcast_to([128, nb, 32])
            S.wait_at("vector", "PE", S.c("PE"))
            S.op("vector", lambda E, a=v3, b=ebb: E.tensor_tensor(
                out=a, in0=a, in1=b, op=ALU.subtract), inc="VE")
            S.wait_at("vector", "VE", S.c("VE"))
            S.op("vector", lambda E, a=v3, b=ebb, q=q3: E.tensor_tensor(
                out=q, in0=a, in1=b, op=ALU.add), inc="VE")
            S.wait_at("tensor", "VE", S.c("VE"))

        def stage_T(i):
            kind, n = jobs[i]
            par = i % 2
            base = 2048 * par
            if kind == "fc":
                transpose8(lambda j, b=base: wqnat[:, b + 128 * j:b + 128 * j + 128],
                           lambda j0, jn, h_=n: wfcTq.ap().rearrange(
                               "p (j r) -> p j r", r=HID)[:, j0:j0 + jn,
                                                          128 * h_:128 * h_ + 128],
                           8, evac_eng="scalar", banks=[psE, psC])
            else:
                h_, hf = n // 2, n % 2
                transpose8(lambda j, b=base: wqnat[:, b + 128 * j:b + 128 * j + 128],
                           lambda j0, jn, hh=h_, f_=hf: wpjTq.ap().rearrange(
                               "p (j r) -> p j r", r=D)[:, 16 * f_ + j0:16 * f_ + j0 + jn,
                                                        128 * hh:128 * hh + 128],
                           16, evac_eng="scalar", banks=[psE, psC])

        NJOB = len(jobs)
        wf_totals[0] = job_dma(0)
        wf_totals[1] = job_dma(1)
        _wstate = {"i": 0}

        def w_step():
            i = _wstate["i"]
            if i > NJOB:
                return False
            _wstate["i"] = i + 1
            if i < NJOB:
                stage_R(i)
                stage_L(i)
            if i >= 1:
                stage_Q(i - 1)
                if i + 1 < NJOB:
                    S.wait_at("sync", "VE", S.c("VE"))
                    wf_totals[i + 1] = job_dma(i + 1)
            if i < NJOB:
                stage_E1(i)
                stage_X(i)
            if i >= 1:
                S.wait_at("tensor", "VE", S.c("VE"))
                stage_T(i - 1)
            if i < NJOB:
                stage_S(i)
            return True

        xs_view = xseq.rearrange("(c p) f -> p c f", p=128)
        NCH = 32
        xsqs = [xsq0, xsq1, xsq2]

        def xs_dma(ch):
            buf = xsqs[ch % 3]
            xs_sem = ["XS0", "XS1", "GE"][ch % 3]
            S.op("gpsimd",
                 lambda E, b=buf, c0=ch * 2: E.dma_start(
                     out=b.ap().rearrange("p (t f) -> p t f", f=D),
                     in_=xs_view[:, c0:c0 + 2, :]),
                 inc=xs_sem, amt=16)
            return xs_sem, S.c(xs_sem)

        def xs_stats(ch, sem_tot):
            buf = xsqs[ch % 3]
            xs_sem, tot = sem_tot
            S.wait_at("vector", xs_sem, tot)
            for tt in range(2):
                t = ch * 2 + tt
                S.op("vector", lambda E, b=buf, o=tt: E.bn_stats(
                    out=bnscr[:, 0:6], in_=b[:, o * D:o * D + 512]), inc="VE")
                S.op("vector", lambda E, b=buf, o=tt: E.bn_stats(
                    out=bnscr[:, 6:12], in_=b[:, o * D + 512:o * D + 1024]), inc="VE")
                S.wait_at("vector", "VE", S.c("VE"))
                S.op("vector", lambda E, t_=t: E.bn_aggr(
                    out=mv_seq[:, 2 * t_:2 * t_ + 2], in_=bnscr[:, :]), inc="VE")
                S.wait_at("vector", "VE", S.c("VE"))
            mvs = mv_seq.ap().rearrange("p (t two) -> p t two", two=2)
            var_sl = mvs[:, ch * 2:(ch + 1) * 2, 1:2]
            mean_sl = mvs[:, ch * 2:(ch + 1) * 2, 0:1]
            lv = lnv_scr[:, ch * 2:(ch + 1) * 2]
            lv3 = lnv_scr.ap().rearrange("p (t one) -> p t one", one=1)[:, ch * 2:(ch + 1) * 2, :]
            S.op("vector", lambda E, v=var_sl, o=lv3: E.tensor_scalar(
                out=o, in0=v, scalar1=EPS, scalar2=None, op0=ALU.add), inc="VE")
            w("scalar", "VE")
            a_sl = a_seq[:, ch * 2:(ch + 1) * 2]
            S.op("scalar", lambda E, i=lv, o=a_sl: E.activation(
                out=o, in_=i, func=ACT.Ln), inc="AE")
            S.wait_at("scalar", "AE", S.c("AE"))
            S.op("scalar", lambda E, o=a_sl: E.activation(
                out=o, in_=o, func=ACT.Exp, scale=-0.5), inc="AE")
            w("vector", "AE")
            am_sl3 = am_seq.ap().rearrange("p (t one) -> p t one", one=1)[:, ch * 2:(ch + 1) * 2, :]
            a_sl3 = a_seq.ap().rearrange("p (t one) -> p t one", one=1)[:, ch * 2:(ch + 1) * 2, :]
            S.op("vector", lambda E, a3=a_sl3, m3=mean_sl, o=am_sl3: E.tensor_tensor(
                out=o, in0=a3, in1=m3, op=ALU.mult), inc="VE")

        def xs_windows(ch):
            buf = xsqs[ch % 3]
            for tt in range(2):
                t = ch * 2 + tt
                ncols = 2 if t == 63 else 3
                pt = patt0 if t % 2 == 0 else patt1
                S.wait_at("vector", "PE", S.c("PE"))
                S.op("vector", lambda E, p=pt: E.memset(p[:, :], 0.0), inc="VE")
                S.wait_at("vector", "VE", S.c("VE"))
                S.op("vector", lambda E, t_=t, n=ncols, p=pt: E.tensor_scalar(
                    out=p[:, 2 * t_:2 * t_ + n],
                    in0=base3[:, 0:n], scalar1=a_seq[:, t_:t_ + 1],
                    scalar2=None, op0=ALU.mult), inc="VE")
                S.op("vector", lambda E, t_=t: E.tensor_copy(
                    out=meanb[:, :], in_=mv_seq[:, 2 * t_:2 * t_ + 1]), inc="VE")
                S.wait_at("tensor", "VE", S.c("VE"))
                S.op("tensor", lambda E, b=buf, o=tt, p=pt: E.matmul(
                    psA[:, 0:512], p[:, :], b[:, o * D:o * D + 512],
                    start=False, stop=False))
                S.op("tensor", lambda E, b=buf, o=tt, p=pt: E.matmul(
                    psA[:, 512:1024], p[:, :], b[:, o * D + 512:o * D + 1024],
                    start=False, stop=False))
                S.op("tensor", lambda E, p=pt: E.matmul(
                    psD[:, 0:1], p[:, :], meanb[:, :],
                    start=False, stop=False), inc="PE")
                S.wait_at("vector", "PE", S.c("PE"))

        sems_ch = {}
        sems_ch[0] = xs_dma(0)
        sems_ch[1] = xs_dma(1)
        for ch in range(NCH + 1):
            if ch < NCH:
                xs_stats(ch, sems_ch[ch])
            if ch >= 1:
                xs_windows(ch - 1)
                if ch + 1 < NCH:
                    S.wait_at("gpsimd", "PE", S.c("PE"))
                    sems_ch[ch + 1] = xs_dma(ch + 1)
            w_step()
            if ch % 2 == 0:
                w_step()
        while w_step():
            pass

        S.op("tensor", lambda E: E.matmul(psA[:, 0:512], zeros_b[:], mask_sb[:, 0:512],
                                          start=False, stop=True))
        S.op("tensor", lambda E: E.matmul(psA[:, 512:1024], zeros_b[:], mask_sb[:, 512:1024],
                                          start=False, stop=True))
        S.op("tensor", lambda E: E.matmul(psD[:, 0:1], zeros_b[:], ones_col[:, :],
                                          start=False, stop=True), inc="PE")
        pe_summary_done = S.c("PE")

        S.wait_at("vector", "PE", pe_summary_done)
        S.op("vector", lambda E: E.tensor_copy(out=sum_bias[:], in_=psD[:, 0:1]), inc="VE")
        S.wait_at("vector", "VE", S.c("VE"))
        S.op("vector", lambda E: E.tensor_scalar(
            out=sum_bias[:], in0=sum_bias[:], scalar1=ilen[:, :], scalar2=-1.0,
            op0=ALU.mult, op1=ALU.mult), inc="VE")
        w("scalar", "VE")
        S.wait_at("scalar", "PE", pe_summary_done)
        S.op("scalar", lambda E: E.activation(
            out=summ_cm[:, :], in_=psA[:, :], func=ACT.Identity,
            bias=sum_bias[:, :], scale=ilen[:, :]), inc="AE")
        ae_summ = S.c("AE")
        S.wait_at("tensor", "AE", ae_summ)
        transpose8(lambda j: summ_cm[:, 128 * j:128 * j + 128],
                   lambda j0, jn: summ_fm.ap().rearrange(
                       "p (j c) -> p j c", c=128)[:, j0:j0 + jn, :], 8,
                   evac_eng="scalar", banks=[psE, psC])

        kdst = [psA[:, 0:128], psA[:, 512:640], psB[:, 0:128], psB[:, 512:640],
                psC[:, 0:128], psC[:, 512:640], psD[:, 0:128], psE[:, 0:128]]
        wk_v = wk_in.rearrange("(k p) f -> p k f", p=128)
        for hf in range(2):
            S.wait_at("gpsimd", "PE", S.c("PE"))
            S.op("gpsimd", lambda E, h=hf: E.dma_start(
                out=wbuf.ap().rearrange("p (k f) -> p k f", f=D),
                in_=wk_v[:, 4 * h:4 * h + 4, :]), inc="WBF", amt=16)
            S.wait_at("tensor", "WBF", S.c("WBF"))
            for m in range(8):
                for kl in range(4):
                    k = 4 * hf + kl
                    S.op("tensor", lambda E, m_=m, k_=k, kk=kl: E.matmul(
                        kdst[m_],
                        wbuf[:, kk * D + m_ * 128: kk * D + m_ * 128 + 128],
                        summ_fm[:, k_ * 128:k_ * 128 + 128],
                        start=(k_ == 0), stop=(k_ == 7)),
                        inc="PE" if (m == 7 and kl == 3) else None)
        w("scalar", "PE")
        for m in range(8):
            S.op("scalar", lambda E, m_=m: E.activation(
                out=kfm[:, m_ * 128:m_ * 128 + 128], in_=kdst[m_],
                func=ACT.Identity, bias=bkr[:, m_:m_ + 1]),
                inc="AE" if m == 7 else None)
        S.wait_at("tensor", "AE", S.c("AE"))

        wv_v = wv_in.rearrange("(k p) f -> p k f", p=128)
        for hf in range(2):
            S.wait_at("gpsimd", "PE", S.c("PE"))
            S.op("gpsimd", lambda E, h=hf: E.dma_start(
                out=wbuf.ap().rearrange("p (k f) -> p k f", f=D),
                in_=wv_v[:, 4 * h:4 * h + 4, :]), inc="WBF", amt=16)
            S.wait_at("tensor", "WBF", S.c("WBF"))
            for cl in (0, 512):
                vdst = psB[:, 0:512] if cl == 0 else psD[:, 0:512]
                for kl in range(4):
                    k = 4 * hf + kl
                    S.op("tensor", lambda E, k_=k, kk=kl, c=cl, dd=vdst: E.matmul(
                        dd, summ_fm[:, k_ * 128:k_ * 128 + 128],
                        wbuf[:, kk * D + c: kk * D + c + 512],
                        start=(k_ == 0), stop=False),
                        inc="PE" if (cl == 512 and kl == 3) else None)
            if hf == 1:
                S.wait_at("tensor", "LG", S.c("LG"))
                S.op("tensor", lambda E: E.matmul(
                    psB[:, 0:512], ones_row[0:1, :], bv_r[0:1, 0:512],
                    start=False, stop=True))
                S.op("tensor", lambda E: E.matmul(
                    psD[:, 0:512], ones_row[0:1, :], bv_r[0:1, 512:1024],
                    start=False, stop=True), inc="PE")
        w("scalar", "PE")
        S.op("scalar", lambda E: E.activation(out=vcm[:, 0:512], in_=psB[:, 0:512],
                                              func=ACT.Identity), inc="AE")
        S.op("scalar", lambda E: E.activation(out=vcm[:, 512:1024], in_=psD[:, 0:512],
                                              func=ACT.Identity), inc="AE")

        es_pro.close()
        es_ph1 = contextlib.ExitStack()
        sb1 = lambda name, shape, dt: es_ph1.enter_context(nc.sbuf_tensor(name, shape, dt))
        xring = sb1("xring", [128, 4 * D], F32)
        lnbuf = sb1("lnbuf", [128, D], BF16)
        ln1fm = sb1("ln1fm", [128, 8 * 512], BF16)
        qfm = sb1("qfm", [128, 8 * 512], BF16)
        ebuf = sb1("ebuf", [128, 2 * 16 * 128], BF16)
        xmid_t = sb1("xmid_t", [128, D], F32)
        ofm = ln1fm

        for e in ("sync", "gpsimd", "tensor", "vector", "scalar"):
            S.wait_at(e, "VE", S.c("VE"))
            S.wait_at(e, "AE", S.c("AE"))
            S.wait_at(e, "PE", S.c("PE"))

        st_count = 0

        def p1_loads_stats(s):
            S.wait_at("sync", "VE", S.c("VE"))
            for tt in range(4):
                t = 4 * s + tt
                slot = t % 4
                dma("sync", "XOW", xring[:, slot * D:(slot + 1) * D],
                    xown.rearrange("(t p) f -> p t f", p=128)[:, t, :])
            ld_xown = S.c("XOW")
            S.wait_at("vector", "XOW", ld_xown)
            for tt in range(4):
                t = 4 * s + tt
                slot = t % 4
                xo = xring[:, slot * D:(slot + 1) * D]
                S.op("vector", lambda E, x=xo: E.bn_stats(out=bnscr[:, 0:6], in_=x[:, 0:512]), inc="VE")
                S.op("vector", lambda E, x=xo: E.bn_stats(out=bnscr[:, 6:12], in_=x[:, 512:1024]), inc="VE")
                S.wait_at("vector", "VE", S.c("VE"))
                S.op("vector", lambda E, t_=t: E.bn_aggr(
                    out=mv_own[:, 2 * t_:2 * t_ + 2], in_=bnscr[:, :]), inc="VE")
                S.wait_at("vector", "VE", S.c("VE"))
            mvo = mv_own.ap().rearrange("p (t two) -> p t two", two=2)
            var_sl = mvo[:, 4 * s:4 * s + 4, 1:2]
            mean_sl = mvo[:, 4 * s:4 * s + 4, 0:1]
            lv3 = lnv_scr.ap().rearrange("p (t one) -> p t one", one=1)[:, 0:4, :]
            S.op("vector", lambda E, v=var_sl, o=lv3: E.tensor_scalar(
                out=o, in0=v, scalar1=EPS, scalar2=None, op0=ALU.add), inc="VE")
            w("scalar", "VE")
            a_sl = a_own[:, 4 * s:4 * s + 4]
            S.op("scalar", lambda E, o=a_sl: E.activation(
                out=o, in_=lnv_scr[:, 0:4], func=ACT.Ln), inc="AE")
            S.wait_at("scalar", "AE", S.c("AE"))
            S.op("scalar", lambda E, o=a_sl: E.activation(
                out=o, in_=o, func=ACT.Exp, scale=-0.5), inc="AE")
            w("vector", "AE")
            a3 = a_own.ap().rearrange("p (t one) -> p t one", one=1)[:, 4 * s:4 * s + 4, :]
            n3 = nam_own.ap().rearrange("p (t one) -> p t one", one=1)[:, 4 * s:4 * s + 4, :]
            S.op("vector", lambda E, a=a3, m=mean_sl, o=n3: E.tensor_tensor(
                out=o, in0=a, in1=m, op=ALU.mult), inc="VE")
            S.wait_at("vector", "VE", S.c("VE"))
            S.op("vector", lambda E, o=n3: E.tensor_scalar(
                out=nam_own[:, 4 * s:4 * s + 4], in0=nam_own[:, 4 * s:4 * s + 4],
                scalar1=-1.0, scalar2=None, op0=ALU.mult), inc="VE")

        def p1_lnq(s):
            S.wait_at("tensor", "VE", S.c("VE"))
            for tt in range(4):
                t = 4 * s + tt
                slot = t % 4
                xo = xring[:, slot * D:(slot + 1) * D]
                w("scalar", "VE")
                S.wait_at("scalar", "PE", S.c("PE"))
                S.op("scalar", lambda E, x=xo, t_=t: E.activation(
                    out=lnbuf[:, 0:D], in_=x, func=ACT.Identity,
                    scale=a_own[:, t_:t_ + 1], bias=nam_own[:, t_:t_ + 1]), inc="AE")
                S.wait_at("tensor", "AE", S.c("AE"))
                transpose8(lambda j: lnbuf[:, 128 * j:128 * j + 128],
                           lambda j0, jn, u=tt: ln1fm.ap().rearrange(
                               "p (j q) -> p j q", q=512)[:, j0:j0 + jn, 128 * u:128 * u + 128],
                           8, evac_eng="scalar", banks=[psE, psD])
            qdst = [psA[:, 0:512], psA[:, 512:1024], psB[:, 0:512], psB[:, 512:1024],
                    psC[:, 0:512], psC[:, 512:1024], psD[:, 0:512], psE[:, 0:512]]
            wq_v = wq_in.rearrange("(k p) f -> p k f", p=128)
            for hf in range(2):
                S.wait_at("gpsimd", "PE", S.c("PE"))
                S.op("gpsimd", lambda E, h=hf: E.dma_start(
                    out=wbuf.ap().rearrange("p (k f) -> p k f", f=D),
                    in_=wq_v[:, 4 * h:4 * h + 4, :]), inc="WBF", amt=16)
                S.wait_at("tensor", "WBF", S.c("WBF"))
                for m in range(8):
                    for kl in range(4):
                        k = 4 * hf + kl
                        S.op("tensor", lambda E, m_=m, k_=k, kk=kl: E.matmul(
                            qdst[m_], wbuf[:, kk * D + m_ * 128: kk * D + m_ * 128 + 128],
                            ln1fm[:, k_ * 512:(k_ + 1) * 512],
                            start=(k_ == 0), stop=(k_ == 7)),
                            inc="PE" if (m == 7 and kl == 3) else None)
            w("scalar", "PE")
            for m in range(8):
                S.op("scalar", lambda E, m_=m: E.activation(
                    out=qfm[:, m_ * 512:(m_ + 1) * 512], in_=qdst[m_],
                    func=ACT.Identity, bias=bqr[:, m_:m_ + 1]),
                    inc="AE" if m == 7 else None)
            S.wait_at("tensor", "AE", S.c("AE"))
            _evmark[id(psD)] = ("AE", S.c("AE"))
            _evmark[id(psE)] = ("AE", S.c("AE"))
            S.wait_at("gpsimd", "PE", S.c("PE"))
            wo_v = wo_in.rearrange("(k p) f -> p k f", p=128)
            S.op("gpsimd", lambda E: E.dma_start(
                out=wbuf.ap().rearrange("p (k f) -> p k f", f=D),
                in_=wo_v[:, 0:4, :]), inc="WBF", amt=16)

        def p1_scores(s, u, epar):
            t = 4 * s + u
            S.wait_at("tensor", "AE", S.c("AE"))
            for m in range(8):
                qsl_e = qfm[0:64, m * 512 + 128 * u: m * 512 + 128 * u + 128]
                qsl_o = qfm[64:128, m * 512 + 128 * u: m * 512 + 128 * u + 128]
                ksl_e = kfm[0:64, m * 128:(m + 1) * 128]
                ksl_o = kfm[64:128, m * 128:(m + 1) * 128]
                dstA = psA[:, 128 * (m % 4):128 * (m % 4) + 128] if m < 4 else \
                    psB[:, 128 * (m % 4):128 * (m % 4) + 128]
                dstB = psA[:, 512 + 128 * (m % 4):512 + 128 * (m % 4) + 128] if m < 4 else \
                    psB[:, 512 + 128 * (m % 4):512 + 128 * (m % 4) + 128]
                S.op("tensor", lambda E, q=qsl_e, kk=ksl_e, d=dstA: E.matmul(
                    d, q, kk, start=True, stop=True, tile_position=(0, 0)))
                S.op("tensor", lambda E, q=qsl_o, kk=ksl_o, d=dstB: E.matmul(
                    d, q, kk, start=True, stop=True, tile_position=(64, 0)),
                    inc="PE" if m == 7 else None)
            w("scalar", "PE")
            S.wait_at("scalar", "PE", S.c("PE"))
            eb = 2048 * epar
            S.op("scalar", lambda E, o=eb: E.activation(
                out=ebuf[:, o:o + 1024], in_=psA[:, :], func=ACT.Exp, scale=0.125), inc="AE")
            S.op("scalar", lambda E, o=eb: E.activation(
                out=ebuf[:, o + 1024:o + 2048], in_=psB[:, :], func=ACT.Exp, scale=0.125), inc="AE")

        def p1_softmax_av(s, u, epar):
            t = 4 * s + u
            eb = 2048 * epar
            w("vector", "AE")
            e3 = ebuf.ap()[:, eb:eb + 2048].rearrange("p (g c) -> p g c", c=128)
            msk = mask_sb.ap()[:, 128 * t:128 * t + 128].unsqueeze(1).broadcast_to([128, 16, 128])
            S.op("vector", lambda E, e=e3, mk=msk: E.tensor_tensor(
                out=e, in0=e, in1=mk, op=ALU.mult), inc="VE")
            S.wait_at("vector", "VE", S.c("VE"))
            S.op("vector", lambda E, e=e3: E.tensor_reduce(
                out=ssum[:, :], in_=e, axis=AXX, op=ALU.add), inc="VE")
            S.wait_at("vector", "VE", S.c("VE"))
            S.op("vector", lambda E: E.reciprocal(out=srcp[:, :], in_=ssum[:, :]), inc="VE")
            S.wait_at("vector", "VE", S.c("VE"))
            rb = srcp.ap().unsqueeze(2).broadcast_to([128, 16, 128])
            S.op("vector", lambda E, e=e3, r=rb: E.tensor_tensor(
                out=e, in0=e, in1=r, op=ALU.mult), inc="VE")
            S.wait_at("tensor", "VE", S.c("VE"))
            transpose8(lambda j, o=eb: ebuf[:, o + 128 * j:o + 128 * j + 128],
                       lambda j0, jn, o=eb: ebuf.ap()[:, o:o + 2048].rearrange(
                           "p (j c) -> p j c", c=128)[:, j0:j0 + jn, :],
                       16, evac_eng="scalar", banks=[psE, psD])
            for m0 in range(0, 8, 4):
                for mm in range(4):
                    m = m0 + mm
                    g_e = (m // 4) * 8 + (m % 4)
                    g_o = (m // 4) * 8 + 4 + (m % 4)
                    h_e, h_o = 2 * m, 2 * m + 1
                    S.op("tensor", lambda E, g=g_e, h=h_e, c=128 * mm, o=eb: E.matmul(
                        psD[0:64, c:c + 128], vcm[:, 64 * h:64 * h + 64],
                        ebuf[:, o + 128 * g:o + 128 * g + 128], start=True, stop=True,
                        tile_position=(0, 0)))
                    S.op("tensor", lambda E, g=g_o, h=h_o, c=128 * mm, o=eb: E.matmul(
                        psD[64:128, c:c + 128], vcm[:, 64 * h:64 * h + 64],
                        ebuf[:, o + 128 * g:o + 128 * g + 128], start=True, stop=True,
                        tile_position=(0, 64)), inc="PE" if mm == 3 else None)
                S.wait_at("scalar", "PE", S.c("PE"))
                dst3 = ofm.ap().rearrange("p (m q) -> p m q", q=512)[
                    :, m0:m0 + 4, 128 * u:128 * u + 128]
                ps3 = psD.ap().rearrange("p (m c) -> p m c", c=128)
                S.op("scalar", lambda E, d=dst3, s_=ps3: E.activation(
                    out=d, in_=s_, func=ACT.Identity), inc="AE")
                S.wait_at("tensor", "AE", S.c("AE"))
                _evmark[id(psD)] = ("AE", S.c("AE"))

        def p1_wo_res(s):
            global_st = None
            odst = [(psA[:, 0:512], psA[:, 512:1024]), (psB[:, 0:512], psB[:, 512:1024]),
                    (psC[:, 0:512], psC[:, 512:1024]), (psD[:, 0:512], psE[:, 0:512])]
            wo_v = wo_in.rearrange("(k p) f -> p k f", p=128)
            for hf in range(2):
                if hf == 1:
                    S.wait_at("gpsimd", "PE", S.c("PE"))
                    S.op("gpsimd", lambda E: E.dma_start(
                        out=wbuf.ap().rearrange("p (k f) -> p k f", f=D),
                        in_=wo_v[:, 4:8, :]), inc="WBF", amt=16)
                S.wait_at("tensor", "WBF", S.c("WBF"))
                for u in range(4):
                    for half in range(2):
                        cl = half * 512
                        dd = odst[u][half]
                        for ml in range(4):
                            m = 4 * hf + ml
                            S.op("tensor", lambda E, m_=m, mm=ml, c=cl, u_=u, d=dd: E.matmul(
                                d, ofm[:, m_ * 512 + 128 * u_: m_ * 512 + 128 * u_ + 128],
                                wbuf[:, mm * D + c: mm * D + c + 512],
                                start=(m_ == 0), stop=False),
                                inc="PE" if (u == 3 and half == 1 and ml == 3) else None)
                        if hf == 1:
                            S.wait_at("tensor", "LG", S.c("LG"))
                            S.op("tensor", lambda E, c=cl, d=dd: E.matmul(
                                d, ones_row[0:1, :], bo_r[0:1, c:c + 512],
                                start=False, stop=True),
                                inc="PE" if (u == 3 and half == 1) else None)
            w("vector", "PE")
            for u in range(4):
                t = 4 * s + u
                slot = t % 4
                xo = xring[:, slot * D:(slot + 1) * D]
                S.op("vector", lambda E, d0=odst[u][0], x=xo: E.tensor_tensor(
                    out=xmid_t[:, 0:512], in0=d0, in1=x[:, 0:512], op=ALU.add), inc="VE")
                S.op("vector", lambda E, d1=odst[u][1], x=xo: E.tensor_tensor(
                    out=xmid_t[:, 512:1024], in0=d1, in1=x[:, 512:1024], op=ALU.add), inc="VE")
                S.wait_at("sync", "VE", S.c("VE"))
                S.op("sync", lambda E, t_=t: E.dma_start(
                    out=xmid_hbm.rearrange("(t p) f -> p t f", p=128)[:, t_, :],
                    in_=xmid_t[:, :]), inc="XM", amt=16)
                S.wait_at("vector", "XM", S.c("XM"))
            _evmark[id(psD)] = ("VE", S.c("VE"))
            _evmark[id(psE)] = ("VE", S.c("VE"))

        p1_next_loads_stats = p1_loads_stats

        p1_loads_stats(0)
        for s in range(NSUP):
            p1_lnq(s)
            p1_scores(s, 0, 0)
            for u in range(4):
                if u + 1 < 4:
                    p1_scores(s, u + 1, (u + 1) % 2)
                p1_softmax_av(s, u, u % 2)
            p1_wo_res(s)
            if s + 1 < NSUP:
                p1_next_loads_stats(s + 1)
        es_ph1.close()
        es_ph2 = contextlib.ExitStack()
        sb2 = lambda name, shape, dt: es_ph2.enter_context(nc.sbuf_tensor(name, shape, dt))
        x2r0 = sb2("x2r0", [128, D], F32)
        x2r1 = sb2("x2r1", [128, D], F32)
        xln2 = sb2("xln2", [128, D], F32)
        xqb = sb2("xqb", [128, D], BF16)
        xqfm = sb2("xqfm", [128, 2 * 8 * 128], BF16)
        hbuf = sb2("hbuf", [128, 4 * D], F32)
        hqb = sb2("hqb", [128, D], BF16)
        hqT = sb2("hqT", [128, 2 * 8 * 128], BF16)
        bfc_r = sb2("bfc_r", [1, HID], BF16)
        bpj_r = sb2("bpj_r", [1, D], BF16)
        for e in ("sync", "gpsimd", "tensor", "vector", "scalar"):
            S.wait_at(e, "XM", 16 * NT)
        dma("gpsimd", "LG", bfc_r[:], bfc_in)
        dma("gpsimd", "LG", bpj_r[:], bpj_in)
        lg_bias2 = S.c("LG")
        S.wait_at("tensor", "LG", lg_bias2)

        def p2_front(t):
            buf = x2r0 if t % 2 == 0 else x2r1
            st_sem = "ST0" if t % 2 == 0 else "ST1"
            S.wait_at("sync", st_sem, S.c(st_sem))
            x2_sem = "X20" if t % 2 == 0 else "X21"
            dma("sync", x2_sem, buf[:, :], xmid_hbm.rearrange("(t p) f -> p t f", p=128)[:, t, :])
            S.wait_at("vector", x2_sem, S.c(x2_sem))
            S.op("vector", lambda E, b=buf: E.bn_stats(out=bnscr[:, 0:6], in_=b[:, 0:512]), inc="VE")
            S.op("vector", lambda E, b=buf: E.bn_stats(out=bnscr[:, 6:12], in_=b[:, 512:1024]), inc="VE")
            S.wait_at("vector", "VE", S.c("VE"))
            S.op("vector", lambda E: E.bn_aggr(out=mv_own[:, 0:2], in_=bnscr[:, :]), inc="VE")
            S.wait_at("vector", "VE", S.c("VE"))
            S.op("vector", lambda E: E.tensor_scalar(
                out=lnv2[:, 0:1], in0=mv_own[:, 1:2], scalar1=EPS, scalar2=None,
                op0=ALU.add), inc="VE")
            w("scalar", "VE")
            S.op("scalar", lambda E: E.activation(out=a2c[:], in_=lnv2[:, 0:1], func=ACT.Ln), inc="AE")
            S.wait_at("scalar", "AE", S.c("AE"))
            S.op("scalar", lambda E: E.activation(out=a2c[:], in_=a2c[:], func=ACT.Exp,
                                                  scale=-0.5), inc="AE")
            w("vector", "AE")
            S.op("vector", lambda E: E.tensor_tensor(out=nam2c[:], in0=a2c[:],
                                                     in1=mv_own[:, 0:1], op=ALU.mult), inc="VE")
            S.wait_at("vector", "VE", S.c("VE"))
            S.op("vector", lambda E: E.tensor_scalar(out=nam2c[:], in0=nam2c[:], scalar1=-1.0,
                                                     scalar2=None, op0=ALU.mult), inc="VE")
            S.wait_at("vector", "VE", S.c("VE"))
            S.op("vector", lambda E, b=buf: E.tensor_scalar(
                out=xln2[:, :], in0=b[:, :], scalar1=a2c[:, :], scalar2=nam2c[:, :],
                op0=ALU.mult, op1=ALU.add), inc="VE")
            S.wait_at("vector", "VE", S.c("VE"))
            o = 128
            v3 = xln2.ap().rearrange("p (b e) -> p b e", e=32)
            S.op("vector", lambda E, v=v3, o_=o: E.tensor_reduce(
                out=qa_amax[:, o_:o_ + 32], in_=v, axis=AXX, op=ALU.max,
                apply_absolute_value=True), inc="VE")
            S.wait_at("vector", "VE", S.c("VE"))
            quant_consts(32, qa_amax[:, o:o + 32], qa_e[:, o:o + 32], qa_s0[:, o:o + 32],
                         qa_s[:, o:o + 32], qa_m[:, o:o + 32], qa_eb[:, o:o + 32])
            ebb = qa_eb.ap()[:, o:o + 32].unsqueeze(2).broadcast_to([128, 32, 32])
            q3 = xqb.ap().rearrange("p (b e) -> p b e", e=32)
            S.wait_at("vector", "PE", S.c("PE"))
            quant_apply(v3, v3, q3, ebb)
            S.wait_at("tensor", "VE", S.c("VE"))
            xpar = t % 2
            transpose8(lambda j: xqb[:, 128 * j:128 * j + 128],
                       lambda j0, jn, xp=xpar: xqfm.ap().rearrange(
                           "p (j c) -> p j c", c=128)[:, 8 * xp + j0:8 * xp + j0 + jn, :],
                       8, evac_eng="scalar", banks=[psE, psD])

        def p2_fc1(t, g):
            xpar = t % 2
            dst = psA if g % 2 == 0 else psB
            for k in range(8):
                S.op("tensor", lambda E, k_=k, g_=g, d=dst, xp=xpar: E.matmul(
                    d[:, 0:512], xqfm[:, 1024 * xp + 128 * k_:1024 * xp + 128 * k_ + 128],
                    wfcTq[:, k_ * HID + g_ * D: k_ * HID + g_ * D + 512],
                    start=(k_ == 0), stop=False))
                S.op("tensor", lambda E, k_=k, g_=g, d=dst, xp=xpar: E.matmul(
                    d[:, 512:1024], xqfm[:, 1024 * xp + 128 * k_:1024 * xp + 128 * k_ + 128],
                    wfcTq[:, k_ * HID + g_ * D + 512: k_ * HID + g_ * D + 1024],
                    start=(k_ == 0), stop=False))
            S.op("tensor", lambda E, g_=g, d=dst: E.matmul(
                d[:, 0:512], ones_row[0:1, :], bfc_r[0:1, g_ * D:g_ * D + 512],
                start=False, stop=True))
            S.op("tensor", lambda E, g_=g, d=dst: E.matmul(
                d[:, 512:1024], ones_row[0:1, :], bfc_r[0:1, g_ * D + 512:g_ * D + 1024],
                start=False, stop=True), inc="PE")

        ae_gmark = {}

        def p2_gelu(t, g):
            dst = psA if g % 2 == 0 else psB
            w("scalar", "PE")
            S.wait_at("scalar", "VE", S.c("VE"))
            S.op("scalar", lambda E, d=dst, s_=g: E.activation(
                out=hbuf[:, s_ * D:(s_ + 1) * D], in_=d[:, :], func=ACT.Gelu), inc="AE")
            ae_gmark[(t, g)] = S.c("AE")

        ve_rmark = {}

        def p2_back(t, do_front):
            p2_gelu(t, 0)
            if (t, -1) in ae_gmark:
                S.wait_at("tensor", "AE", ae_gmark[(t, -1)])
            p2_fc1(t, 1)
            p2_gelu(t, 1)
            S.wait_at("tensor", "AE", ae_gmark[(t, 0)])
            p2_fc1(t, 2)
            p2_gelu(t, 2)
            S.wait_at("tensor", "AE", ae_gmark[(t, 1)])
            p2_fc1(t, 3)
            if do_front:
                p2_front(t + 1)
            p2_gelu(t, 3)
            ae_g3 = ae_gmark[(t, 3)]
            S.wait_at("vector", "AE", ae_g3)
            hh = hbuf.ap().rearrange("p (b e) -> p b e", e=32)
            S.op("vector", lambda E, v=hh: E.tensor_reduce(
                out=qa_amax[:, 0:128], in_=v, axis=AXX, op=ALU.max,
                apply_absolute_value=True), inc="VE")
            ve_red = S.c("VE")
            S.wait_at("vector", "VE", ve_red)
            quant_consts(128, qa_amax[:, 0:128], qa_e[:, 0:128], qa_s0[:, 0:128],
                         qa_s[:, 0:128], qa_m[:, 0:128], qa_eb[:, 0:128])
            for g in range(4):
                gs = g % 2
                hg = hbuf.ap()[:, g * D:(g + 1) * D].rearrange("p (b e) -> p b e", e=32)
                ebg = qa_eb.ap()[:, 32 * g:32 * g + 32].unsqueeze(2).broadcast_to([128, 32, 32])
                qg = hqb.ap().rearrange("p (b e) -> p b e", e=32)
                S.wait_at("vector", "PE", S.c("PE"))
                quant_apply(hg, hg, qg, ebg)
                S.wait_at("tensor", "VE", S.c("VE"))
                if g == 0 and (t - 1) in ve_rmark:
                    S.wait_at("tensor", "VE", ve_rmark[t - 1])
                transpose8(lambda j: hqb[:, 128 * j:128 * j + 128],
                           lambda j0, jn, s_=gs: hqT.ap().rearrange(
                               "p (j c) -> p j c", c=128)[:, 8 * s_ + j0:8 * s_ + j0 + jn, :],
                           8, evac_eng="scalar", banks=[psE, psD])
                for half in range(2):
                    cl = half * 512
                    for cj in range(8):
                        c = 8 * g + cj
                        S.op("tensor", lambda E, c_=c, cc=cl, cjj=cj, s_=gs: E.matmul(
                            psC[:, cc:cc + 512],
                            hqT[:, 1024 * s_ + 128 * cjj:1024 * s_ + 128 * cjj + 128],
                            wpjTq[:, c_ * D + cc: c_ * D + cc + 512],
                            start=(c_ == 0), stop=False))
                    if g == 3:
                        S.op("tensor", lambda E, cc=cl: E.matmul(
                            psC[:, cc:cc + 512], ones_row[0:1, :], bpj_r[0:1, cc:cc + 512],
                            start=False, stop=True),
                            inc="PE" if half == 1 else None)
            w("vector", "PE")
            buf = x2r0 if t % 2 == 0 else x2r1
            S.op("vector", lambda E, b=buf: E.tensor_tensor(
                out=b[:, :], in0=psC[:, :], in1=b[:, :], op=ALU.add), inc="VE")
            ve_rmark[t] = S.c("VE")
            S.wait_at("sync", "VE", ve_rmark[t])
            st_sem = "ST0" if t % 2 == 0 else "ST1"
            S.op("sync", lambda E, t_=t, b=buf: E.dma_start(
                out=y_out.rearrange("(t p) f -> p t f", p=128)[:, t_, :],
                in_=b[:, :]), inc=st_sem, amt=16)

        p2_front(0)
        for t in range(NT):
            if t >= 1:
                S.wait_at("tensor", "AE", ae_gmark[(t - 1, 2)])
                ae_gmark[(t, -1)] = ae_gmark[(t - 1, 3)]
            p2_fc1(t, 0)
            p2_back(t, t + 1 < NT)

        S.wait_at("sync", "ST0", 16 * (NT // 2))
        S.wait_at("sync", "ST1", 16 * (NT // 2))

        es_ph2.close()
        def runner(eng_name):
            def run(E):
                for item in S.q[eng_name]:
                    if item[0] == "w":
                        E.wait_ge(sems[item[1]], item[2])
                    else:
                        ins = item[1](E)
                        if item[2] is not None:
                            ins.then_inc(sems[item[2]], item[3])
            return run

        block.sync(runner("sync"))
        block.gpsimd(runner("gpsimd"))
        block.tensor(runner("tensor"))
        block.vector(runner("vector"))
        block.scalar(runner("scalar"))

    return nc


_NC_CACHE = None


def _get_nc():
    global _NC_CACHE
    if _NC_CACHE is None:
        _NC_CACHE = build_nc()
    return _NC_CACHE


def kernel(L0_attn_x, L0_attn_Wq, L0_attn_bq, L0_attn_Wk, L0_attn_bk, L0_attn_Wv,
           L0_attn_bv, L0_attn_Wo, L0_attn_bo, L0_ffn_w_fc, L0_ffn_b_fc,
           L0_ffn_w_proj, L0_ffn_b_proj):
    x = np.ascontiguousarray(np.asarray(L0_attn_x, dtype=np.float32))
    B = x.shape[0]
    wq = np.ascontiguousarray(np.asarray(L0_attn_Wq, dtype=np.float32))
    wk = np.ascontiguousarray(np.asarray(L0_attn_Wk, dtype=np.float32))
    wv = np.ascontiguousarray(np.asarray(L0_attn_Wv, dtype=np.float32))
    wo = np.ascontiguousarray(np.asarray(L0_attn_Wo, dtype=np.float32))
    bq = np.asarray(L0_attn_bq, dtype=np.float32)
    bk = np.asarray(L0_attn_bk, dtype=np.float32)
    bv = np.asarray(L0_attn_bv, dtype=np.float32).reshape(1, D)
    bo = np.asarray(L0_attn_bo, dtype=np.float32).reshape(1, D)
    wfc = np.ascontiguousarray(np.asarray(L0_ffn_w_fc, dtype=np.float32))
    bfc = np.asarray(L0_ffn_b_fc, dtype=np.float32).reshape(1, HID)
    wpj = np.ascontiguousarray(np.asarray(L0_ffn_w_proj, dtype=np.float32))
    bpj = np.asarray(L0_ffn_b_proj, dtype=np.float32).reshape(1, D)

    bqr = np.ascontiguousarray(bq.reshape(8, 128).T)
    bkr = np.ascontiguousarray(bk.reshape(8, 128).T)

    in_maps = []
    for c in range(8):
        b = c // 4
        J = c % 4
        pos = 2048 * J + np.arange(2048)[:, None]
        ends = 64 * np.arange(128)[None, :]
        mask01 = (ends <= pos).astype(ml_dtypes.bfloat16)
        in_maps.append({
            "xseq": x[b],
            "xown": np.ascontiguousarray(x[b, 2048 * J:2048 * (J + 1)]),
            "wq_in": wq, "wk_in": wk, "wv_in": wv, "wo_in": wo,
            "bqr_in": bqr, "bkr_in": bkr,
            "bv_in": bv, "bo_in": bo, "bfc_in": bfc, "bpj_in": bpj,
            "wfc_in": wfc, "wpj_in": wpj,
            "mask_in": mask01,
        })
    nc = _get_nc()
    res = run_bass_kernel_spmd(nc, in_maps, core_ids=list(range(8)))
    out = np.empty((B, T_SEQ, D), dtype=np.float32)
    for c in range(8):
        b, J = c // 4, c % 4
        out[b, 2048 * J:2048 * (J + 1)] = res.results[c]["y"]
    return out


# version 07
# speedup vs baseline: 1.3636x; 1.0700x over previous
import sys
import contextlib

import numpy as np
import ml_dtypes

sys.path.insert(0, "/opt/trn_rl_repo")
import concourse.bass as bass
import concourse.mybir as mybir
from concourse.bass_utils import run_bass_kernel_spmd

F32 = mybir.dt.float32
BF16 = mybir.dt.bfloat16
ALU = mybir.AluOpType
ACT = mybir.ActivationFunctionType
AXX = mybir.AxisListType.X

LN2 = float(np.log(2.0))
K1 = float(3 * 2**22)
K2 = float(3 * 2**51)
EPS = 1e-5

T_SEQ = 8192
D = 1024
HID = 4096
TOK = 2048
NT = 16
NSUP = 4
NSEQT = 64
NC = 128

ENGS = ("sync", "gpsimd", "tensor", "vector", "scalar")


class Sched:

    def __init__(self):
        self.q = {e: [] for e in ENGS}
        self.cnt = {}

    def wait(self, eng, sem_name):
        v = self.cnt.get(sem_name, 0)
        if v == 0:
            return
        self.q[eng].append(("w", sem_name, v))

    def wait_at(self, eng, sem_name, v):
        if v > 0:
            self.q[eng].append(("w", sem_name, v))

    def op(self, eng, fn, inc=None, amt=1):
        self.q[eng].append(("o", fn, inc, amt))
        if inc is not None:
            self.cnt[inc] = self.cnt.get(inc, 0) + amt

    def c(self, sem_name):
        return self.cnt.get(sem_name, 0)


def build_nc():
    nc = bass.Bass("TRN2", target_bir_lowering=False)

    xseq = nc.dram_tensor("xseq", [T_SEQ, D], F32, kind="ExternalInput").ap()
    xown = nc.dram_tensor("xown", [TOK, D], F32, kind="ExternalInput").ap()
    wq_in = nc.dram_tensor("wq_in", [D, D], F32, kind="ExternalInput").ap()
    wk_in = nc.dram_tensor("wk_in", [D, D], F32, kind="ExternalInput").ap()
    wv_in = nc.dram_tensor("wv_in", [D, D], F32, kind="ExternalInput").ap()
    wo_in = nc.dram_tensor("wo_in", [D, D], F32, kind="ExternalInput").ap()
    bqr_in = nc.dram_tensor("bqr_in", [128, 8], F32, kind="ExternalInput").ap()
    bkr_in = nc.dram_tensor("bkr_in", [128, 8], F32, kind="ExternalInput").ap()
    bv_in = nc.dram_tensor("bv_in", [1, D], F32, kind="ExternalInput").ap()
    bo_in = nc.dram_tensor("bo_in", [1, D], F32, kind="ExternalInput").ap()
    bfc_in = nc.dram_tensor("bfc_in", [1, HID], F32, kind="ExternalInput").ap()
    bpj_in = nc.dram_tensor("bpj_in", [1, D], F32, kind="ExternalInput").ap()
    wfc_in = nc.dram_tensor("wfc_in", [HID, D], F32, kind="ExternalInput").ap()
    wpj_in = nc.dram_tensor("wpj_in", [D, HID], F32, kind="ExternalInput").ap()
    mask_in = nc.dram_tensor("mask_in", [TOK, NC], BF16, kind="ExternalInput").ap()
    y_out = nc.dram_tensor("y", [TOK, D], F32, kind="ExternalOutput").ap()
    xmid_hbm = nc.dram_tensor("xmid_scr", [TOK, D], F32).ap()

    ident_h = nc.inline_tensor(np.eye(128, dtype=ml_dtypes.bfloat16), name="ident_h")
    base3_np = np.zeros((128, 3), dtype=ml_dtypes.bfloat16)
    base3_np[0, 0] = 1.0
    base3_np[1:65, 1] = 1.0
    base3_np[65:128, 2] = 1.0
    base3_h = nc.inline_tensor(base3_np, name="base3_h")
    ilen_np = np.full((128, 1), 1.0 / 64.0, dtype=np.float32)
    ilen_np[0, 0] = 1.0
    ilen_h = nc.inline_tensor(ilen_np, name="ilen_h")

    es = contextlib.ExitStack()
    S = Sched()

    with es:
        sb = lambda name, shape, dt: es.enter_context(nc.sbuf_tensor(name, shape, dt))
        psb = lambda name, shape: es.enter_context(nc.psum_tensor(name, shape, F32))

        ident = sb("ident", [128, 128], BF16)
        base3 = sb("base3", [128, 3], BF16)
        ilen = sb("ilen", [128, 1], F32)
        zeros_b = sb("zeros_b", [128, 128], BF16)
        ones_col = sb("ones_col", [128, 1], BF16)
        ones_row = sb("ones_row", [1, 128], BF16)
        bqr = sb("bqr", [128, 8], F32)
        bkr = sb("bkr", [128, 8], F32)

        Dt = sb("Dt", [128, 1], F32)
        DK2pp = sb("DK2pp", [128, 1], F32)
        EB1pp = sb("EB1pp", [128, 1], F32)
        TINYpp = sb("TINYpp", [128, 1], F32)

        bnscr = sb("bnscr", [128, 12], F32)
        mv_seq = sb("mv_seq", [128, 2 * NSEQT], F32)
        a_seq = sb("a_seq", [128, NSEQT], F32)
        am_seq = sb("am_seq", [128, NSEQT], F32)
        lnv_scr = sb("lnv_scr", [128, NSEQT], F32)
        mv_own = sb("mv_own", [128, 2 * NT], F32)
        a_own = sb("a_own", [128, NT], F32)
        nam_own = sb("nam_own", [128, NT], F32)

        summ_cm = sb("summ_cm", [128, D], BF16)
        summ_fm = sb("summ_fm", [128, 8 * 128], BF16)
        kfm = sb("kfm", [128, 8 * 128], BF16)
        vcm = sb("vcm", [128, D], BF16)
        sum_bias = sb("sum_bias", [128, 1], F32)

        wfcTq = sb("wfcTq", [128, 8 * HID], BF16)
        wpjTq = sb("wpjTq", [128, 32 * D], BF16)

        wbuf = sb("wbuf", [128, 4 * D], BF16)

        qa_amax = sb("qa_amax", [128, 160], F32)
        qa_e = sb("qa_e", [128, 160], F32)
        qa_s0 = sb("qa_s0", [128, 160], F32)
        qa_s = sb("qa_s", [128, 160], F32)
        qa_m = sb("qa_m", [128, 160], F32)
        qa_eb = sb("qa_eb", [128, 160], F32)

        a2c = sb("a2c", [128, 1], F32)
        nam2c = sb("nam2c", [128, 1], F32)
        lnv2 = sb("lnv2", [128, 1], F32)
        bo_r = sb("bo_r", [1, D], BF16)
        patt0 = sb("patt0", [128, 128], BF16)
        patt1 = sb("patt1", [128, 128], BF16)
        meanb = sb("meanb", [128, 1], BF16)
        ssum = sb("ssum", [128, 16], F32)
        srcp = sb("srcp", [128, 16], F32)

        es_pro = contextlib.ExitStack()
        sbp = lambda name, shape, dt: es_pro.enter_context(nc.sbuf_tensor(name, shape, dt))
        bv_r = sbp("bv_r", [1, D], BF16)
        wstr0 = sbp("wstr0", [128, 2048], F32)
        wstr1 = sbp("wstr1", [128, 2048], F32)
        wqnat = sbp("wqnat", [128, 4096], BF16)
        xsq0 = sbp("xsq0", [128, 2 * D], BF16)
        xsq1 = sbp("xsq1", [128, 2 * D], BF16)
        xsq2 = sbp("xsq2", [128, 2 * D], BF16)

        psA = psb("psA", [128, 1024])
        psB = psb("psB", [128, 1024])
        psC = psb("psC", [128, 1024])
        psD = psb("psD", [128, 512])
        psE = psb("psE", [128, 512])

        sem_names = ["LD", "LG", "ST0", "ST1", "PE", "VE", "AE", "GE", "XM",
                     "XS0", "XS1", "WF0", "WF1", "X20", "X21", "XOW", "WBF", "MSK0", "MSK1", "XS2"]
        sems = {n: es.enter_context(nc.semaphore("s" + n)) for n in sem_names}
        block = es.enter_context(nc.Block())

        def w(eng, *names):
            for n in names:
                S.wait(eng, n)

        def dma(eng, sem_name, out, in_):
            S.op(eng, lambda E, o=out, i=in_: E.dma_start(out=o, in_=i), inc=sem_name, amt=16)

        def quant_consts(nb, amax_ap, e_ap, s0_ap, s_ap, m_ap, eb_ap):
            w("scalar", "VE")
            S.op("scalar", lambda E, a=amax_ap, e=e_ap: E.activation(
                out=e, in_=a, func=ACT.Ln, bias=TINYpp[:, :], scale=1.0), inc="AE")
            w("vector", "AE")
            S.op("vector", lambda E, e=e_ap: E.tensor_scalar(
                out=e, in0=e, scalar1=float(1.0 / LN2), scalar2=float(1e-11 - 0.5),
                op0=ALU.mult, op1=ALU.add), inc="VE")
            S.wait_at("vector", "VE", S.c("VE"))
            S.op("vector", lambda E, e=e_ap: E.tensor_scalar(
                out=e, in0=e, scalar1=DK2pp[:, :], scalar2=EB1pp[:, :],
                op0=ALU.add, op1=ALU.subtract), inc="VE")
            S.wait_at("vector", "VE", S.c("VE"))
            S.op("vector", lambda E, e=e_ap: E.tensor_scalar(
                out=e, in0=e, scalar1=K1, scalar2=3.0,
                op0=ALU.subtract, op1=ALU.subtract), inc="VE")
            w("scalar", "VE")
            S.op("scalar", lambda E, e=e_ap, s0=s0_ap: E.activation(
                out=s0, in_=e, func=ACT.Exp, scale=LN2), inc="AE")
            w("vector", "AE")
            S.op("vector", lambda E, s0=s0_ap, m=m_ap: E.tensor_scalar(
                out=m, in0=s0, scalar1=K1, scalar2=None, op0=ALU.mult), inc="VE")
            S.wait_at("vector", "VE", S.c("VE"))
            S.op("vector", lambda E, s0=s0_ap, s=s_ap, m=m_ap: E.tensor_tensor(
                out=s, in0=s0, in1=m, op=ALU.add), inc="VE")
            S.wait_at("vector", "VE", S.c("VE"))
            S.op("vector", lambda E, s=s_ap, m=m_ap: E.tensor_tensor(
                out=s, in0=s, in1=m, op=ALU.subtract), inc="VE")
            S.wait_at("vector", "VE", S.c("VE"))
            S.op("vector", lambda E, s=s_ap, m=m_ap: E.tensor_scalar(
                out=m, in0=s, scalar1=K1, scalar2=None, op0=ALU.mult), inc="VE")
            S.op("vector", lambda E, s=s_ap, eb=eb_ap: E.tensor_scalar(
                out=eb, in0=s, scalar1=K2, scalar2=Dt[:, :], op0=ALU.mult,
                op1=ALU.mult), inc="VE")
            S.wait_at("vector", "VE", S.c("VE"))
            S.op("vector", lambda E, eb=eb_ap, m=m_ap: E.tensor_tensor(
                out=eb, in0=eb, in1=m, op=ALU.subtract), inc="VE")
            S.wait_at("vector", "VE", S.c("VE"))

        def quant_apply(v_ap, u_ap, q_ap, eb_bc):
            S.op("vector", lambda E: E.tensor_tensor(out=u_ap, in0=v_ap, in1=eb_bc,
                                                     op=ALU.subtract), inc="VE")
            S.wait_at("vector", "VE", S.c("VE"))
            S.op("vector", lambda E: E.tensor_tensor(out=q_ap, in0=u_ap, in1=eb_bc,
                                                     op=ALU.add), inc="VE")
            S.wait_at("vector", "VE", S.c("VE"))

        _evmark = {}

        def transpose8(src_tile_fn, dst0_ap_fn, nchunk, evac_eng="scalar", bank=None,
                       banks=None):
            bks = banks if banks is not None else [bank if bank is not None else psE]
            sem = "VE" if evac_eng == "vector" else "AE"
            for gi, j0 in enumerate(range(0, nchunk, 4)):
                bk = bks[gi % len(bks)]
                mk = _evmark.get(id(bk))
                if mk is not None:
                    S.wait_at("tensor", mk[0], mk[1])
                jn = min(4, nchunk - j0)
                for j in range(j0, j0 + jn):
                    S.op("tensor", lambda E, s=src_tile_fn(j), d=bk[:, 128 * (j - j0):128 * (j - j0) + 128]:
                         E.matmul(d, s, ident[:, :], start=True, stop=True),
                         inc="PE" if j == j0 + jn - 1 else None)
                w(evac_eng, "PE")
                ps3 = bk[:, 0:128 * jn].rearrange("p (j c) -> p j c", c=128)
                dst3 = dst0_ap_fn(j0, jn)
                if evac_eng == "vector":
                    S.op("vector", lambda E, s=ps3, d=dst3: E.tensor_copy(out=d, in_=s), inc="VE")
                else:
                    S.op("scalar", lambda E, s=ps3, d=dst3: E.activation(
                        out=d, in_=s, func=ACT.Identity), inc="AE")
                _evmark[id(bk)] = (sem, S.c(sem))
            S.wait(("tensor"), sem)

        def strided_dst(base_ap_fn):
            def f(j0, jn):
                a0 = base_ap_fn(j0)
                if jn == 1:
                    return a0.unsqueeze(1)
                a1 = base_ap_fn(j0 + 1)
                raise RuntimeError("use explicit dst fns")
            return f

        dma("sync", "LD", ident[:], ident_h[:])
        dma("sync", "LD", base3[:], base3_h[:])
        dma("sync", "LD", ilen[:], ilen_h[:])
        dma("sync", "LD", bqr[:], bqr_in)
        dma("sync", "LD", bkr[:], bkr_in)
        dma("gpsimd", "LG", bv_r[:], bv_in)
        dma("gpsimd", "LG", bo_r[:], bo_in)

        S.op("vector", lambda E: E.memset(zeros_b[:], 0.0), inc="VE")
        S.op("vector", lambda E: E.memset(ones_col[:], 1.0), inc="VE")
        S.op("vector", lambda E: E.memset(ones_row[:], 1.0), inc="VE")
        S.op("vector", lambda E: E.memset(Dt[:], 1.0), inc="VE")
        S.op("vector", lambda E: E.memset(TINYpp[:], float(2.0**-100)), inc="VE")
        S.wait_at("vector", "VE", S.c("VE"))
        S.op("vector", lambda E: E.tensor_scalar(out=Dt[:], in0=Dt[:], scalar1=K2,
                                                 scalar2=-K2, op0=ALU.add, op1=ALU.add), inc="VE")
        S.wait_at("vector", "VE", S.c("VE"))
        S.op("vector", lambda E: E.tensor_scalar(out=DK2pp[:], in0=Dt[:], scalar1=K2,
                                                 scalar2=None, op0=ALU.mult), inc="VE")
        S.wait_at("vector", "VE", S.c("VE"))
        S.op("vector", lambda E: E.tensor_scalar(out=EB1pp[:], in0=DK2pp[:], scalar1=K1,
                                                 scalar2=None, op0=ALU.subtract), inc="VE")
        S.wait_at("vector", "VE", S.c("VE"))

        w("tensor", "LD", "VE")
        S.wait_at("tensor", "XS0", 16)
        S.op("tensor", lambda E: E.matmul(psA[:, 0:512], zeros_b[:], xsq0[:, 0:512],
                                          start=True, stop=False))
        S.op("tensor", lambda E: E.matmul(psA[:, 512:1024], zeros_b[:], xsq0[:, 512:1024],
                                          start=True, stop=False))
        S.op("tensor", lambda E: E.matmul(psD[:, 0:1], zeros_b[:], ones_col[:],
                                          start=True, stop=False), inc="PE")

        wfc_v = wfc_in.rearrange("(n p) f -> p n f", p=128)
        wpj_v = wpj_in.rearrange("(n p) f -> p n f", p=128)
        jobs = [("fc", n) for n in range(32)] + [("pj", q) for q in range(16)]

        def job_dma(i):
            kind, n = jobs[i]
            par = i % 2
            buf = wstr0 if par == 0 else wstr1
            wf_sem = "WF0" if par == 0 else "WF1"
            if kind == "fc":
                dma("sync", wf_sem, buf[:, 0:D], wfc_v[:, n, :])
            else:
                dma("sync", wf_sem, buf[:, :], wpj_v[:, n // 2, 2048 * (n % 2):2048 * (n % 2) + 2048])
            return S.c(wf_sem)

        def job_nb(i):
            return 32 if jobs[i][0] == "fc" else 64

        wf_totals = {}
        def stage_R(i):
            par, nb = i % 2, job_nb(i)
            buf = wstr0 if par == 0 else wstr1
            o = 64 * par
            S.wait_at("vector", "WF0" if par == 0 else "WF1", wf_totals[i])
            v3 = buf.ap()[:, 0:32 * nb].rearrange("p (b e) -> p b e", e=32)
            S.op("vector", lambda E, v=v3, o_=o, nb_=nb: E.tensor_reduce(
                out=qa_amax[:, o_:o_ + nb_], in_=v, axis=AXX, op=ALU.max,
                apply_absolute_value=True), inc="VE")

        def stage_L(i):
            par, nb = i % 2, job_nb(i)
            o = 64 * par
            w("scalar", "VE")
            S.op("scalar", lambda E, o_=o, nb_=nb: E.activation(
                out=qa_e[:, o_:o_ + nb_], in_=qa_amax[:, o_:o_ + nb_], func=ACT.Ln,
                bias=TINYpp[:, :], scale=1.0), inc="AE")

        def stage_E1(i):
            par, nb = i % 2, job_nb(i)
            o = 64 * par
            w("vector", "AE")
            S.op("vector", lambda E, o_=o, nb_=nb: E.tensor_scalar(
                out=qa_e[:, o_:o_ + nb_], in0=qa_e[:, o_:o_ + nb_],
                scalar1=float(1.0 / LN2), scalar2=float(1e-11 - 0.5),
                op0=ALU.mult, op1=ALU.add), inc="VE")
            S.wait_at("vector", "VE", S.c("VE"))
            S.op("vector", lambda E, o_=o, nb_=nb: E.tensor_scalar(
                out=qa_e[:, o_:o_ + nb_], in0=qa_e[:, o_:o_ + nb_],
                scalar1=DK2pp[:, :], scalar2=EB1pp[:, :], op0=ALU.add, op1=ALU.subtract), inc="VE")
            S.wait_at("vector", "VE", S.c("VE"))
            S.op("vector", lambda E, o_=o, nb_=nb: E.tensor_scalar(
                out=qa_e[:, o_:o_ + nb_], in0=qa_e[:, o_:o_ + nb_],
                scalar1=K1, scalar2=3.0, op0=ALU.subtract, op1=ALU.subtract), inc="VE")

        def stage_X(i):
            par, nb = i % 2, job_nb(i)
            o = 64 * par
            w("scalar", "VE")
            S.op("scalar", lambda E, o_=o, nb_=nb: E.activation(
                out=qa_s0[:, o_:o_ + nb_], in_=qa_e[:, o_:o_ + nb_], func=ACT.Exp,
                scale=LN2), inc="AE")

        def stage_S(i):
            par, nb = i % 2, job_nb(i)
            o = 64 * par
            w("vector", "AE")
            for fn in (
                lambda E, o_=o, nb_=nb: E.tensor_scalar(
                    out=qa_m[:, o_:o_ + nb_], in0=qa_s0[:, o_:o_ + nb_], scalar1=K1,
                    scalar2=None, op0=ALU.mult),
                lambda E, o_=o, nb_=nb: E.tensor_tensor(
                    out=qa_s[:, o_:o_ + nb_], in0=qa_s0[:, o_:o_ + nb_],
                    in1=qa_m[:, o_:o_ + nb_], op=ALU.add),
                lambda E, o_=o, nb_=nb: E.tensor_tensor(
                    out=qa_s[:, o_:o_ + nb_], in0=qa_s[:, o_:o_ + nb_],
                    in1=qa_m[:, o_:o_ + nb_], op=ALU.subtract),
                lambda E, o_=o, nb_=nb: E.tensor_scalar(
                    out=qa_m[:, o_:o_ + nb_], in0=qa_s[:, o_:o_ + nb_], scalar1=K1,
                    scalar2=None, op0=ALU.mult),
                lambda E, o_=o, nb_=nb: E.tensor_scalar(
                    out=qa_eb[:, o_:o_ + nb_], in0=qa_s[:, o_:o_ + nb_], scalar1=K2,
                    scalar2=Dt[:, :], op0=ALU.mult, op1=ALU.mult),
                lambda E, o_=o, nb_=nb: E.tensor_tensor(
                    out=qa_eb[:, o_:o_ + nb_], in0=qa_eb[:, o_:o_ + nb_],
                    in1=qa_m[:, o_:o_ + nb_], op=ALU.subtract),
            ):
                S.op("vector", fn, inc="VE")
                S.wait_at("vector", "VE", S.c("VE"))

        def stage_Q(i):
            par, nb = i % 2, job_nb(i)
            buf = wstr0 if par == 0 else wstr1
            o = 64 * par
            v3 = buf.ap()[:, 0:32 * nb].rearrange("p (b e) -> p b e", e=32)
            q3 = wqnat.ap()[:, 2048 * par:2048 * par + 32 * nb].rearrange("p (b e) -> p b e", e=32)
            ebb = qa_eb.ap()[:, o:o + nb].unsqueeze(2).broadcast_to([128, nb, 32])
            S.wait_at("vector", "PE", S.c("PE"))
            S.op("vector", lambda E, a=v3, b=ebb: E.tensor_tensor(
                out=a, in0=a, in1=b, op=ALU.subtract), inc="VE")
            S.wait_at("vector", "VE", S.c("VE"))
            S.op("vector", lambda E, a=v3, b=ebb, q=q3: E.tensor_tensor(
                out=q, in0=a, in1=b, op=ALU.add), inc="VE")
            S.wait_at("tensor", "VE", S.c("VE"))

        def stage_T(i):
            kind, n = jobs[i]
            par = i % 2
            base = 2048 * par
            if kind == "fc":
                transpose8(lambda j, b=base: wqnat[:, b + 128 * j:b + 128 * j + 128],
                           lambda j0, jn, h_=n: wfcTq.ap().rearrange(
                               "p (j r) -> p j r", r=HID)[:, j0:j0 + jn,
                                                          128 * h_:128 * h_ + 128],
                           8, evac_eng="scalar", banks=[psE, psC])
            else:
                h_, hf = n // 2, n % 2
                transpose8(lambda j, b=base: wqnat[:, b + 128 * j:b + 128 * j + 128],
                           lambda j0, jn, hh=h_, f_=hf: wpjTq.ap().rearrange(
                               "p (j r) -> p j r", r=D)[:, 16 * f_ + j0:16 * f_ + j0 + jn,
                                                        128 * hh:128 * hh + 128],
                           16, evac_eng="scalar", banks=[psE, psC])

        NJOB = len(jobs)
        wf_totals[0] = job_dma(0)
        wf_totals[1] = job_dma(1)
        _wstate = {"i": 0}

        def w_step():
            i = _wstate["i"]
            if i > NJOB:
                return False
            _wstate["i"] = i + 1
            if i < NJOB:
                stage_R(i)
                stage_L(i)
            if i >= 1:
                stage_Q(i - 1)
                if i + 1 < NJOB:
                    S.wait_at("sync", "VE", S.c("VE"))
                    wf_totals[i + 1] = job_dma(i + 1)
            if i < NJOB:
                stage_E1(i)
                stage_X(i)
            if i >= 1:
                S.wait_at("tensor", "VE", S.c("VE"))
                stage_T(i - 1)
            if i < NJOB:
                stage_S(i)
            return True

        xs_ae = {}
        xs_view = xseq.rearrange("(c p) f -> p c f", p=128)
        NCH = 32
        xsqs = [xsq0, xsq1, xsq2]

        def xs_dma(ch):
            buf = xsqs[ch % 3]
            xs_sem = ["XS0", "XS1", "XS2"][ch % 3]
            S.op("gpsimd",
                 lambda E, b=buf, c0=ch * 2: E.dma_start(
                     out=b.ap().rearrange("p (t f) -> p t f", f=D),
                     in_=xs_view[:, c0:c0 + 2, :]),
                 inc=xs_sem, amt=16)
            return xs_sem, S.c(xs_sem)

        def xs_stats(ch, sem_tot):
            buf = xsqs[ch % 3]
            xs_sem, tot = sem_tot
            S.wait_at("vector", xs_sem, tot)
            for tt in range(2):
                t = ch * 2 + tt
                S.op("vector", lambda E, b=buf, o=tt: E.bn_stats(
                    out=bnscr[:, 0:6], in_=b[:, o * D:o * D + 512]), inc="VE")
                S.op("vector", lambda E, b=buf, o=tt: E.bn_stats(
                    out=bnscr[:, 6:12], in_=b[:, o * D + 512:o * D + 1024]), inc="VE")
                S.wait_at("vector", "VE", S.c("VE"))
                S.op("vector", lambda E, t_=t: E.bn_aggr(
                    out=mv_seq[:, 2 * t_:2 * t_ + 2], in_=bnscr[:, :]), inc="VE")
                S.wait_at("vector", "VE", S.c("VE"))
            mvs = mv_seq.ap().rearrange("p (t two) -> p t two", two=2)
            var_sl = mvs[:, ch * 2:(ch + 1) * 2, 1:2]
            mean_sl = mvs[:, ch * 2:(ch + 1) * 2, 0:1]
            lv = lnv_scr[:, ch * 2:(ch + 1) * 2]
            lv3 = lnv_scr.ap().rearrange("p (t one) -> p t one", one=1)[:, ch * 2:(ch + 1) * 2, :]
            S.op("vector", lambda E, v=var_sl, o=lv3: E.tensor_scalar(
                out=o, in0=v, scalar1=EPS, scalar2=None, op0=ALU.add), inc="VE")
            w("scalar", "VE")
            a_sl = a_seq[:, ch * 2:(ch + 1) * 2]
            S.op("scalar", lambda E, i=lv, o=a_sl: E.activation(
                out=o, in_=i, func=ACT.Ln), inc="AE")
            S.wait_at("scalar", "AE", S.c("AE"))
            S.op("scalar", lambda E, o=a_sl: E.activation(
                out=o, in_=o, func=ACT.Exp, scale=-0.5), inc="AE")
            xs_ae[ch] = S.c("AE")
            w("vector", "AE")
            am_sl3 = am_seq.ap().rearrange("p (t one) -> p t one", one=1)[:, ch * 2:(ch + 1) * 2, :]
            a_sl3 = a_seq.ap().rearrange("p (t one) -> p t one", one=1)[:, ch * 2:(ch + 1) * 2, :]
            S.op("vector", lambda E, a3=a_sl3, m3=mean_sl, o=am_sl3: E.tensor_tensor(
                out=o, in0=a3, in1=m3, op=ALU.mult), inc="VE")

        def xs_windows(ch):
            buf = xsqs[ch % 3]
            for tt in range(2):
                t = ch * 2 + tt
                ncols = 2 if t == 63 else 3
                pt = patt0 if t % 2 == 0 else patt1
                S.wait_at("gpsimd", "PE", S.c("PE"))
                S.op("gpsimd", lambda E, p=pt: E.memset(p[:, :], 0.0), inc="GE")
                S.wait_at("gpsimd", "GE", S.c("GE"))
                S.wait_at("gpsimd", "AE", xs_ae[ch])
                S.op("gpsimd", lambda E, t_=t, n=ncols, p=pt: E.tensor_scalar(
                    out=p[:, 2 * t_:2 * t_ + n],
                    in0=base3[:, 0:n], scalar1=a_seq[:, t_:t_ + 1],
                    scalar2=None, op0=ALU.mult), inc="GE")
                S.op("gpsimd", lambda E, t_=t: E.tensor_copy(
                    out=meanb[:, :], in_=mv_seq[:, 2 * t_:2 * t_ + 1]), inc="GE")
                S.wait_at("tensor", "GE", S.c("GE"))
                S.op("tensor", lambda E, b=buf, o=tt, p=pt: E.matmul(
                    psA[:, 0:512], p[:, :], b[:, o * D:o * D + 512],
                    start=False, stop=False))
                S.op("tensor", lambda E, b=buf, o=tt, p=pt: E.matmul(
                    psA[:, 512:1024], p[:, :], b[:, o * D + 512:o * D + 1024],
                    start=False, stop=False))
                S.op("tensor", lambda E, p=pt: E.matmul(
                    psD[:, 0:1], p[:, :], meanb[:, :],
                    start=False, stop=False), inc="PE")

        sems_ch = {}
        sems_ch[0] = xs_dma(0)
        sems_ch[1] = xs_dma(1)
        for ch in range(NCH + 1):
            if ch < NCH:
                xs_stats(ch, sems_ch[ch])
            if ch >= 1:
                xs_windows(ch - 1)
                if ch + 1 < NCH:
                    S.wait_at("gpsimd", "PE", S.c("PE"))
                    sems_ch[ch + 1] = xs_dma(ch + 1)
            w_step()
            if ch % 2 == 0:
                w_step()
        while w_step():
            pass

        S.op("tensor", lambda E: E.matmul(psA[:, 0:512], zeros_b[:], xsq0[:, 0:512],
                                          start=False, stop=True))
        S.op("tensor", lambda E: E.matmul(psA[:, 512:1024], zeros_b[:], xsq0[:, 512:1024],
                                          start=False, stop=True))
        S.op("tensor", lambda E: E.matmul(psD[:, 0:1], zeros_b[:], ones_col[:, :],
                                          start=False, stop=True), inc="PE")
        pe_summary_done = S.c("PE")

        S.wait_at("vector", "PE", pe_summary_done)
        S.op("vector", lambda E: E.tensor_copy(out=sum_bias[:], in_=psD[:, 0:1]), inc="VE")
        S.wait_at("vector", "VE", S.c("VE"))
        S.op("vector", lambda E: E.tensor_scalar(
            out=sum_bias[:], in0=sum_bias[:], scalar1=ilen[:, :], scalar2=-1.0,
            op0=ALU.mult, op1=ALU.mult), inc="VE")
        w("scalar", "VE")
        S.wait_at("scalar", "PE", pe_summary_done)
        S.op("scalar", lambda E: E.activation(
            out=summ_cm[:, :], in_=psA[:, :], func=ACT.Identity,
            bias=sum_bias[:, :], scale=ilen[:, :]), inc="AE")
        ae_summ = S.c("AE")
        S.wait_at("tensor", "AE", ae_summ)
        transpose8(lambda j: summ_cm[:, 128 * j:128 * j + 128],
                   lambda j0, jn: summ_fm.ap().rearrange(
                       "p (j c) -> p j c", c=128)[:, j0:j0 + jn, :], 8,
                   evac_eng="scalar", banks=[psE, psC])

        kdst = [psA[:, 0:128], psA[:, 512:640], psB[:, 0:128], psB[:, 512:640],
                psC[:, 0:128], psC[:, 512:640], psD[:, 0:128], psE[:, 0:128]]
        wk_v = wk_in.rearrange("(k p) f -> p k f", p=128)
        for hf in range(2):
            S.wait_at("gpsimd", "PE", S.c("PE"))
            S.op("gpsimd", lambda E, h=hf: E.dma_start(
                out=wbuf.ap().rearrange("p (k f) -> p k f", f=D),
                in_=wk_v[:, 4 * h:4 * h + 4, :]), inc="WBF", amt=16)
            S.wait_at("tensor", "WBF", S.c("WBF"))
            for m in range(8):
                for kl in range(4):
                    k = 4 * hf + kl
                    S.op("tensor", lambda E, m_=m, k_=k, kk=kl: E.matmul(
                        kdst[m_],
                        wbuf[:, kk * D + m_ * 128: kk * D + m_ * 128 + 128],
                        summ_fm[:, k_ * 128:k_ * 128 + 128],
                        start=(k_ == 0), stop=(k_ == 7)),
                        inc="PE" if (m == 7 and kl == 3) else None)
        w("scalar", "PE")
        for m in range(8):
            S.op("scalar", lambda E, m_=m: E.activation(
                out=kfm[:, m_ * 128:m_ * 128 + 128], in_=kdst[m_],
                func=ACT.Identity, bias=bkr[:, m_:m_ + 1]),
                inc="AE" if m == 7 else None)
        S.wait_at("tensor", "AE", S.c("AE"))

        wv_v = wv_in.rearrange("(k p) f -> p k f", p=128)
        for hf in range(2):
            S.wait_at("gpsimd", "PE", S.c("PE"))
            S.op("gpsimd", lambda E, h=hf: E.dma_start(
                out=wbuf.ap().rearrange("p (k f) -> p k f", f=D),
                in_=wv_v[:, 4 * h:4 * h + 4, :]), inc="WBF", amt=16)
            S.wait_at("tensor", "WBF", S.c("WBF"))
            for cl in (0, 512):
                vdst = psB[:, 0:512] if cl == 0 else psD[:, 0:512]
                for kl in range(4):
                    k = 4 * hf + kl
                    S.op("tensor", lambda E, k_=k, kk=kl, c=cl, dd=vdst: E.matmul(
                        dd, summ_fm[:, k_ * 128:k_ * 128 + 128],
                        wbuf[:, kk * D + c: kk * D + c + 512],
                        start=(k_ == 0), stop=False),
                        inc="PE" if (cl == 512 and kl == 3) else None)
            if hf == 1:
                S.wait_at("tensor", "LG", S.c("LG"))
                S.op("tensor", lambda E: E.matmul(
                    psB[:, 0:512], ones_row[0:1, :], bv_r[0:1, 0:512],
                    start=False, stop=True))
                S.op("tensor", lambda E: E.matmul(
                    psD[:, 0:512], ones_row[0:1, :], bv_r[0:1, 512:1024],
                    start=False, stop=True), inc="PE")
        w("scalar", "PE")
        S.op("scalar", lambda E: E.activation(out=vcm[:, 0:512], in_=psB[:, 0:512],
                                              func=ACT.Identity), inc="AE")
        S.op("scalar", lambda E: E.activation(out=vcm[:, 512:1024], in_=psD[:, 0:512],
                                              func=ACT.Identity), inc="AE")

        es_pro.close()
        es_ph1 = contextlib.ExitStack()
        sb1 = lambda name, shape, dt: es_ph1.enter_context(nc.sbuf_tensor(name, shape, dt))
        xring = sb1("xring", [128, 4 * D], F32)
        lnbuf = sb1("lnbuf", [128, D], BF16)
        ln1fm = sb1("ln1fm", [128, 8 * 512], BF16)
        qfm = sb1("qfm", [128, 8 * 512], BF16)
        ebuf = sb1("ebuf", [128, 2 * 16 * 128], BF16)
        xmid_t = sb1("xmid_t", [128, D], F32)
        mexp = sb1("mexp", [128, 2 * 16 * 128], BF16)
        ofm = ln1fm

        for e in ("sync", "gpsimd", "tensor", "vector", "scalar"):
            S.wait_at(e, "VE", S.c("VE"))
            S.wait_at(e, "AE", S.c("AE"))
            S.wait_at(e, "PE", S.c("PE"))

        st_count = 0
        msk_marks = {}
        exp_marks = {}

        def p1_loads_stats(s):
            S.wait_at("sync", "VE", S.c("VE"))
            for tt in range(4):
                t = 4 * s + tt
                slot = t % 4
                dma("sync", "XOW", xring[:, slot * D:(slot + 1) * D],
                    xown.rearrange("(t p) f -> p t f", p=128)[:, t, :])
            ld_xown = S.c("XOW")
            S.wait_at("vector", "XOW", ld_xown)
            for tt in range(4):
                t = 4 * s + tt
                slot = t % 4
                xo = xring[:, slot * D:(slot + 1) * D]
                S.op("vector", lambda E, x=xo: E.bn_stats(out=bnscr[:, 0:6], in_=x[:, 0:512]), inc="VE")
                S.op("vector", lambda E, x=xo: E.bn_stats(out=bnscr[:, 6:12], in_=x[:, 512:1024]), inc="VE")
                S.wait_at("vector", "VE", S.c("VE"))
                S.op("vector", lambda E, t_=t: E.bn_aggr(
                    out=mv_own[:, 2 * t_:2 * t_ + 2], in_=bnscr[:, :]), inc="VE")
                S.wait_at("vector", "VE", S.c("VE"))
            mvo = mv_own.ap().rearrange("p (t two) -> p t two", two=2)
            var_sl = mvo[:, 4 * s:4 * s + 4, 1:2]
            mean_sl = mvo[:, 4 * s:4 * s + 4, 0:1]
            lv3 = lnv_scr.ap().rearrange("p (t one) -> p t one", one=1)[:, 0:4, :]
            S.op("vector", lambda E, v=var_sl, o=lv3: E.tensor_scalar(
                out=o, in0=v, scalar1=EPS, scalar2=None, op0=ALU.add), inc="VE")
            w("scalar", "VE")
            a_sl = a_own[:, 4 * s:4 * s + 4]
            S.op("scalar", lambda E, o=a_sl: E.activation(
                out=o, in_=lnv_scr[:, 0:4], func=ACT.Ln), inc="AE")
            S.wait_at("scalar", "AE", S.c("AE"))
            S.op("scalar", lambda E, o=a_sl: E.activation(
                out=o, in_=o, func=ACT.Exp, scale=-0.5), inc="AE")
            w("vector", "AE")
            a3 = a_own.ap().rearrange("p (t one) -> p t one", one=1)[:, 4 * s:4 * s + 4, :]
            n3 = nam_own.ap().rearrange("p (t one) -> p t one", one=1)[:, 4 * s:4 * s + 4, :]
            S.op("vector", lambda E, a=a3, m=mean_sl, o=n3: E.tensor_tensor(
                out=o, in0=a, in1=m, op=ALU.mult), inc="VE")
            S.wait_at("vector", "VE", S.c("VE"))
            S.op("vector", lambda E, o=n3: E.tensor_scalar(
                out=nam_own[:, 4 * s:4 * s + 4], in0=nam_own[:, 4 * s:4 * s + 4],
                scalar1=-1.0, scalar2=None, op0=ALU.mult), inc="VE")

        def p1_lnq(s):
            S.wait_at("tensor", "VE", S.c("VE"))
            for tt in range(4):
                t = 4 * s + tt
                slot = t % 4
                xo = xring[:, slot * D:(slot + 1) * D]
                w("scalar", "VE")
                S.wait_at("scalar", "PE", S.c("PE"))
                S.op("scalar", lambda E, x=xo, t_=t: E.activation(
                    out=lnbuf[:, 0:D], in_=x, func=ACT.Identity,
                    scale=a_own[:, t_:t_ + 1], bias=nam_own[:, t_:t_ + 1]), inc="AE")
                S.wait_at("tensor", "AE", S.c("AE"))
                transpose8(lambda j: lnbuf[:, 128 * j:128 * j + 128],
                           lambda j0, jn, u=tt: ln1fm.ap().rearrange(
                               "p (j q) -> p j q", q=512)[:, j0:j0 + jn, 128 * u:128 * u + 128],
                           8, evac_eng="scalar", banks=[psE, psD])
            qdst = [psA[:, 0:512], psA[:, 512:1024], psB[:, 0:512], psB[:, 512:1024],
                    psC[:, 0:512], psC[:, 512:1024], psD[:, 0:512], psE[:, 0:512]]
            wq_v = wq_in.rearrange("(k p) f -> p k f", p=128)
            for hf in range(2):
                S.wait_at("gpsimd", "PE", S.c("PE"))
                S.op("gpsimd", lambda E, h=hf: E.dma_start(
                    out=wbuf.ap().rearrange("p (k f) -> p k f", f=D),
                    in_=wq_v[:, 4 * h:4 * h + 4, :]), inc="WBF", amt=16)
                S.wait_at("tensor", "WBF", S.c("WBF"))
                for m in range(8):
                    for kl in range(4):
                        k = 4 * hf + kl
                        S.op("tensor", lambda E, m_=m, k_=k, kk=kl: E.matmul(
                            qdst[m_], wbuf[:, kk * D + m_ * 128: kk * D + m_ * 128 + 128],
                            ln1fm[:, k_ * 512:(k_ + 1) * 512],
                            start=(k_ == 0), stop=(k_ == 7)),
                            inc="PE" if (m == 7 and kl == 3) else None)
            w("scalar", "PE")
            for m in range(8):
                S.op("scalar", lambda E, m_=m: E.activation(
                    out=qfm[:, m_ * 512:(m_ + 1) * 512], in_=qdst[m_],
                    func=ACT.Identity, bias=bqr[:, m_:m_ + 1]),
                    inc="AE" if m == 7 else None)
            S.wait_at("tensor", "AE", S.c("AE"))
            _evmark[id(psD)] = ("AE", S.c("AE"))
            _evmark[id(psE)] = ("AE", S.c("AE"))
            S.wait_at("gpsimd", "PE", S.c("PE"))
            wo_v = wo_in.rearrange("(k p) f -> p k f", p=128)
            S.op("gpsimd", lambda E: E.dma_start(
                out=wbuf.ap().rearrange("p (k f) -> p k f", f=D),
                in_=wo_v[:, 0:4, :]), inc="WBF", amt=16)

        def p1_scores(s, u, epar):
            t = 4 * s + u
            S.wait_at("sync", "GE", S.c("GE"))
            msem = "MSK0" if u % 2 == 0 else "MSK1"
            dma("sync", msem,
                mexp.ap().rearrange("p (s_ g c) -> p s_ g c", s_=2, c=128)[:, u % 2, :, :],
                mask_in.rearrange("(t p) c -> p t c", p=128)[:, t, :]
                .unsqueeze(1).broadcast_to([128, 16, 128]))
            msk_marks[t] = (msem, S.c(msem))
            S.wait_at("tensor", "AE", S.c("AE"))
            for m in range(8):
                qsl_e = qfm[0:64, m * 512 + 128 * u: m * 512 + 128 * u + 128]
                qsl_o = qfm[64:128, m * 512 + 128 * u: m * 512 + 128 * u + 128]
                ksl_e = kfm[0:64, m * 128:(m + 1) * 128]
                ksl_o = kfm[64:128, m * 128:(m + 1) * 128]
                dstA = psA[:, 128 * (m % 4):128 * (m % 4) + 128] if m < 4 else \
                    psB[:, 128 * (m % 4):128 * (m % 4) + 128]
                dstB = psA[:, 512 + 128 * (m % 4):512 + 128 * (m % 4) + 128] if m < 4 else \
                    psB[:, 512 + 128 * (m % 4):512 + 128 * (m % 4) + 128]
                S.op("tensor", lambda E, q=qsl_e, kk=ksl_e, d=dstA: E.matmul(
                    d, q, kk, start=True, stop=True, tile_position=(0, 0)))
                S.op("tensor", lambda E, q=qsl_o, kk=ksl_o, d=dstB: E.matmul(
                    d, q, kk, start=True, stop=True, tile_position=(64, 0)),
                    inc="PE" if m == 7 else None)
            w("scalar", "PE")
            S.wait_at("scalar", "PE", S.c("PE"))
            eb = 2048 * epar
            S.op("scalar", lambda E, o=eb: E.activation(
                out=ebuf[:, o:o + 1024], in_=psA[:, :], func=ACT.Exp, scale=0.125), inc="AE")
            S.op("scalar", lambda E, o=eb: E.activation(
                out=ebuf[:, o + 1024:o + 2048], in_=psB[:, :], func=ACT.Exp, scale=0.125), inc="AE")
            exp_marks[t] = S.c("AE")

        def p1_softmax_av(s, u, epar):
            t = 4 * s + u
            eb = 2048 * epar
            e3 = ebuf.ap()[:, eb:eb + 2048].rearrange("p (g c) -> p g c", c=128)
            m3 = mexp.ap().rearrange("p (s_ g c) -> p s_ g c", s_=2, c=128)[:, u % 2, :, :]
            S.wait_at("gpsimd", "AE", exp_marks[t])
            S.wait_at("gpsimd", msk_marks[t][0], msk_marks[t][1])
            S.op("gpsimd", lambda E, e=e3, mk=m3: E.tensor_tensor(
                out=e, in0=e, in1=mk, op=ALU.mult), inc="GE")
            w("vector", "GE")
            S.op("vector", lambda E, e=e3: E.tensor_reduce(
                out=ssum[:, :], in_=e, axis=AXX, op=ALU.add), inc="VE")
            S.wait_at("vector", "VE", S.c("VE"))
            S.op("vector", lambda E: E.reciprocal(out=srcp[:, :], in_=ssum[:, :]), inc="VE")
            S.wait_at("vector", "VE", S.c("VE"))
            rb = srcp.ap().unsqueeze(2).broadcast_to([128, 16, 128])
            S.op("vector", lambda E, e=e3, r=rb: E.tensor_tensor(
                out=e, in0=e, in1=r, op=ALU.mult), inc="VE")
            S.wait_at("tensor", "VE", S.c("VE"))
            transpose8(lambda j, o=eb: ebuf[:, o + 128 * j:o + 128 * j + 128],
                       lambda j0, jn, o=eb: ebuf.ap()[:, o:o + 2048].rearrange(
                           "p (j c) -> p j c", c=128)[:, j0:j0 + jn, :],
                       16, evac_eng="scalar", banks=[psE, psD])
            for m0 in range(0, 8, 4):
                for mm in range(4):
                    m = m0 + mm
                    g_e = (m // 4) * 8 + (m % 4)
                    g_o = (m // 4) * 8 + 4 + (m % 4)
                    h_e, h_o = 2 * m, 2 * m + 1
                    S.op("tensor", lambda E, g=g_e, h=h_e, c=128 * mm, o=eb: E.matmul(
                        psD[0:64, c:c + 128], vcm[:, 64 * h:64 * h + 64],
                        ebuf[:, o + 128 * g:o + 128 * g + 128], start=True, stop=True,
                        tile_position=(0, 0)))
                    S.op("tensor", lambda E, g=g_o, h=h_o, c=128 * mm, o=eb: E.matmul(
                        psD[64:128, c:c + 128], vcm[:, 64 * h:64 * h + 64],
                        ebuf[:, o + 128 * g:o + 128 * g + 128], start=True, stop=True,
                        tile_position=(0, 64)), inc="PE" if mm == 3 else None)
                S.wait_at("scalar", "PE", S.c("PE"))
                dst3 = ofm.ap().rearrange("p (m q) -> p m q", q=512)[
                    :, m0:m0 + 4, 128 * u:128 * u + 128]
                ps3 = psD.ap().rearrange("p (m c) -> p m c", c=128)
                S.op("scalar", lambda E, d=dst3, s_=ps3: E.activation(
                    out=d, in_=s_, func=ACT.Identity), inc="AE")
                S.wait_at("tensor", "AE", S.c("AE"))
                _evmark[id(psD)] = ("AE", S.c("AE"))

        def p1_wo_res(s):
            global_st = None
            odst = [(psA[:, 0:512], psA[:, 512:1024]), (psB[:, 0:512], psB[:, 512:1024]),
                    (psC[:, 0:512], psC[:, 512:1024]), (psD[:, 0:512], psE[:, 0:512])]
            wo_v = wo_in.rearrange("(k p) f -> p k f", p=128)
            for hf in range(2):
                if hf == 1:
                    S.wait_at("gpsimd", "PE", S.c("PE"))
                    S.op("gpsimd", lambda E: E.dma_start(
                        out=wbuf.ap().rearrange("p (k f) -> p k f", f=D),
                        in_=wo_v[:, 4:8, :]), inc="WBF", amt=16)
                S.wait_at("tensor", "WBF", S.c("WBF"))
                for u in range(4):
                    for half in range(2):
                        cl = half * 512
                        dd = odst[u][half]
                        for ml in range(4):
                            m = 4 * hf + ml
                            S.op("tensor", lambda E, m_=m, mm=ml, c=cl, u_=u, d=dd: E.matmul(
                                d, ofm[:, m_ * 512 + 128 * u_: m_ * 512 + 128 * u_ + 128],
                                wbuf[:, mm * D + c: mm * D + c + 512],
                                start=(m_ == 0), stop=False),
                                inc="PE" if (u == 3 and half == 1 and ml == 3) else None)
                        if hf == 1:
                            S.wait_at("tensor", "LG", S.c("LG"))
                            S.op("tensor", lambda E, c=cl, d=dd: E.matmul(
                                d, ones_row[0:1, :], bo_r[0:1, c:c + 512],
                                start=False, stop=True),
                                inc="PE" if (u == 3 and half == 1) else None)
            w("vector", "PE")
            for u in range(4):
                t = 4 * s + u
                slot = t % 4
                xo = xring[:, slot * D:(slot + 1) * D]
                S.op("vector", lambda E, d0=odst[u][0], x=xo: E.tensor_tensor(
                    out=xmid_t[:, 0:512], in0=d0, in1=x[:, 0:512], op=ALU.add), inc="VE")
                S.op("vector", lambda E, d1=odst[u][1], x=xo: E.tensor_tensor(
                    out=xmid_t[:, 512:1024], in0=d1, in1=x[:, 512:1024], op=ALU.add), inc="VE")
                S.wait_at("sync", "VE", S.c("VE"))
                S.op("sync", lambda E, t_=t: E.dma_start(
                    out=xmid_hbm.rearrange("(t p) f -> p t f", p=128)[:, t_, :],
                    in_=xmid_t[:, :]), inc="XM", amt=16)
                S.wait_at("vector", "XM", S.c("XM"))
            _evmark[id(psD)] = ("VE", S.c("VE"))
            _evmark[id(psE)] = ("VE", S.c("VE"))

        p1_next_loads_stats = p1_loads_stats

        p1_loads_stats(0)
        for s in range(NSUP):
            p1_lnq(s)
            p1_scores(s, 0, 0)
            for u in range(4):
                if u + 1 < 4:
                    p1_scores(s, u + 1, (u + 1) % 2)
                p1_softmax_av(s, u, u % 2)
            p1_wo_res(s)
            if s + 1 < NSUP:
                p1_next_loads_stats(s + 1)
        es_ph1.close()
        es_ph2 = contextlib.ExitStack()
        sb2 = lambda name, shape, dt: es_ph2.enter_context(nc.sbuf_tensor(name, shape, dt))
        x2r0 = sb2("x2r0", [128, D], F32)
        x2r1 = sb2("x2r1", [128, D], F32)
        xln2 = sb2("xln2", [128, D], F32)
        xqb = sb2("xqb", [128, D], BF16)
        xqfm = sb2("xqfm", [128, 2 * 8 * 128], BF16)
        hbuf = sb2("hbuf", [128, 4 * D], F32)
        hqb = sb2("hqb", [128, D], BF16)
        hqT = sb2("hqT", [128, 2 * 8 * 128], BF16)
        bfc_r = sb2("bfc_r", [1, HID], BF16)
        bpj_r = sb2("bpj_r", [1, D], BF16)
        for e in ("sync", "gpsimd", "tensor", "vector", "scalar"):
            S.wait_at(e, "XM", 16 * NT)
        dma("gpsimd", "LG", bfc_r[:], bfc_in)
        dma("gpsimd", "LG", bpj_r[:], bpj_in)
        lg_bias2 = S.c("LG")
        S.wait_at("tensor", "LG", lg_bias2)

        def p2_front(t):
            buf = x2r0 if t % 2 == 0 else x2r1
            st_sem = "ST0" if t % 2 == 0 else "ST1"
            S.wait_at("sync", st_sem, S.c(st_sem))
            x2_sem = "X20" if t % 2 == 0 else "X21"
            dma("sync", x2_sem, buf[:, :], xmid_hbm.rearrange("(t p) f -> p t f", p=128)[:, t, :])
            S.wait_at("vector", x2_sem, S.c(x2_sem))
            S.op("vector", lambda E, b=buf: E.bn_stats(out=bnscr[:, 0:6], in_=b[:, 0:512]), inc="VE")
            S.op("vector", lambda E, b=buf: E.bn_stats(out=bnscr[:, 6:12], in_=b[:, 512:1024]), inc="VE")
            S.wait_at("vector", "VE", S.c("VE"))
            S.op("vector", lambda E: E.bn_aggr(out=mv_own[:, 0:2], in_=bnscr[:, :]), inc="VE")
            S.wait_at("vector", "VE", S.c("VE"))
            S.op("vector", lambda E: E.tensor_scalar(
                out=lnv2[:, 0:1], in0=mv_own[:, 1:2], scalar1=EPS, scalar2=None,
                op0=ALU.add), inc="VE")
            w("scalar", "VE")
            S.op("scalar", lambda E: E.activation(out=a2c[:], in_=lnv2[:, 0:1], func=ACT.Ln), inc="AE")
            S.wait_at("scalar", "AE", S.c("AE"))
            S.op("scalar", lambda E: E.activation(out=a2c[:], in_=a2c[:], func=ACT.Exp,
                                                  scale=-0.5), inc="AE")
            w("vector", "AE")
            S.op("vector", lambda E: E.tensor_tensor(out=nam2c[:], in0=a2c[:],
                                                     in1=mv_own[:, 0:1], op=ALU.mult), inc="VE")
            S.wait_at("vector", "VE", S.c("VE"))
            S.op("vector", lambda E: E.tensor_scalar(out=nam2c[:], in0=nam2c[:], scalar1=-1.0,
                                                     scalar2=None, op0=ALU.mult), inc="VE")
            S.wait_at("vector", "VE", S.c("VE"))
            S.op("vector", lambda E, b=buf: E.tensor_scalar(
                out=xln2[:, :], in0=b[:, :], scalar1=a2c[:, :], scalar2=nam2c[:, :],
                op0=ALU.mult, op1=ALU.add), inc="VE")
            S.wait_at("vector", "VE", S.c("VE"))
            o = 128
            v3 = xln2.ap().rearrange("p (b e) -> p b e", e=32)
            S.op("vector", lambda E, v=v3, o_=o: E.tensor_reduce(
                out=qa_amax[:, o_:o_ + 32], in_=v, axis=AXX, op=ALU.max,
                apply_absolute_value=True), inc="VE")
            S.wait_at("vector", "VE", S.c("VE"))
            quant_consts(32, qa_amax[:, o:o + 32], qa_e[:, o:o + 32], qa_s0[:, o:o + 32],
                         qa_s[:, o:o + 32], qa_m[:, o:o + 32], qa_eb[:, o:o + 32])
            ebb = qa_eb.ap()[:, o:o + 32].unsqueeze(2).broadcast_to([128, 32, 32])
            q3 = xqb.ap().rearrange("p (b e) -> p b e", e=32)
            S.wait_at("vector", "PE", S.c("PE"))
            quant_apply(v3, v3, q3, ebb)
            S.wait_at("tensor", "VE", S.c("VE"))
            xpar = t % 2
            transpose8(lambda j: xqb[:, 128 * j:128 * j + 128],
                       lambda j0, jn, xp=xpar: xqfm.ap().rearrange(
                           "p (j c) -> p j c", c=128)[:, 8 * xp + j0:8 * xp + j0 + jn, :],
                       8, evac_eng="scalar", banks=[psE, psD])

        def p2_fc1(t, g):
            xpar = t % 2
            dst = psA if g % 2 == 0 else psB
            for k in range(8):
                S.op("tensor", lambda E, k_=k, g_=g, d=dst, xp=xpar: E.matmul(
                    d[:, 0:512], xqfm[:, 1024 * xp + 128 * k_:1024 * xp + 128 * k_ + 128],
                    wfcTq[:, k_ * HID + g_ * D: k_ * HID + g_ * D + 512],
                    start=(k_ == 0), stop=False))
                S.op("tensor", lambda E, k_=k, g_=g, d=dst, xp=xpar: E.matmul(
                    d[:, 512:1024], xqfm[:, 1024 * xp + 128 * k_:1024 * xp + 128 * k_ + 128],
                    wfcTq[:, k_ * HID + g_ * D + 512: k_ * HID + g_ * D + 1024],
                    start=(k_ == 0), stop=False))
            S.op("tensor", lambda E, g_=g, d=dst: E.matmul(
                d[:, 0:512], ones_row[0:1, :], bfc_r[0:1, g_ * D:g_ * D + 512],
                start=False, stop=True))
            S.op("tensor", lambda E, g_=g, d=dst: E.matmul(
                d[:, 512:1024], ones_row[0:1, :], bfc_r[0:1, g_ * D + 512:g_ * D + 1024],
                start=False, stop=True), inc="PE")

        ae_gmark = {}

        def p2_gelu(t, g):
            dst = psA if g % 2 == 0 else psB
            w("scalar", "PE")
            S.wait_at("scalar", "VE", S.c("VE"))
            S.op("scalar", lambda E, d=dst, s_=g: E.activation(
                out=hbuf[:, s_ * D:(s_ + 1) * D], in_=d[:, :], func=ACT.Gelu), inc="AE")
            ae_gmark[(t, g)] = S.c("AE")

        ve_rmark = {}

        def p2_back(t, do_front):
            p2_gelu(t, 0)
            if (t, -1) in ae_gmark:
                S.wait_at("tensor", "AE", ae_gmark[(t, -1)])
            p2_fc1(t, 1)
            p2_gelu(t, 1)
            S.wait_at("tensor", "AE", ae_gmark[(t, 0)])
            p2_fc1(t, 2)
            p2_gelu(t, 2)
            S.wait_at("tensor", "AE", ae_gmark[(t, 1)])
            p2_fc1(t, 3)
            if do_front:
                p2_front(t + 1)
            p2_gelu(t, 3)
            ae_g3 = ae_gmark[(t, 3)]
            S.wait_at("vector", "AE", ae_g3)
            hh = hbuf.ap().rearrange("p (b e) -> p b e", e=32)
            S.op("vector", lambda E, v=hh: E.tensor_reduce(
                out=qa_amax[:, 0:128], in_=v, axis=AXX, op=ALU.max,
                apply_absolute_value=True), inc="VE")
            ve_red = S.c("VE")
            S.wait_at("vector", "VE", ve_red)
            quant_consts(128, qa_amax[:, 0:128], qa_e[:, 0:128], qa_s0[:, 0:128],
                         qa_s[:, 0:128], qa_m[:, 0:128], qa_eb[:, 0:128])
            for g in range(4):
                gs = g % 2
                hg = hbuf.ap()[:, g * D:(g + 1) * D].rearrange("p (b e) -> p b e", e=32)
                ebg = qa_eb.ap()[:, 32 * g:32 * g + 32].unsqueeze(2).broadcast_to([128, 32, 32])
                qg = hqb.ap().rearrange("p (b e) -> p b e", e=32)
                S.wait_at("vector", "PE", S.c("PE"))
                quant_apply(hg, hg, qg, ebg)
                S.wait_at("tensor", "VE", S.c("VE"))
                if g == 0 and (t - 1) in ve_rmark:
                    S.wait_at("tensor", "VE", ve_rmark[t - 1])
                transpose8(lambda j: hqb[:, 128 * j:128 * j + 128],
                           lambda j0, jn, s_=gs: hqT.ap().rearrange(
                               "p (j c) -> p j c", c=128)[:, 8 * s_ + j0:8 * s_ + j0 + jn, :],
                           8, evac_eng="scalar", banks=[psE, psD])
                for half in range(2):
                    cl = half * 512
                    for cj in range(8):
                        c = 8 * g + cj
                        S.op("tensor", lambda E, c_=c, cc=cl, cjj=cj, s_=gs: E.matmul(
                            psC[:, cc:cc + 512],
                            hqT[:, 1024 * s_ + 128 * cjj:1024 * s_ + 128 * cjj + 128],
                            wpjTq[:, c_ * D + cc: c_ * D + cc + 512],
                            start=(c_ == 0), stop=False))
                    if g == 3:
                        S.op("tensor", lambda E, cc=cl: E.matmul(
                            psC[:, cc:cc + 512], ones_row[0:1, :], bpj_r[0:1, cc:cc + 512],
                            start=False, stop=True),
                            inc="PE" if half == 1 else None)
            w("vector", "PE")
            buf = x2r0 if t % 2 == 0 else x2r1
            S.op("vector", lambda E, b=buf: E.tensor_tensor(
                out=b[:, :], in0=psC[:, :], in1=b[:, :], op=ALU.add), inc="VE")
            ve_rmark[t] = S.c("VE")
            S.wait_at("sync", "VE", ve_rmark[t])
            st_sem = "ST0" if t % 2 == 0 else "ST1"
            S.op("sync", lambda E, t_=t, b=buf: E.dma_start(
                out=y_out.rearrange("(t p) f -> p t f", p=128)[:, t_, :],
                in_=b[:, :]), inc=st_sem, amt=16)

        p2_front(0)
        for t in range(NT):
            if t >= 1:
                S.wait_at("tensor", "AE", ae_gmark[(t - 1, 2)])
                ae_gmark[(t, -1)] = ae_gmark[(t - 1, 3)]
            p2_fc1(t, 0)
            p2_back(t, t + 1 < NT)

        S.wait_at("sync", "ST0", 16 * (NT // 2))
        S.wait_at("sync", "ST1", 16 * (NT // 2))

        es_ph2.close()
        def runner(eng_name):
            def run(E):
                for item in S.q[eng_name]:
                    if item[0] == "w":
                        E.wait_ge(sems[item[1]], item[2])
                    else:
                        ins = item[1](E)
                        if item[2] is not None:
                            ins.then_inc(sems[item[2]], item[3])
            return run

        block.sync(runner("sync"))
        block.gpsimd(runner("gpsimd"))
        block.tensor(runner("tensor"))
        block.vector(runner("vector"))
        block.scalar(runner("scalar"))

    return nc


_NC_CACHE = None


def _get_nc():
    global _NC_CACHE
    if _NC_CACHE is None:
        _NC_CACHE = build_nc()
    return _NC_CACHE


def kernel(L0_attn_x, L0_attn_Wq, L0_attn_bq, L0_attn_Wk, L0_attn_bk, L0_attn_Wv,
           L0_attn_bv, L0_attn_Wo, L0_attn_bo, L0_ffn_w_fc, L0_ffn_b_fc,
           L0_ffn_w_proj, L0_ffn_b_proj):
    x = np.ascontiguousarray(np.asarray(L0_attn_x, dtype=np.float32))
    B = x.shape[0]
    wq = np.ascontiguousarray(np.asarray(L0_attn_Wq, dtype=np.float32))
    wk = np.ascontiguousarray(np.asarray(L0_attn_Wk, dtype=np.float32))
    wv = np.ascontiguousarray(np.asarray(L0_attn_Wv, dtype=np.float32))
    wo = np.ascontiguousarray(np.asarray(L0_attn_Wo, dtype=np.float32))
    bq = np.asarray(L0_attn_bq, dtype=np.float32)
    bk = np.asarray(L0_attn_bk, dtype=np.float32)
    bv = np.asarray(L0_attn_bv, dtype=np.float32).reshape(1, D)
    bo = np.asarray(L0_attn_bo, dtype=np.float32).reshape(1, D)
    wfc = np.ascontiguousarray(np.asarray(L0_ffn_w_fc, dtype=np.float32))
    bfc = np.asarray(L0_ffn_b_fc, dtype=np.float32).reshape(1, HID)
    wpj = np.ascontiguousarray(np.asarray(L0_ffn_w_proj, dtype=np.float32))
    bpj = np.asarray(L0_ffn_b_proj, dtype=np.float32).reshape(1, D)

    bqr = np.ascontiguousarray(bq.reshape(8, 128).T)
    bkr = np.ascontiguousarray(bk.reshape(8, 128).T)

    in_maps = []
    for c in range(8):
        b = c // 4
        J = c % 4
        pos = 2048 * J + np.arange(2048)[:, None]
        ends = 64 * np.arange(128)[None, :]
        mask01 = (ends <= pos).astype(ml_dtypes.bfloat16)
        in_maps.append({
            "xseq": x[b],
            "xown": np.ascontiguousarray(x[b, 2048 * J:2048 * (J + 1)]),
            "wq_in": wq, "wk_in": wk, "wv_in": wv, "wo_in": wo,
            "bqr_in": bqr, "bkr_in": bkr,
            "bv_in": bv, "bo_in": bo, "bfc_in": bfc, "bpj_in": bpj,
            "wfc_in": wfc, "wpj_in": wpj,
            "mask_in": mask01,
        })
    nc = _get_nc()
    res = run_bass_kernel_spmd(nc, in_maps, core_ids=list(range(8)))
    out = np.empty((B, T_SEQ, D), dtype=np.float32)
    for c in range(8):
        b, J = c // 4, c % 4
        out[b, 2048 * J:2048 * (J + 1)] = res.results[c]["y"]
    return out


# version 08
# speedup vs baseline: 1.3939x; 1.0222x over previous
import sys
import contextlib

import numpy as np
import ml_dtypes

sys.path.insert(0, "/opt/trn_rl_repo")
import concourse.bass as bass
import concourse.mybir as mybir
from concourse.bass_utils import run_bass_kernel_spmd

F32 = mybir.dt.float32
BF16 = mybir.dt.bfloat16
ALU = mybir.AluOpType
ACT = mybir.ActivationFunctionType
AXX = mybir.AxisListType.X

LN2 = float(np.log(2.0))
K1 = float(3 * 2**22)
K2 = float(3 * 2**51)
EPS = 1e-5

T_SEQ = 8192
D = 1024
HID = 4096
TOK = 2048
NT = 16
NSUP = 4
NSEQT = 64
NC = 128

ENGS = ("sync", "gpsimd", "tensor", "vector", "scalar")


class Sched:

    def __init__(self):
        self.q = {e: [] for e in ENGS}
        self.cnt = {}

    def wait(self, eng, sem_name):
        v = self.cnt.get(sem_name, 0)
        if v == 0:
            return
        self.q[eng].append(("w", sem_name, v))

    def wait_at(self, eng, sem_name, v):
        if v > 0:
            self.q[eng].append(("w", sem_name, v))

    def op(self, eng, fn, inc=None, amt=1):
        self.q[eng].append(("o", fn, inc, amt))
        if inc is not None:
            self.cnt[inc] = self.cnt.get(inc, 0) + amt

    def c(self, sem_name):
        return self.cnt.get(sem_name, 0)


def build_nc():
    nc = bass.Bass("TRN2", target_bir_lowering=False)

    xseq = nc.dram_tensor("xseq", [T_SEQ, D], F32, kind="ExternalInput").ap()
    xown = nc.dram_tensor("xown", [TOK, D], F32, kind="ExternalInput").ap()
    wq_in = nc.dram_tensor("wq_in", [D, D], F32, kind="ExternalInput").ap()
    wk_in = nc.dram_tensor("wk_in", [D, D], F32, kind="ExternalInput").ap()
    wv_in = nc.dram_tensor("wv_in", [D, D], F32, kind="ExternalInput").ap()
    wo_in = nc.dram_tensor("wo_in", [D, D], F32, kind="ExternalInput").ap()
    bqr_in = nc.dram_tensor("bqr_in", [128, 8], F32, kind="ExternalInput").ap()
    bkr_in = nc.dram_tensor("bkr_in", [128, 8], F32, kind="ExternalInput").ap()
    bv_in = nc.dram_tensor("bv_in", [1, D], F32, kind="ExternalInput").ap()
    bo_in = nc.dram_tensor("bo_in", [1, D], F32, kind="ExternalInput").ap()
    bfc_in = nc.dram_tensor("bfc_in", [1, HID], F32, kind="ExternalInput").ap()
    bpj_in = nc.dram_tensor("bpj_in", [1, D], F32, kind="ExternalInput").ap()
    wfc_in = nc.dram_tensor("wfc_in", [HID, D], F32, kind="ExternalInput").ap()
    wpj_in = nc.dram_tensor("wpj_in", [D, HID], F32, kind="ExternalInput").ap()
    mask_in = nc.dram_tensor("mask_in", [TOK, NC], BF16, kind="ExternalInput").ap()
    y_out = nc.dram_tensor("y", [TOK, D], F32, kind="ExternalOutput").ap()
    xmid_hbm = nc.dram_tensor("xmid_scr", [TOK, D], F32).ap()

    ident_h = nc.inline_tensor(np.eye(128, dtype=ml_dtypes.bfloat16), name="ident_h")
    base3_np = np.zeros((128, 3), dtype=ml_dtypes.bfloat16)
    base3_np[0, 0] = 1.0
    base3_np[1:65, 1] = 1.0
    base3_np[65:128, 2] = 1.0
    base3_h = nc.inline_tensor(base3_np, name="base3_h")
    ilen_np = np.full((128, 1), 1.0 / 64.0, dtype=np.float32)
    ilen_np[0, 0] = 1.0
    ilen_h = nc.inline_tensor(ilen_np, name="ilen_h")

    es = contextlib.ExitStack()
    S = Sched()

    with es:
        sb = lambda name, shape, dt: es.enter_context(nc.sbuf_tensor(name, shape, dt))
        psb = lambda name, shape: es.enter_context(nc.psum_tensor(name, shape, F32))

        ident = sb("ident", [128, 128], BF16)
        base3 = sb("base3", [128, 3], BF16)
        ilen = sb("ilen", [128, 1], F32)
        zeros_b = sb("zeros_b", [128, 128], BF16)
        ones_col = sb("ones_col", [128, 1], BF16)
        ones_row = sb("ones_row", [1, 128], BF16)
        bqr = sb("bqr", [128, 8], F32)
        bkr = sb("bkr", [128, 8], F32)

        Dt = sb("Dt", [128, 1], F32)
        DK2pp = sb("DK2pp", [128, 1], F32)
        EB1pp = sb("EB1pp", [128, 1], F32)
        TINYpp = sb("TINYpp", [128, 1], F32)

        bnscr = sb("bnscr", [128, 12], F32)
        mv_seq = sb("mv_seq", [128, 2 * NSEQT], F32)
        a_seq = sb("a_seq", [128, NSEQT], F32)
        am_seq = sb("am_seq", [128, NSEQT], F32)
        lnv_scr = sb("lnv_scr", [128, NSEQT], F32)
        mv_own = sb("mv_own", [128, 2 * NT], F32)
        a_own = sb("a_own", [128, NT], F32)
        nam_own = sb("nam_own", [128, NT], F32)

        summ_cm = sb("summ_cm", [128, D], BF16)
        summ_fm = sb("summ_fm", [128, 8 * 128], BF16)
        kfm = sb("kfm", [128, 8 * 128], BF16)
        vcm = sb("vcm", [128, D], BF16)
        sum_bias = sb("sum_bias", [128, 1], F32)

        wfcTq = sb("wfcTq", [128, 8 * HID], BF16)
        wpjTq = sb("wpjTq", [128, 32 * D], BF16)

        wbuf = sb("wbuf", [128, 4 * D], BF16)

        qa_amax = sb("qa_amax", [128, 160], F32)
        qa_e = sb("qa_e", [128, 160], F32)
        qa_s0 = sb("qa_s0", [128, 160], F32)
        qa_s = sb("qa_s", [128, 160], F32)
        qa_m = sb("qa_m", [128, 160], F32)
        qa_eb = sb("qa_eb", [128, 160], F32)

        a2c = sb("a2c", [128, 1], F32)
        nam2c = sb("nam2c", [128, 1], F32)
        lnv2 = sb("lnv2", [128, 1], F32)
        bo_r = sb("bo_r", [1, D], BF16)
        patt0 = sb("patt0", [128, 128], BF16)
        patt1 = sb("patt1", [128, 128], BF16)
        meanb = sb("meanb", [128, 1], BF16)
        ssum = sb("ssum", [128, 16], F32)
        srcp = sb("srcp", [128, 16], F32)

        es_pro = contextlib.ExitStack()
        sbp = lambda name, shape, dt: es_pro.enter_context(nc.sbuf_tensor(name, shape, dt))
        bv_r = sbp("bv_r", [1, D], BF16)
        wstr0 = sbp("wstr0", [128, 2048], F32)
        wstr1 = sbp("wstr1", [128, 2048], F32)
        wqnat = sbp("wqnat", [128, 4096], BF16)
        xsq0 = sbp("xsq0", [128, 2 * D], BF16)
        xsq1 = sbp("xsq1", [128, 2 * D], BF16)
        xsq2 = sbp("xsq2", [128, 2 * D], BF16)

        psA = psb("psA", [128, 1024])
        psB = psb("psB", [128, 1024])
        psC = psb("psC", [128, 1024])
        psD = psb("psD", [128, 512])
        psE = psb("psE", [128, 512])

        sem_names = ["LD", "LG", "ST0", "ST1", "PE", "VE", "AE", "GE", "XM",
                     "XS0", "XS1", "WF0", "WF1", "X20", "X21", "XOW", "WBF", "MSK0", "MSK1", "XS2"]
        sems = {n: es.enter_context(nc.semaphore("s" + n)) for n in sem_names}
        block = es.enter_context(nc.Block())

        def w(eng, *names):
            for n in names:
                S.wait(eng, n)

        def dma(eng, sem_name, out, in_):
            S.op(eng, lambda E, o=out, i=in_: E.dma_start(out=o, in_=i), inc=sem_name, amt=16)

        def quant_consts(nb, amax_ap, e_ap, s0_ap, s_ap, m_ap, eb_ap):
            w("scalar", "VE")
            S.op("scalar", lambda E, a=amax_ap, e=e_ap: E.activation(
                out=e, in_=a, func=ACT.Ln, bias=TINYpp[:, :], scale=1.0), inc="AE")
            w("vector", "AE")
            S.op("vector", lambda E, e=e_ap: E.tensor_scalar(
                out=e, in0=e, scalar1=float(1.0 / LN2), scalar2=float(1e-11 - 0.5),
                op0=ALU.mult, op1=ALU.add), inc="VE")
            S.wait_at("vector", "VE", S.c("VE"))
            S.op("vector", lambda E, e=e_ap: E.tensor_scalar(
                out=e, in0=e, scalar1=DK2pp[:, :], scalar2=EB1pp[:, :],
                op0=ALU.add, op1=ALU.subtract), inc="VE")
            S.wait_at("vector", "VE", S.c("VE"))
            S.op("vector", lambda E, e=e_ap: E.tensor_scalar(
                out=e, in0=e, scalar1=K1, scalar2=3.0,
                op0=ALU.subtract, op1=ALU.subtract), inc="VE")
            w("scalar", "VE")
            S.op("scalar", lambda E, e=e_ap, s0=s0_ap: E.activation(
                out=s0, in_=e, func=ACT.Exp, scale=LN2), inc="AE")
            w("vector", "AE")
            S.op("vector", lambda E, s0=s0_ap, m=m_ap: E.tensor_scalar(
                out=m, in0=s0, scalar1=K1, scalar2=None, op0=ALU.mult), inc="VE")
            S.wait_at("vector", "VE", S.c("VE"))
            S.op("vector", lambda E, s0=s0_ap, s=s_ap, m=m_ap: E.tensor_tensor(
                out=s, in0=s0, in1=m, op=ALU.add), inc="VE")
            S.wait_at("vector", "VE", S.c("VE"))
            S.op("vector", lambda E, s=s_ap, m=m_ap: E.tensor_tensor(
                out=s, in0=s, in1=m, op=ALU.subtract), inc="VE")
            S.wait_at("vector", "VE", S.c("VE"))
            S.op("vector", lambda E, s=s_ap, m=m_ap: E.tensor_scalar(
                out=m, in0=s, scalar1=K1, scalar2=None, op0=ALU.mult), inc="VE")
            S.op("vector", lambda E, s=s_ap, eb=eb_ap: E.tensor_scalar(
                out=eb, in0=s, scalar1=K2, scalar2=Dt[:, :], op0=ALU.mult,
                op1=ALU.mult), inc="VE")
            S.wait_at("vector", "VE", S.c("VE"))
            S.op("vector", lambda E, eb=eb_ap, m=m_ap: E.tensor_tensor(
                out=eb, in0=eb, in1=m, op=ALU.subtract), inc="VE")
            S.wait_at("vector", "VE", S.c("VE"))

        def quant_apply(v_ap, u_ap, q_ap, eb_bc):
            S.op("vector", lambda E: E.tensor_tensor(out=u_ap, in0=v_ap, in1=eb_bc,
                                                     op=ALU.subtract), inc="VE")
            S.wait_at("vector", "VE", S.c("VE"))
            S.op("vector", lambda E: E.tensor_tensor(out=q_ap, in0=u_ap, in1=eb_bc,
                                                     op=ALU.add), inc="VE")
            S.wait_at("vector", "VE", S.c("VE"))

        _evmark = {}

        def transpose8(src_tile_fn, dst0_ap_fn, nchunk, evac_eng="scalar", bank=None,
                       banks=None):
            bks = banks if banks is not None else [bank if bank is not None else psE]
            sem = "VE" if evac_eng == "vector" else "AE"
            for gi, j0 in enumerate(range(0, nchunk, 4)):
                bk = bks[gi % len(bks)]
                mk = _evmark.get(id(bk))
                if mk is not None:
                    S.wait_at("tensor", mk[0], mk[1])
                jn = min(4, nchunk - j0)
                for j in range(j0, j0 + jn):
                    S.op("tensor", lambda E, s=src_tile_fn(j), d=bk[:, 128 * (j - j0):128 * (j - j0) + 128]:
                         E.matmul(d, s, ident[:, :], start=True, stop=True),
                         inc="PE" if j == j0 + jn - 1 else None)
                w(evac_eng, "PE")
                ps3 = bk[:, 0:128 * jn].rearrange("p (j c) -> p j c", c=128)
                dst3 = dst0_ap_fn(j0, jn)
                if evac_eng == "vector":
                    S.op("vector", lambda E, s=ps3, d=dst3: E.tensor_copy(out=d, in_=s), inc="VE")
                else:
                    S.op("scalar", lambda E, s=ps3, d=dst3: E.activation(
                        out=d, in_=s, func=ACT.Identity), inc="AE")
                _evmark[id(bk)] = (sem, S.c(sem))
            S.wait(("tensor"), sem)

        def strided_dst(base_ap_fn):
            def f(j0, jn):
                a0 = base_ap_fn(j0)
                if jn == 1:
                    return a0.unsqueeze(1)
                a1 = base_ap_fn(j0 + 1)
                raise RuntimeError("use explicit dst fns")
            return f

        dma("sync", "LD", ident[:], ident_h[:])
        dma("sync", "LD", base3[:], base3_h[:])
        dma("sync", "LD", ilen[:], ilen_h[:])
        dma("sync", "LD", bqr[:], bqr_in)
        dma("sync", "LD", bkr[:], bkr_in)
        dma("gpsimd", "LG", bv_r[:], bv_in)
        dma("gpsimd", "LG", bo_r[:], bo_in)

        S.op("vector", lambda E: E.memset(zeros_b[:], 0.0), inc="VE")
        S.op("vector", lambda E: E.memset(ones_col[:], 1.0), inc="VE")
        S.op("vector", lambda E: E.memset(ones_row[:], 1.0), inc="VE")
        S.op("vector", lambda E: E.memset(Dt[:], 1.0), inc="VE")
        S.op("vector", lambda E: E.memset(TINYpp[:], float(2.0**-100)), inc="VE")
        S.wait_at("vector", "VE", S.c("VE"))
        S.op("vector", lambda E: E.tensor_scalar(out=Dt[:], in0=Dt[:], scalar1=K2,
                                                 scalar2=-K2, op0=ALU.add, op1=ALU.add), inc="VE")
        S.wait_at("vector", "VE", S.c("VE"))
        S.op("vector", lambda E: E.tensor_scalar(out=DK2pp[:], in0=Dt[:], scalar1=K2,
                                                 scalar2=None, op0=ALU.mult), inc="VE")
        S.wait_at("vector", "VE", S.c("VE"))
        S.op("vector", lambda E: E.tensor_scalar(out=EB1pp[:], in0=DK2pp[:], scalar1=K1,
                                                 scalar2=None, op0=ALU.subtract), inc="VE")
        S.wait_at("vector", "VE", S.c("VE"))

        w("tensor", "LD", "VE")
        S.wait_at("tensor", "XS0", 16)
        S.op("tensor", lambda E: E.matmul(psA[:, 0:512], zeros_b[:], xsq0[:, 0:512],
                                          start=True, stop=False))
        S.op("tensor", lambda E: E.matmul(psA[:, 512:1024], zeros_b[:], xsq0[:, 512:1024],
                                          start=True, stop=False))
        S.op("tensor", lambda E: E.matmul(psD[:, 0:1], zeros_b[:], ones_col[:],
                                          start=True, stop=False), inc="PE")

        wfc_v = wfc_in.rearrange("(n p) f -> p n f", p=128)
        wpj_v = wpj_in.rearrange("(n p) f -> p n f", p=128)
        jobs = [("fc", n) for n in range(32)] + [("pj", q) for q in range(16)]

        def job_dma(i):
            kind, n = jobs[i]
            par = i % 2
            buf = wstr0 if par == 0 else wstr1
            wf_sem = "WF0" if par == 0 else "WF1"
            if kind == "fc":
                dma("sync", wf_sem, buf[:, 0:D], wfc_v[:, n, :])
            else:
                dma("sync", wf_sem, buf[:, :], wpj_v[:, n // 2, 2048 * (n % 2):2048 * (n % 2) + 2048])
            return S.c(wf_sem)

        def job_nb(i):
            return 32 if jobs[i][0] == "fc" else 64

        wf_totals = {}
        def stage_R(i):
            par, nb = i % 2, job_nb(i)
            buf = wstr0 if par == 0 else wstr1
            o = 64 * par
            S.wait_at("vector", "WF0" if par == 0 else "WF1", wf_totals[i])
            v3 = buf.ap()[:, 0:32 * nb].rearrange("p (b e) -> p b e", e=32)
            S.op("vector", lambda E, v=v3, o_=o, nb_=nb: E.tensor_reduce(
                out=qa_amax[:, o_:o_ + nb_], in_=v, axis=AXX, op=ALU.max,
                apply_absolute_value=True), inc="VE")

        def stage_L(i):
            par, nb = i % 2, job_nb(i)
            o = 64 * par
            w("scalar", "VE")
            S.op("scalar", lambda E, o_=o, nb_=nb: E.activation(
                out=qa_e[:, o_:o_ + nb_], in_=qa_amax[:, o_:o_ + nb_], func=ACT.Ln,
                bias=TINYpp[:, :], scale=1.0), inc="AE")

        def stage_E1(i):
            par, nb = i % 2, job_nb(i)
            o = 64 * par
            w("vector", "AE")
            S.op("vector", lambda E, o_=o, nb_=nb: E.tensor_scalar(
                out=qa_e[:, o_:o_ + nb_], in0=qa_e[:, o_:o_ + nb_],
                scalar1=float(1.0 / LN2), scalar2=float(1e-11 - 0.5),
                op0=ALU.mult, op1=ALU.add), inc="VE")
            S.wait_at("vector", "VE", S.c("VE"))
            S.op("vector", lambda E, o_=o, nb_=nb: E.tensor_scalar(
                out=qa_e[:, o_:o_ + nb_], in0=qa_e[:, o_:o_ + nb_],
                scalar1=DK2pp[:, :], scalar2=EB1pp[:, :], op0=ALU.add, op1=ALU.subtract), inc="VE")
            S.wait_at("vector", "VE", S.c("VE"))
            S.op("vector", lambda E, o_=o, nb_=nb: E.tensor_scalar(
                out=qa_e[:, o_:o_ + nb_], in0=qa_e[:, o_:o_ + nb_],
                scalar1=K1, scalar2=3.0, op0=ALU.subtract, op1=ALU.subtract), inc="VE")

        def stage_X(i):
            par, nb = i % 2, job_nb(i)
            o = 64 * par
            w("scalar", "VE")
            S.op("scalar", lambda E, o_=o, nb_=nb: E.activation(
                out=qa_s0[:, o_:o_ + nb_], in_=qa_e[:, o_:o_ + nb_], func=ACT.Exp,
                scale=LN2), inc="AE")

        def stage_S(i):
            par, nb = i % 2, job_nb(i)
            o = 64 * par
            w("vector", "AE")
            for fn in (
                lambda E, o_=o, nb_=nb: E.tensor_scalar(
                    out=qa_m[:, o_:o_ + nb_], in0=qa_s0[:, o_:o_ + nb_], scalar1=K1,
                    scalar2=None, op0=ALU.mult),
                lambda E, o_=o, nb_=nb: E.tensor_tensor(
                    out=qa_s[:, o_:o_ + nb_], in0=qa_s0[:, o_:o_ + nb_],
                    in1=qa_m[:, o_:o_ + nb_], op=ALU.add),
                lambda E, o_=o, nb_=nb: E.tensor_tensor(
                    out=qa_s[:, o_:o_ + nb_], in0=qa_s[:, o_:o_ + nb_],
                    in1=qa_m[:, o_:o_ + nb_], op=ALU.subtract),
                lambda E, o_=o, nb_=nb: E.tensor_scalar(
                    out=qa_m[:, o_:o_ + nb_], in0=qa_s[:, o_:o_ + nb_], scalar1=K1,
                    scalar2=None, op0=ALU.mult),
                lambda E, o_=o, nb_=nb: E.tensor_scalar(
                    out=qa_eb[:, o_:o_ + nb_], in0=qa_s[:, o_:o_ + nb_], scalar1=K2,
                    scalar2=Dt[:, :], op0=ALU.mult, op1=ALU.mult),
                lambda E, o_=o, nb_=nb: E.tensor_tensor(
                    out=qa_eb[:, o_:o_ + nb_], in0=qa_eb[:, o_:o_ + nb_],
                    in1=qa_m[:, o_:o_ + nb_], op=ALU.subtract),
            ):
                S.op("vector", fn, inc="VE")
                S.wait_at("vector", "VE", S.c("VE"))

        def stage_Q(i):
            par, nb = i % 2, job_nb(i)
            buf = wstr0 if par == 0 else wstr1
            o = 64 * par
            v3 = buf.ap()[:, 0:32 * nb].rearrange("p (b e) -> p b e", e=32)
            q3 = wqnat.ap()[:, 2048 * par:2048 * par + 32 * nb].rearrange("p (b e) -> p b e", e=32)
            ebb = qa_eb.ap()[:, o:o + nb].unsqueeze(2).broadcast_to([128, nb, 32])
            S.wait_at("vector", "PE", S.c("PE"))
            S.op("vector", lambda E, a=v3, b=ebb: E.tensor_tensor(
                out=a, in0=a, in1=b, op=ALU.subtract), inc="VE")
            S.wait_at("vector", "VE", S.c("VE"))
            S.op("vector", lambda E, a=v3, b=ebb, q=q3: E.tensor_tensor(
                out=q, in0=a, in1=b, op=ALU.add), inc="VE")
            S.wait_at("tensor", "VE", S.c("VE"))

        def stage_T(i):
            kind, n = jobs[i]
            par = i % 2
            base = 2048 * par
            if kind == "fc":
                transpose8(lambda j, b=base: wqnat[:, b + 128 * j:b + 128 * j + 128],
                           lambda j0, jn, h_=n: wfcTq.ap().rearrange(
                               "p (j r) -> p j r", r=HID)[:, j0:j0 + jn,
                                                          128 * h_:128 * h_ + 128],
                           8, evac_eng="scalar", banks=[psE, psC])
            else:
                h_, hf = n // 2, n % 2
                transpose8(lambda j, b=base: wqnat[:, b + 128 * j:b + 128 * j + 128],
                           lambda j0, jn, hh=h_, f_=hf: wpjTq.ap().rearrange(
                               "p (j r) -> p j r", r=D)[:, 16 * f_ + j0:16 * f_ + j0 + jn,
                                                        128 * hh:128 * hh + 128],
                           16, evac_eng="scalar", banks=[psE, psC])

        NJOB = len(jobs)
        wf_totals[0] = job_dma(0)
        wf_totals[1] = job_dma(1)
        _wstate = {"i": 0}

        def w_step():
            i = _wstate["i"]
            if i > NJOB:
                return False
            _wstate["i"] = i + 1
            if i < NJOB:
                stage_R(i)
                stage_L(i)
            if i >= 1:
                stage_Q(i - 1)
                if i + 1 < NJOB:
                    S.wait_at("sync", "VE", S.c("VE"))
                    wf_totals[i + 1] = job_dma(i + 1)
            if i < NJOB:
                stage_E1(i)
                stage_X(i)
            if i >= 1:
                S.wait_at("tensor", "VE", S.c("VE"))
                stage_T(i - 1)
            if i < NJOB:
                stage_S(i)
            return True

        xs_ae = {}
        xs_view = xseq.rearrange("(c p) f -> p c f", p=128)
        NCH = 32
        xsqs = [xsq0, xsq1, xsq2]

        def xs_dma(ch):
            buf = xsqs[ch % 3]
            xs_sem = ["XS0", "XS1", "XS2"][ch % 3]
            S.op("gpsimd",
                 lambda E, b=buf, c0=ch * 2: E.dma_start(
                     out=b.ap().rearrange("p (t f) -> p t f", f=D),
                     in_=xs_view[:, c0:c0 + 2, :]),
                 inc=xs_sem, amt=16)
            return xs_sem, S.c(xs_sem)

        def xs_stats(ch, sem_tot):
            buf = xsqs[ch % 3]
            xs_sem, tot = sem_tot
            S.wait_at("vector", xs_sem, tot)
            for tt in range(2):
                t = ch * 2 + tt
                S.op("vector", lambda E, b=buf, o=tt: E.bn_stats(
                    out=bnscr[:, 0:6], in_=b[:, o * D:o * D + 512]), inc="VE")
                S.op("vector", lambda E, b=buf, o=tt: E.bn_stats(
                    out=bnscr[:, 6:12], in_=b[:, o * D + 512:o * D + 1024]), inc="VE")
                S.wait_at("vector", "VE", S.c("VE"))
                S.op("vector", lambda E, t_=t: E.bn_aggr(
                    out=mv_seq[:, 2 * t_:2 * t_ + 2], in_=bnscr[:, :]), inc="VE")
                S.wait_at("vector", "VE", S.c("VE"))
            mvs = mv_seq.ap().rearrange("p (t two) -> p t two", two=2)
            var_sl = mvs[:, ch * 2:(ch + 1) * 2, 1:2]
            mean_sl = mvs[:, ch * 2:(ch + 1) * 2, 0:1]
            lv = lnv_scr[:, ch * 2:(ch + 1) * 2]
            lv3 = lnv_scr.ap().rearrange("p (t one) -> p t one", one=1)[:, ch * 2:(ch + 1) * 2, :]
            S.op("vector", lambda E, v=var_sl, o=lv3: E.tensor_scalar(
                out=o, in0=v, scalar1=EPS, scalar2=None, op0=ALU.add), inc="VE")
            w("scalar", "VE")
            a_sl = a_seq[:, ch * 2:(ch + 1) * 2]
            S.op("scalar", lambda E, i=lv, o=a_sl: E.activation(
                out=o, in_=i, func=ACT.Ln), inc="AE")
            S.wait_at("scalar", "AE", S.c("AE"))
            S.op("scalar", lambda E, o=a_sl: E.activation(
                out=o, in_=o, func=ACT.Exp, scale=-0.5), inc="AE")
            xs_ae[ch] = S.c("AE")
            w("vector", "AE")
            am_sl3 = am_seq.ap().rearrange("p (t one) -> p t one", one=1)[:, ch * 2:(ch + 1) * 2, :]
            a_sl3 = a_seq.ap().rearrange("p (t one) -> p t one", one=1)[:, ch * 2:(ch + 1) * 2, :]
            S.op("vector", lambda E, a3=a_sl3, m3=mean_sl, o=am_sl3: E.tensor_tensor(
                out=o, in0=a3, in1=m3, op=ALU.mult), inc="VE")

        def xs_windows(ch):
            buf = xsqs[ch % 3]
            for tt in range(2):
                t = ch * 2 + tt
                ncols = 2 if t == 63 else 3
                pt = patt0 if t % 2 == 0 else patt1
                S.wait_at("gpsimd", "PE", S.c("PE"))
                S.op("gpsimd", lambda E, p=pt: E.memset(p[:, :], 0.0), inc="GE")
                S.wait_at("gpsimd", "GE", S.c("GE"))
                S.wait_at("gpsimd", "AE", xs_ae[ch])
                S.op("gpsimd", lambda E, t_=t, n=ncols, p=pt: E.tensor_scalar(
                    out=p[:, 2 * t_:2 * t_ + n],
                    in0=base3[:, 0:n], scalar1=a_seq[:, t_:t_ + 1],
                    scalar2=None, op0=ALU.mult), inc="GE")
                S.op("gpsimd", lambda E, t_=t: E.tensor_copy(
                    out=meanb[:, :], in_=mv_seq[:, 2 * t_:2 * t_ + 1]), inc="GE")
                S.wait_at("tensor", "GE", S.c("GE"))
                S.op("tensor", lambda E, b=buf, o=tt, p=pt: E.matmul(
                    psA[:, 0:512], p[:, :], b[:, o * D:o * D + 512],
                    start=False, stop=False))
                S.op("tensor", lambda E, b=buf, o=tt, p=pt: E.matmul(
                    psA[:, 512:1024], p[:, :], b[:, o * D + 512:o * D + 1024],
                    start=False, stop=False))
                S.op("tensor", lambda E, p=pt: E.matmul(
                    psD[:, 0:1], p[:, :], meanb[:, :],
                    start=False, stop=False), inc="PE")

        sems_ch = {}
        sems_ch[0] = xs_dma(0)
        sems_ch[1] = xs_dma(1)
        for ch in range(NCH + 1):
            if ch < NCH:
                xs_stats(ch, sems_ch[ch])
            if ch >= 1:
                xs_windows(ch - 1)
                if ch + 1 < NCH:
                    S.wait_at("gpsimd", "PE", S.c("PE"))
                    sems_ch[ch + 1] = xs_dma(ch + 1)
            w_step()
            if ch % 2 == 0:
                w_step()
        while w_step():
            pass

        S.op("tensor", lambda E: E.matmul(psA[:, 0:512], zeros_b[:], xsq0[:, 0:512],
                                          start=False, stop=True))
        S.op("tensor", lambda E: E.matmul(psA[:, 512:1024], zeros_b[:], xsq0[:, 512:1024],
                                          start=False, stop=True))
        S.op("tensor", lambda E: E.matmul(psD[:, 0:1], zeros_b[:], ones_col[:, :],
                                          start=False, stop=True), inc="PE")
        pe_summary_done = S.c("PE")

        S.wait_at("vector", "PE", pe_summary_done)
        S.op("vector", lambda E: E.tensor_copy(out=sum_bias[:], in_=psD[:, 0:1]), inc="VE")
        S.wait_at("vector", "VE", S.c("VE"))
        S.op("vector", lambda E: E.tensor_scalar(
            out=sum_bias[:], in0=sum_bias[:], scalar1=ilen[:, :], scalar2=-1.0,
            op0=ALU.mult, op1=ALU.mult), inc="VE")
        w("scalar", "VE")
        S.wait_at("scalar", "PE", pe_summary_done)
        S.op("scalar", lambda E: E.activation(
            out=summ_cm[:, :], in_=psA[:, :], func=ACT.Identity,
            bias=sum_bias[:, :], scale=ilen[:, :]), inc="AE")
        ae_summ = S.c("AE")
        S.wait_at("tensor", "AE", ae_summ)
        transpose8(lambda j: summ_cm[:, 128 * j:128 * j + 128],
                   lambda j0, jn: summ_fm.ap().rearrange(
                       "p (j c) -> p j c", c=128)[:, j0:j0 + jn, :], 8,
                   evac_eng="scalar", banks=[psE, psC])

        kdst = [psA[:, 0:128], psA[:, 512:640], psB[:, 0:128], psB[:, 512:640],
                psC[:, 0:128], psC[:, 512:640], psD[:, 0:128], psE[:, 0:128]]
        wk_v = wk_in.rearrange("(k p) f -> p k f", p=128)
        for hf in range(2):
            S.wait_at("gpsimd", "PE", S.c("PE"))
            S.op("gpsimd", lambda E, h=hf: E.dma_start(
                out=wbuf.ap().rearrange("p (k f) -> p k f", f=D),
                in_=wk_v[:, 4 * h:4 * h + 4, :]), inc="WBF", amt=16)
            S.wait_at("tensor", "WBF", S.c("WBF"))
            for m in range(8):
                for kl in range(4):
                    k = 4 * hf + kl
                    S.op("tensor", lambda E, m_=m, k_=k, kk=kl: E.matmul(
                        kdst[m_],
                        wbuf[:, kk * D + m_ * 128: kk * D + m_ * 128 + 128],
                        summ_fm[:, k_ * 128:k_ * 128 + 128],
                        start=(k_ == 0), stop=(k_ == 7)),
                        inc="PE" if (m == 7 and kl == 3) else None)
        w("scalar", "PE")
        for m in range(8):
            S.op("scalar", lambda E, m_=m: E.activation(
                out=kfm[:, m_ * 128:m_ * 128 + 128], in_=kdst[m_],
                func=ACT.Identity, bias=bkr[:, m_:m_ + 1]),
                inc="AE" if m == 7 else None)
        S.wait_at("tensor", "AE", S.c("AE"))

        wv_v = wv_in.rearrange("(k p) f -> p k f", p=128)
        for hf in range(2):
            S.wait_at("gpsimd", "PE", S.c("PE"))
            S.op("gpsimd", lambda E, h=hf: E.dma_start(
                out=wbuf.ap().rearrange("p (k f) -> p k f", f=D),
                in_=wv_v[:, 4 * h:4 * h + 4, :]), inc="WBF", amt=16)
            S.wait_at("tensor", "WBF", S.c("WBF"))
            for cl in (0, 512):
                vdst = psB[:, 0:512] if cl == 0 else psD[:, 0:512]
                for kl in range(4):
                    k = 4 * hf + kl
                    S.op("tensor", lambda E, k_=k, kk=kl, c=cl, dd=vdst: E.matmul(
                        dd, summ_fm[:, k_ * 128:k_ * 128 + 128],
                        wbuf[:, kk * D + c: kk * D + c + 512],
                        start=(k_ == 0), stop=False),
                        inc="PE" if (cl == 512 and kl == 3) else None)
            if hf == 1:
                S.wait_at("tensor", "LG", S.c("LG"))
                S.op("tensor", lambda E: E.matmul(
                    psB[:, 0:512], ones_row[0:1, :], bv_r[0:1, 0:512],
                    start=False, stop=True))
                S.op("tensor", lambda E: E.matmul(
                    psD[:, 0:512], ones_row[0:1, :], bv_r[0:1, 512:1024],
                    start=False, stop=True), inc="PE")
        w("scalar", "PE")
        S.op("scalar", lambda E: E.activation(out=vcm[:, 0:512], in_=psB[:, 0:512],
                                              func=ACT.Identity), inc="AE")
        S.op("scalar", lambda E: E.activation(out=vcm[:, 512:1024], in_=psD[:, 0:512],
                                              func=ACT.Identity), inc="AE")

        es_pro.close()
        es_ph1 = contextlib.ExitStack()
        sb1 = lambda name, shape, dt: es_ph1.enter_context(nc.sbuf_tensor(name, shape, dt))
        xring = sb1("xring", [128, 4 * D], F32)
        lnbuf = sb1("lnbuf", [128, D], BF16)
        ln1fm = sb1("ln1fm", [128, 8 * 512], BF16)
        qfm = sb1("qfm", [128, 8 * 512], BF16)
        ebuf = sb1("ebuf", [128, 2 * 16 * 128], BF16)
        xmid_t = sb1("xmid_t", [128, D], F32)
        mexp = sb1("mexp", [128, 2 * 16 * 128], BF16)
        ofm = ln1fm

        for e in ("sync", "gpsimd", "tensor", "vector", "scalar"):
            S.wait_at(e, "VE", S.c("VE"))
            S.wait_at(e, "AE", S.c("AE"))
            S.wait_at(e, "PE", S.c("PE"))

        st_count = 0
        msk_marks = {}
        exp_marks = {}

        def p1_loads_stats(s):
            S.wait_at("sync", "VE", S.c("VE"))
            for tt in range(4):
                t = 4 * s + tt
                slot = t % 4
                dma("sync", "XOW", xring[:, slot * D:(slot + 1) * D],
                    xown.rearrange("(t p) f -> p t f", p=128)[:, t, :])
            ld_xown = S.c("XOW")
            S.wait_at("vector", "XOW", ld_xown)
            for tt in range(4):
                t = 4 * s + tt
                slot = t % 4
                xo = xring[:, slot * D:(slot + 1) * D]
                S.op("vector", lambda E, x=xo: E.bn_stats(out=bnscr[:, 0:6], in_=x[:, 0:512]), inc="VE")
                S.op("vector", lambda E, x=xo: E.bn_stats(out=bnscr[:, 6:12], in_=x[:, 512:1024]), inc="VE")
                S.wait_at("vector", "VE", S.c("VE"))
                S.op("vector", lambda E, t_=t: E.bn_aggr(
                    out=mv_own[:, 2 * t_:2 * t_ + 2], in_=bnscr[:, :]), inc="VE")
                S.wait_at("vector", "VE", S.c("VE"))
            mvo = mv_own.ap().rearrange("p (t two) -> p t two", two=2)
            var_sl = mvo[:, 4 * s:4 * s + 4, 1:2]
            mean_sl = mvo[:, 4 * s:4 * s + 4, 0:1]
            lv3 = lnv_scr.ap().rearrange("p (t one) -> p t one", one=1)[:, 0:4, :]
            S.op("vector", lambda E, v=var_sl, o=lv3: E.tensor_scalar(
                out=o, in0=v, scalar1=EPS, scalar2=None, op0=ALU.add), inc="VE")
            w("scalar", "VE")
            a_sl = a_own[:, 4 * s:4 * s + 4]
            S.op("scalar", lambda E, o=a_sl: E.activation(
                out=o, in_=lnv_scr[:, 0:4], func=ACT.Ln), inc="AE")
            S.wait_at("scalar", "AE", S.c("AE"))
            S.op("scalar", lambda E, o=a_sl: E.activation(
                out=o, in_=o, func=ACT.Exp, scale=-0.5), inc="AE")
            w("vector", "AE")
            a3 = a_own.ap().rearrange("p (t one) -> p t one", one=1)[:, 4 * s:4 * s + 4, :]
            n3 = nam_own.ap().rearrange("p (t one) -> p t one", one=1)[:, 4 * s:4 * s + 4, :]
            S.op("vector", lambda E, a=a3, m=mean_sl, o=n3: E.tensor_tensor(
                out=o, in0=a, in1=m, op=ALU.mult), inc="VE")
            S.wait_at("vector", "VE", S.c("VE"))
            S.op("vector", lambda E, o=n3: E.tensor_scalar(
                out=nam_own[:, 4 * s:4 * s + 4], in0=nam_own[:, 4 * s:4 * s + 4],
                scalar1=-1.0, scalar2=None, op0=ALU.mult), inc="VE")

        def p1_lnq(s):
            S.wait_at("tensor", "VE", S.c("VE"))
            for tt in range(4):
                t = 4 * s + tt
                slot = t % 4
                xo = xring[:, slot * D:(slot + 1) * D]
                w("scalar", "VE")
                S.wait_at("scalar", "PE", S.c("PE"))
                S.op("scalar", lambda E, x=xo, t_=t: E.activation(
                    out=lnbuf[:, 0:D], in_=x, func=ACT.Identity,
                    scale=a_own[:, t_:t_ + 1], bias=nam_own[:, t_:t_ + 1]), inc="AE")
                S.wait_at("tensor", "AE", S.c("AE"))
                transpose8(lambda j: lnbuf[:, 128 * j:128 * j + 128],
                           lambda j0, jn, u=tt: ln1fm.ap().rearrange(
                               "p (j q) -> p j q", q=512)[:, j0:j0 + jn, 128 * u:128 * u + 128],
                           8, evac_eng="scalar", banks=[psE, psD])
            qdst = [psA[:, 0:512], psA[:, 512:1024], psB[:, 0:512], psB[:, 512:1024],
                    psC[:, 0:512], psC[:, 512:1024], psD[:, 0:512], psE[:, 0:512]]
            wq_v = wq_in.rearrange("(k p) f -> p k f", p=128)
            for hf in range(2):
                S.wait_at("gpsimd", "PE", S.c("PE"))
                S.op("gpsimd", lambda E, h=hf: E.dma_start(
                    out=wbuf.ap().rearrange("p (k f) -> p k f", f=D),
                    in_=wq_v[:, 4 * h:4 * h + 4, :]), inc="WBF", amt=16)
                S.wait_at("tensor", "WBF", S.c("WBF"))
                for m in range(8):
                    for kl in range(4):
                        k = 4 * hf + kl
                        S.op("tensor", lambda E, m_=m, k_=k, kk=kl: E.matmul(
                            qdst[m_], wbuf[:, kk * D + m_ * 128: kk * D + m_ * 128 + 128],
                            ln1fm[:, k_ * 512:(k_ + 1) * 512],
                            start=(k_ == 0), stop=(k_ == 7)),
                            inc="PE" if (m == 7 and kl == 3) else None)
            w("scalar", "PE")
            for m in range(8):
                S.op("scalar", lambda E, m_=m: E.activation(
                    out=qfm[:, m_ * 512:(m_ + 1) * 512], in_=qdst[m_],
                    func=ACT.Identity, bias=bqr[:, m_:m_ + 1]),
                    inc="AE" if m == 7 else None)
            S.wait_at("tensor", "AE", S.c("AE"))
            _evmark[id(psD)] = ("AE", S.c("AE"))
            _evmark[id(psE)] = ("AE", S.c("AE"))
            S.wait_at("gpsimd", "PE", S.c("PE"))
            wo_v = wo_in.rearrange("(k p) f -> p k f", p=128)
            S.op("gpsimd", lambda E: E.dma_start(
                out=wbuf.ap().rearrange("p (k f) -> p k f", f=D),
                in_=wo_v[:, 0:4, :]), inc="WBF", amt=16)

        def p1_scores(s, u, epar):
            t = 4 * s + u
            S.wait_at("sync", "GE", S.c("GE"))
            msem = "MSK0" if u % 2 == 0 else "MSK1"
            dma("sync", msem,
                mexp.ap().rearrange("p (s_ g c) -> p s_ g c", s_=2, c=128)[:, u % 2, :, :],
                mask_in.rearrange("(t p) c -> p t c", p=128)[:, t, :]
                .unsqueeze(1).broadcast_to([128, 16, 128]))
            msk_marks[t] = (msem, S.c(msem))
            S.wait_at("tensor", "AE", S.c("AE"))
            for m in range(8):
                qsl_e = qfm[0:64, m * 512 + 128 * u: m * 512 + 128 * u + 128]
                qsl_o = qfm[64:128, m * 512 + 128 * u: m * 512 + 128 * u + 128]
                ksl_e = kfm[0:64, m * 128:(m + 1) * 128]
                ksl_o = kfm[64:128, m * 128:(m + 1) * 128]
                dstA = psA[:, 128 * (m % 4):128 * (m % 4) + 128] if m < 4 else \
                    psB[:, 128 * (m % 4):128 * (m % 4) + 128]
                dstB = psA[:, 512 + 128 * (m % 4):512 + 128 * (m % 4) + 128] if m < 4 else \
                    psB[:, 512 + 128 * (m % 4):512 + 128 * (m % 4) + 128]
                S.op("tensor", lambda E, q=qsl_e, kk=ksl_e, d=dstA: E.matmul(
                    d, q, kk, start=True, stop=True, tile_position=(0, 0)))
                S.op("tensor", lambda E, q=qsl_o, kk=ksl_o, d=dstB: E.matmul(
                    d, q, kk, start=True, stop=True, tile_position=(64, 0)),
                    inc="PE" if m == 7 else None)
            w("scalar", "PE")
            S.wait_at("scalar", "PE", S.c("PE"))
            eb = 2048 * epar
            S.op("scalar", lambda E, o=eb: E.activation(
                out=ebuf[:, o:o + 1024], in_=psA[:, :], func=ACT.Exp, scale=0.125), inc="AE")
            S.op("scalar", lambda E, o=eb: E.activation(
                out=ebuf[:, o + 1024:o + 2048], in_=psB[:, :], func=ACT.Exp, scale=0.125), inc="AE")
            exp_marks[t] = S.c("AE")

        def p1_softmax_av(s, u, epar):
            t = 4 * s + u
            eb = 2048 * epar
            e3 = ebuf.ap()[:, eb:eb + 2048].rearrange("p (g c) -> p g c", c=128)
            m3 = mexp.ap().rearrange("p (s_ g c) -> p s_ g c", s_=2, c=128)[:, u % 2, :, :]
            S.wait_at("gpsimd", "AE", exp_marks[t])
            S.wait_at("gpsimd", msk_marks[t][0], msk_marks[t][1])
            S.op("gpsimd", lambda E, e=e3, mk=m3: E.tensor_tensor(
                out=e, in0=e, in1=mk, op=ALU.mult), inc="GE")
            w("vector", "GE")
            S.op("vector", lambda E, e=e3: E.tensor_reduce(
                out=ssum[:, :], in_=e, axis=AXX, op=ALU.add), inc="VE")
            S.wait_at("vector", "VE", S.c("VE"))
            S.op("vector", lambda E: E.reciprocal(out=srcp[:, :], in_=ssum[:, :]), inc="VE")
            S.wait_at("vector", "VE", S.c("VE"))
            rb = srcp.ap().unsqueeze(2).broadcast_to([128, 16, 128])
            S.op("vector", lambda E, e=e3, r=rb: E.tensor_tensor(
                out=e, in0=e, in1=r, op=ALU.mult), inc="VE")
            S.wait_at("tensor", "VE", S.c("VE"))
            transpose8(lambda j, o=eb: ebuf[:, o + 128 * j:o + 128 * j + 128],
                       lambda j0, jn, o=eb: ebuf.ap()[:, o:o + 2048].rearrange(
                           "p (j c) -> p j c", c=128)[:, j0:j0 + jn, :],
                       16, evac_eng="scalar", banks=[psE, psD])
            for m0 in range(0, 8, 4):
                for mm in range(4):
                    m = m0 + mm
                    g_e = (m // 4) * 8 + (m % 4)
                    g_o = (m // 4) * 8 + 4 + (m % 4)
                    h_e, h_o = 2 * m, 2 * m + 1
                    S.op("tensor", lambda E, g=g_e, h=h_e, c=128 * mm, o=eb: E.matmul(
                        psD[0:64, c:c + 128], vcm[:, 64 * h:64 * h + 64],
                        ebuf[:, o + 128 * g:o + 128 * g + 128], start=True, stop=True,
                        tile_position=(0, 0)))
                    S.op("tensor", lambda E, g=g_o, h=h_o, c=128 * mm, o=eb: E.matmul(
                        psD[64:128, c:c + 128], vcm[:, 64 * h:64 * h + 64],
                        ebuf[:, o + 128 * g:o + 128 * g + 128], start=True, stop=True,
                        tile_position=(0, 64)), inc="PE" if mm == 3 else None)
                S.wait_at("scalar", "PE", S.c("PE"))
                dst3 = ofm.ap().rearrange("p (m q) -> p m q", q=512)[
                    :, m0:m0 + 4, 128 * u:128 * u + 128]
                ps3 = psD.ap().rearrange("p (m c) -> p m c", c=128)
                S.op("scalar", lambda E, d=dst3, s_=ps3: E.activation(
                    out=d, in_=s_, func=ACT.Identity), inc="AE")
                S.wait_at("tensor", "AE", S.c("AE"))
                _evmark[id(psD)] = ("AE", S.c("AE"))

        def p1_wo_res(s):
            global_st = None
            odst = [(psA[:, 0:512], psA[:, 512:1024]), (psB[:, 0:512], psB[:, 512:1024]),
                    (psC[:, 0:512], psC[:, 512:1024]), (psD[:, 0:512], psE[:, 0:512])]
            wo_v = wo_in.rearrange("(k p) f -> p k f", p=128)
            for hf in range(2):
                if hf == 1:
                    S.wait_at("gpsimd", "PE", S.c("PE"))
                    S.op("gpsimd", lambda E: E.dma_start(
                        out=wbuf.ap().rearrange("p (k f) -> p k f", f=D),
                        in_=wo_v[:, 4:8, :]), inc="WBF", amt=16)
                S.wait_at("tensor", "WBF", S.c("WBF"))
                for u in range(4):
                    for half in range(2):
                        cl = half * 512
                        dd = odst[u][half]
                        for ml in range(4):
                            m = 4 * hf + ml
                            S.op("tensor", lambda E, m_=m, mm=ml, c=cl, u_=u, d=dd: E.matmul(
                                d, ofm[:, m_ * 512 + 128 * u_: m_ * 512 + 128 * u_ + 128],
                                wbuf[:, mm * D + c: mm * D + c + 512],
                                start=(m_ == 0), stop=False),
                                inc="PE" if (u == 3 and half == 1 and ml == 3) else None)
                        if hf == 1:
                            S.wait_at("tensor", "LG", S.c("LG"))
                            S.op("tensor", lambda E, c=cl, d=dd: E.matmul(
                                d, ones_row[0:1, :], bo_r[0:1, c:c + 512],
                                start=False, stop=True),
                                inc="PE" if (u == 3 and half == 1) else None)
            w("vector", "PE")
            for u in range(4):
                t = 4 * s + u
                slot = t % 4
                xo = xring[:, slot * D:(slot + 1) * D]
                S.op("vector", lambda E, d0=odst[u][0], x=xo: E.tensor_tensor(
                    out=xmid_t[:, 0:512], in0=d0, in1=x[:, 0:512], op=ALU.add), inc="VE")
                S.op("vector", lambda E, d1=odst[u][1], x=xo: E.tensor_tensor(
                    out=xmid_t[:, 512:1024], in0=d1, in1=x[:, 512:1024], op=ALU.add), inc="VE")
                S.wait_at("sync", "VE", S.c("VE"))
                S.op("sync", lambda E, t_=t: E.dma_start(
                    out=xmid_hbm.rearrange("(t p) f -> p t f", p=128)[:, t_, :],
                    in_=xmid_t[:, :]), inc="XM", amt=16)
                S.wait_at("vector", "XM", S.c("XM"))
            _evmark[id(psD)] = ("VE", S.c("VE"))
            _evmark[id(psE)] = ("VE", S.c("VE"))

        p1_next_loads_stats = p1_loads_stats

        p1_loads_stats(0)
        for s in range(NSUP):
            p1_lnq(s)
            p1_scores(s, 0, 0)
            for u in range(4):
                if u + 1 < 4:
                    p1_scores(s, u + 1, (u + 1) % 2)
                p1_softmax_av(s, u, u % 2)
            p1_wo_res(s)
            if s + 1 < NSUP:
                p1_next_loads_stats(s + 1)
        es_ph1.close()
        es_ph2 = contextlib.ExitStack()
        sb2 = lambda name, shape, dt: es_ph2.enter_context(nc.sbuf_tensor(name, shape, dt))
        x2r0 = sb2("x2r0", [128, D], F32)
        x2r1 = sb2("x2r1", [128, D], F32)
        xln2 = sb2("xln2", [128, D], F32)
        xqb = sb2("xqb", [128, D], BF16)
        xqfm = sb2("xqfm", [128, 2 * 8 * 128], BF16)
        hbuf = sb2("hbuf", [128, 4 * D], F32)
        hqb = sb2("hqb", [128, D], BF16)
        hqT = sb2("hqT", [128, 2 * 8 * 128], BF16)
        bfc_r = sb2("bfc_r", [1, HID], BF16)
        bpj_r = sb2("bpj_r", [1, D], BF16)
        for e in ("sync", "gpsimd", "tensor", "vector", "scalar"):
            S.wait_at(e, "XM", 16 * NT)
        dma("gpsimd", "LG", bfc_r[:], bfc_in)
        dma("gpsimd", "LG", bpj_r[:], bpj_in)
        lg_bias2 = S.c("LG")
        S.wait_at("tensor", "LG", lg_bias2)

        def p2_front(t):
            buf = x2r0 if t % 2 == 0 else x2r1
            st_sem = "ST0" if t % 2 == 0 else "ST1"
            S.wait_at("sync", st_sem, S.c(st_sem))
            x2_sem = "X20" if t % 2 == 0 else "X21"
            dma("sync", x2_sem, buf[:, :], xmid_hbm.rearrange("(t p) f -> p t f", p=128)[:, t, :])
            S.wait_at("vector", x2_sem, S.c(x2_sem))
            S.op("vector", lambda E, b=buf: E.bn_stats(out=bnscr[:, 0:6], in_=b[:, 0:512]), inc="VE")
            S.op("vector", lambda E, b=buf: E.bn_stats(out=bnscr[:, 6:12], in_=b[:, 512:1024]), inc="VE")
            S.wait_at("vector", "VE", S.c("VE"))
            S.op("vector", lambda E: E.bn_aggr(out=mv_own[:, 0:2], in_=bnscr[:, :]), inc="VE")
            S.wait_at("vector", "VE", S.c("VE"))
            S.op("vector", lambda E: E.tensor_scalar(
                out=lnv2[:, 0:1], in0=mv_own[:, 1:2], scalar1=EPS, scalar2=None,
                op0=ALU.add), inc="VE")
            w("scalar", "VE")
            S.op("scalar", lambda E: E.activation(out=a2c[:], in_=lnv2[:, 0:1], func=ACT.Ln), inc="AE")
            S.wait_at("scalar", "AE", S.c("AE"))
            S.op("scalar", lambda E: E.activation(out=a2c[:], in_=a2c[:], func=ACT.Exp,
                                                  scale=-0.5), inc="AE")
            w("vector", "AE")
            S.op("vector", lambda E: E.tensor_tensor(out=nam2c[:], in0=a2c[:],
                                                     in1=mv_own[:, 0:1], op=ALU.mult), inc="VE")
            S.wait_at("vector", "VE", S.c("VE"))
            S.op("vector", lambda E: E.tensor_scalar(out=nam2c[:], in0=nam2c[:], scalar1=-1.0,
                                                     scalar2=None, op0=ALU.mult), inc="VE")
            S.wait_at("vector", "VE", S.c("VE"))
            S.op("vector", lambda E, b=buf: E.tensor_scalar(
                out=xln2[:, :], in0=b[:, :], scalar1=a2c[:, :], scalar2=nam2c[:, :],
                op0=ALU.mult, op1=ALU.add), inc="VE")
            S.wait_at("vector", "VE", S.c("VE"))
            o = 128
            v3 = xln2.ap().rearrange("p (b e) -> p b e", e=32)
            S.op("vector", lambda E, v=v3, o_=o: E.tensor_reduce(
                out=qa_amax[:, o_:o_ + 32], in_=v, axis=AXX, op=ALU.max,
                apply_absolute_value=True), inc="VE")
            S.wait_at("vector", "VE", S.c("VE"))
            quant_consts(32, qa_amax[:, o:o + 32], qa_e[:, o:o + 32], qa_s0[:, o:o + 32],
                         qa_s[:, o:o + 32], qa_m[:, o:o + 32], qa_eb[:, o:o + 32])
            ebb = qa_eb.ap()[:, o:o + 32].unsqueeze(2).broadcast_to([128, 32, 32])
            q3 = xqb.ap().rearrange("p (b e) -> p b e", e=32)
            S.wait_at("vector", "PE", S.c("PE"))
            quant_apply(v3, v3, q3, ebb)
            S.wait_at("tensor", "VE", S.c("VE"))
            xpar = t % 2
            transpose8(lambda j: xqb[:, 128 * j:128 * j + 128],
                       lambda j0, jn, xp=xpar: xqfm.ap().rearrange(
                           "p (j c) -> p j c", c=128)[:, 8 * xp + j0:8 * xp + j0 + jn, :],
                       8, evac_eng="scalar", banks=[psE, psD])

        def p2_fc1(t, g):
            xpar = t % 2
            dst = psA if g % 2 == 0 else psB
            for k in range(8):
                S.op("tensor", lambda E, k_=k, g_=g, d=dst, xp=xpar: E.matmul(
                    d[:, 0:512], xqfm[:, 1024 * xp + 128 * k_:1024 * xp + 128 * k_ + 128],
                    wfcTq[:, k_ * HID + g_ * D: k_ * HID + g_ * D + 512],
                    start=(k_ == 0), stop=False))
                S.op("tensor", lambda E, k_=k, g_=g, d=dst, xp=xpar: E.matmul(
                    d[:, 512:1024], xqfm[:, 1024 * xp + 128 * k_:1024 * xp + 128 * k_ + 128],
                    wfcTq[:, k_ * HID + g_ * D + 512: k_ * HID + g_ * D + 1024],
                    start=(k_ == 0), stop=False))
            S.op("tensor", lambda E, g_=g, d=dst: E.matmul(
                d[:, 0:512], ones_row[0:1, :], bfc_r[0:1, g_ * D:g_ * D + 512],
                start=False, stop=True))
            S.op("tensor", lambda E, g_=g, d=dst: E.matmul(
                d[:, 512:1024], ones_row[0:1, :], bfc_r[0:1, g_ * D + 512:g_ * D + 1024],
                start=False, stop=True), inc="PE")

        ae_gmark = {}

        def p2_gelu(t, g):
            dst = psA if g % 2 == 0 else psB
            w("scalar", "PE")
            S.wait_at("scalar", "VE", S.c("VE"))
            S.op("scalar", lambda E, d=dst, s_=g: E.activation(
                out=hbuf[:, s_ * D:(s_ + 1) * D], in_=d[:, :], func=ACT.Gelu), inc="AE")
            ae_gmark[(t, g)] = S.c("AE")

        ve_rmark = {}

        def p2_chain(pair):
            o = 64 * pair
            h2 = hbuf.ap()[:, 2 * pair * D:(2 * pair + 2) * D].rearrange(
                "p (b e) -> p b e", e=32)
            S.op("vector", lambda E, v=h2, o_=o: E.tensor_reduce(
                out=qa_amax[:, o_:o_ + 64], in_=v, axis=AXX, op=ALU.max,
                apply_absolute_value=True), inc="VE")
            S.wait_at("vector", "VE", S.c("VE"))
            quant_consts(64, qa_amax[:, o:o + 64], qa_e[:, o:o + 64], qa_s0[:, o:o + 64],
                         qa_s[:, o:o + 64], qa_m[:, o:o + 64], qa_eb[:, o:o + 64])

        def p2_group(t, g):
            gs = g % 2
            hg = hbuf.ap()[:, g * D:(g + 1) * D].rearrange("p (b e) -> p b e", e=32)
            ebg = qa_eb.ap()[:, 32 * g:32 * g + 32].unsqueeze(2).broadcast_to([128, 32, 32])
            qg = hqb.ap().rearrange("p (b e) -> p b e", e=32)
            S.wait_at("vector", "PE", S.c("PE"))
            quant_apply(hg, hg, qg, ebg)
            S.wait_at("tensor", "VE", S.c("VE"))
            if g == 0 and (t - 1) in ve_rmark:
                S.wait_at("tensor", "VE", ve_rmark[t - 1])
            transpose8(lambda j: hqb[:, 128 * j:128 * j + 128],
                       lambda j0, jn, s_=gs: hqT.ap().rearrange(
                           "p (j c) -> p j c", c=128)[:, 8 * s_ + j0:8 * s_ + j0 + jn, :],
                       8, evac_eng="scalar", banks=[psE, psD])
            for half in range(2):
                cl = half * 512
                for cj in range(8):
                    c = 8 * g + cj
                    S.op("tensor", lambda E, c_=c, cc=cl, cjj=cj, s_=gs: E.matmul(
                        psC[:, cc:cc + 512],
                        hqT[:, 1024 * s_ + 128 * cjj:1024 * s_ + 128 * cjj + 128],
                        wpjTq[:, c_ * D + cc: c_ * D + cc + 512],
                        start=(c_ == 0), stop=False))
                if g == 3:
                    S.op("tensor", lambda E, cc=cl: E.matmul(
                        psC[:, cc:cc + 512], ones_row[0:1, :], bpj_r[0:1, cc:cc + 512],
                        start=False, stop=True),
                        inc="PE" if half == 1 else None)

        def p2_back(t, do_front):
            p2_gelu(t, 0)
            if (t, -1) in ae_gmark:
                S.wait_at("tensor", "AE", ae_gmark[(t, -1)])
            p2_fc1(t, 1)
            p2_gelu(t, 1)
            S.wait_at("vector", "AE", ae_gmark[(t, 1)])
            p2_chain(0)
            S.wait_at("tensor", "AE", ae_gmark[(t, 0)])
            p2_fc1(t, 2)
            p2_gelu(t, 2)
            S.wait_at("tensor", "AE", ae_gmark[(t, 1)])
            p2_fc1(t, 3)
            p2_gelu(t, 3)
            p2_group(t, 0)
            p2_group(t, 1)
            if do_front:
                p2_front(t + 1)
            S.wait_at("vector", "AE", ae_gmark[(t, 3)])
            p2_chain(1)
            p2_group(t, 2)
            p2_group(t, 3)
            w("vector", "PE")
            buf = x2r0 if t % 2 == 0 else x2r1
            S.op("vector", lambda E, b=buf: E.tensor_tensor(
                out=b[:, :], in0=psC[:, :], in1=b[:, :], op=ALU.add), inc="VE")
            ve_rmark[t] = S.c("VE")
            S.wait_at("sync", "VE", ve_rmark[t])
            st_sem = "ST0" if t % 2 == 0 else "ST1"
            S.op("sync", lambda E, t_=t, b=buf: E.dma_start(
                out=y_out.rearrange("(t p) f -> p t f", p=128)[:, t_, :],
                in_=b[:, :]), inc=st_sem, amt=16)

        p2_front(0)
        for t in range(NT):
            if t >= 1:
                S.wait_at("tensor", "AE", ae_gmark[(t - 1, 2)])
                ae_gmark[(t, -1)] = ae_gmark[(t - 1, 3)]
            p2_fc1(t, 0)
            p2_back(t, t + 1 < NT)

        S.wait_at("sync", "ST0", 16 * (NT // 2))
        S.wait_at("sync", "ST1", 16 * (NT // 2))

        es_ph2.close()
        def runner(eng_name):
            def run(E):
                for item in S.q[eng_name]:
                    if item[0] == "w":
                        E.wait_ge(sems[item[1]], item[2])
                    else:
                        ins = item[1](E)
                        if item[2] is not None:
                            ins.then_inc(sems[item[2]], item[3])
            return run

        block.sync(runner("sync"))
        block.gpsimd(runner("gpsimd"))
        block.tensor(runner("tensor"))
        block.vector(runner("vector"))
        block.scalar(runner("scalar"))

    return nc


_NC_CACHE = None


def _get_nc():
    global _NC_CACHE
    if _NC_CACHE is None:
        _NC_CACHE = build_nc()
    return _NC_CACHE


def kernel(L0_attn_x, L0_attn_Wq, L0_attn_bq, L0_attn_Wk, L0_attn_bk, L0_attn_Wv,
           L0_attn_bv, L0_attn_Wo, L0_attn_bo, L0_ffn_w_fc, L0_ffn_b_fc,
           L0_ffn_w_proj, L0_ffn_b_proj):
    x = np.ascontiguousarray(np.asarray(L0_attn_x, dtype=np.float32))
    B = x.shape[0]
    wq = np.ascontiguousarray(np.asarray(L0_attn_Wq, dtype=np.float32))
    wk = np.ascontiguousarray(np.asarray(L0_attn_Wk, dtype=np.float32))
    wv = np.ascontiguousarray(np.asarray(L0_attn_Wv, dtype=np.float32))
    wo = np.ascontiguousarray(np.asarray(L0_attn_Wo, dtype=np.float32))
    bq = np.asarray(L0_attn_bq, dtype=np.float32)
    bk = np.asarray(L0_attn_bk, dtype=np.float32)
    bv = np.asarray(L0_attn_bv, dtype=np.float32).reshape(1, D)
    bo = np.asarray(L0_attn_bo, dtype=np.float32).reshape(1, D)
    wfc = np.ascontiguousarray(np.asarray(L0_ffn_w_fc, dtype=np.float32))
    bfc = np.asarray(L0_ffn_b_fc, dtype=np.float32).reshape(1, HID)
    wpj = np.ascontiguousarray(np.asarray(L0_ffn_w_proj, dtype=np.float32))
    bpj = np.asarray(L0_ffn_b_proj, dtype=np.float32).reshape(1, D)

    bqr = np.ascontiguousarray(bq.reshape(8, 128).T)
    bkr = np.ascontiguousarray(bk.reshape(8, 128).T)

    in_maps = []
    for c in range(8):
        b = c // 4
        J = c % 4
        pos = 2048 * J + np.arange(2048)[:, None]
        ends = 64 * np.arange(128)[None, :]
        mask01 = (ends <= pos).astype(ml_dtypes.bfloat16)
        in_maps.append({
            "xseq": x[b],
            "xown": np.ascontiguousarray(x[b, 2048 * J:2048 * (J + 1)]),
            "wq_in": wq, "wk_in": wk, "wv_in": wv, "wo_in": wo,
            "bqr_in": bqr, "bkr_in": bkr,
            "bv_in": bv, "bo_in": bo, "bfc_in": bfc, "bpj_in": bpj,
            "wfc_in": wfc, "wpj_in": wpj,
            "mask_in": mask01,
        })
    nc = _get_nc()
    res = run_bass_kernel_spmd(nc, in_maps, core_ids=list(range(8)))
    out = np.empty((B, T_SEQ, D), dtype=np.float32)
    for c in range(8):
        b, J = c // 4, c % 4
        out[b, 2048 * J:2048 * (J + 1)] = res.results[c]["y"]
    return out
